# version 03
# baseline (speedup 1.0000x reference)
import numpy as np
import ml_dtypes
import concourse.bass as bass
import concourse.mybir as mybir
from concourse.bass_utils import run_bass_kernel_spmd
from contextlib import ExitStack

F32 = mybir.dt.float32
BF16 = mybir.dt.bfloat16
I32 = mybir.dt.int32
U8 = mybir.dt.uint8
AF = mybir.ActivationFunctionType
ALU = mybir.AluOpType
AX = mybir.AxisListType

D_MODEL = 1024
D_STATE = 16
HEADDIM = 64
CHUNK = 64
D_INNER = 2048
NHEADS = 32
CONV_DIM = 2080
D_IN_PROJ = 4160
D_HIDDEN = 4096
SEQ = 8192
SEG = 2048
NSEG = 4
GRID = 2176
NCH = 34
NSUB = 17
KC = 8
EPS = 1e-5
SLOT = 192
SCW = 704

ENGINES = ("sync", "vector", "scalar", "tensor", "gpsimd")
PA_TILES = [(0, 512), (512, 512), (1024, 512), (1536, 512), (2048, 128)]


def bcast_free(ap_col, part, dims):
    return bass.AP(ap_col.tensor, ap_col.offset,
                   [[ap_col.ap[0][0], part]] + [list(d) for d in dims])



def _host_shards(x):
    shards = []
    for core in range(8):
        b, j = core // NSEG, core % NSEG
        start = j * SEG
        lo = start - CHUNK
        g = np.zeros((GRID, D_MODEL), np.float32)
        s0, s1 = max(lo, 0), min(start + SEG + CHUNK, SEQ)
        g[s0 - lo:s1 - lo] = x[b, s0:s1]
        h3 = np.zeros((4, D_MODEL), np.float32)
        h0, h1 = max(lo - 3, 0), max(lo, 0)
        h3[3 - (h1 - h0):3] = x[b, h0:h1]
        maskrow = np.ones((NSUB * 128,), np.float32)
        if j == 0:
            maskrow[CHUNK - 1] = 0.0
        if j == NSEG - 1:
            maskrow[SEG + CHUNK] = 0.0
        mask_tm = np.ascontiguousarray(maskrow.reshape(NSUB, 128).T)
        pm = np.zeros((128, 4), np.float32)
        for k in (1, 2, 3):
            pm[:, k - 1] = 1.0 if j >= k else 0.0
        pm[:, 3] = 1.0 if j >= 1 else 0.0
        slot = np.zeros((4, 1), np.int32)
        for k in (1, 2, 3):
            slot[k - 1, 0] = SLOT * max(j - k, 0)
        shards.append(dict(xg=g, xh3=h3, maskrow=mask_tm, pmc=pm, slotoff=slot))
    return shards


def _host_consts(inputs):
    c = {}
    c["idn"] = np.eye(128, dtype=np.float32)
    cp = np.zeros((128, 4), np.float32)
    cp[:, 0] = 1e-5; cp[:, 1] = 1.0; cp[:, 2] = -0.5; cp[:, 3] = 0.0
    c["cpool"] = cp
    e = np.zeros((32, 64), np.float32); e[0, :] = 1.0
    c["e064"] = e
    e2 = np.zeros((32, 128), np.float32); e2[0, :] = 1.0
    c["e0128"] = e2
    c["onescol"] = np.ones((128, 1), np.float32)
    hsel = np.zeros((32, 128), np.float32)
    for h in range(32):
        g = h // 11
        hsel[h, 32 * g:32 * g + 16] = 1.0
    c["hsel"] = hsel
    gmask = np.zeros((32, SCW), np.float32)
    for h in range(32):
        i = h - 11 * (h // 11)
        gmask[h, 64 * i:64 * i + 64] = 1.0
    c["gmask"] = gmask
    negio = np.zeros((32, 2048), np.float32)
    for h in range(32):
        negio[h, 64 * h:64 * h + 64] = -1.0
    c["negio"] = negio
    t = np.zeros((64, 64), np.float32)
    for s in range(64):
        t[s, s:] = 1.0
    c["tri"] = t
    dtb = np.asarray(inputs["L0_attn_dt_bias"], np.float32)
    c["dtbrep"] = np.ascontiguousarray(np.tile(dtb[None, :], (128, 1)))
    Dsk = np.asarray(inputs["L0_attn_D"], np.float32)
    c["dskrep"] = np.ascontiguousarray(
        np.tile(np.repeat(Dsk, HEADDIM)[None, :], (128, 1)))
    nw = np.asarray(inputs["L0_attn_norm_w"], np.float32)
    c["nwrep"] = np.ascontiguousarray(np.tile(nw[None, :], (128, 1)))
    c["alog"] = np.ascontiguousarray(
        np.asarray(inputs["L0_attn_A_log"], np.float32).reshape(32, 1))
    cw = np.asarray(inputs["L0_attn_conv_w"], np.float32)
    cwp = np.zeros((17 * 128, 4), np.float32)
    cwp[:CONV_DIM] = cw
    c["convw"] = np.ascontiguousarray(
        cwp.reshape(17, 128, 4).transpose(1, 0, 2).reshape(128, 68))
    cb = np.asarray(inputs["L0_attn_conv_b"], np.float32)
    cbp = np.zeros((17 * 128,), np.float32)
    cbp[:CONV_DIM] = cb
    c["convb"] = np.ascontiguousarray(cbp.reshape(17, 128).T)
    fcw = np.asarray(inputs["L0_ffn_conv_w"], np.float32).reshape(D_HIDDEN, 3)
    c["fcw"] = np.ascontiguousarray(
        fcw.reshape(32, 128, 3).transpose(1, 0, 2).reshape(128, 96))
    fcb = np.asarray(inputs["L0_ffn_conv_b"], np.float32)
    c["fcb"] = np.ascontiguousarray(fcb.reshape(32, 128).T)
    c["wint"] = np.ascontiguousarray(np.asarray(inputs["L0_attn_W_in"], np.float32).T)
    c["wout"] = np.ascontiguousarray(np.asarray(inputs["L0_attn_W_out"], np.float32).T)
    c["w1t"] = np.ascontiguousarray(np.asarray(inputs["L0_ffn_w1"], np.float32).T)
    c["w2t"] = np.ascontiguousarray(np.asarray(inputs["L0_ffn_w2"], np.float32).T)
    return c



class P:
    def __init__(self, nc, ctx):
        self.nc = nc
        self.ops = {e: [] for e in ENGINES}
        self.h = {}
        self.n = {}
        for name in ("qs", "qg", "sv", "sa", "sp", "sg", "cc", "st",
                     "qw", "qe", "q1", "q2", "q3", "q4", "q5", "q6", "q7",
                     "q8", "q9", "q10", "q11", "q12", "q13", "q14", "q15",
                     "q16", "q17", "q18", "q19", "q20", "q21", "q22", "q23",
                     "q24", "q25", "qt1", "qt2", "qt3", "qt4", "qt5"):
            self.h[name] = ctx.enter_context(nc.semaphore(name))
            self.n[name] = 0
        self.env = {}

    def do(self, eng, f, inc=None):
        if inc is not None:
            sem, amt = inc
            self.n[sem] += amt
            handle = self.h[sem]

            def g(e, f=f, handle=handle, amt=amt):
                f(e).then_inc(handle, amt)
        else:
            def g(e, f=f):
                f(e)
        self.ops[eng].append(g)

    def w(self, eng, sem, val=None):
        v = self.n[sem] if val is None else val
        h = self.h[sem]
        self.ops[eng].append(lambda e, h=h, v=v: e.wait_ge(h, v))

    def barrier(self, engs=ENGINES, sems=None):
        if sems is None:
            sems = [k for k in self.n if k != "st"]
        for eng in engs:
            for s in sems:
                if self.n[s] > 0:
                    self.w(eng, s)

    def emit(self, block):
        for e in ENGINES:
            fs = self.ops[e]
            getattr(block, e)(lambda eng, fs=fs: [f(eng) for f in fs])



def build_nc():
    nc = bass.Bass("TRN2", target_bir_lowering=False, dynamic_dma_scratch_size=8192)
    ctx = ExitStack()

    def din(name, shape, dtype=F32):
        return nc.dram_tensor(name, list(shape), dtype, kind="ExternalInput").ap()

    def dint(name, shape, dtype=F32):
        return nc.dram_tensor(name, list(shape), dtype, kind="Internal").ap()

    def sb(name, shape, dtype=F32):
        return ctx.enter_context(nc.sbuf_tensor(name, shape, dtype))

    xg_d = din("xg", [GRID, D_MODEL])
    xh3_d = din("xh3", [4, D_MODEL])
    maskrow_d = din("maskrow", [128, NSUB])
    pmc_d = din("pmc", [128, 4])
    slotoff_d = din("slotoff", [4, 1], I32)
    idn_d = din("idn", [128, 128])
    cpool_d = din("cpool", [128, 4])
    e064_d = din("e064", [32, 64])
    e0128_d = din("e0128", [32, 128])
    onescol_d = din("onescol", [128, 1])
    hsel_d = din("hsel", [32, 128])
    gmask_d = din("gmask", [32, SCW])
    negio_d = din("negio", [32, 2048])
    tri_d = din("tri", [64, 64])
    dtbrep_d = din("dtbrep", [128, 32])
    dskrep_d = din("dskrep", [128, D_INNER])
    nwrep_d = din("nwrep", [128, D_INNER])
    alog_d = din("alog", [32, 1])
    convw_d = din("convw", [128, 68])
    convb_d = din("convb", [128, 17])
    fcw_d = din("fcw", [128, 96])
    fcb_d = din("fcb", [128, 32])
    wint_d = din("wint", [D_MODEL, D_IN_PROJ])
    wout_d = din("wout", [D_INNER, D_MODEL])
    w1t_d = din("w1t", [D_MODEL, D_HIDDEN])
    w2t_d = din("w2t", [D_HIDDEN, D_MODEL])
    out_d = nc.dram_tensor("out", [SEG, D_MODEL], F32, kind="ExternalOutput").ap()

    zsp_d = dint("zsp", [GRID, D_INNER], BF16)
    x2_d = dint("x2d", [GRID, D_MODEL])
    hpost_d = dint("hpostd", [D_HIDDEN, GRID], BF16)
    acs_d = dint("acsd", [NHEADS, GRID])
    alast_d = dint("alastd", [NCH, NHEADS])
    w1q_d = dint("w1qd", [D_MODEL, D_HIDDEN], BF16)
    w2q_d = dint("w2qd", [D_HIDDEN, D_MODEL], BF16)
    agin_d = dint("agin", [SLOT, SCW])
    agout_d = dint("agout", [4 * SLOT, SCW])
    snap_d = dint("snapd", [33 * 80, SCW], BF16)

    arenaA = ctx.enter_context(nc.sbuf_tensor("arenaA", [128, 69632], U8))
    arenaB = ctx.enter_context(nc.sbuf_tensor("arenaB", [128, 66560], U8))
    arenaC = ctx.enter_context(nc.sbuf_tensor("arenaC", [128, 16384], U8))
    arenaD = ctx.enter_context(nc.sbuf_tensor("arenaD", [128, 8192], U8))

    def overlay(base, name, shape, dtype, off=0):
        h = nc.virtual_sbuf_tensor(name, list(shape), dtype)
        nc._tensor(h.name, h.shape, h.dtype, type="SB")
        mloc = nc.lookup_mloc(h)
        mloc.allocated = True
        mloc.addr = nc.lookup_mloc(base).addr + off
        return h

    xhtm = overlay(arenaA, "xhtm", [128, NSUB * D_INNER], BF16)
    xq1fm = overlay(arenaA, "xq1fm", [128, KC * GRID], BF16)
    qv4 = overlay(arenaA, "qv4", [128, 4096], F32, 0)
    qyi4 = overlay(arenaA, "qyi4", [128, 4096], I32, 16384)
    qt14 = overlay(arenaA, "qt14", [128, 4096], F32, 16384)
    qy4 = overlay(arenaA, "qy4", [128, 4096], F32, 32768)
    qr4 = overlay(arenaA, "qr4", [128, 4096], F32, 49152)
    qv1 = overlay(arenaA, "qv1", [128, 1024], F32, 34816)
    qyi1 = overlay(arenaA, "qyi1", [128, 1024], I32, 38912)
    qt11 = overlay(arenaA, "qt11", [128, 1024], F32, 38912)
    qy1 = overlay(arenaA, "qy1", [128, 1024], F32, 43008)
    qr1 = overlay(arenaA, "qr1", [128, 1024], F32, 47104)
    winb = overlay(arenaB, "winb", [128, KC * D_IN_PROJ], BF16)
    snapbuf = overlay(arenaB, "snapbuf", [128, 2 * SCW], BF16, 0)
    snapld = overlay(arenaB, "snapld", [128, 2 * SCW], F32, 2816)
    woutb = overlay(arenaB, "woutb", [128, 16 * D_MODEL], BF16)
    w1qb = overlay(arenaB, "w1qb", [128, KC * D_HIDDEN], BF16)
    w2qb = overlay(arenaB, "w2qb", [128, 32 * D_MODEL], BF16)
    sz = overlay(arenaB, "sz", [128, D_INNER], F32, 8448)
    ytmp = overlay(arenaB, "ytmp", [128, D_INNER], F32, 16640)
    ytmp2 = overlay(arenaB, "ytmp2", [128, D_INNER], F32, 24832)
    flatbuf = overlay(arenaB, "flatbuf", [32, 2048], F32, 33024)
    wtile = overlay(arenaC, "wtile", [128, 4096], F32)
    htmb = overlay(arenaC, "htmb", [128, D_HIDDEN], BF16)
    wqtile = overlay(arenaD, "wqtile", [128, 4096], BF16)
    xq2tm = overlay(arenaD, "xq2tm", [128, D_HIDDEN], BF16)
    xnf = overlay(arenaC, "xnf", [128, KC * 512], BF16)
    ebuf = overlay(arenaC, "ebuf", [128, D_INNER], F32, 0)
    dmin = overlay(arenaC, "dmin", [128, D_INNER], F32, 8192)
    hconv = overlay(arenaA, "hconv", [128, GRID], F32, 34816)
    h1buf = overlay(arenaA, "h1buf", [128, GRID], BF16, 43520)
    hpostb = overlay(arenaA, "hpostb", [128, GRID], BF16, 47872)
    xtm_t = overlay(arenaD, "xtm_t", [128, D_MODEL], F32, 0)
    xntm = overlay(arenaD, "xntm", [128, D_MODEL], F32, 4096)
    gbuf = overlay(arenaD, "gbuf", [128, D_INNER], BF16, 0)
    xdt = overlay(arenaD, "xdt", [128, D_INNER], BF16, 4096)
    slotb = overlay(arenaC, "slotb", [128, 3 * SCW], F32, 0)
    p30b = overlay(arenaC, "p30b", [128, SCW], F32, 8448)
    tcomb = overlay(arenaC, "tcomb", [128, SCW], F32, 11264)
    ynf = overlay(arenaC, "ynf", [128, D_INNER], BF16, 0)
    motm = overlay(arenaD, "motm", [128, D_MODEL], F32, 0)
    xgtm = overlay(arenaD, "xgtm", [128, D_MODEL], F32, 4096)
    x2n_t = overlay(arenaC, "x2n_t", [128, D_MODEL], F32, 0)
    xq1tm = overlay(arenaC, "xq1tm", [128, D_MODEL], BF16, 4096)
    acspad = overlay(arenaC, "acspad", [32, 64 + GRID], F32, 0)
    srun = overlay(arenaC, "srun", [128, SCW], F32, 0)
    xh3tm = overlay(arenaA, "xh3tm", [4, D_MODEL], F32, 65536)
    zchunkA = overlay(arenaC, "zchunkA", [128, D_INNER], BF16, 8192)
    convt = overlay(arenaC, "convt", [128, 512], F32, 12288)
    xbca = overlay(arenaC, "xbca", [128, 512], BF16, 14336)
    zchunk2 = overlay(arenaB, "zchunk2", [128, D_INNER], BF16, 24832)
    ucomb = overlay(arenaB, "ucomb", [128, SCW], F32, 41216)
    x2ld1 = overlay(arenaD, "x2ld1", [128, D_MODEL], F32, 0)
    x2ld3 = overlay(arenaC, "x2ld3", [128, D_MODEL], F32, 8192)
    fotm = overlay(arenaC, "fotm", [128, D_MODEL], F32, 12288)
    xq2fm = overlay(arenaA, "xq2fm", [128, D_HIDDEN], BF16, 49152)

    idn_s = sb("idn_s", [128, 128])
    cpool_s = sb("cpool_s", [128, 4])
    idnb_s = sb("idnb_s", [128, 128], BF16)
    e064_s = sb("e064_s", [32, 64])
    e0128_s = sb("e0128_s", [32, 128])
    onescol_s = sb("onescol_s", [128, 1])
    hsel_s = sb("hsel_s", [32, 128], BF16)
    gmask_s = sb("gmask_s", [32, SCW], BF16)
    negio_s = sb("negio_s", [32, 2048])
    eacsb = sb("eacsb", [128, 32])
    tri_s = sb("tri_s", [64, 64])
    tri64hi = sb("tri64hi", [128, 64])
    dtbrep_s = sb("dtbrep_s", [128, 32])
    dskrep_s = sb("dskrep_s", [128, D_INNER], BF16)
    nwrep_s = sb("nwrep_s", [128, D_INNER], BF16)
    alog_s = sb("alog_s", [32, 1])
    nega_s = sb("nega_s", [32, 1])
    convw_s = sb("convw_s", [128, 68])
    convb_s = sb("convb_s", [128, 17])
    fcw_s = sb("fcw_s", [128, 96])
    fcb_s = sb("fcb_s", [128, 32])
    maskrow_s = sb("maskrow_s", [128, NSUB])
    pmc_s = sb("pmc_s", [128, 4])
    slotoff_s = sb("slotoff_s", [4, 1], I32)
    dttm = sb("dttm", [128, NSUB * 32])
    acsfm = sb("acsfm", [32, GRID])
    acstm = sb("acstm", [128, NSUB * 32])
    alast_s = sb("alast_s", [32, NCH])
    expal_s = sb("expal_s", [32, NCH])
    gamlog = sb("gamlog", [32, NCH])
    expgam_s = sb("expgam_s", [32, NCH])
    eg_s = sb("eg_s", [32, 2])
    bfm = sb("bfm", [16, GRID], BF16)
    cfm4 = sb("cfm4", [128, GRID], BF16)
    btm = sb("btm", [128, NSUB * 16])
    xh3buf = sb("xh3buf", [128, 17 * 3], BF16)
    xbchal = sb("xbchal", [128, 17 * 3], BF16)
    xnhf = sb("xnhf", [128, KC * 4], BF16)
    alrow = sb("alrow", [32, 2 * 32])
    decaytm = sb("decaytm", [128, 2 * 32])
    bd = sb("bd", [128, 512], BF16)
    m1m = sb("m1m", [128, 64], BF16)
    sin = sb("sin", [128, SCW])
    stbuf = sb("stbuf", [128, SCW], BF16)
    gslot = sb("gslot", [32, 3 * 2])
    gmg = sb("gmg", [32, SCW], BF16)
    rmscol = sb("rmscol", [128, 2])
    stat6 = sb("stat6", [128, 24])
    stat2 = sb("stat2", [128, 2])
    lncol = sb("lncol", [128, 8])
    xbcbuf = sb("xbcbuf", [128, 2 * 516], BF16)
    dtraw = sb("dtraw", [128, 2 * 32])
    qcol = sb("qcol", [128, 8])
    wabs = sb("wabs", [128, 1])
    wsum1 = sb("wsum1", [128, 1])
    scol = sb("scol", [128, 4])
    valbuf = sb("valbuf", [32, 2])
    psc = sb("psc", [1, 8])

    pB1 = ctx.enter_context(nc.psum_tensor("pB1", [128, 2048], F32))
    pB2 = ctx.enter_context(nc.psum_tensor("pB2", [128, 2048], F32))

    def overlay_ps(base, name, shape, dtype, off_bytes):
        h = nc.virtual_psum_tensor(name, list(shape), dtype)
        nc._tensor(h.name, h.shape, h.dtype, type="PSUM")
        mloc = nc.lookup_mloc(h)
        mloc.allocated = True
        mloc.addr = nc.lookup_mloc(base).addr + off_bytes
        return h

    pTb = overlay_ps(pB2, "pTb", [128, 1024], BF16, 6144)

    p = P(nc, ctx)

    def dma(queue, out, in_, sem=None):
        q = sem if sem else ("qg" if queue == "gpsimd" else "qs")
        p.do(queue, lambda e, out=out, in_=in_: e.dma_start(out=out, in_=in_),
             inc=(q, 16))

    def ln_tm(xap, outap, width, part=128):
        nq = width // 512
        for q in range(nq):
            p.do("vector", lambda v, q=q, xap=xap, part=part: v.bn_stats(
                stat6[0:part, 6 * q:6 * q + 6], xap[:, 512 * q:512 * (q + 1)]),
                inc=("sv", 1))
        p.w("vector", "sv")
        p.do("vector", lambda v, nq=nq, part=part: v.bn_aggr(
            stat2[0:part, 0:2], stat6[0:part, 0:6 * nq]), inc=("sv", 1))
        p.w("scalar", "sv")
        p.do("scalar", lambda sc, part=part: sc.activation(
            lncol[0:part, 1:2], stat2[0:part, 1:2], AF.Ln, bias=EPS),
            inc=("sa", 1))
        p.w("scalar", "sa")
        p.do("scalar", lambda sc, part=part: sc.activation(
            lncol[0:part, 1:2], lncol[0:part, 1:2], AF.Exp, scale=-0.5),
            inc=("sa", 1))
        p.w("vector", "sa")
        p.do("vector", lambda v, xap=xap, outap=outap, part=part:
             v.tensor_scalar(outap, xap, stat2[0:part, 0:1],
                             lncol[0:part, 1:2], ALU.subtract, ALU.mult),
             inc=("sv", 1))

    nc.const_aps.register(F32, 1e-5, cpool_s[:, 0:1])
    nc.const_aps.register(F32, 1.0, cpool_s[:, 1:2])
    nc.const_aps.register(F32, -0.5, cpool_s[:, 2:3])
    nc.const_aps.register(F32, 0.0, cpool_s[:, 3:4])
    for dst, src in [(idn_s, idn_d), (cpool_s, cpool_d),
                     (e064_s, e064_d), (e0128_s, e0128_d),
                     (onescol_s, onescol_d), (negio_s, negio_d),
                     (tri_s, tri_d), (dtbrep_s, dtbrep_d),
                     (alog_s, alog_d), (convw_s, convw_d),
                     (convb_s, convb_d), (fcw_s, fcw_d), (fcb_s, fcb_d),
                     (pmc_s, pmc_d), (slotoff_s, slotoff_d),
                     (maskrow_s, maskrow_d), (xh3tm, xh3_d)]:
        dma("sync", dst[:, :], src[:, :])
    dma("gpsimd", dskrep_s[:, :], dskrep_d[:, :])
    dma("gpsimd", nwrep_s[:, :], nwrep_d[:, :])
    dma("gpsimd", hsel_s[:, :], hsel_d[:, :])
    dma("gpsimd", gmask_s[:, :], gmask_d[:, :])
    for kc in range(KC):
        dma("gpsimd", winb[:, kc * D_IN_PROJ:(kc + 1) * D_IN_PROJ],
            wint_d[kc * 128:(kc + 1) * 128, :])
    p.w("vector", "qs")
    p.do("vector", lambda v: v.tensor_copy(idnb_s[:, :], idn_s[:, :]),
         inc=("sv", 1))
    p.do("gpsimd", lambda g: g.memset(xh3buf[:, :], 0.0), inc=("sg", 1))
    p.do("gpsimd", lambda g: g.memset(xbchal[:, :], 0.0), inc=("sg", 1))
    p.do("gpsimd", lambda g: g.memset(tri64hi[:, :], 0.0), inc=("sg", 1))
    p.w("gpsimd", "sg")
    p.do("gpsimd", lambda g: g.iota(
        tri64hi[:, :], [[1, 64]], base=64, channel_multiplier=-1,
        allow_small_or_imprecise_dtypes=True), inc=("sg", 1))
    p.w("vector", "sg")
    p.do("vector", lambda v: v.tensor_scalar(tri64hi[:, :], tri64hi[:, :], 0.0,
                                             None, ALU.is_ge), inc=("sv", 1))
    p.w("scalar", "qs")
    p.do("scalar", lambda sc: sc.activation(nega_s[:, :], alog_s[:, :], AF.Exp),
         inc=("sa", 1))
    p.w("vector", "sa")
    p.do("vector", lambda v: v.tensor_scalar(nega_s[:, :], nega_s[:, :], -1.0,
                                             None, ALU.mult), inc=("sv", 1))

    ln_tm(xh3tm[0:4, :], xh3tm[0:4, :], D_MODEL, part=4)
    p.w("tensor", "sv")
    for kc in range(KC):
        p.do("tensor", lambda t, kc=kc: t.transpose(
            pB1[:, kc * 4:(kc + 1) * 4], xh3tm[0:4, kc * 128:(kc + 1) * 128],
            idn_s[0:4, 0:4]), inc=("sp", 1))
    p.w("vector", "sp")
    p.do("vector", lambda v: v.tensor_copy(xnhf[:, :], pB1[:, 0:KC * 4]),
         inc=("sv", 1))
    sv_xnhf = p.n["sv"]
    p.w("tensor", "sv", sv_xnhf)
    p.w("tensor", "qg")
    for fc in range(17):
        f0 = D_INNER + fc * 128
        fw = min(128, D_IN_PROJ - 32 - f0)
        p.w("tensor", "sv")
        for kc in range(KC):
            p.do("tensor", lambda t, kc=kc, f0=f0, fw=fw, fc=fc: t.matmul(
                pB2[0:fw, 1792 + fc % 2 * 4: 1792 + fc % 2 * 4 + 3],
                winb[:, kc * D_IN_PROJ + f0: kc * D_IN_PROJ + f0 + fw],
                xnhf[:, kc * 4: kc * 4 + 3],
                start=(kc == 0), stop=(kc == KC - 1)), inc=("sp", 1))
        p.w("vector", "sp")
        p.do("vector", lambda v, fc=fc, fw=fw: v.tensor_copy(
            xh3buf[0:fw, fc * 3:fc * 3 + 3],
            pB2[0:fw, 1792 + fc % 2 * 4: 1792 + fc % 2 * 4 + 3]), inc=("sv", 1))

    def wquant(wsrc_d, wdst_d, nrc, ncols, si):
        p.w("gpsimd", "sv")
        p.do("gpsimd", lambda g: g.memset(wsum1[:, :], 0.0), inc=("sg", 1))
        for rc in range(nrc):
            p.w("sync", "sv")
            if rc == 0:
                p.w("sync", "sg")
            dma("sync", wtile[:, 0:ncols], wsrc_d[rc * 128:(rc + 1) * 128, :],
                sem="qw")
            p.w("vector", "qw")
            if rc == 0:
                p.w("vector", "sg")
            p.do("vector", lambda v, n=ncols: v.tensor_reduce(
                wabs[:, :], wtile[:, 0:n], AX.X, ALU.add,
                apply_absolute_value=True), inc=("sv", 1))
            p.w("vector", "sv")
            p.do("vector", lambda v: v.tensor_tensor(
                wsum1[:, :], wsum1[:, :], wabs[:, :], ALU.add), inc=("sv", 1))
        p.w("tensor", "sv")
        p.do("tensor", lambda t: t.matmul(pB1[0:1, 0:1], onescol_s[:, :],
                                          wsum1[:, :], start=True, stop=True),
             inc=("sp", 1))
        p.w("vector", "sp")
        p.do("vector", lambda v, nrc=nrc, ncols=ncols: v.tensor_scalar(
            psc[0:1, 0:1], pB1[0:1, 0:1], 1.0 / (nrc * 128 * ncols), None,
            ALU.mult), inc=("sv", 1))
        p.w("vector", "sv")
        p.do("vector", lambda v: v.tensor_scalar(psc[0:1, 0:1], psc[0:1, 0:1],
                                                 1e-5, None, ALU.max),
             inc=("sv", 1))
        p.w("vector", "sv")
        p.do("vector", lambda v: v.reciprocal(psc[0:1, 1:2], psc[0:1, 0:1]),
             inc=("sv", 1))
        p.w("gpsimd", "sv")
        p.do("gpsimd", lambda g: g.memset(valbuf[:, :], 0.0), inc=("sg", 1))
        p.w("vector", "sg")
        p.do("vector", lambda v: v.tensor_copy(valbuf[0:1, 0:1], psc[0:1, 1:2]),
             inc=("sv", 1))
        p.w("vector", "sv")
        p.do("vector", lambda v: v.tensor_copy(valbuf[0:1, 1:2], psc[0:1, 0:1]),
             inc=("sv", 1))
        p.w("tensor", "sv")
        p.do("tensor", lambda t: t.matmul(pB1[0:128, 8:10], e0128_s[:, :],
                                          valbuf[:, :], start=True, stop=True),
             inc=("sp", 1))
        p.w("vector", "sp")
        p.do("vector", lambda v, si=si: v.tensor_copy(
            scol[:, si:si + 1], pB1[0:128, 8:9]), inc=("sv", 1))
        p.w("vector", "sv")
        p.do("vector", lambda v, si=si: v.tensor_copy(
            scol[:, si + 1:si + 2], pB1[0:128, 9:10]), inc=("sv", 1))
        for rc in range(nrc):
            p.w("sync", "sv")
            dma("sync", wtile[:, 0:ncols], wsrc_d[rc * 128:(rc + 1) * 128, :],
                sem="qw")
            p.w("vector", "qw")
            p.do("vector", lambda v, n=ncols, si=si: v.tensor_scalar(
                qv4[:, 0:n], wtile[:, 0:n], scol[:, si:si + 1], None, ALU.mult),
                inc=("sv", 1))
            p.w("vector", "sv")
            p.do("vector", lambda v, n=ncols: v.tensor_scalar(
                qt14[:, 0:n], qv4[:, 0:n], 0.5, None, ALU.is_gt), inc=("sv", 1))
            p.w("vector", "sv")
            p.do("vector", lambda v, n=ncols: v.scalar_tensor_tensor(
                qr4[:, 0:n], qv4[:, 0:n], -0.5, qt14[:, 0:n], ALU.is_lt,
                ALU.subtract), inc=("sv", 1))
            p.w("vector", "sv")
            p.do("vector", lambda v, n=ncols, si=si: v.tensor_scalar(
                qv4[:, 0:n], qr4[:, 0:n], scol[:, si + 1:si + 2], -1.0,
                ALU.mult, ALU.mult), inc=("sv", 1))
            p.w("vector", "sv")
            p.w("vector", "qe")
            p.do("vector", lambda v, n=ncols: v.tensor_copy(
                wqtile[:, 0:n], qv4[:, 0:n]), inc=("sv", 1))
            sv_here = p.n["sv"]
            p.w("sync", "sv", sv_here)
            dma("sync", wdst_d[rc * 128:(rc + 1) * 128, :], wqtile[:, 0:ncols],
                sem="qe")

    wquant(w1t_d, w1q_d, 8, 4096, 0)
    wquant(w2t_d, w2q_d, 32, 1024, 2)

    p.barrier()
    for (t0, tw) in PA_TILES:
        nsub_t = tw // 128
        tpar = (t0 // 512) % 2
        for si in range(nsub_t):
            g0 = t0 + si * 128
            p.w("sync", "sv")
            dma("sync", xtm_t[:, :], xg_d[g0:g0 + 128, :], sem="q1")
            p.w("vector", "q1")
            ln_tm(xtm_t[:, :], xntm[:, :], D_MODEL)
            p.w("tensor", "sv")
            for kc in range(KC):
                p.do("tensor", lambda t, kc=kc: t.transpose(
                    pB1[:, kc * 128:(kc + 1) * 128],
                    xntm[:, kc * 128:(kc + 1) * 128], idn_s[:, :]),
                    inc=("sp", 1))
            p.w("vector", "sp")
            xnfb = xnf[:, si * 128:si * 128 + 1]
            xnfap = bass.AP(xnfb.tensor, xnfb.offset,
                            [[KC * 512, 128], [512, KC], [1, 128]])
            p.do("vector", lambda v, xnfap=xnfap: v.tensor_copy(
                xnfap, pB1[:, 0:KC * 128]), inc=("sv", 1))
            p.w("tensor", "sv")
        sv_xnf = p.n["sv"]

        for fc in range(17):
            f0 = D_INNER + fc * 128
            fw = min(128, D_IN_PROJ - 32 - f0)
            p.w("tensor", "sv")
            for kc in range(KC):
                p.do("tensor", lambda t, kc=kc, f0=f0, fw=fw, tw=tw: t.matmul(
                    pB2[0:fw, 0:tw],
                    winb[:, kc * D_IN_PROJ + f0: kc * D_IN_PROJ + f0 + fw],
                    xnf[:, kc * 512: kc * 512 + tw],
                    start=(kc == 0), stop=(kc == KC - 1)), inc=("sp", 1))
            b0 = 516 * tpar
            ob0 = 516 * (1 - tpar)
            p.w("vector", "sp")
            if t0 == 0:
                p.do("vector", lambda v, b0=b0, fc=fc: v.tensor_copy(
                    xbcbuf[:, b0:b0 + 3], xh3buf[:, fc * 3:fc * 3 + 3]),
                    inc=("sv", 1))
            else:
                p.do("vector", lambda v, b0=b0, fc=fc: v.tensor_copy(
                    xbcbuf[:, b0:b0 + 3], xbchal[:, fc * 3:fc * 3 + 3]),
                    inc=("sv", 1))
            p.do("vector", lambda v, b0=b0, fw=fw, tw=tw: v.tensor_copy(
                xbcbuf[0:fw, b0 + 3:b0 + 3 + tw], pB2[0:fw, 0:tw]),
                inc=("sv", 1))
            p.w("vector", "sv")
            p.do("vector", lambda v, b0=b0, tw=tw, fc=fc: v.tensor_copy(
                xbchal[:, fc * 3:fc * 3 + 3], xbcbuf[:, b0 + tw:b0 + tw + 3]),
                inc=("sv", 1))
            p.w("vector", "sv")
            p.w("vector", "sa")
            p.do("vector", lambda v, b0=b0, tw=tw, fc=fc: v.tensor_scalar(
                convt[:, 0:tw], xbcbuf[:, b0:b0 + tw],
                convw_s[:, fc * 4:fc * 4 + 1], None, ALU.mult), inc=("sv", 1))
            for k in (1, 2, 3):
                p.w("vector", "sv")
                p.do("vector", lambda v, b0=b0, tw=tw, fc=fc, k=k:
                     v.scalar_tensor_tensor(
                         convt[:, 0:tw], xbcbuf[:, b0 + k:b0 + k + tw],
                         convw_s[:, fc * 4 + k:fc * 4 + k + 1], convt[:, 0:tw],
                         ALU.mult, ALU.add), inc=("sv", 1))
            p.w("scalar", "sv")
            p.w("scalar", "q2")
            p.w("scalar", "qt5")
            p.do("scalar", lambda sc, tw=tw, fc=fc: sc.activation(
                xbca[:, 0:tw], convt[:, 0:tw], AF.Silu,
                bias=convb_s[:, fc:fc + 1]), inc=("sa", 1))
            if fc < 16:
                p.w("sync", "sa")
                for si in range(nsub_t):
                    gsub = t0 // 128 + si
                    p.do("sync", lambda e, si=si, gsub=gsub, fc=fc:
                         e.dma_start_transpose(
                             out=xhtm[:, gsub * D_INNER + fc * 128:
                                      gsub * D_INNER + fc * 128 + 128],
                             in_=xbca[:, si * 128:(si + 1) * 128]),
                         inc=("qt5", 16))
            else:
                p.w("vector", "sa")
                p.do("vector", lambda v, t0=t0, tw=tw: v.tensor_copy(
                    bfm[:, t0:t0 + tw], xbca[0:16, 0:tw]), inc=("sv", 1))
                p.w("sync", "sa")
                for cg in range(3):
                    dma("sync", cfm4[32 * cg:32 * cg + 16, t0:t0 + tw],
                        xbca[16:32, 0:tw], sem="q2")
                p.w("tensor", "sa")
                p.w("tensor", "sv")
                for si in range(nsub_t):
                    p.do("tensor", lambda t, si=si: t.transpose(
                        pTb[:, si * 128:si * 128 + 32],
                        xbca[0:32, si * 128:(si + 1) * 128], idnb_s[0:32, 0:32]),
                        inc=("sp", 1))
                p.w("vector", "sp")
                ptb0 = pTb[:, 0:1]
                ptbap = bass.AP(ptb0.tensor, ptb0.offset,
                                [[1024, 128], [128, nsub_t], [1, 16]])
                p.do("vector", lambda v, ptbap=ptbap, t0=t0, nsub_t=nsub_t:
                     v.tensor_copy(
                         btm[:, (t0 // 128) * 16:(t0 // 128 + nsub_t) * 16],
                         ptbap), inc=("sv", 1))

        for si in range(nsub_t):
            gsub = t0 // 128 + si
            p.w("tensor", "sv")
            for kc in range(KC):
                for zs in range(4):
                    p.do("tensor", lambda t, kc=kc, si=si, zs=zs: t.matmul(
                        pB1[:, zs * 512:(zs + 1) * 512],
                        xnf[:, kc * 512 + si * 128: kc * 512 + si * 128 + 128],
                        winb[:, kc * D_IN_PROJ + zs * 512:
                             kc * D_IN_PROJ + (zs + 1) * 512],
                        start=(kc == 0), stop=(kc == KC - 1)), inc=("sp", 1))
                p.do("tensor", lambda t, kc=kc, si=si: t.matmul(
                    pB2[:, 1024:1056],
                    xnf[:, kc * 512 + si * 128: kc * 512 + si * 128 + 128],
                    winb[:, kc * D_IN_PROJ + 4128: kc * D_IN_PROJ + 4160],
                    start=(kc == 0), stop=(kc == KC - 1)), inc=("sp", 1))
            p.w("vector", "sp")
            p.w("scalar", "sp")
            p.w("scalar", "q3")
            p.do("scalar", lambda sc: sc.copy(zchunkA[:, :], pB1[:, 0:2048]),
                 inc=("sa", 1))
            sa_z = p.n["sa"]
            p.w("sync", "sa", sa_z)
            dma("sync", zsp_d[gsub * 128:(gsub + 1) * 128, :], zchunkA[:, :],
                sem="q3")
            p.do("vector", lambda v: v.tensor_tensor(
                dtraw[:, 0:32], pB2[:, 1024:1056], dtbrep_s[:, :], ALU.add),
                inc=("sv", 1))
            p.w("scalar", "sv")
            p.do("scalar", lambda sc: sc.activation(dtraw[:, 32:64],
                                                    dtraw[:, 0:32], AF.Exp),
                 inc=("sa", 1))
            p.w("scalar", "sa")
            p.do("scalar", lambda sc, gsub=gsub: sc.activation(
                dttm[:, gsub * 32:(gsub + 1) * 32], dtraw[:, 32:64], AF.Ln,
                bias=1.0), inc=("sa", 1))
            p.w("tensor", "sa")

    p.barrier()
    for gsub in range(NSUB):
        sl = gsub % 8
        p.do("tensor", lambda t, gsub=gsub, sl=sl: t.transpose(
            pB1[0:32, sl * 128:sl * 128 + 128],
            dttm[:, gsub * 32:(gsub + 1) * 32], idn_s[:, :]), inc=("sp", 1))
        p.w("vector", "sp")
        p.do("vector", lambda v, gsub=gsub, sl=sl: v.tensor_copy(
            acspad[:, 64 + gsub * 128:64 + (gsub + 1) * 128],
            pB1[0:32, sl * 128:sl * 128 + 128]), inc=("sv", 1))
        p.w("tensor", "sv")

    p.w("vector", "sv")
    p.do("vector", lambda v: v.tensor_scalar(
        acspad[:, 64:64 + GRID], acspad[:, 64:64 + GRID], nega_s[0:32, 0:1],
        None, ALU.mult), inc=("sv", 1))
    p.w("vector", "sv")
    p.w("gpsimd", "sv")
    p.do("gpsimd", lambda g: g.memset(acspad[:, 0:64], 0.0), inc=("sg", 1))
    p.w("vector", "sg")
    p.w("vector", "sv")
    p.do("vector", lambda v: v.tensor_tensor_scan(
        acspad[:, 64:64 + GRID], acspad[:, 64:64 + GRID],
        acspad[:, 64:64 + GRID], 0.0, ALU.add, ALU.bypass), inc=("sv", 1))
    p.w("vector", "sv")
    off_ap = acspad[:, 63:64]
    p.do("vector", lambda v: v.tensor_tensor(
        acsfm[:, :].rearrange("p (c u) -> p c u", u=64),
        acspad[:, 64:64 + GRID].rearrange("p (c u) -> p c u", u=64),
        bcast_free(off_ap, 32, [[64, NCH], [0, 64]]), ALU.subtract),
        inc=("sv", 1))
    p.w("vector", "sv")
    al_ap = acsfm[:, 63:64]
    p.do("vector", lambda v: v.tensor_copy(
        alast_s[:, :], bcast_free(al_ap, 32, [[64, NCH]])), inc=("sv", 1))
    p.w("scalar", "sv")
    p.do("scalar", lambda sc: sc.activation(expal_s[:, :], alast_s[:, :],
                                            AF.Exp), inc=("sa", 1))
    p.w("gpsimd", "sa")
    p.do("gpsimd", lambda g: g.memset(gamlog[:, :], 0.0), inc=("sg", 1))
    p.w("vector", "sg")
    p.do("vector", lambda v: v.tensor_tensor_scan(
        gamlog[:, 2:34], alast_s[:, 1:33], alast_s[:, 1:33], 0.0,
        ALU.add, ALU.bypass), inc=("sv", 1))
    p.w("scalar", "sv")
    p.do("scalar", lambda sc: sc.activation(expgam_s[:, :], gamlog[:, :],
                                            AF.Exp), inc=("sa", 1))
    p.do("vector", lambda v: v.tensor_reduce(eg_s[:, 0:1], alast_s[:, 1:33],
                                             AX.X, ALU.add), inc=("sv", 1))
    p.w("vector", "sv")
    p.do("vector", lambda v: v.tensor_reduce(eg_s[:, 1:2], alast_s[:, 1:32],
                                             AX.X, ALU.add), inc=("sv", 1))
    p.w("scalar", "sv")
    p.do("scalar", lambda sc: sc.activation(eg_s[:, :], eg_s[:, :], AF.Exp),
         inc=("sa", 1))
    p.w("sync", "sv")
    p.w("sync", "sa")
    dma("sync", acs_d[:, :], acsfm[:, :], sem="q4")
    p.do("sync", lambda e: e.dma_start(out=alast_d[:, :].transpose([1, 0]),
                                       in_=alast_s[:, :],
                                       allow_slow_non_contiguous=True),
         inc=("q4", 16))
    p.w("tensor", "sv")
    for gsub in range(NSUB):
        sl = gsub % 8
        p.do("tensor", lambda t, gsub=gsub, sl=sl: t.transpose(
            pB1[:, sl * 128:sl * 128 + 32],
            acsfm[:, gsub * 128:(gsub + 1) * 128], idn_s[0:32, 0:32]),
            inc=("sp", 1))
        p.w("vector", "sp")
        p.do("vector", lambda v, gsub=gsub, sl=sl: v.tensor_copy(
            acstm[:, gsub * 32:(gsub + 1) * 32],
            pB1[:, sl * 128:sl * 128 + 32]), inc=("sv", 1))
        p.w("tensor", "sv")

    p.barrier()
    p.do("gpsimd", lambda g: g.memset(srun[:, :], 0.0), inc=("sg", 1))
    p.w("vector", "sg")
    p.do("vector", lambda v: v.tensor_copy(pB2[0:80, 1024:1024 + SCW],
                                           srun[0:80, :]), inc=("sv", 1))
    p.w("gpsimd", "sg")
    p.do("gpsimd", lambda g: g.memset(flatbuf[:, :], 0.0), inc=("sg", 1))
    p.w("gpsimd", "sg")
    p.do("gpsimd", lambda g: g.memset(alrow[:, :], 0.0), inc=("sg", 1))
    p.w("vector", "sg")
    p.w("sync", "sg")

    def emit_xdt(ci):
        par = ci % 2
        gsub = ci // 2
        xsl = xhtm[par * 64:par * 64 + 64, gsub * D_INNER:(gsub + 1) * D_INNER]
        dsl = dttm[par * 64:par * 64 + 64, gsub * 32:(gsub + 1) * 32]
        p.do("vector", lambda v, xsl=xsl, dsl=dsl, par=par: v.tensor_tensor(
            xdt[par * 64:par * 64 + 64, :].rearrange("p (h q) -> p h q", h=32),
            xsl.rearrange("p (h q) -> p h q", h=32),
            bcast_free(dsl[:, 0:1], 64, [[1, 32], [0, 64]]), ALU.mult),
            inc=("sv", 1))

    for ci in range(1, 33):
        buf = ci % 2
        par = ci % 2
        gsub = ci // 2
        qa = "q5" if buf == 0 else "q6"
        qn = "q7" if buf == 0 else "q8"
        p.w("sync", "sa")
        p.w("sync", "sp")
        dma("sync", alrow[0:1, buf * 32:buf * 32 + 32], alast_d[ci:ci + 1, :],
            sem=qa)
        emit_xdt(ci)
        p.w("tensor", qa)
        p.w("tensor", "sv")
        p0_ = par * 64
        p.do("tensor", lambda t, buf=buf, p0_=p0_: t.matmul(
            pB1[p0_:p0_ + 64, 1920:1952], e064_s[:, :],
            alrow[:, buf * 32:buf * 32 + 32], start=True, stop=True),
            inc=("sp", 1))
        p.w("vector", "sp")
        asl = acstm[p0_:p0_ + 64, gsub * 32:(gsub + 1) * 32]
        p.do("vector", lambda v, asl=asl, buf=buf, p0_=p0_: v.tensor_tensor(
            decaytm[p0_:p0_ + 64, buf * 32:buf * 32 + 32],
            pB1[p0_:p0_ + 64, 1920:1952], asl, ALU.subtract), inc=("sv", 1))
        p.w("scalar", "sv")
        p.do("scalar", lambda sc, buf=buf, p0_=p0_: sc.activation(
            decaytm[p0_:p0_ + 64, buf * 32:buf * 32 + 32],
            decaytm[p0_:p0_ + 64, buf * 32:buf * 32 + 32], AF.Exp),
            inc=("sa", 1))
        bsl = btm[p0_:p0_ + 64, gsub * 16:(gsub + 1) * 16]
        dcl = decaytm[p0_:p0_ + 64, buf * 32:buf * 32 + 32]
        p.w("vector", "sa")
        p.do("vector", lambda v, bsl=bsl, dcl=dcl, p0_=p0_: v.tensor_tensor(
            bd[p0_:p0_ + 64, :].rearrange("p (h n) -> p h n", h=32),
            bcast_free(dcl[:, 0:1], 64, [[1, 32], [0, 16]]),
            bcast_free(bsl[:, 0:1], 64, [[0, 32], [1, 16]]), ALU.mult),
            inc=("sv", 1))
        p.w("tensor", "sv")
        for h in range(NHEADS):
            hg = h // 11
            hi = h - 11 * hg
            p.do("tensor", lambda t, hg=hg, hi=hi, h=h, p0_=p0_: t.matmul(
                pB2[32 * hg:32 * hg + 16, 1024 + 64 * hi:1024 + 64 * hi + 64],
                bd[p0_:p0_ + 64, h * 16:(h + 1) * 16],
                xdt[p0_:p0_ + 64, h * 64:(h + 1) * 64],
                start=True, stop=True), inc=("sp", 1))
        p.w("vector", qn)
        p.do("vector", lambda v, buf=buf: v.tensor_copy(
            snapbuf[0:80, buf * SCW:buf * SCW + SCW], srun[0:80, :]),
            inc=("sv", 1))
        if ci == 32:
            p.w("vector", "sv")
            p.do("vector", lambda v: v.tensor_copy(tcomb[0:80, :],
                                                   srun[0:80, :]),
                 inc=("sv", 1))
        p.w("sync", "sv")
        dma("sync", snap_d[(ci - 1) * 80:ci * 80, :],
            snapbuf[0:80, buf * SCW:buf * SCW + SCW], sem=qn)
        p.do("vector", lambda v, ci=ci: v.tensor_scalar(
            gmg[:, :], gmask_s[:, :], expal_s[:, ci:ci + 1], None, ALU.mult),
            inc=("sv", 1))
        p.w("tensor", "sv")
        p.do("tensor", lambda t: t.matmul(pB1[0:80, 1024:1536],
                                          hsel_s[:, 0:80], gmg[:, 0:512],
                                          start=True, stop=True),
             inc=("sp", 1))
        p.do("tensor", lambda t: t.matmul(pB1[0:80, 1536:1536 + SCW - 512],
                                          hsel_s[:, 0:80], gmg[:, 512:SCW],
                                          start=True, stop=True),
             inc=("sp", 1))
        p.w("vector", "sp")
        p.do("vector", lambda v: v.tensor_tensor(srun[0:80, :], srun[0:80, :],
                                                 pB1[0:80, 1024:1024 + SCW],
                                                 ALU.mult), inc=("sv", 1))
        p.w("vector", "sv")
        p.do("vector", lambda v: v.tensor_tensor(srun[0:80, :], srun[0:80, :],
                                                 pB2[0:80, 1024:1024 + SCW],
                                                 ALU.add), inc=("sv", 1))
        p.w("tensor", "sv")
    p.w("vector", "sv")
    p.w("vector", "q8")
    p.do("vector", lambda v: v.tensor_copy(snapbuf[0:80, SCW:2 * SCW],
                                           srun[0:80, :]), inc=("sv", 1))
    p.do("gpsimd", lambda g: g.memset(ucomb[0:32, :], 0.0), inc=("sg", 1))
    sv_ag = p.n["sv"]
    p.w("sync", "sv", sv_ag)
    p.w("sync", "sg")
    dma("sync", snap_d[32 * 80:33 * 80, :], snapbuf[0:80, SCW:2 * SCW],
        sem="q8")
    dma("sync", agin_d[0:80, :], srun[0:80, :], sem="q9")
    dma("sync", agin_d[80:160, :], tcomb[0:80, :], sem="q9")
    dma("sync", agin_d[160:192, :], ucomb[0:32, :], sem="q9")
    p.w("gpsimd", "q9")
    dma("gpsimd", agin_d[160:192, 0:2], eg_s[:, :], sem="st")
    p.w("gpsimd", "st")
    p.do("gpsimd", lambda g: g.collective_compute(
        "AllGather", ALU.bypass, replica_groups=[[0, 1, 2, 3], [4, 5, 6, 7]],
        ins=[agin_d[:, :].opt()], outs=[agout_d[:, :].opt()]), inc=("cc", 1))

    def f_load_offsets(e):
        for k in (1, 2, 3):
            p.env[f"so{k}"] = e.value_load(slotoff_s[k - 1:k, 0:1])
    p.w("sync", "cc")
    p.ops["sync"].append(f_load_offsets)
    for k in (1, 2, 3):
        p.do("sync", lambda e, k=k: e.dma_start(
            out=slotb[0:80, (k - 1) * SCW:k * SCW],
            in_=agout_d[bass.ds(p.env[f"so{k}"], 80), :]), inc=("q10", 16))
        p.do("sync", lambda e, k=k: e.dma_start(
            out=gslot[:, (k - 1) * 2:k * 2],
            in_=agout_d[bass.ds(p.env[f"so{k}"] + 160, 32), 0:2]),
            inc=("q10", 16))
    p.do("sync", lambda e: e.dma_start(
        out=p30b[0:80, :], in_=agout_d[bass.ds(p.env["so1"] + 80, 80), :]),
        inc=("q10", 16))

    def grep(gcol_ap):
        p.w("vector", "sp")
        p.do("vector", lambda v, g=gcol_ap: v.tensor_scalar(
            gmg[:, :], gmask_s[:, :], g, None, ALU.mult), inc=("sv", 1))
        p.w("tensor", "sv")
        p.do("tensor", lambda t: t.matmul(
            pB1[0:80, 1024:1536], hsel_s[:, 0:80], gmg[:, 0:512],
            start=True, stop=True), inc=("sp", 1))
        p.do("tensor", lambda t: t.matmul(
            pB1[0:80, 1536:1536 + SCW - 512], hsel_s[:, 0:80], gmg[:, 512:SCW],
            start=True, stop=True), inc=("sp", 1))

    p.w("vector", "q10")
    p.do("vector", lambda v: v.tensor_scalar(
        tcomb[0:80, :], slotb[0:80, 2 * SCW:3 * SCW], pmc_s[0:80, 2:3], None,
        ALU.mult), inc=("sv", 1))
    grep(gslot[:, 2:3])
    p.w("vector", "sp")
    p.w("vector", "sv")
    p.do("vector", lambda v: v.tensor_tensor(ucomb[0:80, :], tcomb[0:80, :],
                                             pB1[0:80, 1024:1024 + SCW],
                                             ALU.mult), inc=("sv", 1))
    p.w("vector", "sv")
    p.do("vector", lambda v: v.tensor_tensor(ucomb[0:80, :], ucomb[0:80, :],
                                             slotb[0:80, SCW:2 * SCW],
                                             ALU.add), inc=("sv", 1))
    p.w("vector", "sv")
    p.do("vector", lambda v: v.tensor_scalar(tcomb[0:80, :], ucomb[0:80, :],
                                             pmc_s[0:80, 1:2], None, ALU.mult),
         inc=("sv", 1))
    grep(gslot[:, 0:1])
    p.w("vector", "sp")
    p.w("vector", "sv")
    p.do("vector", lambda v: v.tensor_tensor(ucomb[0:80, :], tcomb[0:80, :],
                                             pB1[0:80, 1024:1024 + SCW],
                                             ALU.mult), inc=("sv", 1))
    p.w("vector", "sv")
    p.do("vector", lambda v: v.tensor_tensor(ucomb[0:80, :], ucomb[0:80, :],
                                             slotb[0:80, 0:SCW], ALU.add),
         inc=("sv", 1))
    p.w("vector", "sv")
    p.do("vector", lambda v: v.tensor_scalar(sin[0:80, :], ucomb[0:80, :],
                                             pmc_s[0:80, 0:1], None, ALU.mult),
         inc=("sv", 1))
    grep(gslot[:, 1:2])
    p.w("vector", "sp")
    p.w("vector", "sv")
    p.do("vector", lambda v: v.tensor_tensor(ucomb[0:80, :], tcomb[0:80, :],
                                             pB1[0:80, 1024:1024 + SCW],
                                             ALU.mult), inc=("sv", 1))
    p.w("vector", "sv")
    p.do("vector", lambda v: v.tensor_tensor(ucomb[0:80, :], ucomb[0:80, :],
                                             p30b[0:80, :], ALU.add),
         inc=("sv", 1))
    p.w("vector", "sv")
    p.do("vector", lambda v: v.tensor_scalar(stbuf[0:80, :], ucomb[0:80, :],
                                             pmc_s[0:80, 3:4], None, ALU.mult),
         inc=("sv", 1))

    p.barrier()
    for ci in range(NCH):
        par = ci % 2
        gsub = ci // 2
        buf = ci % 2
        p0_ = par * 64
        p.w("sync", "sp")
        fl0 = acs_d[0:1, ci * 64:ci * 64 + 1]
        p.do("sync", lambda e, fl0=fl0: e.dma_start(
            out=flatbuf[0:1, 0:2048],
            in_=bass.AP(fl0.tensor, fl0.offset, [[1, 1], [GRID, 32], [1, 64]])),
            inc=("q11", 16))
        p.w("tensor", "q11")
        p.w("tensor", "sv")
        p.do("tensor", lambda t, ci=ci, p0_=p0_: t.matmul(
            pB2[p0_:p0_ + 64, 1024:1088],
            bfm[:, ci * 64:(ci + 1) * 64], cfm4[0:16, ci * 64:(ci + 1) * 64],
            start=True, stop=True), inc=("sp", 1))
        p.w("vector", "sp")
        p.do("vector", lambda v, p0_=p0_: v.tensor_tensor(
            m1m[p0_:p0_ + 64, :], pB2[p0_:p0_ + 64, 1024:1088],
            tri_s[:, :] if p0_ == 0 else tri64hi[64:128, :], ALU.mult),
             inc=("sv", 1))
        p.w("tensor", "sv")
        for q in range(4):
            p.do("tensor", lambda t, q=q, p0_=p0_: t.matmul(
                pB1[p0_:p0_ + 64, q * 512:(q + 1) * 512], e064_s[:, :],
                flatbuf[:, q * 512:(q + 1) * 512],
                start=True, stop=False), inc=("sp", 1))
            p.do("tensor", lambda t, q=q, ci=ci, p0_=p0_: t.matmul(
                pB1[p0_:p0_ + 64, q * 512:(q + 1) * 512],
                acsfm[:, ci * 64:(ci + 1) * 64],
                negio_s[:, q * 512:(q + 1) * 512],
                start=False, stop=True), inc=("sp", 1))
        p.w("vector", "sp")
        p.do("vector", lambda v, p0_=p0_: v.tensor_scalar(
            dmin[p0_:p0_ + 64, :], pB1[p0_:p0_ + 64, 0:2048],
            0.0, None, ALU.min), inc=("sv", 1))
        p.w("scalar", "sv")
        p.do("scalar", lambda sc, p0_=p0_: sc.activation(
            ebuf[p0_:p0_ + 64, :], dmin[p0_:p0_ + 64, :], AF.Exp),
            inc=("sa", 1))
        p.w("vector", "sa")
        p.do("vector", lambda v, p0_=p0_: v.tensor_tensor(
            gbuf[p0_:p0_ + 64, :].rearrange("p (h l) -> p h l", h=32),
            ebuf[p0_:p0_ + 64, :].rearrange("p (h l) -> p h l", h=32),
            bcast_free(m1m[p0_:p0_ + 64, 0:1], 64, [[0, 32], [1, 64]]),
            ALU.mult), inc=("sv", 1))
        if ci + 1 <= NCH - 1:
            qm2 = "q14" if (ci + 1) % 2 == 0 else "q15"
            p.w("gpsimd", "sv")
            p.do("gpsimd", lambda g, ci=ci: g.dma_start(
                out=snapld[0:80, ((ci + 1) % 2) * SCW:((ci + 1) % 2 + 1) * SCW],
                in_=snap_d[ci * 80:(ci + 1) * 80, :]), inc=(qm2, 16))
        if ci == 0:
            pass
        else:
            grep(expgam_s[:, ci:ci + 1])
            p.w("vector", "sp")
            p.do("vector", lambda v: v.tensor_tensor(
                ucomb[0:80, :], sin[0:80, :], pB1[0:80, 1024:1024 + SCW],
                ALU.mult), inc=("sv", 1))
            p.w("vector", "sv")
            p.w("vector", "q14" if ci % 2 == 0 else "q15")
            p.do("vector", lambda v, ci=ci: v.tensor_tensor(
                stbuf[0:80, :], ucomb[0:80, :],
                snapld[0:80, (ci % 2) * SCW:(ci % 2 + 1) * SCW], ALU.add),
                inc=("sv", 1))
        emit_xdt(ci)
        asl2 = acstm[p0_:p0_ + 64, gsub * 32:(gsub + 1) * 32]
        p.w("tensor", "sv")
        for h in range(NHEADS):
            pp = 32 * (h % 4)
            ff = 64 * (h // 4)
            p.do("tensor", lambda t, h=h, p0_=p0_: t.matmul(
                pB2[p0_:p0_ + 64, h * 64:(h + 1) * 64],
                gbuf[p0_:p0_ + 64, h * 64:(h + 1) * 64],
                xdt[p0_:p0_ + 64, h * 64:(h + 1) * 64],
                start=True, stop=True), inc=("sp", 1))
            hg = h // 11
            hi = h - 11 * hg
            p.do("tensor", lambda t, h=h, hg=hg, hi=hi, ci=ci, p0_=p0_: t.matmul(
                pB1[p0_:p0_ + 64, h * 64:(h + 1) * 64],
                cfm4[32 * hg:32 * hg + 16, ci * 64:(ci + 1) * 64],
                stbuf[32 * hg:32 * hg + 16, 64 * hi:64 * hi + 64],
                start=True, stop=True), inc=("sp", 1))
        p.w("sync", "sv")
        p.w("sync", "sa")
        qz = "q12" if par == 0 else "q13"
        dma("sync", zchunk2[p0_:p0_ + 64, :], zsp_d[ci * 64:(ci + 1) * 64, :],
            sem=qz)
        p.w("scalar", qz)
        p.w("scalar", "sv")
        p.do("scalar", lambda sc, p0_=p0_: sc.activation(
            sz[p0_:p0_ + 64, :], zchunk2[p0_:p0_ + 64, :], AF.Silu),
            inc=("sa", 1))
        p.do("scalar", lambda sc, asl2=asl2, p0_=p0_: sc.activation(
            eacsb[p0_:p0_ + 64, 0:32], asl2, AF.Exp), inc=("sa", 1))
        p.w("vector", "sa")
        p.do("vector", lambda v, p0_=p0_, gsub=gsub: v.tensor_tensor(
            ytmp[p0_:p0_ + 64, :],
            xhtm[p0_:p0_ + 64, gsub * D_INNER:(gsub + 1) * D_INNER],
            dskrep_s[p0_:p0_ + 64, :], ALU.mult), inc=("sv", 1))
        p.w("vector", "sp")
        p.w("vector", "sv")
        p.do("vector", lambda v, p0_=p0_: v.tensor_tensor(
            ytmp[p0_:p0_ + 64, :], ytmp[p0_:p0_ + 64, :],
            pB2[p0_:p0_ + 64, 0:2048], ALU.add), inc=("sv", 1))
        p.w("vector", "sv")
        p.do("vector", lambda v, p0_=p0_: v.tensor_tensor(
            ytmp2[p0_:p0_ + 64, :].rearrange("p (h q) -> p h q", h=32),
            pB1[p0_:p0_ + 64, 0:2048].rearrange("p (h q) -> p h q", h=32),
            bcast_free(eacsb[p0_:p0_ + 64, 0:1], 64, [[1, 32], [0, 64]]),
            ALU.mult), inc=("sv", 1))
        p.w("vector", "sv")
        p.do("vector", lambda v, p0_=p0_: v.tensor_tensor(
            ytmp[p0_:p0_ + 64, :], ytmp[p0_:p0_ + 64, :],
            ytmp2[p0_:p0_ + 64, :], ALU.add), inc=("sv", 1))
        p.w("vector", "sv")
        p.do("vector", lambda v, p0_=p0_: v.tensor_tensor(
            ytmp[p0_:p0_ + 64, :], ytmp[p0_:p0_ + 64, :], sz[p0_:p0_ + 64, :],
            ALU.mult), inc=("sv", 1))
        p.w("vector", "sv")
        for q in range(4):
            p.do("vector", lambda v, q=q, p0_=p0_: v.bn_stats(
                stat6[p0_:p0_ + 64, 6 * q:6 * q + 6],
                ytmp[p0_:p0_ + 64, 512 * q:512 * (q + 1)]), inc=("sv", 1))
        p.w("vector", "sv")
        p.do("vector", lambda v, p0_=p0_: v.bn_aggr(
            stat2[p0_:p0_ + 64, 0:2], stat6[p0_:p0_ + 64, 0:24]),
            inc=("sv", 1))
        p.w("vector", "sv")
        p.do("vector", lambda v, p0_=p0_: v.tensor_tensor(
            rmscol[p0_:p0_ + 64, 0:1], stat2[p0_:p0_ + 64, 0:1],
            stat2[p0_:p0_ + 64, 0:1], ALU.mult), inc=("sv", 1))
        p.w("vector", "sv")
        p.do("vector", lambda v, p0_=p0_: v.tensor_tensor(
            rmscol[p0_:p0_ + 64, 0:1], rmscol[p0_:p0_ + 64, 0:1],
            stat2[p0_:p0_ + 64, 1:2], ALU.add), inc=("sv", 1))
        p.w("scalar", "sv")
        p.do("scalar", lambda sc, p0_=p0_: sc.activation(
            rmscol[p0_:p0_ + 64, 1:2], rmscol[p0_:p0_ + 64, 0:1], AF.Ln,
            bias=EPS), inc=("sa", 1))
        p.w("scalar", "sa")
        p.do("scalar", lambda sc, p0_=p0_: sc.activation(
            rmscol[p0_:p0_ + 64, 1:2], rmscol[p0_:p0_ + 64, 1:2], AF.Exp,
            scale=-0.5), inc=("sa", 1))
        p.w("vector", "sa")
        p.do("vector", lambda v, p0_=p0_, gsub=gsub: v.scalar_tensor_tensor(
            xhtm[p0_:p0_ + 64, gsub * D_INNER:(gsub + 1) * D_INNER],
            ytmp[p0_:p0_ + 64, :], rmscol[p0_:p0_ + 64, 1:2],
            nwrep_s[p0_:p0_ + 64, :], ALU.mult, ALU.mult), inc=("sv", 1))
        p.w("tensor", "sv")

    p.barrier()
    for kc16 in range(16):
        dma("gpsimd", woutb[:, kc16 * D_MODEL:(kc16 + 1) * D_MODEL],
            wout_d[kc16 * 128:(kc16 + 1) * 128, :], sem="q16")
    p.w("tensor", "q16")
    for su in range(NSUB):
        p.w("sync", "sp")
        for fc in range(16):
            p.do("sync", lambda e, su=su, fc=fc: e.dma_start_transpose(
                out=ynf[:, fc * 128:(fc + 1) * 128],
                in_=xhtm[:, su * D_INNER + fc * 128:
                         su * D_INNER + fc * 128 + 128]), inc=("qt1", 16))
        p.w("tensor", "qt1")
        for fc in range(16):
            for q in range(2):
                p.do("tensor", lambda t, fc=fc, q=q: t.matmul(
                    pB1[:, q * 512:(q + 1) * 512],
                    ynf[:, fc * 128:(fc + 1) * 128],
                    woutb[:, fc * D_MODEL + q * 512: fc * D_MODEL + (q + 1) * 512],
                    start=(fc == 0), stop=(fc == 15)), inc=("sp", 1))
        p.w("scalar", "sp")
        p.do("scalar", lambda sc: sc.copy(motm[:, :], pB1[:, 0:1024]),
             inc=("sa", 1))
        p.w("vector", "sa")
        p.w("vector", "sv")
        ln_tm(motm[:, :], motm[:, :], D_MODEL)
        p.w("sync", "sv")
        p.w("sync", "q18")
        dma("sync", xgtm[:, :], xg_d[su * 128:(su + 1) * 128, :], sem="q17")
        p.w("vector", "q17")
        p.do("vector", lambda v: v.tensor_tensor(xgtm[:, :], motm[:, :],
                                                 xgtm[:, :], ALU.add),
             inc=("sv", 1))
        p.w("vector", "sv")
        p.do("vector", lambda v, su=su: v.tensor_scalar(
            xgtm[:, :], xgtm[:, :], maskrow_s[:, su:su + 1], None, ALU.mult),
            inc=("sv", 1))
        sv_x2 = p.n["sv"]
        p.w("sync", "sv", sv_x2)
        dma("sync", x2_d[su * 128:(su + 1) * 128, :], xgtm[:, :], sem="q18")

    p.barrier()
    for su in range(NSUB):
        p.w("sync", "sv")
        dma("sync", x2ld1[:, :], x2_d[su * 128:(su + 1) * 128, :], sem="q19")
        p.w("vector", "q19")
        ln_tm(x2ld1[:, :], x2n_t[:, :], D_MODEL)
        p.w("vector", "sv")
        p.do("vector", lambda v: v.tensor_reduce(
            qcol[:, 0:1], x2n_t[:, 0:1024], AX.X, ALU.max,
            apply_absolute_value=True), inc=("sv", 1))
        p.w("vector", "sv")
        p.do("vector", lambda v: v.tensor_scalar(qcol[:, 0:1], qcol[:, 0:1],
                                                 1e-5, None, ALU.max),
             inc=("sv", 1))
        p.w("vector", "sv")
        p.do("vector", lambda v: v.reciprocal(qcol[:, 1:2], qcol[:, 0:1]),
             inc=("sv", 1))
        p.w("vector", "sv")
        p.do("vector", lambda v: v.tensor_scalar(qcol[:, 1:2], qcol[:, 1:2],
                                                 127.0, None, ALU.mult),
             inc=("sv", 1))
        p.do("vector", lambda v: v.tensor_scalar(qcol[:, 2:3], qcol[:, 0:1],
                                                 1.0 / 127.0, None, ALU.mult),
             inc=("sv", 1))
        p.w("vector", "sv")
        p.do("vector", lambda v: v.tensor_scalar(qv1[:, :], x2n_t[:, :],
                                                 qcol[:, 1:2], None, ALU.mult),
             inc=("sv", 1))
        p.w("vector", "sv")
        p.do("vector", lambda v: v.tensor_copy(qyi1[:, :], qv1[:, :]),
             inc=("sv", 1))
        p.w("vector", "sv")
        p.do("vector", lambda v: v.tensor_copy(qy1[:, :], qyi1[:, :]),
             inc=("sv", 1))
        p.w("vector", "sv")
        p.do("vector", lambda v: v.tensor_tensor(qr1[:, :], qv1[:, :],
                                                 qy1[:, :], ALU.subtract),
             inc=("sv", 1))
        p.w("vector", "sv")
        p.do("vector", lambda v: v.scalar_tensor_tensor(
            qt11[:, :], qr1[:, :], 0.5, qy1[:, :], ALU.is_gt,
            ALU.add), inc=("sv", 1))
        p.w("vector", "sv")
        p.do("vector", lambda v: v.scalar_tensor_tensor(
            qv1[:, :], qr1[:, :], -0.5, qt11[:, :], ALU.is_ge,
            ALU.add), inc=("sv", 1))
        p.w("vector", "sv")
        p.w("vector", "qt2")
        p.do("vector", lambda v: v.tensor_scalar(
            xq1tm[:, :], qv1[:, :], -1.0, qcol[:, 2:3], ALU.add, ALU.mult),
            inc=("sv", 1))
        p.w("sync", "sv")
        for kc in range(KC):
            p.do("sync", lambda e, kc=kc, su=su: e.dma_start_transpose(
                out=xq1fm[:, kc * GRID + su * 128: kc * GRID + su * 128 + 128],
                in_=xq1tm[:, kc * 128:(kc + 1) * 128]), inc=("qt2", 16))

    p.barrier()
    for kc in range(KC):
        dma("sync", w1qb[:, kc * D_HIDDEN:(kc + 1) * D_HIDDEN],
            w1q_d[kc * 128:(kc + 1) * 128, :], sem="q20")
    p.w("tensor", "q20")
    for hf in range(32):
        p.w("tensor", "sv")
        for q, (t0, tw) in enumerate(PA_TILES):
            reg = q % 4
            for kc in range(KC):
                p.do("tensor", lambda t, hf=hf, kc=kc, t0=t0, tw=tw, reg=reg:
                     t.matmul(
                         pB1[:, reg * 512: reg * 512 + tw],
                         w1qb[:, kc * D_HIDDEN + hf * 128:
                              kc * D_HIDDEN + hf * 128 + 128],
                         xq1fm[:, kc * GRID + t0: kc * GRID + t0 + tw],
                         start=(kc == 0), stop=(kc == KC - 1)), inc=("sp", 1))
            p.w("scalar", "sp")
            p.do("scalar", lambda sc, t0=t0, tw=tw, reg=reg: sc.copy(
                h1buf[:, t0:t0 + tw], pB1[:, reg * 512: reg * 512 + tw]),
                inc=("sa", 1))
            if reg == 3:
                p.w("tensor", "sa")
        p.w("vector", "sv")
        p.w("vector", "sa")
        p.do("vector", lambda v, hf=hf: v.tensor_scalar(
            hconv[:, :], h1buf[:, :], fcw_s[:, hf * 3 + 1:hf * 3 + 2], None,
            ALU.mult), inc=("sv", 1))
        p.w("vector", "sv")
        p.do("vector", lambda v, hf=hf: v.scalar_tensor_tensor(
            hconv[:, 1:GRID], h1buf[:, 0:GRID - 1],
            fcw_s[:, hf * 3:hf * 3 + 1], hconv[:, 1:GRID], ALU.mult, ALU.add),
            inc=("sv", 1))
        p.w("vector", "sv")
        p.do("vector", lambda v, hf=hf: v.scalar_tensor_tensor(
            hconv[:, 0:GRID - 1], h1buf[:, 1:GRID],
            fcw_s[:, hf * 3 + 2:hf * 3 + 3], hconv[:, 0:GRID - 1], ALU.mult,
            ALU.add), inc=("sv", 1))
        p.w("scalar", "sv")
        p.do("scalar", lambda sc, hf=hf: sc.activation(
            hpostb[:, :], hconv[:, :], AF.Silu, bias=fcb_s[:, hf:hf + 1]),
            inc=("sa", 1))
        sa_here = p.n["sa"]
        p.w("sync", "sa", sa_here)
        dma("sync", hpost_d[hf * 128:(hf + 1) * 128, :], hpostb[:, :],
            sem="q21")
        p.w("scalar", "q21")

    p.barrier()
    for q32 in range(32):
        dma("sync", w2qb[:, q32 * D_MODEL:(q32 + 1) * D_MODEL],
            w2q_d[q32 * 128:(q32 + 1) * 128, :], sem="q22")
    p.w("tensor", "q22")
    for su in range(NSUB):
        p.w("sync", "sv")
        hp0 = hpost_d[0:1, su * 128:su * 128 + 1]
        p.do("sync", lambda e, hp0=hp0: e.dma_start(
            out=xq2fm[:, :],
            in_=bass.AP(hp0.tensor, hp0.offset,
                        [[GRID, 128], [128 * GRID, 32], [1, 128]])),
            inc=("q23", 16))
        p.w("sync", "q23")
        p.w("sync", "sv")
        for fc in range(32):
            p.do("sync", lambda e, fc=fc: e.dma_start_transpose(
                out=htmb[:, fc * 128:(fc + 1) * 128],
                in_=xq2fm[:, fc * 128:(fc + 1) * 128]), inc=("qt3", 16))
        p.w("vector", "qt3")
        p.w("vector", "sv")
        p.do("vector", lambda v: v.tensor_reduce(
            qcol[:, 4:5], htmb[:, :], AX.X, ALU.max,
            apply_absolute_value=True), inc=("sv", 1))
        p.w("vector", "sv")
        p.do("vector", lambda v: v.tensor_scalar(qcol[:, 4:5], qcol[:, 4:5],
                                                 1e-5, None, ALU.max),
             inc=("sv", 1))
        p.w("vector", "sv")
        p.do("vector", lambda v: v.reciprocal(qcol[:, 5:6], qcol[:, 4:5]),
             inc=("sv", 1))
        p.w("vector", "sv")
        p.do("vector", lambda v: v.tensor_scalar(qcol[:, 5:6], qcol[:, 5:6],
                                                 127.0, None, ALU.mult),
             inc=("sv", 1))
        p.do("vector", lambda v: v.tensor_scalar(qcol[:, 6:7], qcol[:, 4:5],
                                                 1.0 / 127.0, None, ALU.mult),
             inc=("sv", 1))
        p.w("vector", "sv")
        p.do("vector", lambda v: v.tensor_scalar(qv4[:, :], htmb[:, :],
                                                 qcol[:, 5:6], None, ALU.mult),
             inc=("sv", 1))
        p.w("vector", "sv")
        p.do("vector", lambda v: v.tensor_copy(qyi4[:, :], qv4[:, :]),
             inc=("sv", 1))
        p.w("vector", "sv")
        p.do("vector", lambda v: v.tensor_copy(qy4[:, :], qyi4[:, :]),
             inc=("sv", 1))
        p.w("vector", "sv")
        p.do("vector", lambda v: v.tensor_tensor(qr4[:, :], qv4[:, :], qy4[:, :],
                                                 ALU.subtract), inc=("sv", 1))
        p.w("vector", "sv")
        p.do("vector", lambda v: v.scalar_tensor_tensor(
            qt14[:, :], qr4[:, :], 0.5, qy4[:, :], ALU.is_gt, ALU.add),
            inc=("sv", 1))
        p.w("vector", "sv")
        p.do("vector", lambda v: v.scalar_tensor_tensor(
            qv4[:, :], qr4[:, :], -0.5, qt14[:, :], ALU.is_ge, ALU.add),
            inc=("sv", 1))
        p.w("vector", "sv")
        p.do("vector", lambda v: v.tensor_scalar(xq2tm[:, :], qv4[:, :], -1.0,
                                                 None, ALU.add), inc=("sv", 1))
        p.w("sync", "sv")
        for fc in range(32):
            p.do("sync", lambda e, fc=fc: e.dma_start_transpose(
                out=xq2fm[:, fc * 128:(fc + 1) * 128],
                in_=xq2tm[:, fc * 128:(fc + 1) * 128]), inc=("qt4", 16))
        p.w("tensor", "qt4")
        p.w("tensor", "sv")
        for fc in range(32):
            for q in range(2):
                p.do("tensor", lambda t, fc=fc, q=q: t.matmul(
                    pB1[:, q * 512:(q + 1) * 512],
                    xq2fm[:, fc * 128:(fc + 1) * 128],
                    w2qb[:, fc * D_MODEL + q * 512: fc * D_MODEL + (q + 1) * 512],
                    start=(fc == 0), stop=(fc == 31)), inc=("sp", 1))
        p.w("scalar", "sp")
        p.w("scalar", "q25")
        p.w("scalar", "sv")
        p.do("scalar", lambda sc: sc.activation(fotm[:, :], pB1[:, 0:1024],
                                                AF.Copy, scale=qcol[:, 6:7]),
             inc=("sa", 1))
        p.w("vector", "sa")
        ln_tm(fotm[:, :], fotm[:, :], D_MODEL)
        p.w("sync", "sv")
        dma("sync", x2ld3[:, :], x2_d[su * 128:(su + 1) * 128, :], sem="q24")
        p.w("vector", "q24")
        p.do("vector", lambda v: v.tensor_tensor(fotm[:, :], fotm[:, :],
                                                 x2ld3[:, :], ALU.add),
             inc=("sv", 1))
        sv_out = p.n["sv"]
        p.w("sync", "sv", sv_out)
        if su == 0:
            dma("sync", out_d[0:64, :], fotm[64:128, :], sem="q25")
        elif su == NSUB - 1:
            dma("sync", out_d[SEG - 64:SEG, :], fotm[0:64, :], sem="q25")
        else:
            dma("sync", out_d[su * 128 - 64:(su + 1) * 128 - 64, :],
                fotm[:, :], sem="q25")
        p.w("tensor", "sv")

    p.w("sync", "q25")

    with nc.Block() as block:
        p.emit(block)
    ctx.close()
    return nc



def kernel(**inputs):
    x = np.ascontiguousarray(np.asarray(inputs["L0_attn_x"], np.float32))
    shards = _host_shards(x)
    consts = _host_consts(inputs)
    nc = build_nc()
    in_maps = []
    for core in range(8):
        m = dict(consts)
        m.update(shards[core])
        in_maps.append(m)
    res = run_bass_kernel_spmd(nc, in_maps, core_ids=list(range(8)))
    out = np.zeros((2, SEQ, D_MODEL), np.float32)
    for core in range(8):
        b, j = core // NSEG, core % NSEG
        out[b, j * SEG:(j + 1) * SEG] = res.results[core]["out"]
    return out


# version 04
# speedup vs baseline: 1.2334x; 1.2334x over previous
import numpy as np
import ml_dtypes
import concourse.bass as bass
import concourse.mybir as mybir
from concourse.bass_utils import run_bass_kernel_spmd
from contextlib import ExitStack

F32 = mybir.dt.float32
BF16 = mybir.dt.bfloat16
I32 = mybir.dt.int32
U8 = mybir.dt.uint8
AF = mybir.ActivationFunctionType
ALU = mybir.AluOpType
AX = mybir.AxisListType

D_MODEL = 1024
D_STATE = 16
HEADDIM = 64
CHUNK = 64
D_INNER = 2048
NHEADS = 32
CONV_DIM = 2080
D_IN_PROJ = 4160
D_HIDDEN = 4096
SEQ = 8192
SEG = 2048
NSEG = 4
GRID = 2176
NCH = 34
NSUB = 17
KC = 8
EPS = 1e-5
SLOT = 192
SCW = 704

ENGINES = ("sync", "vector", "scalar", "tensor", "gpsimd")
PA_TILES = [(0, 512), (512, 512), (1024, 512), (1536, 512), (2048, 128)]


def bcast_free(ap_col, part, dims):
    return bass.AP(ap_col.tensor, ap_col.offset,
                   [[ap_col.ap[0][0], part]] + [list(d) for d in dims])



def _host_shards(x):
    shards = []
    for core in range(8):
        b, j = core // NSEG, core % NSEG
        start = j * SEG
        lo = start - CHUNK
        g = np.zeros((GRID, D_MODEL), np.float32)
        s0, s1 = max(lo, 0), min(start + SEG + CHUNK, SEQ)
        g[s0 - lo:s1 - lo] = x[b, s0:s1]
        h3 = np.zeros((4, D_MODEL), np.float32)
        h0, h1 = max(lo - 3, 0), max(lo, 0)
        h3[3 - (h1 - h0):3] = x[b, h0:h1]
        maskrow = np.ones((NSUB * 128,), np.float32)
        if j == 0:
            maskrow[CHUNK - 1] = 0.0
        if j == NSEG - 1:
            maskrow[SEG + CHUNK] = 0.0
        mask_tm = np.ascontiguousarray(maskrow.reshape(NSUB, 128).T)
        pm = np.zeros((128, 4), np.float32)
        for k in (1, 2, 3):
            pm[:, k - 1] = 1.0 if j >= k else 0.0
        pm[:, 3] = 1.0 if j >= 1 else 0.0
        slot = np.zeros((4, 1), np.int32)
        for k in (1, 2, 3):
            slot[k - 1, 0] = SLOT * max(j - k, 0)
        shards.append(dict(xg=g, xh3=h3, maskrow=mask_tm, pmc=pm, slotoff=slot))
    return shards


def _host_consts(inputs):
    c = {}
    c["idn"] = np.eye(128, dtype=np.float32)
    cp = np.zeros((128, 4), np.float32)
    cp[:, 0] = 1e-5; cp[:, 1] = 1.0; cp[:, 2] = -0.5; cp[:, 3] = 0.0
    c["cpool"] = cp
    e = np.zeros((32, 64), np.float32); e[0, :] = 1.0
    c["e064"] = e
    e2 = np.zeros((32, 128), np.float32); e2[0, :] = 1.0
    c["e0128"] = e2
    c["onescol"] = np.ones((128, 1), np.float32)
    hsel = np.zeros((32, 128), np.float32)
    for h in range(32):
        g = h // 11
        hsel[h, 32 * g:32 * g + 16] = 1.0
    c["hsel"] = hsel
    gmask = np.zeros((32, SCW), np.float32)
    for h in range(32):
        i = h - 11 * (h // 11)
        gmask[h, 64 * i:64 * i + 64] = 1.0
    c["gmask"] = gmask
    negio = np.zeros((32, 2048), np.float32)
    for h in range(32):
        negio[h, 64 * h:64 * h + 64] = -1.0
    c["negio"] = negio
    t = np.zeros((64, 64), np.float32)
    for s in range(64):
        t[s, s:] = 1.0
    c["tri"] = t
    dtb = np.asarray(inputs["L0_attn_dt_bias"], np.float32)
    c["dtbrep"] = np.ascontiguousarray(np.tile(dtb[None, :], (128, 1)))
    Dsk = np.asarray(inputs["L0_attn_D"], np.float32)
    c["dskrep"] = np.ascontiguousarray(
        np.tile(np.repeat(Dsk, HEADDIM)[None, :], (128, 1)))
    nw = np.asarray(inputs["L0_attn_norm_w"], np.float32)
    c["nwrep"] = np.ascontiguousarray(np.tile(nw[None, :], (128, 1)))
    c["alog"] = np.ascontiguousarray(
        np.asarray(inputs["L0_attn_A_log"], np.float32).reshape(32, 1))
    cw = np.asarray(inputs["L0_attn_conv_w"], np.float32)
    cwp = np.zeros((17 * 128, 4), np.float32)
    cwp[:CONV_DIM] = cw
    c["convw"] = np.ascontiguousarray(
        cwp.reshape(17, 128, 4).transpose(1, 0, 2).reshape(128, 68))
    cb = np.asarray(inputs["L0_attn_conv_b"], np.float32)
    cbp = np.zeros((17 * 128,), np.float32)
    cbp[:CONV_DIM] = cb
    c["convb"] = np.ascontiguousarray(cbp.reshape(17, 128).T)
    fcw = np.asarray(inputs["L0_ffn_conv_w"], np.float32).reshape(D_HIDDEN, 3)
    c["fcw"] = np.ascontiguousarray(
        fcw.reshape(32, 128, 3).transpose(1, 0, 2).reshape(128, 96))
    fcb = np.asarray(inputs["L0_ffn_conv_b"], np.float32)
    c["fcb"] = np.ascontiguousarray(fcb.reshape(32, 128).T)
    c["wint"] = np.ascontiguousarray(np.asarray(inputs["L0_attn_W_in"], np.float32).T)
    c["wout"] = np.ascontiguousarray(np.asarray(inputs["L0_attn_W_out"], np.float32).T)
    c["w1t"] = np.ascontiguousarray(np.asarray(inputs["L0_ffn_w1"], np.float32).T)
    c["w2t"] = np.ascontiguousarray(np.asarray(inputs["L0_ffn_w2"], np.float32).T)
    return c



class P:
    def __init__(self, nc, ctx):
        self.nc = nc
        self.ops = {e: [] for e in ENGINES}
        self.h = {}
        self.n = {}
        for name in ("qs", "qg", "sv", "sa", "sp", "sg", "cc", "st",
                     "qw", "qe", "q1", "q2", "q3", "q4", "q5", "q6", "q7",
                     "q8", "q9", "q10", "q11", "q12", "q13", "q14", "q15",
                     "q16", "q17", "q18", "q19", "q20", "q21", "q22", "q23",
                     "q24", "q25"):
            self.h[name] = ctx.enter_context(nc.semaphore(name))
            self.n[name] = 0
        self.env = {}

    def do(self, eng, f, inc=None):
        if inc is not None:
            sem, amt = inc
            self.n[sem] += amt
            handle = self.h[sem]

            def g(e, f=f, handle=handle, amt=amt):
                f(e).then_inc(handle, amt)
        else:
            def g(e, f=f):
                f(e)
        self.ops[eng].append(g)

    def w(self, eng, sem, val=None):
        v = self.n[sem] if val is None else val
        h = self.h[sem]
        self.ops[eng].append(lambda e, h=h, v=v: e.wait_ge(h, v))

    def barrier(self, engs=ENGINES, sems=None):
        if sems is None:
            sems = [k for k in self.n if k != "st"]
        for eng in engs:
            for s in sems:
                if self.n[s] > 0:
                    self.w(eng, s)

    def emit(self, block):
        for e in ENGINES:
            fs = self.ops[e]
            getattr(block, e)(lambda eng, fs=fs: [f(eng) for f in fs])



def build_nc():
    nc = bass.Bass("TRN2", target_bir_lowering=False, dynamic_dma_scratch_size=8192)
    ctx = ExitStack()

    def din(name, shape, dtype=F32):
        return nc.dram_tensor(name, list(shape), dtype, kind="ExternalInput").ap()

    def dint(name, shape, dtype=F32):
        return nc.dram_tensor(name, list(shape), dtype, kind="Internal").ap()

    def sb(name, shape, dtype=F32):
        return ctx.enter_context(nc.sbuf_tensor(name, shape, dtype))

    xg_d = din("xg", [GRID, D_MODEL])
    xh3_d = din("xh3", [4, D_MODEL])
    maskrow_d = din("maskrow", [128, NSUB])
    pmc_d = din("pmc", [128, 4])
    slotoff_d = din("slotoff", [4, 1], I32)
    idn_d = din("idn", [128, 128])
    cpool_d = din("cpool", [128, 4])
    e064_d = din("e064", [32, 64])
    e0128_d = din("e0128", [32, 128])
    onescol_d = din("onescol", [128, 1])
    hsel_d = din("hsel", [32, 128])
    gmask_d = din("gmask", [32, SCW])
    negio_d = din("negio", [32, 2048])
    tri_d = din("tri", [64, 64])
    dtbrep_d = din("dtbrep", [128, 32])
    dskrep_d = din("dskrep", [128, D_INNER])
    nwrep_d = din("nwrep", [128, D_INNER])
    alog_d = din("alog", [32, 1])
    convw_d = din("convw", [128, 68])
    convb_d = din("convb", [128, 17])
    fcw_d = din("fcw", [128, 96])
    fcb_d = din("fcb", [128, 32])
    wint_d = din("wint", [D_MODEL, D_IN_PROJ])
    wout_d = din("wout", [D_INNER, D_MODEL])
    w1t_d = din("w1t", [D_MODEL, D_HIDDEN])
    w2t_d = din("w2t", [D_HIDDEN, D_MODEL])
    out_d = nc.dram_tensor("out", [SEG, D_MODEL], F32, kind="ExternalOutput").ap()

    zsp_d = dint("zsp", [GRID, D_INNER], BF16)
    x2_d = dint("x2d", [GRID, D_MODEL])
    hpost_d = dint("hpostd", [D_HIDDEN, GRID], BF16)
    acs_d = dint("acsd", [NHEADS, GRID])
    alast_d = dint("alastd", [NCH, NHEADS])
    w1q_d = dint("w1qd", [D_MODEL, D_HIDDEN], BF16)
    w2q_d = dint("w2qd", [D_HIDDEN, D_MODEL], BF16)
    agin_d = dint("agin", [SLOT, SCW])
    agout_d = dint("agout", [4 * SLOT, SCW])
    snap_d = dint("snapd", [33 * 80, SCW], BF16)

    arenaA = ctx.enter_context(nc.sbuf_tensor("arenaA", [128, 69632], U8))
    arenaB = ctx.enter_context(nc.sbuf_tensor("arenaB", [128, 66560], U8))
    arenaC = ctx.enter_context(nc.sbuf_tensor("arenaC", [128, 16384], U8))
    arenaD = ctx.enter_context(nc.sbuf_tensor("arenaD", [128, 8192], U8))

    def overlay(base, name, shape, dtype, off=0):
        h = nc.virtual_sbuf_tensor(name, list(shape), dtype)
        nc._tensor(h.name, h.shape, h.dtype, type="SB")
        mloc = nc.lookup_mloc(h)
        mloc.allocated = True
        mloc.addr = nc.lookup_mloc(base).addr + off
        return h

    xhtm = overlay(arenaA, "xhtm", [128, NSUB * D_INNER], BF16)
    xq1fm = overlay(arenaA, "xq1fm", [128, KC * GRID], BF16)
    qv4 = overlay(arenaA, "qv4", [128, 4096], F32, 0)
    qyi4 = overlay(arenaA, "qyi4", [128, 4096], I32, 16384)
    qt14 = overlay(arenaA, "qt14", [128, 4096], F32, 16384)
    qy4 = overlay(arenaA, "qy4", [128, 4096], F32, 32768)
    qr4 = overlay(arenaA, "qr4", [128, 4096], F32, 49152)
    qv1 = overlay(arenaA, "qv1", [128, 1024], F32, 34816)
    qyi1 = overlay(arenaA, "qyi1", [128, 1024], I32, 38912)
    qt11 = overlay(arenaA, "qt11", [128, 1024], F32, 38912)
    qy1 = overlay(arenaA, "qy1", [128, 1024], F32, 43008)
    qr1 = overlay(arenaA, "qr1", [128, 1024], F32, 47104)
    winb = overlay(arenaB, "winb", [128, KC * D_IN_PROJ], BF16)
    snapbuf = overlay(arenaB, "snapbuf", [128, 2 * SCW], BF16, 0)
    snapld = overlay(arenaB, "snapld", [128, 2 * SCW], F32, 2816)
    woutb = overlay(arenaB, "woutb", [128, 16 * D_MODEL], BF16)
    w1qb = overlay(arenaB, "w1qb", [128, KC * D_HIDDEN], BF16)
    w2qb = overlay(arenaB, "w2qb", [128, 32 * D_MODEL], BF16)
    sz = overlay(arenaB, "sz", [128, D_INNER], F32, 8448)
    ytmp = overlay(arenaB, "ytmp", [128, D_INNER], F32, 16640)
    ytmp2 = overlay(arenaB, "ytmp2", [128, D_INNER], F32, 24832)
    flatbuf = overlay(arenaB, "flatbuf", [32, 2048], F32, 33024)
    wtile = overlay(arenaC, "wtile", [128, 4096], F32)
    htmb = overlay(arenaC, "htmb", [128, D_HIDDEN], BF16)
    wqtile = overlay(arenaD, "wqtile", [128, 4096], BF16)
    xq2tm = overlay(arenaD, "xq2tm", [128, D_HIDDEN], BF16)
    xnf = overlay(arenaC, "xnf", [128, KC * 512], BF16)
    ebuf = overlay(arenaC, "ebuf", [128, D_INNER], F32, 0)
    dmin = overlay(arenaC, "dmin", [128, D_INNER], F32, 8192)
    hconv = overlay(arenaA, "hconv", [128, GRID], F32, 34816)
    hconvB = overlay(arenaA, "hconvB", [128, GRID], F32, 56576)
    h1bufB = overlay(arenaA, "h1bufB", [128, GRID], BF16, 52224)
    h1buf = overlay(arenaA, "h1buf", [128, GRID], BF16, 43520)
    hpostb = overlay(arenaA, "hpostb", [128, GRID], BF16, 47872)
    xtm_t = overlay(arenaD, "xtm_t", [128, D_MODEL], F32, 0)
    xntm = overlay(arenaD, "xntm", [128, D_MODEL], F32, 4096)
    gbuf = overlay(arenaD, "gbuf", [128, D_INNER], BF16, 0)
    xdt = overlay(arenaD, "xdt", [128, D_INNER], BF16, 4096)
    slotb = overlay(arenaC, "slotb", [128, 3 * SCW], F32, 0)
    p30b = overlay(arenaC, "p30b", [128, SCW], F32, 8448)
    tcomb = overlay(arenaC, "tcomb", [128, SCW], F32, 11264)
    ynf = overlay(arenaC, "ynf", [128, D_INNER], BF16, 0)
    motm = overlay(arenaD, "motm", [128, D_MODEL], F32, 0)
    xgtm = overlay(arenaD, "xgtm", [128, D_MODEL], F32, 4096)
    x2n_t = overlay(arenaC, "x2n_t", [128, D_MODEL], F32, 0)
    xq1tm = overlay(arenaC, "xq1tm", [128, D_MODEL], BF16, 4096)
    acspad = overlay(arenaC, "acspad", [32, 64 + GRID], F32, 0)
    srun = overlay(arenaC, "srun", [128, SCW], F32, 0)
    xh3tm = overlay(arenaA, "xh3tm", [4, D_MODEL], F32, 65536)
    zchunkA = overlay(arenaC, "zchunkA", [128, D_INNER], BF16, 8192)
    convt = overlay(arenaC, "convt", [128, 512], F32, 12288)
    xbca = overlay(arenaC, "xbca", [128, 512], BF16, 14336)
    zchunk2 = overlay(arenaB, "zchunk2", [128, D_INNER], BF16, 24832)
    ucomb = overlay(arenaB, "ucomb", [128, SCW], F32, 41216)
    x2ld1 = overlay(arenaD, "x2ld1", [128, D_MODEL], F32, 0)
    x2ld3 = overlay(arenaC, "x2ld3", [128, D_MODEL], F32, 8192)
    fotm = overlay(arenaC, "fotm", [128, D_MODEL], F32, 12288)
    xq2fm = overlay(arenaA, "xq2fm", [128, D_HIDDEN], BF16, 49152)

    idn_s = sb("idn_s", [128, 128])
    cpool_s = sb("cpool_s", [128, 4])
    idnb_s = sb("idnb_s", [128, 128], BF16)
    e064_s = sb("e064_s", [32, 64])
    e0128_s = sb("e0128_s", [32, 128])
    onescol_s = sb("onescol_s", [128, 1])
    hsel_s = sb("hsel_s", [32, 128], BF16)
    gmask_s = sb("gmask_s", [32, SCW], BF16)
    negio_s = sb("negio_s", [32, 2048])
    eacsb = sb("eacsb", [128, 32])
    tri_s = sb("tri_s", [64, 64])
    tri64hi = sb("tri64hi", [128, 64])
    dtbrep_s = sb("dtbrep_s", [128, 32])
    dskrep_s = sb("dskrep_s", [128, D_INNER], BF16)
    nwrep_s = sb("nwrep_s", [128, D_INNER], BF16)
    alog_s = sb("alog_s", [32, 1])
    nega_s = sb("nega_s", [32, 1])
    convw_s = sb("convw_s", [128, 68])
    convb_s = sb("convb_s", [128, 17])
    fcw_s = sb("fcw_s", [128, 96])
    fcb_s = sb("fcb_s", [128, 32])
    maskrow_s = sb("maskrow_s", [128, NSUB])
    pmc_s = sb("pmc_s", [128, 4])
    slotoff_s = sb("slotoff_s", [4, 1], I32)
    dttm = sb("dttm", [128, NSUB * 32])
    acsfm = sb("acsfm", [32, GRID])
    acstm = sb("acstm", [128, NSUB * 32])
    alast_s = sb("alast_s", [32, NCH])
    expal_s = sb("expal_s", [32, NCH])
    gamlog = sb("gamlog", [32, NCH])
    expgam_s = sb("expgam_s", [32, NCH])
    eg_s = sb("eg_s", [32, 2])
    bfm = sb("bfm", [16, GRID], BF16)
    cfm4 = sb("cfm4", [128, GRID], BF16)
    btm = sb("btm", [128, NSUB * 16])
    xh3buf = sb("xh3buf", [128, 17 * 3], BF16)
    xbchal = sb("xbchal", [128, 17 * 3], BF16)
    xnhf = sb("xnhf", [128, KC * 4], BF16)
    alrow = sb("alrow", [32, 2 * 32])
    decaytm = sb("decaytm", [128, 2 * 32])
    bd = sb("bd", [128, 512], BF16)
    m1m = sb("m1m", [128, 64], BF16)
    sin = sb("sin", [128, SCW])
    stbuf = sb("stbuf", [128, SCW], BF16)
    gslot = sb("gslot", [32, 3 * 2])
    gmg = sb("gmg", [32, SCW], BF16)
    rmscol = sb("rmscol", [128, 2])
    stat6 = sb("stat6", [128, 24])
    stat2 = sb("stat2", [128, 2])
    lncol = sb("lncol", [128, 8])
    xbcbuf = sb("xbcbuf", [128, 2 * 516], BF16)
    dtraw = sb("dtraw", [128, 2 * 32])
    qcol = sb("qcol", [128, 8])
    wabs = sb("wabs", [128, 1])
    wsum1 = sb("wsum1", [128, 1])
    scol = sb("scol", [128, 4])
    valbuf = sb("valbuf", [32, 2])
    psc = sb("psc", [1, 8])

    pB1 = ctx.enter_context(nc.psum_tensor("pB1", [128, 2048], F32))
    pB2 = ctx.enter_context(nc.psum_tensor("pB2", [128, 2048], F32))

    def overlay_ps(base, name, shape, dtype, off_bytes):
        h = nc.virtual_psum_tensor(name, list(shape), dtype)
        nc._tensor(h.name, h.shape, h.dtype, type="PSUM")
        mloc = nc.lookup_mloc(h)
        mloc.allocated = True
        mloc.addr = nc.lookup_mloc(base).addr + off_bytes
        return h

    pTb = overlay_ps(pB2, "pTb", [128, 1024], BF16, 6144)

    p = P(nc, ctx)

    def dma(queue, out, in_, sem=None):
        q = sem if sem else ("qg" if queue == "gpsimd" else "qs")
        p.do(queue, lambda e, out=out, in_=in_: e.dma_start(out=out, in_=in_),
             inc=(q, 16))

    def ln_tm(xap, outap, width, part=128):
        nq = width // 512
        for q in range(nq):
            p.do("vector", lambda v, q=q, xap=xap, part=part: v.bn_stats(
                stat6[0:part, 6 * q:6 * q + 6], xap[:, 512 * q:512 * (q + 1)]),
                inc=("sv", 1))
        p.w("vector", "sv")
        p.do("vector", lambda v, nq=nq, part=part: v.bn_aggr(
            stat2[0:part, 0:2], stat6[0:part, 0:6 * nq]), inc=("sv", 1))
        p.w("scalar", "sv")
        p.do("scalar", lambda sc, part=part: sc.activation(
            lncol[0:part, 1:2], stat2[0:part, 1:2], AF.Ln, bias=EPS),
            inc=("sa", 1))
        p.w("scalar", "sa")
        p.do("scalar", lambda sc, part=part: sc.activation(
            lncol[0:part, 1:2], lncol[0:part, 1:2], AF.Exp, scale=-0.5),
            inc=("sa", 1))
        p.w("vector", "sa")
        p.do("vector", lambda v, xap=xap, outap=outap, part=part:
             v.tensor_scalar(outap, xap, stat2[0:part, 0:1],
                             lncol[0:part, 1:2], ALU.subtract, ALU.mult),
             inc=("sv", 1))

    nc.const_aps.register(F32, 1e-5, cpool_s[:, 0:1])
    nc.const_aps.register(F32, 1.0, cpool_s[:, 1:2])
    nc.const_aps.register(F32, -0.5, cpool_s[:, 2:3])
    nc.const_aps.register(F32, 0.0, cpool_s[:, 3:4])
    for dst, src in [(idn_s, idn_d), (cpool_s, cpool_d),
                     (e064_s, e064_d), (e0128_s, e0128_d),
                     (onescol_s, onescol_d), (negio_s, negio_d),
                     (tri_s, tri_d), (dtbrep_s, dtbrep_d),
                     (alog_s, alog_d), (convw_s, convw_d),
                     (convb_s, convb_d), (fcw_s, fcw_d), (fcb_s, fcb_d),
                     (pmc_s, pmc_d), (slotoff_s, slotoff_d),
                     (maskrow_s, maskrow_d), (xh3tm, xh3_d)]:
        dma("sync", dst[:, :], src[:, :])
    dma("gpsimd", dskrep_s[:, :], dskrep_d[:, :])
    dma("gpsimd", nwrep_s[:, :], nwrep_d[:, :])
    dma("gpsimd", hsel_s[:, :], hsel_d[:, :])
    dma("gpsimd", gmask_s[:, :], gmask_d[:, :])
    for kc in range(KC):
        dma("gpsimd", winb[:, kc * D_IN_PROJ:(kc + 1) * D_IN_PROJ],
            wint_d[kc * 128:(kc + 1) * 128, :])
    p.w("vector", "qs")
    p.do("vector", lambda v: v.tensor_copy(idnb_s[:, :], idn_s[:, :]),
         inc=("sv", 1))
    p.do("gpsimd", lambda g: g.memset(xh3buf[:, :], 0.0), inc=("sg", 1))
    p.do("gpsimd", lambda g: g.memset(xbchal[:, :], 0.0), inc=("sg", 1))
    p.do("gpsimd", lambda g: g.memset(tri64hi[:, :], 0.0), inc=("sg", 1))
    p.w("gpsimd", "sg")
    p.do("gpsimd", lambda g: g.iota(
        tri64hi[:, :], [[1, 64]], base=64, channel_multiplier=-1,
        allow_small_or_imprecise_dtypes=True), inc=("sg", 1))
    p.w("vector", "sg")
    p.do("vector", lambda v: v.tensor_scalar(tri64hi[:, :], tri64hi[:, :], 0.0,
                                             None, ALU.is_ge), inc=("sv", 1))
    p.w("scalar", "qs")
    p.do("scalar", lambda sc: sc.activation(nega_s[:, :], alog_s[:, :], AF.Exp),
         inc=("sa", 1))
    p.w("vector", "sa")
    p.do("vector", lambda v: v.tensor_scalar(nega_s[:, :], nega_s[:, :], -1.0,
                                             None, ALU.mult), inc=("sv", 1))

    ln_tm(xh3tm[0:4, :], xh3tm[0:4, :], D_MODEL, part=4)
    p.w("tensor", "sv")
    for kc in range(KC):
        p.do("tensor", lambda t, kc=kc: t.transpose(
            pB1[:, kc * 4:(kc + 1) * 4], xh3tm[0:4, kc * 128:(kc + 1) * 128],
            idn_s[0:4, 0:4]), inc=("sp", 1))
    p.w("vector", "sp")
    p.do("vector", lambda v: v.tensor_copy(xnhf[:, :], pB1[:, 0:KC * 4]),
         inc=("sv", 1))
    sv_xnhf = p.n["sv"]
    p.w("tensor", "sv", sv_xnhf)
    p.w("tensor", "qg")
    for fc in range(17):
        f0 = D_INNER + fc * 128
        fw = min(128, D_IN_PROJ - 32 - f0)
        p.w("tensor", "sv")
        for kc in range(KC):
            p.do("tensor", lambda t, kc=kc, f0=f0, fw=fw, fc=fc: t.matmul(
                pB2[0:fw, 1792 + fc % 2 * 4: 1792 + fc % 2 * 4 + 3],
                winb[:, kc * D_IN_PROJ + f0: kc * D_IN_PROJ + f0 + fw],
                xnhf[:, kc * 4: kc * 4 + 3],
                start=(kc == 0), stop=(kc == KC - 1)), inc=("sp", 1))
        p.w("vector", "sp")
        p.do("vector", lambda v, fc=fc, fw=fw: v.tensor_copy(
            xh3buf[0:fw, fc * 3:fc * 3 + 3],
            pB2[0:fw, 1792 + fc % 2 * 4: 1792 + fc % 2 * 4 + 3]), inc=("sv", 1))

    def wquant(wsrc_d, wdst_d, nrc, ncols, si):
        p.w("gpsimd", "sv")
        p.do("gpsimd", lambda g: g.memset(wsum1[:, :], 0.0), inc=("sg", 1))
        for rc in range(nrc):
            p.w("sync", "sv")
            if rc == 0:
                p.w("sync", "sg")
            dma("sync", wtile[:, 0:ncols], wsrc_d[rc * 128:(rc + 1) * 128, :],
                sem="qw")
            p.w("vector", "qw")
            if rc == 0:
                p.w("vector", "sg")
            p.do("vector", lambda v, n=ncols: v.tensor_reduce(
                wabs[:, :], wtile[:, 0:n], AX.X, ALU.add,
                apply_absolute_value=True), inc=("sv", 1))
            p.w("vector", "sv")
            p.do("vector", lambda v: v.tensor_tensor(
                wsum1[:, :], wsum1[:, :], wabs[:, :], ALU.add), inc=("sv", 1))
        p.w("tensor", "sv")
        p.do("tensor", lambda t: t.matmul(pB1[0:1, 0:1], onescol_s[:, :],
                                          wsum1[:, :], start=True, stop=True),
             inc=("sp", 1))
        p.w("vector", "sp")
        p.do("vector", lambda v, nrc=nrc, ncols=ncols: v.tensor_scalar(
            psc[0:1, 0:1], pB1[0:1, 0:1], 1.0 / (nrc * 128 * ncols), None,
            ALU.mult), inc=("sv", 1))
        p.w("vector", "sv")
        p.do("vector", lambda v: v.tensor_scalar(psc[0:1, 0:1], psc[0:1, 0:1],
                                                 1e-5, None, ALU.max),
             inc=("sv", 1))
        p.w("vector", "sv")
        p.do("vector", lambda v: v.reciprocal(psc[0:1, 1:2], psc[0:1, 0:1]),
             inc=("sv", 1))
        p.w("gpsimd", "sv")
        p.do("gpsimd", lambda g: g.memset(valbuf[:, :], 0.0), inc=("sg", 1))
        p.w("vector", "sg")
        p.do("vector", lambda v: v.tensor_copy(valbuf[0:1, 0:1], psc[0:1, 1:2]),
             inc=("sv", 1))
        p.w("vector", "sv")
        p.do("vector", lambda v: v.tensor_copy(valbuf[0:1, 1:2], psc[0:1, 0:1]),
             inc=("sv", 1))
        p.w("tensor", "sv")
        p.do("tensor", lambda t: t.matmul(pB1[0:128, 8:10], e0128_s[:, :],
                                          valbuf[:, :], start=True, stop=True),
             inc=("sp", 1))
        p.w("vector", "sp")
        p.do("vector", lambda v, si=si: v.tensor_copy(
            scol[:, si:si + 1], pB1[0:128, 8:9]), inc=("sv", 1))
        p.w("vector", "sv")
        p.do("vector", lambda v, si=si: v.tensor_copy(
            scol[:, si + 1:si + 2], pB1[0:128, 9:10]), inc=("sv", 1))
        for rc in range(nrc):
            p.w("sync", "sv")
            dma("sync", wtile[:, 0:ncols], wsrc_d[rc * 128:(rc + 1) * 128, :],
                sem="qw")
            p.w("vector", "qw")
            p.do("vector", lambda v, n=ncols, si=si: v.tensor_scalar(
                qv4[:, 0:n], wtile[:, 0:n], scol[:, si:si + 1], None, ALU.mult),
                inc=("sv", 1))
            p.w("vector", "sv")
            p.do("vector", lambda v, n=ncols: v.tensor_scalar(
                qt14[:, 0:n], qv4[:, 0:n], 0.5, None, ALU.is_gt), inc=("sv", 1))
            p.w("vector", "sv")
            p.do("vector", lambda v, n=ncols: v.scalar_tensor_tensor(
                qr4[:, 0:n], qv4[:, 0:n], -0.5, qt14[:, 0:n], ALU.is_lt,
                ALU.subtract), inc=("sv", 1))
            p.w("vector", "sv")
            p.do("vector", lambda v, n=ncols, si=si: v.tensor_scalar(
                qv4[:, 0:n], qr4[:, 0:n], scol[:, si + 1:si + 2], -1.0,
                ALU.mult, ALU.mult), inc=("sv", 1))
            p.w("vector", "sv")
            p.w("vector", "qe")
            p.do("vector", lambda v, n=ncols: v.tensor_copy(
                wqtile[:, 0:n], qv4[:, 0:n]), inc=("sv", 1))
            sv_here = p.n["sv"]
            p.w("sync", "sv", sv_here)
            dma("sync", wdst_d[rc * 128:(rc + 1) * 128, :], wqtile[:, 0:ncols],
                sem="qe")

    wquant(w1t_d, w1q_d, 8, 4096, 0)
    wquant(w2t_d, w2q_d, 32, 1024, 2)

    p.barrier()
    for (t0, tw) in PA_TILES:
        nsub_t = tw // 128
        tpar = (t0 // 512) % 2
        for si in range(nsub_t):
            g0 = t0 + si * 128
            p.w("sync", "sv")
            dma("sync", xtm_t[:, :], xg_d[g0:g0 + 128, :], sem="q1")
            p.w("vector", "q1")
            ln_tm(xtm_t[:, :], xntm[:, :], D_MODEL)
            p.w("tensor", "sv")
            for kc in range(KC):
                p.do("tensor", lambda t, kc=kc: t.transpose(
                    pB1[:, kc * 128:(kc + 1) * 128],
                    xntm[:, kc * 128:(kc + 1) * 128], idn_s[:, :]),
                    inc=("sp", 1))
            p.w("vector", "sp")
            xnfb = xnf[:, si * 128:si * 128 + 1]
            xnfap = bass.AP(xnfb.tensor, xnfb.offset,
                            [[KC * 512, 128], [512, KC], [1, 128]])
            p.do("vector", lambda v, xnfap=xnfap: v.tensor_copy(
                xnfap, pB1[:, 0:KC * 128]), inc=("sv", 1))
            p.w("tensor", "sv")
        sv_xnf = p.n["sv"]

        for fc in range(17):
            f0 = D_INNER + fc * 128
            fw = min(128, D_IN_PROJ - 32 - f0)
            p.w("tensor", "sv")
            for kc in range(KC):
                p.do("tensor", lambda t, kc=kc, f0=f0, fw=fw, tw=tw: t.matmul(
                    pB2[0:fw, 0:tw],
                    winb[:, kc * D_IN_PROJ + f0: kc * D_IN_PROJ + f0 + fw],
                    xnf[:, kc * 512: kc * 512 + tw],
                    start=(kc == 0), stop=(kc == KC - 1)), inc=("sp", 1))
            b0 = 516 * tpar
            ob0 = 516 * (1 - tpar)
            p.w("vector", "sp")
            if t0 == 0:
                p.do("vector", lambda v, b0=b0, fc=fc: v.tensor_copy(
                    xbcbuf[:, b0:b0 + 3], xh3buf[:, fc * 3:fc * 3 + 3]),
                    inc=("sv", 1))
            else:
                p.do("vector", lambda v, b0=b0, fc=fc: v.tensor_copy(
                    xbcbuf[:, b0:b0 + 3], xbchal[:, fc * 3:fc * 3 + 3]),
                    inc=("sv", 1))
            p.do("vector", lambda v, b0=b0, fw=fw, tw=tw: v.tensor_copy(
                xbcbuf[0:fw, b0 + 3:b0 + 3 + tw], pB2[0:fw, 0:tw]),
                inc=("sv", 1))
            p.w("vector", "sv")
            p.do("vector", lambda v, b0=b0, tw=tw, fc=fc: v.tensor_copy(
                xbchal[:, fc * 3:fc * 3 + 3], xbcbuf[:, b0 + tw:b0 + tw + 3]),
                inc=("sv", 1))
            p.w("vector", "sv")
            p.do("vector", lambda v, b0=b0, tw=tw, fc=fc: v.tensor_scalar(
                convt[:, 0:tw], xbcbuf[:, b0:b0 + tw],
                convw_s[:, fc * 4:fc * 4 + 1], None, ALU.mult), inc=("sv", 1))
            for k in (1, 2, 3):
                p.w("vector", "sv")
                p.do("vector", lambda v, b0=b0, tw=tw, fc=fc, k=k:
                     v.scalar_tensor_tensor(
                         convt[:, 0:tw], xbcbuf[:, b0 + k:b0 + k + tw],
                         convw_s[:, fc * 4 + k:fc * 4 + k + 1], convt[:, 0:tw],
                         ALU.mult, ALU.add), inc=("sv", 1))
            p.w("scalar", "sv")
            p.w("scalar", "q2")
            p.do("scalar", lambda sc, tw=tw, fc=fc: sc.activation(
                xbca[:, 0:tw], convt[:, 0:tw], AF.Silu,
                bias=convb_s[:, fc:fc + 1]), inc=("sa", 1))
            if fc < 16:
                p.w("tensor", "sa")
                p.w("tensor", "sv")
                for si in range(nsub_t):
                    p.do("tensor", lambda t, si=si: t.transpose(
                        pTb[:, si * 128:si * 128 + 128],
                        xbca[:, si * 128:(si + 1) * 128], idnb_s[:, :]),
                        inc=("sp", 1))
                p.w("vector", "sp")
                xhb = xhtm[:, (t0 // 128) * D_INNER + fc * 128:
                           (t0 // 128) * D_INNER + fc * 128 + 1]
                xhap = bass.AP(xhb.tensor, xhb.offset,
                               [[NSUB * D_INNER, 128], [D_INNER, nsub_t],
                                [1, 128]])
                p.do("vector", lambda v, xhap=xhap, nsub_t=nsub_t:
                     v.tensor_copy(xhap, pTb[:, 0:nsub_t * 128]),
                     inc=("sv", 1))
            else:
                p.w("vector", "sa")
                p.do("vector", lambda v, t0=t0, tw=tw: v.tensor_copy(
                    bfm[:, t0:t0 + tw], xbca[0:16, 0:tw]), inc=("sv", 1))
                p.w("sync", "sa")
                for cg in range(3):
                    dma("sync", cfm4[32 * cg:32 * cg + 16, t0:t0 + tw],
                        xbca[16:32, 0:tw], sem="q2")
                p.w("tensor", "sa")
                p.w("tensor", "sv")
                for si in range(nsub_t):
                    p.do("tensor", lambda t, si=si: t.transpose(
                        pTb[:, si * 128:si * 128 + 32],
                        xbca[0:32, si * 128:(si + 1) * 128], idnb_s[0:32, 0:32]),
                        inc=("sp", 1))
                p.w("vector", "sp")
                ptb0 = pTb[:, 0:1]
                ptbap = bass.AP(ptb0.tensor, ptb0.offset,
                                [[1024, 128], [128, nsub_t], [1, 16]])
                p.do("vector", lambda v, ptbap=ptbap, t0=t0, nsub_t=nsub_t:
                     v.tensor_copy(
                         btm[:, (t0 // 128) * 16:(t0 // 128 + nsub_t) * 16],
                         ptbap), inc=("sv", 1))

        for si in range(nsub_t):
            gsub = t0 // 128 + si
            p.w("tensor", "sv")
            for kc in range(KC):
                for zs in range(4):
                    p.do("tensor", lambda t, kc=kc, si=si, zs=zs: t.matmul(
                        pB1[:, zs * 512:(zs + 1) * 512],
                        xnf[:, kc * 512 + si * 128: kc * 512 + si * 128 + 128],
                        winb[:, kc * D_IN_PROJ + zs * 512:
                             kc * D_IN_PROJ + (zs + 1) * 512],
                        start=(kc == 0), stop=(kc == KC - 1)), inc=("sp", 1))
                p.do("tensor", lambda t, kc=kc, si=si: t.matmul(
                    pB2[:, 1024:1056],
                    xnf[:, kc * 512 + si * 128: kc * 512 + si * 128 + 128],
                    winb[:, kc * D_IN_PROJ + 4128: kc * D_IN_PROJ + 4160],
                    start=(kc == 0), stop=(kc == KC - 1)), inc=("sp", 1))
            p.w("vector", "sp")
            p.w("scalar", "sp")
            p.w("scalar", "q3")
            p.do("scalar", lambda sc: sc.copy(zchunkA[:, :], pB1[:, 0:2048]),
                 inc=("sa", 1))
            sa_z = p.n["sa"]
            p.w("sync", "sa", sa_z)
            dma("sync", zsp_d[gsub * 128:(gsub + 1) * 128, :], zchunkA[:, :],
                sem="q3")
            p.do("vector", lambda v: v.tensor_tensor(
                dtraw[:, 0:32], pB2[:, 1024:1056], dtbrep_s[:, :], ALU.add),
                inc=("sv", 1))
            p.w("scalar", "sv")
            p.do("scalar", lambda sc: sc.activation(dtraw[:, 32:64],
                                                    dtraw[:, 0:32], AF.Exp),
                 inc=("sa", 1))
            p.w("scalar", "sa")
            p.do("scalar", lambda sc, gsub=gsub: sc.activation(
                dttm[:, gsub * 32:(gsub + 1) * 32], dtraw[:, 32:64], AF.Ln,
                bias=1.0), inc=("sa", 1))
            p.w("tensor", "sa")

    p.barrier()
    for gsub in range(NSUB):
        sl = gsub % 8
        p.do("tensor", lambda t, gsub=gsub, sl=sl: t.transpose(
            pB1[0:32, sl * 128:sl * 128 + 128],
            dttm[:, gsub * 32:(gsub + 1) * 32], idn_s[:, :]), inc=("sp", 1))
        p.w("vector", "sp")
        p.do("vector", lambda v, gsub=gsub, sl=sl: v.tensor_copy(
            acspad[:, 64 + gsub * 128:64 + (gsub + 1) * 128],
            pB1[0:32, sl * 128:sl * 128 + 128]), inc=("sv", 1))
        p.w("tensor", "sv")

    p.w("vector", "sv")
    p.do("vector", lambda v: v.tensor_scalar(
        acspad[:, 64:64 + GRID], acspad[:, 64:64 + GRID], nega_s[0:32, 0:1],
        None, ALU.mult), inc=("sv", 1))
    p.w("vector", "sv")
    p.w("gpsimd", "sv")
    p.do("gpsimd", lambda g: g.memset(acspad[:, 0:64], 0.0), inc=("sg", 1))
    p.w("vector", "sg")
    p.w("vector", "sv")
    p.do("vector", lambda v: v.tensor_tensor_scan(
        acspad[:, 64:64 + GRID], acspad[:, 64:64 + GRID],
        acspad[:, 64:64 + GRID], 0.0, ALU.add, ALU.bypass), inc=("sv", 1))
    p.w("vector", "sv")
    off_ap = acspad[:, 63:64]
    p.do("vector", lambda v: v.tensor_tensor(
        acsfm[:, :].rearrange("p (c u) -> p c u", u=64),
        acspad[:, 64:64 + GRID].rearrange("p (c u) -> p c u", u=64),
        bcast_free(off_ap, 32, [[64, NCH], [0, 64]]), ALU.subtract),
        inc=("sv", 1))
    p.w("vector", "sv")
    al_ap = acsfm[:, 63:64]
    p.do("vector", lambda v: v.tensor_copy(
        alast_s[:, :], bcast_free(al_ap, 32, [[64, NCH]])), inc=("sv", 1))
    p.w("scalar", "sv")
    p.do("scalar", lambda sc: sc.activation(expal_s[:, :], alast_s[:, :],
                                            AF.Exp), inc=("sa", 1))
    p.w("gpsimd", "sa")
    p.do("gpsimd", lambda g: g.memset(gamlog[:, :], 0.0), inc=("sg", 1))
    p.w("vector", "sg")
    p.do("vector", lambda v: v.tensor_tensor_scan(
        gamlog[:, 2:34], alast_s[:, 1:33], alast_s[:, 1:33], 0.0,
        ALU.add, ALU.bypass), inc=("sv", 1))
    p.w("scalar", "sv")
    p.do("scalar", lambda sc: sc.activation(expgam_s[:, :], gamlog[:, :],
                                            AF.Exp), inc=("sa", 1))
    p.do("vector", lambda v: v.tensor_reduce(eg_s[:, 0:1], alast_s[:, 1:33],
                                             AX.X, ALU.add), inc=("sv", 1))
    p.w("vector", "sv")
    p.do("vector", lambda v: v.tensor_reduce(eg_s[:, 1:2], alast_s[:, 1:32],
                                             AX.X, ALU.add), inc=("sv", 1))
    p.w("scalar", "sv")
    p.do("scalar", lambda sc: sc.activation(eg_s[:, :], eg_s[:, :], AF.Exp),
         inc=("sa", 1))
    p.w("sync", "sv")
    p.w("sync", "sa")
    dma("sync", acs_d[:, :], acsfm[:, :], sem="q4")
    p.do("sync", lambda e: e.dma_start(out=alast_d[:, :].transpose([1, 0]),
                                       in_=alast_s[:, :],
                                       allow_slow_non_contiguous=True),
         inc=("q4", 16))
    p.w("tensor", "sv")
    for gsub in range(NSUB):
        sl = gsub % 8
        p.do("tensor", lambda t, gsub=gsub, sl=sl: t.transpose(
            pB1[:, sl * 128:sl * 128 + 32],
            acsfm[:, gsub * 128:(gsub + 1) * 128], idn_s[0:32, 0:32]),
            inc=("sp", 1))
        p.w("vector", "sp")
        p.do("vector", lambda v, gsub=gsub, sl=sl: v.tensor_copy(
            acstm[:, gsub * 32:(gsub + 1) * 32],
            pB1[:, sl * 128:sl * 128 + 32]), inc=("sv", 1))
        p.w("tensor", "sv")

    p.barrier()
    p.do("gpsimd", lambda g: g.memset(srun[:, :], 0.0), inc=("sg", 1))
    p.w("vector", "sg")
    p.do("vector", lambda v: v.tensor_copy(pB2[0:80, 1024:1024 + SCW],
                                           srun[0:80, :]), inc=("sv", 1))
    p.w("gpsimd", "sg")
    p.do("gpsimd", lambda g: g.memset(flatbuf[:, :], 0.0), inc=("sg", 1))
    p.w("gpsimd", "sg")
    p.do("gpsimd", lambda g: g.memset(alrow[:, :], 0.0), inc=("sg", 1))
    p.w("vector", "sg")
    p.w("sync", "sg")

    def emit_xdt(ci):
        par = ci % 2
        gsub = ci // 2
        xsl = xhtm[par * 64:par * 64 + 64, gsub * D_INNER:(gsub + 1) * D_INNER]
        dsl = dttm[par * 64:par * 64 + 64, gsub * 32:(gsub + 1) * 32]
        p.do("vector", lambda v, xsl=xsl, dsl=dsl, par=par: v.tensor_tensor(
            xdt[par * 64:par * 64 + 64, :].rearrange("p (h q) -> p h q", h=32),
            xsl.rearrange("p (h q) -> p h q", h=32),
            bcast_free(dsl[:, 0:1], 64, [[1, 32], [0, 64]]), ALU.mult),
            inc=("sv", 1))

    for ci in range(1, 33):
        buf = ci % 2
        par = ci % 2
        gsub = ci // 2
        qa = "q5" if buf == 0 else "q6"
        qn = "q7" if buf == 0 else "q8"
        p.w("sync", "sa")
        p.w("sync", "sp")
        dma("sync", alrow[0:1, buf * 32:buf * 32 + 32], alast_d[ci:ci + 1, :],
            sem=qa)
        emit_xdt(ci)
        p.w("tensor", qa)
        p.w("tensor", "sv")
        p0_ = par * 64
        p.do("tensor", lambda t, buf=buf, p0_=p0_: t.matmul(
            pB1[p0_:p0_ + 64, 1920:1952], e064_s[:, :],
            alrow[:, buf * 32:buf * 32 + 32], start=True, stop=True),
            inc=("sp", 1))
        p.w("vector", "sp")
        asl = acstm[p0_:p0_ + 64, gsub * 32:(gsub + 1) * 32]
        p.do("vector", lambda v, asl=asl, buf=buf, p0_=p0_: v.tensor_tensor(
            decaytm[p0_:p0_ + 64, buf * 32:buf * 32 + 32],
            pB1[p0_:p0_ + 64, 1920:1952], asl, ALU.subtract), inc=("sv", 1))
        p.w("scalar", "sv")
        p.do("scalar", lambda sc, buf=buf, p0_=p0_: sc.activation(
            decaytm[p0_:p0_ + 64, buf * 32:buf * 32 + 32],
            decaytm[p0_:p0_ + 64, buf * 32:buf * 32 + 32], AF.Exp),
            inc=("sa", 1))
        bsl = btm[p0_:p0_ + 64, gsub * 16:(gsub + 1) * 16]
        dcl = decaytm[p0_:p0_ + 64, buf * 32:buf * 32 + 32]
        p.w("vector", "sa")
        p.do("vector", lambda v, bsl=bsl, dcl=dcl, p0_=p0_: v.tensor_tensor(
            bd[p0_:p0_ + 64, :].rearrange("p (h n) -> p h n", h=32),
            bcast_free(dcl[:, 0:1], 64, [[1, 32], [0, 16]]),
            bcast_free(bsl[:, 0:1], 64, [[0, 32], [1, 16]]), ALU.mult),
            inc=("sv", 1))
        p.w("tensor", "sv")
        for h in range(NHEADS):
            hg = h // 11
            hi = h - 11 * hg
            p.do("tensor", lambda t, hg=hg, hi=hi, h=h, p0_=p0_: t.matmul(
                pB2[32 * hg:32 * hg + 16, 1024 + 64 * hi:1024 + 64 * hi + 64],
                bd[p0_:p0_ + 64, h * 16:(h + 1) * 16],
                xdt[p0_:p0_ + 64, h * 64:(h + 1) * 64],
                start=True, stop=True), inc=("sp", 1))
        p.w("vector", qn)
        p.do("vector", lambda v, buf=buf: v.tensor_copy(
            snapbuf[0:80, buf * SCW:buf * SCW + SCW], srun[0:80, :]),
            inc=("sv", 1))
        if ci == 32:
            p.w("vector", "sv")
            p.do("vector", lambda v: v.tensor_copy(tcomb[0:80, :],
                                                   srun[0:80, :]),
                 inc=("sv", 1))
        p.w("sync", "sv")
        dma("sync", snap_d[(ci - 1) * 80:ci * 80, :],
            snapbuf[0:80, buf * SCW:buf * SCW + SCW], sem=qn)
        p.do("vector", lambda v, ci=ci: v.tensor_scalar(
            gmg[:, :], gmask_s[:, :], expal_s[:, ci:ci + 1], None, ALU.mult),
            inc=("sv", 1))
        p.w("tensor", "sv")
        p.do("tensor", lambda t: t.matmul(pB1[0:80, 1024:1536],
                                          hsel_s[:, 0:80], gmg[:, 0:512],
                                          start=True, stop=True),
             inc=("sp", 1))
        p.do("tensor", lambda t: t.matmul(pB1[0:80, 1536:1536 + SCW - 512],
                                          hsel_s[:, 0:80], gmg[:, 512:SCW],
                                          start=True, stop=True),
             inc=("sp", 1))
        p.w("vector", "sp")
        p.do("vector", lambda v: v.tensor_tensor(srun[0:80, :], srun[0:80, :],
                                                 pB1[0:80, 1024:1024 + SCW],
                                                 ALU.mult), inc=("sv", 1))
        p.w("vector", "sv")
        p.do("vector", lambda v: v.tensor_tensor(srun[0:80, :], srun[0:80, :],
                                                 pB2[0:80, 1024:1024 + SCW],
                                                 ALU.add), inc=("sv", 1))
        p.w("tensor", "sv")
    p.w("vector", "sv")
    p.w("vector", "q8")
    p.do("vector", lambda v: v.tensor_copy(snapbuf[0:80, SCW:2 * SCW],
                                           srun[0:80, :]), inc=("sv", 1))
    p.do("gpsimd", lambda g: g.memset(ucomb[0:32, :], 0.0), inc=("sg", 1))
    sv_ag = p.n["sv"]
    p.w("sync", "sv", sv_ag)
    p.w("sync", "sg")
    dma("sync", snap_d[32 * 80:33 * 80, :], snapbuf[0:80, SCW:2 * SCW],
        sem="q8")
    dma("sync", agin_d[0:80, :], srun[0:80, :], sem="q9")
    dma("sync", agin_d[80:160, :], tcomb[0:80, :], sem="q9")
    dma("sync", agin_d[160:192, :], ucomb[0:32, :], sem="q9")
    p.w("gpsimd", "q9")
    dma("gpsimd", agin_d[160:192, 0:2], eg_s[:, :], sem="st")
    p.w("gpsimd", "st")
    p.do("gpsimd", lambda g: g.collective_compute(
        "AllGather", ALU.bypass, replica_groups=[[0, 1, 2, 3], [4, 5, 6, 7]],
        ins=[agin_d[:, :].opt()], outs=[agout_d[:, :].opt()]), inc=("cc", 1))

    def f_load_offsets(e):
        for k in (1, 2, 3):
            p.env[f"so{k}"] = e.value_load(slotoff_s[k - 1:k, 0:1])
    p.w("sync", "cc")
    p.ops["sync"].append(f_load_offsets)
    for k in (1, 2, 3):
        p.do("sync", lambda e, k=k: e.dma_start(
            out=slotb[0:80, (k - 1) * SCW:k * SCW],
            in_=agout_d[bass.ds(p.env[f"so{k}"], 80), :]), inc=("q10", 16))
        p.do("sync", lambda e, k=k: e.dma_start(
            out=gslot[:, (k - 1) * 2:k * 2],
            in_=agout_d[bass.ds(p.env[f"so{k}"] + 160, 32), 0:2]),
            inc=("q10", 16))
    p.do("sync", lambda e: e.dma_start(
        out=p30b[0:80, :], in_=agout_d[bass.ds(p.env["so1"] + 80, 80), :]),
        inc=("q10", 16))

    def grep(gcol_ap):
        p.w("vector", "sp")
        p.do("vector", lambda v, g=gcol_ap: v.tensor_scalar(
            gmg[:, :], gmask_s[:, :], g, None, ALU.mult), inc=("sv", 1))
        p.w("tensor", "sv")
        p.do("tensor", lambda t: t.matmul(
            pB1[0:80, 1024:1536], hsel_s[:, 0:80], gmg[:, 0:512],
            start=True, stop=True), inc=("sp", 1))
        p.do("tensor", lambda t: t.matmul(
            pB1[0:80, 1536:1536 + SCW - 512], hsel_s[:, 0:80], gmg[:, 512:SCW],
            start=True, stop=True), inc=("sp", 1))

    p.w("vector", "q10")
    p.do("vector", lambda v: v.tensor_scalar(
        tcomb[0:80, :], slotb[0:80, 2 * SCW:3 * SCW], pmc_s[0:80, 2:3], None,
        ALU.mult), inc=("sv", 1))
    grep(gslot[:, 2:3])
    p.w("vector", "sp")
    p.w("vector", "sv")
    p.do("vector", lambda v: v.tensor_tensor(ucomb[0:80, :], tcomb[0:80, :],
                                             pB1[0:80, 1024:1024 + SCW],
                                             ALU.mult), inc=("sv", 1))
    p.w("vector", "sv")
    p.do("vector", lambda v: v.tensor_tensor(ucomb[0:80, :], ucomb[0:80, :],
                                             slotb[0:80, SCW:2 * SCW],
                                             ALU.add), inc=("sv", 1))
    p.w("vector", "sv")
    p.do("vector", lambda v: v.tensor_scalar(tcomb[0:80, :], ucomb[0:80, :],
                                             pmc_s[0:80, 1:2], None, ALU.mult),
         inc=("sv", 1))
    grep(gslot[:, 0:1])
    p.w("vector", "sp")
    p.w("vector", "sv")
    p.do("vector", lambda v: v.tensor_tensor(ucomb[0:80, :], tcomb[0:80, :],
                                             pB1[0:80, 1024:1024 + SCW],
                                             ALU.mult), inc=("sv", 1))
    p.w("vector", "sv")
    p.do("vector", lambda v: v.tensor_tensor(ucomb[0:80, :], ucomb[0:80, :],
                                             slotb[0:80, 0:SCW], ALU.add),
         inc=("sv", 1))
    p.w("vector", "sv")
    p.do("vector", lambda v: v.tensor_scalar(sin[0:80, :], ucomb[0:80, :],
                                             pmc_s[0:80, 0:1], None, ALU.mult),
         inc=("sv", 1))
    grep(gslot[:, 1:2])
    p.w("vector", "sp")
    p.w("vector", "sv")
    p.do("vector", lambda v: v.tensor_tensor(ucomb[0:80, :], tcomb[0:80, :],
                                             pB1[0:80, 1024:1024 + SCW],
                                             ALU.mult), inc=("sv", 1))
    p.w("vector", "sv")
    p.do("vector", lambda v: v.tensor_tensor(ucomb[0:80, :], ucomb[0:80, :],
                                             p30b[0:80, :], ALU.add),
         inc=("sv", 1))
    p.w("vector", "sv")
    p.do("vector", lambda v: v.tensor_scalar(stbuf[0:80, :], ucomb[0:80, :],
                                             pmc_s[0:80, 3:4], None, ALU.mult),
         inc=("sv", 1))

    p.barrier()
    for ci in range(NCH):
        par = ci % 2
        gsub = ci // 2
        buf = ci % 2
        p0_ = par * 64
        p.w("sync", "sp")
        fl0 = acs_d[0:1, ci * 64:ci * 64 + 1]
        p.do("sync", lambda e, fl0=fl0: e.dma_start(
            out=flatbuf[0:1, 0:2048],
            in_=bass.AP(fl0.tensor, fl0.offset, [[1, 1], [GRID, 32], [1, 64]])),
            inc=("q11", 16))
        p.w("tensor", "q11")
        p.w("tensor", "sv")
        p.do("tensor", lambda t, ci=ci, p0_=p0_: t.matmul(
            pB2[p0_:p0_ + 64, 1024:1088],
            bfm[:, ci * 64:(ci + 1) * 64], cfm4[0:16, ci * 64:(ci + 1) * 64],
            start=True, stop=True), inc=("sp", 1))
        p.w("vector", "sp")
        p.do("vector", lambda v, p0_=p0_: v.tensor_tensor(
            m1m[p0_:p0_ + 64, :], pB2[p0_:p0_ + 64, 1024:1088],
            tri_s[:, :] if p0_ == 0 else tri64hi[64:128, :], ALU.mult),
             inc=("sv", 1))
        p.w("tensor", "sv")
        for q in range(4):
            p.do("tensor", lambda t, q=q, p0_=p0_: t.matmul(
                pB1[p0_:p0_ + 64, q * 512:(q + 1) * 512], e064_s[:, :],
                flatbuf[:, q * 512:(q + 1) * 512],
                start=True, stop=False), inc=("sp", 1))
            p.do("tensor", lambda t, q=q, ci=ci, p0_=p0_: t.matmul(
                pB1[p0_:p0_ + 64, q * 512:(q + 1) * 512],
                acsfm[:, ci * 64:(ci + 1) * 64],
                negio_s[:, q * 512:(q + 1) * 512],
                start=False, stop=True), inc=("sp", 1))
        p.w("vector", "sp")
        p.do("vector", lambda v, p0_=p0_: v.tensor_scalar(
            dmin[p0_:p0_ + 64, :], pB1[p0_:p0_ + 64, 0:2048],
            0.0, None, ALU.min), inc=("sv", 1))
        p.w("scalar", "sv")
        p.do("scalar", lambda sc, p0_=p0_: sc.activation(
            ebuf[p0_:p0_ + 64, :], dmin[p0_:p0_ + 64, :], AF.Exp),
            inc=("sa", 1))
        p.w("vector", "sa")
        p.do("vector", lambda v, p0_=p0_: v.tensor_tensor(
            gbuf[p0_:p0_ + 64, :].rearrange("p (h l) -> p h l", h=32),
            ebuf[p0_:p0_ + 64, :].rearrange("p (h l) -> p h l", h=32),
            bcast_free(m1m[p0_:p0_ + 64, 0:1], 64, [[0, 32], [1, 64]]),
            ALU.mult), inc=("sv", 1))
        if ci + 1 <= NCH - 1:
            qm2 = "q14" if (ci + 1) % 2 == 0 else "q15"
            p.w("gpsimd", "sv")
            p.do("gpsimd", lambda g, ci=ci: g.dma_start(
                out=snapld[0:80, ((ci + 1) % 2) * SCW:((ci + 1) % 2 + 1) * SCW],
                in_=snap_d[ci * 80:(ci + 1) * 80, :]), inc=(qm2, 16))
        if ci == 0:
            pass
        else:
            grep(expgam_s[:, ci:ci + 1])
            p.w("vector", "sp")
            p.do("vector", lambda v: v.tensor_tensor(
                ucomb[0:80, :], sin[0:80, :], pB1[0:80, 1024:1024 + SCW],
                ALU.mult), inc=("sv", 1))
            p.w("vector", "sv")
            p.w("vector", "q14" if ci % 2 == 0 else "q15")
            p.do("vector", lambda v, ci=ci: v.tensor_tensor(
                stbuf[0:80, :], ucomb[0:80, :],
                snapld[0:80, (ci % 2) * SCW:(ci % 2 + 1) * SCW], ALU.add),
                inc=("sv", 1))
        emit_xdt(ci)
        asl2 = acstm[p0_:p0_ + 64, gsub * 32:(gsub + 1) * 32]
        p.w("tensor", "sv")
        for h in range(NHEADS):
            pp = 32 * (h % 4)
            ff = 64 * (h // 4)
            p.do("tensor", lambda t, h=h, p0_=p0_: t.matmul(
                pB2[p0_:p0_ + 64, h * 64:(h + 1) * 64],
                gbuf[p0_:p0_ + 64, h * 64:(h + 1) * 64],
                xdt[p0_:p0_ + 64, h * 64:(h + 1) * 64],
                start=True, stop=True), inc=("sp", 1))
            hg = h // 11
            hi = h - 11 * hg
            p.do("tensor", lambda t, h=h, hg=hg, hi=hi, ci=ci, p0_=p0_: t.matmul(
                pB1[p0_:p0_ + 64, h * 64:(h + 1) * 64],
                cfm4[32 * hg:32 * hg + 16, ci * 64:(ci + 1) * 64],
                stbuf[32 * hg:32 * hg + 16, 64 * hi:64 * hi + 64],
                start=True, stop=True), inc=("sp", 1))
        p.w("sync", "sv")
        p.w("sync", "sa")
        qz = "q12" if par == 0 else "q13"
        dma("sync", zchunk2[p0_:p0_ + 64, :], zsp_d[ci * 64:(ci + 1) * 64, :],
            sem=qz)
        p.w("scalar", qz)
        p.w("scalar", "sv")
        p.do("scalar", lambda sc, p0_=p0_: sc.activation(
            sz[p0_:p0_ + 64, :], zchunk2[p0_:p0_ + 64, :], AF.Silu),
            inc=("sa", 1))
        p.do("scalar", lambda sc, asl2=asl2, p0_=p0_: sc.activation(
            eacsb[p0_:p0_ + 64, 0:32], asl2, AF.Exp), inc=("sa", 1))
        p.w("vector", "sa")
        p.do("vector", lambda v, p0_=p0_, gsub=gsub: v.tensor_tensor(
            ytmp[p0_:p0_ + 64, :],
            xhtm[p0_:p0_ + 64, gsub * D_INNER:(gsub + 1) * D_INNER],
            dskrep_s[p0_:p0_ + 64, :], ALU.mult), inc=("sv", 1))
        p.w("vector", "sp")
        p.w("vector", "sv")
        p.do("vector", lambda v, p0_=p0_: v.tensor_tensor(
            ytmp[p0_:p0_ + 64, :], ytmp[p0_:p0_ + 64, :],
            pB2[p0_:p0_ + 64, 0:2048], ALU.add), inc=("sv", 1))
        p.w("vector", "sv")
        p.do("vector", lambda v, p0_=p0_: v.tensor_tensor(
            ytmp2[p0_:p0_ + 64, :].rearrange("p (h q) -> p h q", h=32),
            pB1[p0_:p0_ + 64, 0:2048].rearrange("p (h q) -> p h q", h=32),
            bcast_free(eacsb[p0_:p0_ + 64, 0:1], 64, [[1, 32], [0, 64]]),
            ALU.mult), inc=("sv", 1))
        p.w("vector", "sv")
        p.do("vector", lambda v, p0_=p0_: v.tensor_tensor(
            ytmp[p0_:p0_ + 64, :], ytmp[p0_:p0_ + 64, :],
            ytmp2[p0_:p0_ + 64, :], ALU.add), inc=("sv", 1))
        p.w("vector", "sv")
        p.do("vector", lambda v, p0_=p0_: v.tensor_tensor(
            ytmp[p0_:p0_ + 64, :], ytmp[p0_:p0_ + 64, :], sz[p0_:p0_ + 64, :],
            ALU.mult), inc=("sv", 1))
        p.w("vector", "sv")
        for q in range(4):
            p.do("vector", lambda v, q=q, p0_=p0_: v.bn_stats(
                stat6[p0_:p0_ + 64, 6 * q:6 * q + 6],
                ytmp[p0_:p0_ + 64, 512 * q:512 * (q + 1)]), inc=("sv", 1))
        p.w("vector", "sv")
        p.do("vector", lambda v, p0_=p0_: v.bn_aggr(
            stat2[p0_:p0_ + 64, 0:2], stat6[p0_:p0_ + 64, 0:24]),
            inc=("sv", 1))
        p.w("vector", "sv")
        p.do("vector", lambda v, p0_=p0_: v.tensor_tensor(
            rmscol[p0_:p0_ + 64, 0:1], stat2[p0_:p0_ + 64, 0:1],
            stat2[p0_:p0_ + 64, 0:1], ALU.mult), inc=("sv", 1))
        p.w("vector", "sv")
        p.do("vector", lambda v, p0_=p0_: v.tensor_tensor(
            rmscol[p0_:p0_ + 64, 0:1], rmscol[p0_:p0_ + 64, 0:1],
            stat2[p0_:p0_ + 64, 1:2], ALU.add), inc=("sv", 1))
        p.w("scalar", "sv")
        p.do("scalar", lambda sc, p0_=p0_: sc.activation(
            rmscol[p0_:p0_ + 64, 1:2], rmscol[p0_:p0_ + 64, 0:1], AF.Ln,
            bias=EPS), inc=("sa", 1))
        p.w("scalar", "sa")
        p.do("scalar", lambda sc, p0_=p0_: sc.activation(
            rmscol[p0_:p0_ + 64, 1:2], rmscol[p0_:p0_ + 64, 1:2], AF.Exp,
            scale=-0.5), inc=("sa", 1))
        p.w("vector", "sa")
        p.do("vector", lambda v, p0_=p0_, gsub=gsub: v.scalar_tensor_tensor(
            xhtm[p0_:p0_ + 64, gsub * D_INNER:(gsub + 1) * D_INNER],
            ytmp[p0_:p0_ + 64, :], rmscol[p0_:p0_ + 64, 1:2],
            nwrep_s[p0_:p0_ + 64, :], ALU.mult, ALU.mult), inc=("sv", 1))
        p.w("tensor", "sv")

    p.barrier()
    for kc16 in range(16):
        dma("gpsimd", woutb[:, kc16 * D_MODEL:(kc16 + 1) * D_MODEL],
            wout_d[kc16 * 128:(kc16 + 1) * 128, :], sem="q16")
    p.w("tensor", "q16")
    sa_motm = {}
    for su in range(NSUB):
        p.w("tensor", "sv")
        for half in range(2):
            for sl in range(8):
                fc = half * 8 + sl
                p.do("tensor", lambda t, su=su, fc=fc, sl=sl: t.transpose(
                    pTb[:, sl * 128:sl * 128 + 128],
                    xhtm[:, su * D_INNER + fc * 128:
                         su * D_INNER + fc * 128 + 128],
                    idnb_s[:, :]), inc=("sp", 1))
            p.w("vector", "sp")
            p.do("vector", lambda v, half=half: v.tensor_copy(
                ynf[:, half * 1024:(half + 1) * 1024], pTb[:, 0:1024]),
                inc=("sv", 1))
            p.w("tensor", "sv")
        pp3 = pB1 if su % 2 == 0 else pB2
        if su >= 2:
            p.w("tensor", "sa", sa_motm[su - 2])
        for fc in range(16):
            for q in range(2):
                p.do("tensor", lambda t, fc=fc, q=q, pp3=pp3: t.matmul(
                    pp3[:, q * 512:(q + 1) * 512],
                    ynf[:, fc * 128:(fc + 1) * 128],
                    woutb[:, fc * D_MODEL + q * 512: fc * D_MODEL + (q + 1) * 512],
                    start=(fc == 0), stop=(fc == 15)), inc=("sp", 1))
        p.w("scalar", "sp")
        p.w("scalar", "sv")
        p.do("scalar", lambda sc, pp3=pp3: sc.copy(motm[:, :], pp3[:, 0:1024]),
             inc=("sa", 1))
        sa_motm[su] = p.n["sa"]
        p.w("vector", "sa")
        p.w("vector", "sv")
        ln_tm(motm[:, :], motm[:, :], D_MODEL)
        p.w("sync", "sv")
        p.w("sync", "q18")
        dma("sync", xgtm[:, :], xg_d[su * 128:(su + 1) * 128, :], sem="q17")
        p.w("vector", "q17")
        p.do("vector", lambda v: v.tensor_tensor(xgtm[:, :], motm[:, :],
                                                 xgtm[:, :], ALU.add),
             inc=("sv", 1))
        p.w("vector", "sv")
        p.do("vector", lambda v, su=su: v.tensor_scalar(
            xgtm[:, :], xgtm[:, :], maskrow_s[:, su:su + 1], None, ALU.mult),
            inc=("sv", 1))
        sv_x2 = p.n["sv"]
        p.w("sync", "sv", sv_x2)
        dma("sync", x2_d[su * 128:(su + 1) * 128, :], xgtm[:, :], sem="q18")

    p.barrier()
    for su in range(NSUB):
        p.w("sync", "sv")
        dma("sync", x2ld1[:, :], x2_d[su * 128:(su + 1) * 128, :], sem="q19")
        p.w("vector", "q19")
        ln_tm(x2ld1[:, :], x2n_t[:, :], D_MODEL)
        p.w("vector", "sv")
        p.do("vector", lambda v: v.tensor_reduce(
            qcol[:, 0:1], x2n_t[:, 0:1024], AX.X, ALU.max,
            apply_absolute_value=True), inc=("sv", 1))
        p.w("vector", "sv")
        p.do("vector", lambda v: v.tensor_scalar(qcol[:, 0:1], qcol[:, 0:1],
                                                 1e-5, None, ALU.max),
             inc=("sv", 1))
        p.w("vector", "sv")
        p.do("vector", lambda v: v.reciprocal(qcol[:, 1:2], qcol[:, 0:1]),
             inc=("sv", 1))
        p.w("vector", "sv")
        p.do("vector", lambda v: v.tensor_scalar(qcol[:, 1:2], qcol[:, 1:2],
                                                 127.0, None, ALU.mult),
             inc=("sv", 1))
        p.do("vector", lambda v: v.tensor_scalar(qcol[:, 2:3], qcol[:, 0:1],
                                                 1.0 / 127.0, None, ALU.mult),
             inc=("sv", 1))
        p.w("vector", "sv")
        p.do("vector", lambda v: v.tensor_scalar(qv1[:, :], x2n_t[:, :],
                                                 qcol[:, 1:2], None, ALU.mult),
             inc=("sv", 1))
        p.w("vector", "sv")
        p.do("vector", lambda v: v.tensor_copy(qyi1[:, :], qv1[:, :]),
             inc=("sv", 1))
        p.w("vector", "sv")
        p.do("vector", lambda v: v.tensor_copy(qy1[:, :], qyi1[:, :]),
             inc=("sv", 1))
        p.w("vector", "sv")
        p.do("vector", lambda v: v.tensor_tensor(qr1[:, :], qv1[:, :],
                                                 qy1[:, :], ALU.subtract),
             inc=("sv", 1))
        p.w("vector", "sv")
        p.do("vector", lambda v: v.scalar_tensor_tensor(
            qt11[:, :], qr1[:, :], 0.5, qy1[:, :], ALU.is_gt,
            ALU.add), inc=("sv", 1))
        p.w("vector", "sv")
        p.do("vector", lambda v: v.scalar_tensor_tensor(
            qv1[:, :], qr1[:, :], -0.5, qt11[:, :], ALU.is_ge,
            ALU.add), inc=("sv", 1))
        p.w("vector", "sv")
        p.do("vector", lambda v: v.tensor_scalar(
            xq1tm[:, :], qv1[:, :], -1.0, qcol[:, 2:3], ALU.add, ALU.mult),
            inc=("sv", 1))
        p.w("tensor", "sv")
        for kc in range(KC):
            sl = kc % 8
            p.do("tensor", lambda t, kc=kc, sl=sl: t.transpose(
                pTb[:, sl * 128:sl * 128 + 128],
                xq1tm[:, kc * 128:(kc + 1) * 128], idnb_s[:, :]), inc=("sp", 1))
        p.w("vector", "sp")
        xq1b = xq1fm[:, su * 128:su * 128 + 1]
        xq1ap = bass.AP(xq1b.tensor, xq1b.offset,
                        [[KC * GRID, 128], [GRID, KC], [1, 128]])
        p.do("vector", lambda v, xq1ap=xq1ap: v.tensor_copy(
            xq1ap, pTb[:, 0:1024]), inc=("sv", 1))

    p.barrier()
    for kc in range(KC):
        dma("sync", w1qb[:, kc * D_HIDDEN:(kc + 1) * D_HIDDEN],
            w1q_d[kc * 128:(kc + 1) * 128, :], sem="q20")
    p.w("tensor", "q20")
    sa_tile0 = {}
    sa_end = {}
    for hf in range(32):
        pp = pB1 if hf % 2 == 0 else pB2
        hb = h1buf if hf % 2 == 0 else h1bufB
        hc = hconv if hf % 2 == 0 else hconvB
        if hf >= 2:
            p.w("tensor", "sa", sa_end[hf - 2])
        for q, (t0, tw) in enumerate(PA_TILES):
            reg = q % 4
            for kc in range(KC):
                p.do("tensor", lambda t, hf=hf, kc=kc, t0=t0, tw=tw, reg=reg,
                     pp=pp: t.matmul(
                         pp[:, reg * 512: reg * 512 + tw],
                         w1qb[:, kc * D_HIDDEN + hf * 128:
                              kc * D_HIDDEN + hf * 128 + 128],
                         xq1fm[:, kc * GRID + t0: kc * GRID + t0 + tw],
                         start=(kc == 0), stop=(kc == KC - 1)), inc=("sp", 1))
            p.w("scalar", "sp")
            if q == 0:
                p.w("scalar", "sv")
            p.do("scalar", lambda sc, t0=t0, tw=tw, reg=reg, pp=pp, hb=hb:
                 sc.copy(hb[:, t0:t0 + tw],
                         pp[:, reg * 512: reg * 512 + tw]), inc=("sa", 1))
            if q == 0:
                sa_tile0[hf] = p.n["sa"]
            if reg == 3:
                p.w("tensor", "sa", sa_tile0[hf])
        sa_end[hf] = p.n["sa"]
        p.w("vector", "sv")
        p.w("vector", "sa")
        p.do("vector", lambda v, hf=hf, hb=hb, hc=hc: v.tensor_scalar(
            hc[:, :], hb[:, :], fcw_s[:, hf * 3 + 1:hf * 3 + 2], None,
            ALU.mult), inc=("sv", 1))
        p.w("vector", "sv")
        p.do("vector", lambda v, hf=hf, hb=hb, hc=hc: v.scalar_tensor_tensor(
            hc[:, 1:GRID], hb[:, 0:GRID - 1],
            fcw_s[:, hf * 3:hf * 3 + 1], hc[:, 1:GRID], ALU.mult, ALU.add),
            inc=("sv", 1))
        p.w("vector", "sv")
        p.do("vector", lambda v, hf=hf, hb=hb, hc=hc: v.scalar_tensor_tensor(
            hc[:, 0:GRID - 1], hb[:, 1:GRID],
            fcw_s[:, hf * 3 + 2:hf * 3 + 3], hc[:, 0:GRID - 1], ALU.mult,
            ALU.add), inc=("sv", 1))
        p.w("scalar", "sv")
        p.w("scalar", "q21")
        p.do("scalar", lambda sc, hf=hf, hc=hc: sc.activation(
            hpostb[:, :], hc[:, :], AF.Silu, bias=fcb_s[:, hf:hf + 1]),
            inc=("sa", 1))
        sa_here = p.n["sa"]
        p.w("sync", "sa", sa_here)
        dma("sync", hpost_d[hf * 128:(hf + 1) * 128, :], hpostb[:, :],
            sem="q21")

    p.barrier()
    for q32 in range(32):
        dma("sync", w2qb[:, q32 * D_MODEL:(q32 + 1) * D_MODEL],
            w2q_d[q32 * 128:(q32 + 1) * 128, :], sem="q22")
    p.w("tensor", "q22")
    for su in range(NSUB):
        p.w("sync", "sv")
        hp0 = hpost_d[0:1, su * 128:su * 128 + 1]
        p.do("sync", lambda e, hp0=hp0: e.dma_start(
            out=xq2fm[:, :],
            in_=bass.AP(hp0.tensor, hp0.offset,
                        [[GRID, 128], [128 * GRID, 32], [1, 128]])),
            inc=("q23", 16))
        p.w("tensor", "q23")
        p.w("tensor", "sv")
        for half in range(4):
            for sl in range(8):
                fc = half * 8 + sl
                p.do("tensor", lambda t, fc=fc, sl=sl: t.transpose(
                    pTb[:, sl * 128:sl * 128 + 128],
                    xq2fm[:, fc * 128:(fc + 1) * 128], idnb_s[:, :]),
                    inc=("sp", 1))
            p.w("vector", "sp")
            p.do("vector", lambda v, half=half: v.tensor_copy(
                htmb[:, half * 1024:(half + 1) * 1024], pTb[:, 0:1024]),
                inc=("sv", 1))
            p.w("tensor", "sv")
        p.w("vector", "sv")
        p.do("vector", lambda v: v.tensor_reduce(
            qcol[:, 4:5], htmb[:, :], AX.X, ALU.max,
            apply_absolute_value=True), inc=("sv", 1))
        p.w("vector", "sv")
        p.do("vector", lambda v: v.tensor_scalar(qcol[:, 4:5], qcol[:, 4:5],
                                                 1e-5, None, ALU.max),
             inc=("sv", 1))
        p.w("vector", "sv")
        p.do("vector", lambda v: v.reciprocal(qcol[:, 5:6], qcol[:, 4:5]),
             inc=("sv", 1))
        p.w("vector", "sv")
        p.do("vector", lambda v: v.tensor_scalar(qcol[:, 5:6], qcol[:, 5:6],
                                                 127.0, None, ALU.mult),
             inc=("sv", 1))
        p.do("vector", lambda v: v.tensor_scalar(qcol[:, 6:7], qcol[:, 4:5],
                                                 1.0 / 127.0, None, ALU.mult),
             inc=("sv", 1))
        p.w("vector", "sv")
        p.do("vector", lambda v: v.tensor_scalar(qv4[:, :], htmb[:, :],
                                                 qcol[:, 5:6], None, ALU.mult),
             inc=("sv", 1))
        p.w("vector", "sv")
        p.do("vector", lambda v: v.tensor_copy(qyi4[:, :], qv4[:, :]),
             inc=("sv", 1))
        p.w("vector", "sv")
        p.do("vector", lambda v: v.tensor_copy(qy4[:, :], qyi4[:, :]),
             inc=("sv", 1))
        p.w("vector", "sv")
        p.do("vector", lambda v: v.tensor_tensor(qr4[:, :], qv4[:, :], qy4[:, :],
                                                 ALU.subtract), inc=("sv", 1))
        p.w("vector", "sv")
        p.do("vector", lambda v: v.scalar_tensor_tensor(
            qt14[:, :], qr4[:, :], 0.5, qy4[:, :], ALU.is_gt, ALU.add),
            inc=("sv", 1))
        p.w("vector", "sv")
        p.do("vector", lambda v: v.scalar_tensor_tensor(
            qv4[:, :], qr4[:, :], -0.5, qt14[:, :], ALU.is_ge, ALU.add),
            inc=("sv", 1))
        p.w("vector", "sv")
        p.do("vector", lambda v: v.tensor_scalar(xq2tm[:, :], qv4[:, :], -1.0,
                                                 None, ALU.add), inc=("sv", 1))
        p.w("tensor", "sv")
        for half in range(4):
            for sl in range(8):
                fc = half * 8 + sl
                p.do("tensor", lambda t, fc=fc, sl=sl: t.transpose(
                    pTb[:, sl * 128:sl * 128 + 128],
                    xq2tm[:, fc * 128:(fc + 1) * 128], idnb_s[:, :]),
                    inc=("sp", 1))
            p.w("vector", "sp")
            p.do("vector", lambda v, half=half: v.tensor_copy(
                xq2fm[:, half * 1024:(half + 1) * 1024], pTb[:, 0:1024]),
                inc=("sv", 1))
            p.w("tensor", "sv")
        p.w("tensor", "sv")
        for fc in range(32):
            for q in range(2):
                p.do("tensor", lambda t, fc=fc, q=q: t.matmul(
                    pB1[:, q * 512:(q + 1) * 512],
                    xq2fm[:, fc * 128:(fc + 1) * 128],
                    w2qb[:, fc * D_MODEL + q * 512: fc * D_MODEL + (q + 1) * 512],
                    start=(fc == 0), stop=(fc == 31)), inc=("sp", 1))
        p.w("scalar", "sp")
        p.w("scalar", "q25")
        p.w("scalar", "sv")
        p.do("scalar", lambda sc: sc.activation(fotm[:, :], pB1[:, 0:1024],
                                                AF.Copy, scale=qcol[:, 6:7]),
             inc=("sa", 1))
        p.w("vector", "sa")
        ln_tm(fotm[:, :], fotm[:, :], D_MODEL)
        p.w("sync", "sv")
        dma("sync", x2ld3[:, :], x2_d[su * 128:(su + 1) * 128, :], sem="q24")
        p.w("vector", "q24")
        p.do("vector", lambda v: v.tensor_tensor(fotm[:, :], fotm[:, :],
                                                 x2ld3[:, :], ALU.add),
             inc=("sv", 1))
        sv_out = p.n["sv"]
        p.w("sync", "sv", sv_out)
        if su == 0:
            dma("sync", out_d[0:64, :], fotm[64:128, :], sem="q25")
        elif su == NSUB - 1:
            dma("sync", out_d[SEG - 64:SEG, :], fotm[0:64, :], sem="q25")
        else:
            dma("sync", out_d[su * 128 - 64:(su + 1) * 128 - 64, :],
                fotm[:, :], sem="q25")
        p.w("tensor", "sv")

    p.w("sync", "q25")

    with nc.Block() as block:
        p.emit(block)
    ctx.close()
    return nc



def kernel(**inputs):
    x = np.ascontiguousarray(np.asarray(inputs["L0_attn_x"], np.float32))
    shards = _host_shards(x)
    consts = _host_consts(inputs)
    nc = build_nc()
    in_maps = []
    for core in range(8):
        m = dict(consts)
        m.update(shards[core])
        in_maps.append(m)
    res = run_bass_kernel_spmd(nc, in_maps, core_ids=list(range(8)))
    out = np.zeros((2, SEQ, D_MODEL), np.float32)
    for core in range(8):
        b, j = core // NSEG, core % NSEG
        out[b, j * SEG:(j + 1) * SEG] = res.results[core]["out"]
    return out


# version 05
# speedup vs baseline: 1.2579x; 1.0199x over previous
import numpy as np
import ml_dtypes
import concourse.bass as bass
import concourse.mybir as mybir
from concourse.bass_utils import run_bass_kernel_spmd
from contextlib import ExitStack

F32 = mybir.dt.float32
BF16 = mybir.dt.bfloat16
I32 = mybir.dt.int32
U8 = mybir.dt.uint8
AF = mybir.ActivationFunctionType
ALU = mybir.AluOpType
AX = mybir.AxisListType

D_MODEL = 1024
D_STATE = 16
HEADDIM = 64
CHUNK = 64
D_INNER = 2048
NHEADS = 32
CONV_DIM = 2080
D_IN_PROJ = 4160
D_HIDDEN = 4096
SEQ = 8192
SEG = 2048
NSEG = 4
GRID = 2176
NCH = 34
NSUB = 17
KC = 8
EPS = 1e-5
SLOT = 192
SCW = 704

ENGINES = ("sync", "vector", "scalar", "tensor", "gpsimd")
PA_TILES = [(0, 512), (512, 512), (1024, 512), (1536, 512), (2048, 128)]


def bcast_free(ap_col, part, dims):
    return bass.AP(ap_col.tensor, ap_col.offset,
                   [[ap_col.ap[0][0], part]] + [list(d) for d in dims])



def _host_shards(x):
    shards = []
    for core in range(8):
        b, j = core // NSEG, core % NSEG
        start = j * SEG
        lo = start - CHUNK
        g = np.zeros((GRID, D_MODEL), np.float32)
        s0, s1 = max(lo, 0), min(start + SEG + CHUNK, SEQ)
        g[s0 - lo:s1 - lo] = x[b, s0:s1]
        h3 = np.zeros((4, D_MODEL), np.float32)
        h0, h1 = max(lo - 3, 0), max(lo, 0)
        h3[3 - (h1 - h0):3] = x[b, h0:h1]
        maskrow = np.ones((NSUB * 128,), np.float32)
        if j == 0:
            maskrow[CHUNK - 1] = 0.0
        if j == NSEG - 1:
            maskrow[SEG + CHUNK] = 0.0
        mask_tm = np.ascontiguousarray(maskrow.reshape(NSUB, 128).T)
        pm = np.zeros((128, 4), np.float32)
        for k in (1, 2, 3):
            pm[:, k - 1] = 1.0 if j >= k else 0.0
        pm[:, 3] = 1.0 if j >= 1 else 0.0
        slot = np.zeros((4, 1), np.int32)
        for k in (1, 2, 3):
            slot[k - 1, 0] = SLOT * max(j - k, 0)
        shards.append(dict(xg=g, xh3=h3, maskrow=mask_tm, pmc=pm, slotoff=slot))
    return shards


def _host_consts(inputs):
    c = {}
    c["idn"] = np.eye(128, dtype=np.float32)
    cp = np.zeros((128, 4), np.float32)
    cp[:, 0] = 1e-5; cp[:, 1] = 1.0; cp[:, 2] = -0.5; cp[:, 3] = 0.0
    c["cpool"] = cp
    e = np.zeros((32, 64), np.float32); e[0, :] = 1.0
    c["e064"] = e
    e2 = np.zeros((32, 128), np.float32); e2[0, :] = 1.0
    c["e0128"] = e2
    c["onescol"] = np.ones((128, 1), np.float32)
    hsel = np.zeros((32, 128), np.float32)
    for h in range(32):
        g = h // 11
        hsel[h, 32 * g:32 * g + 16] = 1.0
    c["hsel"] = hsel
    gmask = np.zeros((32, SCW), np.float32)
    for h in range(32):
        i = h - 11 * (h // 11)
        gmask[h, 64 * i:64 * i + 64] = 1.0
    c["gmask"] = gmask
    negio = np.zeros((32, 2048), np.float32)
    for h in range(32):
        negio[h, 64 * h:64 * h + 64] = -1.0
    c["negio"] = negio
    t = np.zeros((64, 64), np.float32)
    for s in range(64):
        t[s, s:] = 1.0
    c["tri"] = t
    dtb = np.asarray(inputs["L0_attn_dt_bias"], np.float32)
    c["dtbrep"] = np.ascontiguousarray(np.tile(dtb[None, :], (128, 1)))
    Dsk = np.asarray(inputs["L0_attn_D"], np.float32)
    c["dskrep"] = np.ascontiguousarray(
        np.tile(np.repeat(Dsk, HEADDIM)[None, :], (128, 1)))
    nw = np.asarray(inputs["L0_attn_norm_w"], np.float32)
    c["nwrep"] = np.ascontiguousarray(np.tile(nw[None, :], (128, 1)))
    c["alog"] = np.ascontiguousarray(
        np.asarray(inputs["L0_attn_A_log"], np.float32).reshape(32, 1))
    cw = np.asarray(inputs["L0_attn_conv_w"], np.float32)
    cwp = np.zeros((17 * 128, 4), np.float32)
    cwp[:CONV_DIM] = cw
    c["convw"] = np.ascontiguousarray(
        cwp.reshape(17, 128, 4).transpose(1, 0, 2).reshape(128, 68))
    cb = np.asarray(inputs["L0_attn_conv_b"], np.float32)
    cbp = np.zeros((17 * 128,), np.float32)
    cbp[:CONV_DIM] = cb
    c["convb"] = np.ascontiguousarray(cbp.reshape(17, 128).T)
    fcw = np.asarray(inputs["L0_ffn_conv_w"], np.float32).reshape(D_HIDDEN, 3)
    c["fcw"] = np.ascontiguousarray(
        fcw.reshape(32, 128, 3).transpose(1, 0, 2).reshape(128, 96))
    fcb = np.asarray(inputs["L0_ffn_conv_b"], np.float32)
    c["fcb"] = np.ascontiguousarray(fcb.reshape(32, 128).T)
    c["wint"] = np.ascontiguousarray(np.asarray(inputs["L0_attn_W_in"], np.float32).T)
    c["wout"] = np.ascontiguousarray(np.asarray(inputs["L0_attn_W_out"], np.float32).T)
    c["w1t"] = np.ascontiguousarray(np.asarray(inputs["L0_ffn_w1"], np.float32).T)
    c["w2t"] = np.ascontiguousarray(np.asarray(inputs["L0_ffn_w2"], np.float32).T)
    return c



class P:
    def __init__(self, nc, ctx):
        self.nc = nc
        self.ops = {e: [] for e in ENGINES}
        self.h = {}
        self.n = {}
        for name in ("qs", "qg", "sv", "sa", "sp", "sg", "cc", "st",
                     "qw", "qe", "q1", "q2", "q3", "q4", "q5", "q6", "q7",
                     "q8", "q9", "q10", "q11", "q12", "q13", "q14", "q15",
                     "q16", "q17", "q18", "q19", "q20", "q21", "q22", "q23",
                     "q24", "q25", "qw2"):
            self.h[name] = ctx.enter_context(nc.semaphore(name))
            self.n[name] = 0
        self.env = {}

    def do(self, eng, f, inc=None):
        if inc is not None:
            sem, amt = inc
            self.n[sem] += amt
            handle = self.h[sem]

            def g(e, f=f, handle=handle, amt=amt):
                f(e).then_inc(handle, amt)
        else:
            def g(e, f=f):
                f(e)
        self.ops[eng].append(g)

    def w(self, eng, sem, val=None):
        v = self.n[sem] if val is None else val
        h = self.h[sem]
        self.ops[eng].append(lambda e, h=h, v=v: e.wait_ge(h, v))

    def barrier(self, engs=ENGINES, sems=None):
        if sems is None:
            sems = [k for k in self.n if k != "st"]
        for eng in engs:
            for s in sems:
                if self.n[s] > 0:
                    self.w(eng, s)

    def emit(self, block):
        for e in ENGINES:
            fs = self.ops[e]
            getattr(block, e)(lambda eng, fs=fs: [f(eng) for f in fs])



def build_nc():
    nc = bass.Bass("TRN2", target_bir_lowering=False, dynamic_dma_scratch_size=8192)
    ctx = ExitStack()

    def din(name, shape, dtype=F32):
        return nc.dram_tensor(name, list(shape), dtype, kind="ExternalInput").ap()

    def dint(name, shape, dtype=F32):
        return nc.dram_tensor(name, list(shape), dtype, kind="Internal").ap()

    def sb(name, shape, dtype=F32):
        return ctx.enter_context(nc.sbuf_tensor(name, shape, dtype))

    xg_d = din("xg", [GRID, D_MODEL])
    xh3_d = din("xh3", [4, D_MODEL])
    maskrow_d = din("maskrow", [128, NSUB])
    pmc_d = din("pmc", [128, 4])
    slotoff_d = din("slotoff", [4, 1], I32)
    idn_d = din("idn", [128, 128])
    cpool_d = din("cpool", [128, 4])
    e064_d = din("e064", [32, 64])
    e0128_d = din("e0128", [32, 128])
    onescol_d = din("onescol", [128, 1])
    hsel_d = din("hsel", [32, 128])
    gmask_d = din("gmask", [32, SCW])
    negio_d = din("negio", [32, 2048])
    tri_d = din("tri", [64, 64])
    dtbrep_d = din("dtbrep", [128, 32])
    dskrep_d = din("dskrep", [128, D_INNER])
    nwrep_d = din("nwrep", [128, D_INNER])
    alog_d = din("alog", [32, 1])
    convw_d = din("convw", [128, 68])
    convb_d = din("convb", [128, 17])
    fcw_d = din("fcw", [128, 96])
    fcb_d = din("fcb", [128, 32])
    wint_d = din("wint", [D_MODEL, D_IN_PROJ])
    wout_d = din("wout", [D_INNER, D_MODEL])
    w1t_d = din("w1t", [D_MODEL, D_HIDDEN])
    w2t_d = din("w2t", [D_HIDDEN, D_MODEL])
    out_d = nc.dram_tensor("out", [SEG, D_MODEL], F32, kind="ExternalOutput").ap()

    zsp_d = dint("zsp", [GRID, D_INNER], BF16)
    x2_d = dint("x2d", [GRID, D_MODEL])
    hpost_d = dint("hpostd", [D_HIDDEN, GRID], BF16)
    acs_d = dint("acsd", [NHEADS, GRID])
    alast_d = dint("alastd", [NCH, NHEADS])
    w1q_d = dint("w1qd", [D_MODEL, D_HIDDEN], BF16)
    w2q_d = dint("w2qd", [D_HIDDEN, D_MODEL], BF16)
    agin_d = dint("agin", [SLOT, SCW])
    agout_d = dint("agout", [4 * SLOT, SCW])
    snap_d = dint("snapd", [33 * 80, SCW], BF16)

    arenaA = ctx.enter_context(nc.sbuf_tensor("arenaA", [128, 69632], U8))
    arenaB = ctx.enter_context(nc.sbuf_tensor("arenaB", [128, 66560], U8))
    arenaC = ctx.enter_context(nc.sbuf_tensor("arenaC", [128, 16384], U8))
    arenaD = ctx.enter_context(nc.sbuf_tensor("arenaD", [128, 8192], U8))

    def overlay(base, name, shape, dtype, off=0):
        h = nc.virtual_sbuf_tensor(name, list(shape), dtype)
        nc._tensor(h.name, h.shape, h.dtype, type="SB")
        mloc = nc.lookup_mloc(h)
        mloc.allocated = True
        mloc.addr = nc.lookup_mloc(base).addr + off
        return h

    xhtm = overlay(arenaA, "xhtm", [128, NSUB * D_INNER], BF16)
    xq1fm = overlay(arenaA, "xq1fm", [128, KC * GRID], BF16)
    qv4 = overlay(arenaA, "qv4", [128, 4096], F32, 0)
    qyi4 = overlay(arenaA, "qyi4", [128, 4096], I32, 16384)
    qt14 = overlay(arenaA, "qt14", [128, 4096], F32, 16384)
    qy4 = overlay(arenaA, "qy4", [128, 4096], F32, 32768)
    qr4 = overlay(arenaA, "qr4", [128, 4096], F32, 49152)
    qv1 = overlay(arenaA, "qv1", [128, 1024], F32, 34816)
    qyi1 = overlay(arenaA, "qyi1", [128, 1024], I32, 38912)
    qt11 = overlay(arenaA, "qt11", [128, 1024], F32, 38912)
    qy1 = overlay(arenaA, "qy1", [128, 1024], F32, 43008)
    qr1 = overlay(arenaA, "qr1", [128, 1024], F32, 47104)
    winb = overlay(arenaB, "winb", [128, KC * D_IN_PROJ], BF16)
    snapbuf = overlay(arenaB, "snapbuf", [128, 2 * SCW], BF16, 0)
    snapld = overlay(arenaB, "snapld", [128, 2 * SCW], F32, 2816)
    woutb = overlay(arenaB, "woutb", [128, 16 * D_MODEL], BF16)
    w1qb = overlay(arenaB, "w1qb", [128, KC * D_HIDDEN], BF16)
    w2qb = overlay(arenaB, "w2qb", [128, 32 * D_MODEL], BF16)
    sz = overlay(arenaB, "sz", [128, D_INNER], F32, 8448)
    ytmp = overlay(arenaB, "ytmp", [128, D_INNER], F32, 16640)
    ytmp2 = overlay(arenaB, "ytmp2", [128, D_INNER], F32, 24832)
    flatbuf = overlay(arenaB, "flatbuf", [32, 2048], F32, 33024)
    wtile = overlay(arenaC, "wtile", [128, 4096], F32)
    wtileB = overlay(arenaA, "wtileB", [128, 4096], F32, 32768)
    htmb = overlay(arenaC, "htmb", [128, D_HIDDEN], BF16)
    wqtile = overlay(arenaD, "wqtile", [128, 4096], BF16)
    xq2tm = overlay(arenaD, "xq2tm", [128, D_HIDDEN], BF16)
    xnf = overlay(arenaC, "xnf", [128, KC * 512], BF16)
    ebuf = overlay(arenaC, "ebuf", [128, D_INNER], F32, 0)
    dmin = overlay(arenaC, "dmin", [128, D_INNER], F32, 8192)
    hconv = overlay(arenaA, "hconv", [128, GRID], F32, 34816)
    hconvB = overlay(arenaA, "hconvB", [128, GRID], F32, 56576)
    h1bufB = overlay(arenaA, "h1bufB", [128, GRID], BF16, 52224)
    h1buf = overlay(arenaA, "h1buf", [128, GRID], BF16, 43520)
    hpostb = overlay(arenaA, "hpostb", [128, GRID], BF16, 47872)
    xtm_t = overlay(arenaD, "xtm_t", [128, D_MODEL], F32, 0)
    xntm = overlay(arenaD, "xntm", [128, D_MODEL], F32, 4096)
    gbuf = overlay(arenaD, "gbuf", [128, D_INNER], BF16, 0)
    xdt = overlay(arenaD, "xdt", [128, D_INNER], BF16, 4096)
    slotb = overlay(arenaC, "slotb", [128, 3 * SCW], F32, 0)
    p30b = overlay(arenaC, "p30b", [128, SCW], F32, 8448)
    tcomb = overlay(arenaC, "tcomb", [128, SCW], F32, 11264)
    ynf = overlay(arenaC, "ynf", [128, D_INNER], BF16, 0)
    motm = overlay(arenaD, "motm", [128, D_MODEL], F32, 0)
    xgtm = overlay(arenaD, "xgtm", [128, D_MODEL], F32, 4096)
    x2n_t = overlay(arenaC, "x2n_t", [128, D_MODEL], F32, 0)
    xq1tm = overlay(arenaC, "xq1tm", [128, D_MODEL], BF16, 4096)
    acspad = overlay(arenaC, "acspad", [32, 64 + GRID], F32, 0)
    srun = overlay(arenaC, "srun", [128, SCW], F32, 0)
    xh3tm = overlay(arenaA, "xh3tm", [4, D_MODEL], F32, 65536)
    zchunkA = overlay(arenaC, "zchunkA", [128, D_INNER], BF16, 8192)
    convt = overlay(arenaC, "convt", [128, 512], F32, 12288)
    xbca = overlay(arenaC, "xbca", [128, 512], BF16, 14336)
    zchunk2 = overlay(arenaB, "zchunk2", [128, D_INNER], BF16, 24832)
    ucomb = overlay(arenaB, "ucomb", [128, SCW], F32, 41216)
    x2ld1 = overlay(arenaD, "x2ld1", [128, D_MODEL], F32, 0)
    x2ld3 = overlay(arenaC, "x2ld3", [128, D_MODEL], F32, 8192)
    fotm = overlay(arenaC, "fotm", [128, D_MODEL], F32, 12288)
    xq2fm = overlay(arenaA, "xq2fm", [128, D_HIDDEN], BF16, 49152)

    idn_s = sb("idn_s", [128, 128])
    cpool_s = sb("cpool_s", [128, 4])
    idnb_s = sb("idnb_s", [128, 128], BF16)
    e064_s = sb("e064_s", [32, 64])
    e0128_s = sb("e0128_s", [32, 128])
    onescol_s = sb("onescol_s", [128, 1])
    hsel_s = sb("hsel_s", [32, 128], BF16)
    gmask_s = sb("gmask_s", [32, SCW], BF16)
    negio_s = sb("negio_s", [32, 2048])
    eacsb = sb("eacsb", [128, 32])
    tri_s = sb("tri_s", [64, 64])
    tri64hi = sb("tri64hi", [128, 64])
    dtbrep_s = sb("dtbrep_s", [128, 32])
    dskrep_s = sb("dskrep_s", [128, D_INNER], BF16)
    nwrep_s = sb("nwrep_s", [128, D_INNER], BF16)
    alog_s = sb("alog_s", [32, 1])
    nega_s = sb("nega_s", [32, 1])
    convw_s = sb("convw_s", [128, 68])
    convb_s = sb("convb_s", [128, 17])
    fcw_s = sb("fcw_s", [128, 96])
    fcb_s = sb("fcb_s", [128, 32])
    maskrow_s = sb("maskrow_s", [128, NSUB])
    pmc_s = sb("pmc_s", [128, 4])
    slotoff_s = sb("slotoff_s", [4, 1], I32)
    dttm = sb("dttm", [128, NSUB * 32])
    acsfm = sb("acsfm", [32, GRID])
    acstm = sb("acstm", [128, NSUB * 32])
    alast_s = sb("alast_s", [32, NCH])
    expal_s = sb("expal_s", [32, NCH])
    gamlog = sb("gamlog", [32, NCH])
    expgam_s = sb("expgam_s", [32, NCH])
    eg_s = sb("eg_s", [32, 2])
    bfm = sb("bfm", [16, GRID], BF16)
    cfm4 = sb("cfm4", [128, GRID], BF16)
    btm = sb("btm", [128, NSUB * 16])
    xh3buf = sb("xh3buf", [128, 17 * 3], BF16)
    xbchal = sb("xbchal", [128, 17 * 3], BF16)
    xnhf = sb("xnhf", [128, KC * 4], BF16)
    alrow = sb("alrow", [32, 2 * 32])
    decaytm = sb("decaytm", [128, 2 * 32])
    bd = sb("bd", [128, 512], BF16)
    m1m = sb("m1m", [128, 64], BF16)
    sin = sb("sin", [128, SCW])
    stbuf = sb("stbuf", [128, SCW], BF16)
    gslot = sb("gslot", [32, 3 * 2])
    gmg = sb("gmg", [32, SCW], BF16)
    rmscol = sb("rmscol", [128, 2])
    stat6 = sb("stat6", [128, 24])
    stat2 = sb("stat2", [128, 2])
    lncol = sb("lncol", [128, 8])
    xbcbuf = sb("xbcbuf", [128, 2 * 516], BF16)
    dtraw = sb("dtraw", [128, 2 * 32])
    qcol = sb("qcol", [128, 8])
    wabs = sb("wabs", [128, 1])
    wsum1 = sb("wsum1", [128, 1])
    scol = sb("scol", [128, 4])
    valbuf = sb("valbuf", [32, 2])
    psc = sb("psc", [1, 8])

    pB1 = ctx.enter_context(nc.psum_tensor("pB1", [128, 2048], F32))
    pB2 = ctx.enter_context(nc.psum_tensor("pB2", [128, 2048], F32))

    def overlay_ps(base, name, shape, dtype, off_bytes):
        h = nc.virtual_psum_tensor(name, list(shape), dtype)
        nc._tensor(h.name, h.shape, h.dtype, type="PSUM")
        mloc = nc.lookup_mloc(h)
        mloc.allocated = True
        mloc.addr = nc.lookup_mloc(base).addr + off_bytes
        return h

    pTb = overlay_ps(pB2, "pTb", [128, 1024], BF16, 6144)

    p = P(nc, ctx)

    def dma(queue, out, in_, sem=None):
        q = sem if sem else ("qg" if queue == "gpsimd" else "qs")
        p.do(queue, lambda e, out=out, in_=in_: e.dma_start(out=out, in_=in_),
             inc=(q, 16))

    def ln_tm(xap, outap, width, part=128):
        nq = width // 512
        for q in range(nq):
            p.do("vector", lambda v, q=q, xap=xap, part=part: v.bn_stats(
                stat6[0:part, 6 * q:6 * q + 6], xap[:, 512 * q:512 * (q + 1)]),
                inc=("sv", 1))
        p.w("vector", "sv")
        p.do("vector", lambda v, nq=nq, part=part: v.bn_aggr(
            stat2[0:part, 0:2], stat6[0:part, 0:6 * nq]), inc=("sv", 1))
        p.w("scalar", "sv")
        p.do("scalar", lambda sc, part=part: sc.activation(
            lncol[0:part, 1:2], stat2[0:part, 1:2], AF.Ln, bias=EPS),
            inc=("sa", 1))
        p.w("scalar", "sa")
        p.do("scalar", lambda sc, part=part: sc.activation(
            lncol[0:part, 1:2], lncol[0:part, 1:2], AF.Exp, scale=-0.5),
            inc=("sa", 1))
        p.w("vector", "sa")
        p.do("vector", lambda v, xap=xap, outap=outap, part=part:
             v.tensor_scalar(outap, xap, stat2[0:part, 0:1],
                             lncol[0:part, 1:2], ALU.subtract, ALU.mult),
             inc=("sv", 1))

    nc.const_aps.register(F32, 1e-5, cpool_s[:, 0:1])
    nc.const_aps.register(F32, 1.0, cpool_s[:, 1:2])
    nc.const_aps.register(F32, -0.5, cpool_s[:, 2:3])
    nc.const_aps.register(F32, 0.0, cpool_s[:, 3:4])
    for dst, src in [(idn_s, idn_d), (cpool_s, cpool_d),
                     (e064_s, e064_d), (e0128_s, e0128_d),
                     (onescol_s, onescol_d), (negio_s, negio_d),
                     (tri_s, tri_d), (dtbrep_s, dtbrep_d),
                     (alog_s, alog_d), (convw_s, convw_d),
                     (convb_s, convb_d), (fcw_s, fcw_d), (fcb_s, fcb_d),
                     (pmc_s, pmc_d), (slotoff_s, slotoff_d),
                     (maskrow_s, maskrow_d), (xh3tm, xh3_d)]:
        dma("sync", dst[:, :], src[:, :])
    dma("gpsimd", dskrep_s[:, :], dskrep_d[:, :])
    dma("gpsimd", nwrep_s[:, :], nwrep_d[:, :])
    dma("gpsimd", hsel_s[:, :], hsel_d[:, :])
    dma("gpsimd", gmask_s[:, :], gmask_d[:, :])
    for kc in range(KC):
        dma("gpsimd", winb[:, kc * D_IN_PROJ:(kc + 1) * D_IN_PROJ],
            wint_d[kc * 128:(kc + 1) * 128, :])
    p.w("vector", "qs")
    p.do("vector", lambda v: v.tensor_copy(idnb_s[:, :], idn_s[:, :]),
         inc=("sv", 1))
    p.do("gpsimd", lambda g: g.memset(xh3buf[:, :], 0.0), inc=("sg", 1))
    p.do("gpsimd", lambda g: g.memset(xbchal[:, :], 0.0), inc=("sg", 1))
    p.do("gpsimd", lambda g: g.memset(tri64hi[:, :], 0.0), inc=("sg", 1))
    p.w("gpsimd", "sg")
    p.do("gpsimd", lambda g: g.iota(
        tri64hi[:, :], [[1, 64]], base=64, channel_multiplier=-1,
        allow_small_or_imprecise_dtypes=True), inc=("sg", 1))
    p.w("vector", "sg")
    p.do("vector", lambda v: v.tensor_scalar(tri64hi[:, :], tri64hi[:, :], 0.0,
                                             None, ALU.is_ge), inc=("sv", 1))
    p.w("scalar", "qs")
    p.do("scalar", lambda sc: sc.activation(nega_s[:, :], alog_s[:, :], AF.Exp),
         inc=("sa", 1))
    p.w("vector", "sa")
    p.do("vector", lambda v: v.tensor_scalar(nega_s[:, :], nega_s[:, :], -1.0,
                                             None, ALU.mult), inc=("sv", 1))

    ln_tm(xh3tm[0:4, :], xh3tm[0:4, :], D_MODEL, part=4)
    p.w("tensor", "sv")
    for kc in range(KC):
        p.do("tensor", lambda t, kc=kc: t.transpose(
            pB1[:, kc * 4:(kc + 1) * 4], xh3tm[0:4, kc * 128:(kc + 1) * 128],
            idn_s[0:4, 0:4]), inc=("sp", 1))
    p.w("vector", "sp")
    p.do("vector", lambda v: v.tensor_copy(xnhf[:, :], pB1[:, 0:KC * 4]),
         inc=("sv", 1))
    sv_xnhf = p.n["sv"]
    p.w("tensor", "sv", sv_xnhf)
    p.w("tensor", "qg")
    for fc in range(17):
        f0 = D_INNER + fc * 128
        fw = min(128, D_IN_PROJ - 32 - f0)
        p.w("tensor", "sv")
        for kc in range(KC):
            p.do("tensor", lambda t, kc=kc, f0=f0, fw=fw, fc=fc: t.matmul(
                pB2[0:fw, 1792 + fc % 2 * 4: 1792 + fc % 2 * 4 + 3],
                winb[:, kc * D_IN_PROJ + f0: kc * D_IN_PROJ + f0 + fw],
                xnhf[:, kc * 4: kc * 4 + 3],
                start=(kc == 0), stop=(kc == KC - 1)), inc=("sp", 1))
        p.w("vector", "sp")
        p.do("vector", lambda v, fc=fc, fw=fw: v.tensor_copy(
            xh3buf[0:fw, fc * 3:fc * 3 + 3],
            pB2[0:fw, 1792 + fc % 2 * 4: 1792 + fc % 2 * 4 + 3]), inc=("sv", 1))

    def wquant(wsrc_d, wdst_d, nrc, ncols, si):
        p.w("gpsimd", "sv")
        p.do("gpsimd", lambda g: g.memset(wsum1[:, :], 0.0), inc=("sg", 1))
        sv_red = {}
        for rc in range(nrc):
            wt = wtile if rc % 2 == 0 else wtileB
            qq = "qw" if rc % 2 == 0 else "qw2"
            if rc >= 2:
                p.w("sync", "sv", sv_red[rc - 2])
            if rc == 0:
                p.w("sync", "sg")
            dma("sync", wt[:, 0:ncols], wsrc_d[rc * 128:(rc + 1) * 128, :],
                sem=qq)
            p.w("vector", qq)
            p.w("vector", "sv")
            if rc == 0:
                p.w("vector", "sg")
            p.do("vector", lambda v, n=ncols, wt=wt: v.tensor_reduce(
                wabs[:, :], wt[:, 0:n], AX.X, ALU.add,
                apply_absolute_value=True), inc=("sv", 1))
            sv_red[rc] = p.n["sv"]
            p.w("vector", "sv")
            p.do("vector", lambda v: v.tensor_tensor(
                wsum1[:, :], wsum1[:, :], wabs[:, :], ALU.add), inc=("sv", 1))
        p.w("tensor", "sv")
        p.do("tensor", lambda t: t.matmul(pB1[0:1, 0:1], onescol_s[:, :],
                                          wsum1[:, :], start=True, stop=True),
             inc=("sp", 1))
        p.w("vector", "sp")
        p.do("vector", lambda v, nrc=nrc, ncols=ncols: v.tensor_scalar(
            psc[0:1, 0:1], pB1[0:1, 0:1], 1.0 / (nrc * 128 * ncols), None,
            ALU.mult), inc=("sv", 1))
        p.w("vector", "sv")
        p.do("vector", lambda v: v.tensor_scalar(psc[0:1, 0:1], psc[0:1, 0:1],
                                                 1e-5, None, ALU.max),
             inc=("sv", 1))
        p.w("vector", "sv")
        p.do("vector", lambda v: v.reciprocal(psc[0:1, 1:2], psc[0:1, 0:1]),
             inc=("sv", 1))
        p.w("gpsimd", "sv")
        p.do("gpsimd", lambda g: g.memset(valbuf[:, :], 0.0), inc=("sg", 1))
        p.w("vector", "sg")
        p.do("vector", lambda v: v.tensor_copy(valbuf[0:1, 0:1], psc[0:1, 1:2]),
             inc=("sv", 1))
        p.w("vector", "sv")
        p.do("vector", lambda v: v.tensor_copy(valbuf[0:1, 1:2], psc[0:1, 0:1]),
             inc=("sv", 1))
        p.w("tensor", "sv")
        p.do("tensor", lambda t: t.matmul(pB1[0:128, 8:10], e0128_s[:, :],
                                          valbuf[:, :], start=True, stop=True),
             inc=("sp", 1))
        p.w("vector", "sp")
        p.do("vector", lambda v, si=si: v.tensor_copy(
            scol[:, si:si + 1], pB1[0:128, 8:9]), inc=("sv", 1))
        p.w("vector", "sv")
        p.do("vector", lambda v, si=si: v.tensor_copy(
            scol[:, si + 1:si + 2], pB1[0:128, 9:10]), inc=("sv", 1))
        sv_q = {}
        for rc in range(nrc):
            wt = wtile if rc % 2 == 0 else wtileB
            qq = "qw" if rc % 2 == 0 else "qw2"
            if rc >= 2:
                p.w("sync", "sv", sv_q[rc - 2])
            else:
                p.w("sync", "sv")
            dma("sync", wt[:, 0:ncols], wsrc_d[rc * 128:(rc + 1) * 128, :],
                sem=qq)
            p.w("vector", qq)
            p.do("vector", lambda v, n=ncols, si=si, wt=wt: v.tensor_scalar(
                qv4[:, 0:n], wt[:, 0:n], scol[:, si:si + 1], None, ALU.mult),
                inc=("sv", 1))
            sv_q[rc] = p.n["sv"]
            p.w("vector", "sv")
            p.do("vector", lambda v, n=ncols: v.tensor_scalar(
                qt14[:, 0:n], qv4[:, 0:n], 0.5, None, ALU.is_gt), inc=("sv", 1))
            p.w("vector", "sv")
            p.do("vector", lambda v, n=ncols: v.scalar_tensor_tensor(
                qr4[:, 0:n], qv4[:, 0:n], -0.5, qt14[:, 0:n], ALU.is_lt,
                ALU.subtract), inc=("sv", 1))
            p.w("vector", "sv")
            p.do("vector", lambda v, n=ncols, si=si: v.tensor_scalar(
                qv4[:, 0:n], qr4[:, 0:n], scol[:, si + 1:si + 2], -1.0,
                ALU.mult, ALU.mult), inc=("sv", 1))
            p.w("vector", "sv")
            p.w("vector", "qe")
            p.do("vector", lambda v, n=ncols: v.tensor_copy(
                wqtile[:, 0:n], qv4[:, 0:n]), inc=("sv", 1))
            sv_here = p.n["sv"]
            p.w("sync", "sv", sv_here)
            dma("sync", wdst_d[rc * 128:(rc + 1) * 128, :], wqtile[:, 0:ncols],
                sem="qe")

    wquant(w1t_d, w1q_d, 8, 4096, 0)
    wquant(w2t_d, w2q_d, 32, 1024, 2)

    p.barrier()
    for (t0, tw) in PA_TILES:
        nsub_t = tw // 128
        tpar = (t0 // 512) % 2
        for si in range(nsub_t):
            g0 = t0 + si * 128
            p.w("sync", "sv")
            dma("sync", xtm_t[:, :], xg_d[g0:g0 + 128, :], sem="q1")
            p.w("vector", "q1")
            ln_tm(xtm_t[:, :], xntm[:, :], D_MODEL)
            p.w("tensor", "sv")
            for kc in range(KC):
                p.do("tensor", lambda t, kc=kc: t.transpose(
                    pB1[:, kc * 128:(kc + 1) * 128],
                    xntm[:, kc * 128:(kc + 1) * 128], idn_s[:, :]),
                    inc=("sp", 1))
            p.w("vector", "sp")
            xnfb = xnf[:, si * 128:si * 128 + 1]
            xnfap = bass.AP(xnfb.tensor, xnfb.offset,
                            [[KC * 512, 128], [512, KC], [1, 128]])
            p.do("vector", lambda v, xnfap=xnfap: v.tensor_copy(
                xnfap, pB1[:, 0:KC * 128]), inc=("sv", 1))
            p.w("tensor", "sv")
        sv_xnf = p.n["sv"]

        for fc in range(17):
            f0 = D_INNER + fc * 128
            fw = min(128, D_IN_PROJ - 32 - f0)
            p.w("tensor", "sv")
            for kc in range(KC):
                p.do("tensor", lambda t, kc=kc, f0=f0, fw=fw, tw=tw: t.matmul(
                    pB2[0:fw, 0:tw],
                    winb[:, kc * D_IN_PROJ + f0: kc * D_IN_PROJ + f0 + fw],
                    xnf[:, kc * 512: kc * 512 + tw],
                    start=(kc == 0), stop=(kc == KC - 1)), inc=("sp", 1))
            b0 = 516 * tpar
            ob0 = 516 * (1 - tpar)
            p.w("vector", "sp")
            if t0 == 0:
                p.do("vector", lambda v, b0=b0, fc=fc: v.tensor_copy(
                    xbcbuf[:, b0:b0 + 3], xh3buf[:, fc * 3:fc * 3 + 3]),
                    inc=("sv", 1))
            else:
                p.do("vector", lambda v, b0=b0, fc=fc: v.tensor_copy(
                    xbcbuf[:, b0:b0 + 3], xbchal[:, fc * 3:fc * 3 + 3]),
                    inc=("sv", 1))
            p.do("vector", lambda v, b0=b0, fw=fw, tw=tw: v.tensor_copy(
                xbcbuf[0:fw, b0 + 3:b0 + 3 + tw], pB2[0:fw, 0:tw]),
                inc=("sv", 1))
            p.w("vector", "sv")
            p.do("vector", lambda v, b0=b0, tw=tw, fc=fc: v.tensor_copy(
                xbchal[:, fc * 3:fc * 3 + 3], xbcbuf[:, b0 + tw:b0 + tw + 3]),
                inc=("sv", 1))
            p.w("vector", "sv")
            p.do("vector", lambda v, b0=b0, tw=tw, fc=fc: v.tensor_scalar(
                convt[:, 0:tw], xbcbuf[:, b0:b0 + tw],
                convw_s[:, fc * 4:fc * 4 + 1], None, ALU.mult), inc=("sv", 1))
            for k in (1, 2, 3):
                p.w("vector", "sv")
                p.do("vector", lambda v, b0=b0, tw=tw, fc=fc, k=k:
                     v.scalar_tensor_tensor(
                         convt[:, 0:tw], xbcbuf[:, b0 + k:b0 + k + tw],
                         convw_s[:, fc * 4 + k:fc * 4 + k + 1], convt[:, 0:tw],
                         ALU.mult, ALU.add), inc=("sv", 1))
            p.w("scalar", "sv")
            p.w("scalar", "q2")
            p.do("scalar", lambda sc, tw=tw, fc=fc: sc.activation(
                xbca[:, 0:tw], convt[:, 0:tw], AF.Silu,
                bias=convb_s[:, fc:fc + 1]), inc=("sa", 1))
            if fc < 16:
                p.w("tensor", "sa")
                p.w("tensor", "sv")
                for si in range(nsub_t):
                    p.do("tensor", lambda t, si=si: t.transpose(
                        pTb[:, si * 128:si * 128 + 128],
                        xbca[:, si * 128:(si + 1) * 128], idnb_s[:, :]),
                        inc=("sp", 1))
                p.w("vector", "sp")
                xhb = xhtm[:, (t0 // 128) * D_INNER + fc * 128:
                           (t0 // 128) * D_INNER + fc * 128 + 1]
                xhap = bass.AP(xhb.tensor, xhb.offset,
                               [[NSUB * D_INNER, 128], [D_INNER, nsub_t],
                                [1, 128]])
                p.do("vector", lambda v, xhap=xhap, nsub_t=nsub_t:
                     v.tensor_copy(xhap, pTb[:, 0:nsub_t * 128]),
                     inc=("sv", 1))
            else:
                p.w("vector", "sa")
                p.do("vector", lambda v, t0=t0, tw=tw: v.tensor_copy(
                    bfm[:, t0:t0 + tw], xbca[0:16, 0:tw]), inc=("sv", 1))
                p.w("sync", "sa")
                for cg in range(3):
                    dma("sync", cfm4[32 * cg:32 * cg + 16, t0:t0 + tw],
                        xbca[16:32, 0:tw], sem="q2")
                p.w("tensor", "sa")
                p.w("tensor", "sv")
                for si in range(nsub_t):
                    p.do("tensor", lambda t, si=si: t.transpose(
                        pTb[:, si * 128:si * 128 + 32],
                        xbca[0:32, si * 128:(si + 1) * 128], idnb_s[0:32, 0:32]),
                        inc=("sp", 1))
                p.w("vector", "sp")
                ptb0 = pTb[:, 0:1]
                ptbap = bass.AP(ptb0.tensor, ptb0.offset,
                                [[1024, 128], [128, nsub_t], [1, 16]])
                p.do("vector", lambda v, ptbap=ptbap, t0=t0, nsub_t=nsub_t:
                     v.tensor_copy(
                         btm[:, (t0 // 128) * 16:(t0 // 128 + nsub_t) * 16],
                         ptbap), inc=("sv", 1))

        for si in range(nsub_t):
            gsub = t0 // 128 + si
            p.w("tensor", "sv")
            for kc in range(KC):
                for zs in range(4):
                    p.do("tensor", lambda t, kc=kc, si=si, zs=zs: t.matmul(
                        pB1[:, zs * 512:(zs + 1) * 512],
                        xnf[:, kc * 512 + si * 128: kc * 512 + si * 128 + 128],
                        winb[:, kc * D_IN_PROJ + zs * 512:
                             kc * D_IN_PROJ + (zs + 1) * 512],
                        start=(kc == 0), stop=(kc == KC - 1)), inc=("sp", 1))
                p.do("tensor", lambda t, kc=kc, si=si: t.matmul(
                    pB2[:, 1024:1056],
                    xnf[:, kc * 512 + si * 128: kc * 512 + si * 128 + 128],
                    winb[:, kc * D_IN_PROJ + 4128: kc * D_IN_PROJ + 4160],
                    start=(kc == 0), stop=(kc == KC - 1)), inc=("sp", 1))
            p.w("vector", "sp")
            p.w("scalar", "sp")
            p.w("scalar", "q3")
            p.do("scalar", lambda sc: sc.copy(zchunkA[:, :], pB1[:, 0:2048]),
                 inc=("sa", 1))
            sa_z = p.n["sa"]
            p.w("sync", "sa", sa_z)
            dma("sync", zsp_d[gsub * 128:(gsub + 1) * 128, :], zchunkA[:, :],
                sem="q3")
            p.do("vector", lambda v: v.tensor_tensor(
                dtraw[:, 0:32], pB2[:, 1024:1056], dtbrep_s[:, :], ALU.add),
                inc=("sv", 1))
            p.w("scalar", "sv")
            p.do("scalar", lambda sc: sc.activation(dtraw[:, 32:64],
                                                    dtraw[:, 0:32], AF.Exp),
                 inc=("sa", 1))
            p.w("scalar", "sa")
            p.do("scalar", lambda sc, gsub=gsub: sc.activation(
                dttm[:, gsub * 32:(gsub + 1) * 32], dtraw[:, 32:64], AF.Ln,
                bias=1.0), inc=("sa", 1))
            p.w("tensor", "sa")

    p.barrier()
    for gsub in range(NSUB):
        sl = gsub % 8
        p.do("tensor", lambda t, gsub=gsub, sl=sl: t.transpose(
            pB1[0:32, sl * 128:sl * 128 + 128],
            dttm[:, gsub * 32:(gsub + 1) * 32], idn_s[:, :]), inc=("sp", 1))
        p.w("vector", "sp")
        p.do("vector", lambda v, gsub=gsub, sl=sl: v.tensor_copy(
            acspad[:, 64 + gsub * 128:64 + (gsub + 1) * 128],
            pB1[0:32, sl * 128:sl * 128 + 128]), inc=("sv", 1))
        p.w("tensor", "sv")

    p.w("vector", "sv")
    p.do("vector", lambda v: v.tensor_scalar(
        acspad[:, 64:64 + GRID], acspad[:, 64:64 + GRID], nega_s[0:32, 0:1],
        None, ALU.mult), inc=("sv", 1))
    p.w("vector", "sv")
    p.w("gpsimd", "sv")
    p.do("gpsimd", lambda g: g.memset(acspad[:, 0:64], 0.0), inc=("sg", 1))
    p.w("vector", "sg")
    p.w("vector", "sv")
    p.do("vector", lambda v: v.tensor_tensor_scan(
        acspad[:, 64:64 + GRID], acspad[:, 64:64 + GRID],
        acspad[:, 64:64 + GRID], 0.0, ALU.add, ALU.bypass), inc=("sv", 1))
    p.w("vector", "sv")
    off_ap = acspad[:, 63:64]
    p.do("vector", lambda v: v.tensor_tensor(
        acsfm[:, :].rearrange("p (c u) -> p c u", u=64),
        acspad[:, 64:64 + GRID].rearrange("p (c u) -> p c u", u=64),
        bcast_free(off_ap, 32, [[64, NCH], [0, 64]]), ALU.subtract),
        inc=("sv", 1))
    p.w("vector", "sv")
    al_ap = acsfm[:, 63:64]
    p.do("vector", lambda v: v.tensor_copy(
        alast_s[:, :], bcast_free(al_ap, 32, [[64, NCH]])), inc=("sv", 1))
    p.w("scalar", "sv")
    p.do("scalar", lambda sc: sc.activation(expal_s[:, :], alast_s[:, :],
                                            AF.Exp), inc=("sa", 1))
    p.w("gpsimd", "sa")
    p.do("gpsimd", lambda g: g.memset(gamlog[:, :], 0.0), inc=("sg", 1))
    p.w("vector", "sg")
    p.do("vector", lambda v: v.tensor_tensor_scan(
        gamlog[:, 2:34], alast_s[:, 1:33], alast_s[:, 1:33], 0.0,
        ALU.add, ALU.bypass), inc=("sv", 1))
    p.w("scalar", "sv")
    p.do("scalar", lambda sc: sc.activation(expgam_s[:, :], gamlog[:, :],
                                            AF.Exp), inc=("sa", 1))
    p.do("vector", lambda v: v.tensor_reduce(eg_s[:, 0:1], alast_s[:, 1:33],
                                             AX.X, ALU.add), inc=("sv", 1))
    p.w("vector", "sv")
    p.do("vector", lambda v: v.tensor_reduce(eg_s[:, 1:2], alast_s[:, 1:32],
                                             AX.X, ALU.add), inc=("sv", 1))
    p.w("scalar", "sv")
    p.do("scalar", lambda sc: sc.activation(eg_s[:, :], eg_s[:, :], AF.Exp),
         inc=("sa", 1))
    p.w("sync", "sv")
    p.w("sync", "sa")
    dma("sync", acs_d[:, :], acsfm[:, :], sem="q4")
    p.do("sync", lambda e: e.dma_start(out=alast_d[:, :].transpose([1, 0]),
                                       in_=alast_s[:, :],
                                       allow_slow_non_contiguous=True),
         inc=("q4", 16))
    p.w("tensor", "sv")
    for gsub in range(NSUB):
        sl = gsub % 8
        p.do("tensor", lambda t, gsub=gsub, sl=sl: t.transpose(
            pB1[:, sl * 128:sl * 128 + 32],
            acsfm[:, gsub * 128:(gsub + 1) * 128], idn_s[0:32, 0:32]),
            inc=("sp", 1))
        p.w("vector", "sp")
        p.do("vector", lambda v, gsub=gsub, sl=sl: v.tensor_copy(
            acstm[:, gsub * 32:(gsub + 1) * 32],
            pB1[:, sl * 128:sl * 128 + 32]), inc=("sv", 1))
        p.w("tensor", "sv")

    p.barrier()
    p.do("gpsimd", lambda g: g.memset(srun[:, :], 0.0), inc=("sg", 1))
    p.w("vector", "sg")
    p.do("vector", lambda v: v.tensor_copy(pB2[0:80, 1024:1024 + SCW],
                                           srun[0:80, :]), inc=("sv", 1))
    p.w("gpsimd", "sg")
    p.do("gpsimd", lambda g: g.memset(flatbuf[:, :], 0.0), inc=("sg", 1))
    p.w("gpsimd", "sg")
    p.do("gpsimd", lambda g: g.memset(alrow[:, :], 0.0), inc=("sg", 1))
    p.w("vector", "sg")
    p.w("sync", "sg")

    def emit_xdt(ci):
        par = ci % 2
        gsub = ci // 2
        xsl = xhtm[par * 64:par * 64 + 64, gsub * D_INNER:(gsub + 1) * D_INNER]
        dsl = dttm[par * 64:par * 64 + 64, gsub * 32:(gsub + 1) * 32]
        p.do("vector", lambda v, xsl=xsl, dsl=dsl, par=par: v.tensor_tensor(
            xdt[par * 64:par * 64 + 64, :].rearrange("p (h q) -> p h q", h=32),
            xsl.rearrange("p (h q) -> p h q", h=32),
            bcast_free(dsl[:, 0:1], 64, [[1, 32], [0, 64]]), ALU.mult),
            inc=("sv", 1))

    for ci in range(1, 33):
        buf = ci % 2
        par = ci % 2
        gsub = ci // 2
        qa = "q5" if buf == 0 else "q6"
        qn = "q7" if buf == 0 else "q8"
        p.w("sync", "sa")
        p.w("sync", "sp")
        dma("sync", alrow[0:1, buf * 32:buf * 32 + 32], alast_d[ci:ci + 1, :],
            sem=qa)
        emit_xdt(ci)
        p.w("tensor", qa)
        p.w("tensor", "sv")
        p0_ = par * 64
        p.do("tensor", lambda t, buf=buf, p0_=p0_: t.matmul(
            pB1[p0_:p0_ + 64, 1920:1952], e064_s[:, :],
            alrow[:, buf * 32:buf * 32 + 32], start=True, stop=True),
            inc=("sp", 1))
        p.w("vector", "sp")
        asl = acstm[p0_:p0_ + 64, gsub * 32:(gsub + 1) * 32]
        p.do("vector", lambda v, asl=asl, buf=buf, p0_=p0_: v.tensor_tensor(
            decaytm[p0_:p0_ + 64, buf * 32:buf * 32 + 32],
            pB1[p0_:p0_ + 64, 1920:1952], asl, ALU.subtract), inc=("sv", 1))
        p.w("scalar", "sv")
        p.do("scalar", lambda sc, buf=buf, p0_=p0_: sc.activation(
            decaytm[p0_:p0_ + 64, buf * 32:buf * 32 + 32],
            decaytm[p0_:p0_ + 64, buf * 32:buf * 32 + 32], AF.Exp),
            inc=("sa", 1))
        bsl = btm[p0_:p0_ + 64, gsub * 16:(gsub + 1) * 16]
        dcl = decaytm[p0_:p0_ + 64, buf * 32:buf * 32 + 32]
        p.w("vector", "sa")
        p.do("vector", lambda v, bsl=bsl, dcl=dcl, p0_=p0_: v.tensor_tensor(
            bd[p0_:p0_ + 64, :].rearrange("p (h n) -> p h n", h=32),
            bcast_free(dcl[:, 0:1], 64, [[1, 32], [0, 16]]),
            bcast_free(bsl[:, 0:1], 64, [[0, 32], [1, 16]]), ALU.mult),
            inc=("sv", 1))
        p.w("tensor", "sv")
        for h in range(NHEADS):
            hg = h // 11
            hi = h - 11 * hg
            p.do("tensor", lambda t, hg=hg, hi=hi, h=h, p0_=p0_: t.matmul(
                pB2[32 * hg:32 * hg + 16, 1024 + 64 * hi:1024 + 64 * hi + 64],
                bd[p0_:p0_ + 64, h * 16:(h + 1) * 16],
                xdt[p0_:p0_ + 64, h * 64:(h + 1) * 64],
                start=True, stop=True), inc=("sp", 1))
        p.w("vector", qn)
        p.do("vector", lambda v, buf=buf: v.tensor_copy(
            snapbuf[0:80, buf * SCW:buf * SCW + SCW], srun[0:80, :]),
            inc=("sv", 1))
        if ci == 32:
            p.w("vector", "sv")
            p.do("vector", lambda v: v.tensor_copy(tcomb[0:80, :],
                                                   srun[0:80, :]),
                 inc=("sv", 1))
        p.w("sync", "sv")
        dma("sync", snap_d[(ci - 1) * 80:ci * 80, :],
            snapbuf[0:80, buf * SCW:buf * SCW + SCW], sem=qn)
        p.do("vector", lambda v, ci=ci: v.tensor_scalar(
            gmg[:, :], gmask_s[:, :], expal_s[:, ci:ci + 1], None, ALU.mult),
            inc=("sv", 1))
        p.w("tensor", "sv")
        p.do("tensor", lambda t: t.matmul(pB1[0:80, 1024:1536],
                                          hsel_s[:, 0:80], gmg[:, 0:512],
                                          start=True, stop=True),
             inc=("sp", 1))
        p.do("tensor", lambda t: t.matmul(pB1[0:80, 1536:1536 + SCW - 512],
                                          hsel_s[:, 0:80], gmg[:, 512:SCW],
                                          start=True, stop=True),
             inc=("sp", 1))
        p.w("vector", "sp")
        p.do("vector", lambda v: v.tensor_tensor(srun[0:80, :], srun[0:80, :],
                                                 pB1[0:80, 1024:1024 + SCW],
                                                 ALU.mult), inc=("sv", 1))
        p.w("vector", "sv")
        p.do("vector", lambda v: v.tensor_tensor(srun[0:80, :], srun[0:80, :],
                                                 pB2[0:80, 1024:1024 + SCW],
                                                 ALU.add), inc=("sv", 1))
        p.w("tensor", "sv")
    p.w("vector", "sv")
    p.w("vector", "q8")
    p.do("vector", lambda v: v.tensor_copy(snapbuf[0:80, SCW:2 * SCW],
                                           srun[0:80, :]), inc=("sv", 1))
    p.do("gpsimd", lambda g: g.memset(ucomb[0:32, :], 0.0), inc=("sg", 1))
    sv_ag = p.n["sv"]
    p.w("sync", "sv", sv_ag)
    p.w("sync", "sg")
    dma("sync", snap_d[32 * 80:33 * 80, :], snapbuf[0:80, SCW:2 * SCW],
        sem="q8")
    dma("sync", agin_d[0:80, :], srun[0:80, :], sem="q9")
    dma("sync", agin_d[80:160, :], tcomb[0:80, :], sem="q9")
    dma("sync", agin_d[160:192, :], ucomb[0:32, :], sem="q9")
    p.w("gpsimd", "q9")
    dma("gpsimd", agin_d[160:192, 0:2], eg_s[:, :], sem="st")
    p.w("gpsimd", "st")
    p.do("gpsimd", lambda g: g.collective_compute(
        "AllGather", ALU.bypass, replica_groups=[[0, 1, 2, 3], [4, 5, 6, 7]],
        ins=[agin_d[:, :].opt()], outs=[agout_d[:, :].opt()]), inc=("cc", 1))

    def f_load_offsets(e):
        for k in (1, 2, 3):
            p.env[f"so{k}"] = e.value_load(slotoff_s[k - 1:k, 0:1])
    p.w("sync", "cc")
    p.ops["sync"].append(f_load_offsets)
    for k in (1, 2, 3):
        p.do("sync", lambda e, k=k: e.dma_start(
            out=slotb[0:80, (k - 1) * SCW:k * SCW],
            in_=agout_d[bass.ds(p.env[f"so{k}"], 80), :]), inc=("q10", 16))
        p.do("sync", lambda e, k=k: e.dma_start(
            out=gslot[:, (k - 1) * 2:k * 2],
            in_=agout_d[bass.ds(p.env[f"so{k}"] + 160, 32), 0:2]),
            inc=("q10", 16))
    p.do("sync", lambda e: e.dma_start(
        out=p30b[0:80, :], in_=agout_d[bass.ds(p.env["so1"] + 80, 80), :]),
        inc=("q10", 16))

    def grep(gcol_ap):
        p.w("vector", "sp")
        p.do("vector", lambda v, g=gcol_ap: v.tensor_scalar(
            gmg[:, :], gmask_s[:, :], g, None, ALU.mult), inc=("sv", 1))
        p.w("tensor", "sv")
        p.do("tensor", lambda t: t.matmul(
            pB1[0:80, 1024:1536], hsel_s[:, 0:80], gmg[:, 0:512],
            start=True, stop=True), inc=("sp", 1))
        p.do("tensor", lambda t: t.matmul(
            pB1[0:80, 1536:1536 + SCW - 512], hsel_s[:, 0:80], gmg[:, 512:SCW],
            start=True, stop=True), inc=("sp", 1))

    p.w("vector", "q10")
    p.do("vector", lambda v: v.tensor_scalar(
        tcomb[0:80, :], slotb[0:80, 2 * SCW:3 * SCW], pmc_s[0:80, 2:3], None,
        ALU.mult), inc=("sv", 1))
    grep(gslot[:, 2:3])
    p.w("vector", "sp")
    p.w("vector", "sv")
    p.do("vector", lambda v: v.tensor_tensor(ucomb[0:80, :], tcomb[0:80, :],
                                             pB1[0:80, 1024:1024 + SCW],
                                             ALU.mult), inc=("sv", 1))
    p.w("vector", "sv")
    p.do("vector", lambda v: v.tensor_tensor(ucomb[0:80, :], ucomb[0:80, :],
                                             slotb[0:80, SCW:2 * SCW],
                                             ALU.add), inc=("sv", 1))
    p.w("vector", "sv")
    p.do("vector", lambda v: v.tensor_scalar(tcomb[0:80, :], ucomb[0:80, :],
                                             pmc_s[0:80, 1:2], None, ALU.mult),
         inc=("sv", 1))
    grep(gslot[:, 0:1])
    p.w("vector", "sp")
    p.w("vector", "sv")
    p.do("vector", lambda v: v.tensor_tensor(ucomb[0:80, :], tcomb[0:80, :],
                                             pB1[0:80, 1024:1024 + SCW],
                                             ALU.mult), inc=("sv", 1))
    p.w("vector", "sv")
    p.do("vector", lambda v: v.tensor_tensor(ucomb[0:80, :], ucomb[0:80, :],
                                             slotb[0:80, 0:SCW], ALU.add),
         inc=("sv", 1))
    p.w("vector", "sv")
    p.do("vector", lambda v: v.tensor_scalar(sin[0:80, :], ucomb[0:80, :],
                                             pmc_s[0:80, 0:1], None, ALU.mult),
         inc=("sv", 1))
    grep(gslot[:, 1:2])
    p.w("vector", "sp")
    p.w("vector", "sv")
    p.do("vector", lambda v: v.tensor_tensor(ucomb[0:80, :], tcomb[0:80, :],
                                             pB1[0:80, 1024:1024 + SCW],
                                             ALU.mult), inc=("sv", 1))
    p.w("vector", "sv")
    p.do("vector", lambda v: v.tensor_tensor(ucomb[0:80, :], ucomb[0:80, :],
                                             p30b[0:80, :], ALU.add),
         inc=("sv", 1))
    p.w("vector", "sv")
    p.do("vector", lambda v: v.tensor_scalar(stbuf[0:80, :], ucomb[0:80, :],
                                             pmc_s[0:80, 3:4], None, ALU.mult),
         inc=("sv", 1))

    p.barrier()
    for ci in range(NCH):
        par = ci % 2
        gsub = ci // 2
        buf = ci % 2
        p0_ = par * 64
        p.w("sync", "sp")
        fl0 = acs_d[0:1, ci * 64:ci * 64 + 1]
        p.do("sync", lambda e, fl0=fl0: e.dma_start(
            out=flatbuf[0:1, 0:2048],
            in_=bass.AP(fl0.tensor, fl0.offset, [[1, 1], [GRID, 32], [1, 64]])),
            inc=("q11", 16))
        p.w("tensor", "q11")
        p.w("tensor", "sv")
        p.do("tensor", lambda t, ci=ci, p0_=p0_: t.matmul(
            pB2[p0_:p0_ + 64, 1024:1088],
            bfm[:, ci * 64:(ci + 1) * 64], cfm4[0:16, ci * 64:(ci + 1) * 64],
            start=True, stop=True), inc=("sp", 1))
        p.w("vector", "sp")
        p.do("vector", lambda v, p0_=p0_: v.tensor_tensor(
            m1m[p0_:p0_ + 64, :], pB2[p0_:p0_ + 64, 1024:1088],
            tri_s[:, :] if p0_ == 0 else tri64hi[64:128, :], ALU.mult),
             inc=("sv", 1))
        p.w("tensor", "sv")
        for q in range(4):
            p.do("tensor", lambda t, q=q, p0_=p0_: t.matmul(
                pB1[p0_:p0_ + 64, q * 512:(q + 1) * 512], e064_s[:, :],
                flatbuf[:, q * 512:(q + 1) * 512],
                start=True, stop=False), inc=("sp", 1))
            p.do("tensor", lambda t, q=q, ci=ci, p0_=p0_: t.matmul(
                pB1[p0_:p0_ + 64, q * 512:(q + 1) * 512],
                acsfm[:, ci * 64:(ci + 1) * 64],
                negio_s[:, q * 512:(q + 1) * 512],
                start=False, stop=True), inc=("sp", 1))
        p.w("vector", "sp")
        p.do("vector", lambda v, p0_=p0_: v.tensor_scalar(
            dmin[p0_:p0_ + 64, :], pB1[p0_:p0_ + 64, 0:2048],
            0.0, None, ALU.min), inc=("sv", 1))
        p.w("scalar", "sv")
        p.do("scalar", lambda sc, p0_=p0_: sc.activation(
            ebuf[p0_:p0_ + 64, :], dmin[p0_:p0_ + 64, :], AF.Exp),
            inc=("sa", 1))
        p.w("vector", "sa")
        p.do("vector", lambda v, p0_=p0_: v.tensor_tensor(
            gbuf[p0_:p0_ + 64, :].rearrange("p (h l) -> p h l", h=32),
            ebuf[p0_:p0_ + 64, :].rearrange("p (h l) -> p h l", h=32),
            bcast_free(m1m[p0_:p0_ + 64, 0:1], 64, [[0, 32], [1, 64]]),
            ALU.mult), inc=("sv", 1))
        if ci + 1 <= NCH - 1:
            qm2 = "q14" if (ci + 1) % 2 == 0 else "q15"
            p.w("gpsimd", "sv")
            p.do("gpsimd", lambda g, ci=ci: g.dma_start(
                out=snapld[0:80, ((ci + 1) % 2) * SCW:((ci + 1) % 2 + 1) * SCW],
                in_=snap_d[ci * 80:(ci + 1) * 80, :]), inc=(qm2, 16))
        if ci == 0:
            pass
        else:
            grep(expgam_s[:, ci:ci + 1])
            p.w("vector", "sp")
            p.do("vector", lambda v: v.tensor_tensor(
                ucomb[0:80, :], sin[0:80, :], pB1[0:80, 1024:1024 + SCW],
                ALU.mult), inc=("sv", 1))
            p.w("vector", "sv")
            p.w("vector", "q14" if ci % 2 == 0 else "q15")
            p.do("vector", lambda v, ci=ci: v.tensor_tensor(
                stbuf[0:80, :], ucomb[0:80, :],
                snapld[0:80, (ci % 2) * SCW:(ci % 2 + 1) * SCW], ALU.add),
                inc=("sv", 1))
        emit_xdt(ci)
        asl2 = acstm[p0_:p0_ + 64, gsub * 32:(gsub + 1) * 32]
        p.w("tensor", "sv")
        for h in range(NHEADS):
            pp = 32 * (h % 4)
            ff = 64 * (h // 4)
            p.do("tensor", lambda t, h=h, p0_=p0_: t.matmul(
                pB2[p0_:p0_ + 64, h * 64:(h + 1) * 64],
                gbuf[p0_:p0_ + 64, h * 64:(h + 1) * 64],
                xdt[p0_:p0_ + 64, h * 64:(h + 1) * 64],
                start=True, stop=True), inc=("sp", 1))
            hg = h // 11
            hi = h - 11 * hg
            p.do("tensor", lambda t, h=h, hg=hg, hi=hi, ci=ci, p0_=p0_: t.matmul(
                pB1[p0_:p0_ + 64, h * 64:(h + 1) * 64],
                cfm4[32 * hg:32 * hg + 16, ci * 64:(ci + 1) * 64],
                stbuf[32 * hg:32 * hg + 16, 64 * hi:64 * hi + 64],
                start=True, stop=True), inc=("sp", 1))
        p.w("sync", "sv")
        p.w("sync", "sa")
        qz = "q12" if par == 0 else "q13"
        dma("sync", zchunk2[p0_:p0_ + 64, :], zsp_d[ci * 64:(ci + 1) * 64, :],
            sem=qz)
        p.w("scalar", qz)
        p.w("scalar", "sv")
        p.do("scalar", lambda sc, p0_=p0_: sc.activation(
            sz[p0_:p0_ + 64, :], zchunk2[p0_:p0_ + 64, :], AF.Silu),
            inc=("sa", 1))
        p.do("scalar", lambda sc, asl2=asl2, p0_=p0_: sc.activation(
            eacsb[p0_:p0_ + 64, 0:32], asl2, AF.Exp), inc=("sa", 1))
        p.w("vector", "sa")
        p.do("vector", lambda v, p0_=p0_, gsub=gsub: v.tensor_tensor(
            ytmp[p0_:p0_ + 64, :],
            xhtm[p0_:p0_ + 64, gsub * D_INNER:(gsub + 1) * D_INNER],
            dskrep_s[p0_:p0_ + 64, :], ALU.mult), inc=("sv", 1))
        p.w("vector", "sp")
        p.w("vector", "sv")
        p.do("vector", lambda v, p0_=p0_: v.tensor_tensor(
            ytmp[p0_:p0_ + 64, :], ytmp[p0_:p0_ + 64, :],
            pB2[p0_:p0_ + 64, 0:2048], ALU.add), inc=("sv", 1))
        p.w("vector", "sv")
        p.do("vector", lambda v, p0_=p0_: v.tensor_tensor(
            ytmp2[p0_:p0_ + 64, :].rearrange("p (h q) -> p h q", h=32),
            pB1[p0_:p0_ + 64, 0:2048].rearrange("p (h q) -> p h q", h=32),
            bcast_free(eacsb[p0_:p0_ + 64, 0:1], 64, [[1, 32], [0, 64]]),
            ALU.mult), inc=("sv", 1))
        p.w("vector", "sv")
        p.do("vector", lambda v, p0_=p0_: v.tensor_tensor(
            ytmp[p0_:p0_ + 64, :], ytmp[p0_:p0_ + 64, :],
            ytmp2[p0_:p0_ + 64, :], ALU.add), inc=("sv", 1))
        p.w("vector", "sv")
        p.do("vector", lambda v, p0_=p0_: v.tensor_tensor(
            ytmp[p0_:p0_ + 64, :], ytmp[p0_:p0_ + 64, :], sz[p0_:p0_ + 64, :],
            ALU.mult), inc=("sv", 1))
        p.w("vector", "sv")
        for q in range(4):
            p.do("vector", lambda v, q=q, p0_=p0_: v.bn_stats(
                stat6[p0_:p0_ + 64, 6 * q:6 * q + 6],
                ytmp[p0_:p0_ + 64, 512 * q:512 * (q + 1)]), inc=("sv", 1))
        p.w("vector", "sv")
        p.do("vector", lambda v, p0_=p0_: v.bn_aggr(
            stat2[p0_:p0_ + 64, 0:2], stat6[p0_:p0_ + 64, 0:24]),
            inc=("sv", 1))
        p.w("vector", "sv")
        p.do("vector", lambda v, p0_=p0_: v.tensor_tensor(
            rmscol[p0_:p0_ + 64, 0:1], stat2[p0_:p0_ + 64, 0:1],
            stat2[p0_:p0_ + 64, 0:1], ALU.mult), inc=("sv", 1))
        p.w("vector", "sv")
        p.do("vector", lambda v, p0_=p0_: v.tensor_tensor(
            rmscol[p0_:p0_ + 64, 0:1], rmscol[p0_:p0_ + 64, 0:1],
            stat2[p0_:p0_ + 64, 1:2], ALU.add), inc=("sv", 1))
        p.w("scalar", "sv")
        p.do("scalar", lambda sc, p0_=p0_: sc.activation(
            rmscol[p0_:p0_ + 64, 1:2], rmscol[p0_:p0_ + 64, 0:1], AF.Ln,
            bias=EPS), inc=("sa", 1))
        p.w("scalar", "sa")
        p.do("scalar", lambda sc, p0_=p0_: sc.activation(
            rmscol[p0_:p0_ + 64, 1:2], rmscol[p0_:p0_ + 64, 1:2], AF.Exp,
            scale=-0.5), inc=("sa", 1))
        p.w("vector", "sa")
        p.do("vector", lambda v, p0_=p0_, gsub=gsub: v.scalar_tensor_tensor(
            xhtm[p0_:p0_ + 64, gsub * D_INNER:(gsub + 1) * D_INNER],
            ytmp[p0_:p0_ + 64, :], rmscol[p0_:p0_ + 64, 1:2],
            nwrep_s[p0_:p0_ + 64, :], ALU.mult, ALU.mult), inc=("sv", 1))
        p.w("tensor", "sv")

    p.barrier()
    for kc16 in range(16):
        dma("gpsimd", woutb[:, kc16 * D_MODEL:(kc16 + 1) * D_MODEL],
            wout_d[kc16 * 128:(kc16 + 1) * 128, :], sem="q16")
    p.w("tensor", "q16")
    sa_motm = {}
    for su in range(NSUB):
        p.w("tensor", "sv")
        for half in range(2):
            for sl in range(8):
                fc = half * 8 + sl
                p.do("tensor", lambda t, su=su, fc=fc, sl=sl: t.transpose(
                    pTb[:, sl * 128:sl * 128 + 128],
                    xhtm[:, su * D_INNER + fc * 128:
                         su * D_INNER + fc * 128 + 128],
                    idnb_s[:, :]), inc=("sp", 1))
            p.w("vector", "sp")
            p.do("vector", lambda v, half=half: v.tensor_copy(
                ynf[:, half * 1024:(half + 1) * 1024], pTb[:, 0:1024]),
                inc=("sv", 1))
            p.w("tensor", "sv")
        pp3 = pB1 if su % 2 == 0 else pB2
        if su >= 2:
            p.w("tensor", "sa", sa_motm[su - 2])
        for fc in range(16):
            for q in range(2):
                p.do("tensor", lambda t, fc=fc, q=q, pp3=pp3: t.matmul(
                    pp3[:, q * 512:(q + 1) * 512],
                    ynf[:, fc * 128:(fc + 1) * 128],
                    woutb[:, fc * D_MODEL + q * 512: fc * D_MODEL + (q + 1) * 512],
                    start=(fc == 0), stop=(fc == 15)), inc=("sp", 1))
        p.w("scalar", "sp")
        p.w("scalar", "sv")
        p.do("scalar", lambda sc, pp3=pp3: sc.copy(motm[:, :], pp3[:, 0:1024]),
             inc=("sa", 1))
        sa_motm[su] = p.n["sa"]
        p.w("vector", "sa")
        p.w("vector", "sv")
        ln_tm(motm[:, :], motm[:, :], D_MODEL)
        p.w("sync", "sv")
        p.w("sync", "q18")
        dma("sync", xgtm[:, :], xg_d[su * 128:(su + 1) * 128, :], sem="q17")
        p.w("vector", "q17")
        p.do("vector", lambda v: v.tensor_tensor(xgtm[:, :], motm[:, :],
                                                 xgtm[:, :], ALU.add),
             inc=("sv", 1))
        p.w("vector", "sv")
        p.do("vector", lambda v, su=su: v.tensor_scalar(
            xgtm[:, :], xgtm[:, :], maskrow_s[:, su:su + 1], None, ALU.mult),
            inc=("sv", 1))
        sv_x2 = p.n["sv"]
        p.w("sync", "sv", sv_x2)
        dma("sync", x2_d[su * 128:(su + 1) * 128, :], xgtm[:, :], sem="q18")

    p.barrier()
    for su in range(NSUB):
        p.w("sync", "sv")
        dma("sync", x2ld1[:, :], x2_d[su * 128:(su + 1) * 128, :], sem="q19")
        p.w("vector", "q19")
        ln_tm(x2ld1[:, :], x2n_t[:, :], D_MODEL)
        p.w("vector", "sv")
        p.do("vector", lambda v: v.tensor_reduce(
            qcol[:, 0:1], x2n_t[:, 0:1024], AX.X, ALU.max,
            apply_absolute_value=True), inc=("sv", 1))
        p.w("vector", "sv")
        p.do("vector", lambda v: v.tensor_scalar(qcol[:, 0:1], qcol[:, 0:1],
                                                 1e-5, None, ALU.max),
             inc=("sv", 1))
        p.w("vector", "sv")
        p.do("vector", lambda v: v.reciprocal(qcol[:, 1:2], qcol[:, 0:1]),
             inc=("sv", 1))
        p.w("vector", "sv")
        p.do("vector", lambda v: v.tensor_scalar(qcol[:, 1:2], qcol[:, 1:2],
                                                 127.0, None, ALU.mult),
             inc=("sv", 1))
        p.do("vector", lambda v: v.tensor_scalar(qcol[:, 2:3], qcol[:, 0:1],
                                                 1.0 / 127.0, None, ALU.mult),
             inc=("sv", 1))
        p.w("vector", "sv")
        p.do("vector", lambda v: v.tensor_scalar(qv1[:, :], x2n_t[:, :],
                                                 qcol[:, 1:2], None, ALU.mult),
             inc=("sv", 1))
        p.w("vector", "sv")
        p.do("vector", lambda v: v.tensor_copy(qyi1[:, :], qv1[:, :]),
             inc=("sv", 1))
        p.w("vector", "sv")
        p.do("vector", lambda v: v.tensor_copy(qy1[:, :], qyi1[:, :]),
             inc=("sv", 1))
        p.w("vector", "sv")
        p.do("vector", lambda v: v.tensor_tensor(qr1[:, :], qv1[:, :],
                                                 qy1[:, :], ALU.subtract),
             inc=("sv", 1))
        p.w("vector", "sv")
        p.do("vector", lambda v: v.scalar_tensor_tensor(
            qt11[:, :], qr1[:, :], 0.5, qy1[:, :], ALU.is_gt,
            ALU.add), inc=("sv", 1))
        p.w("vector", "sv")
        p.do("vector", lambda v: v.scalar_tensor_tensor(
            qv1[:, :], qr1[:, :], -0.5, qt11[:, :], ALU.is_ge,
            ALU.add), inc=("sv", 1))
        p.w("vector", "sv")
        p.do("vector", lambda v: v.tensor_scalar(
            xq1tm[:, :], qv1[:, :], -1.0, qcol[:, 2:3], ALU.add, ALU.mult),
            inc=("sv", 1))
        p.w("tensor", "sv")
        for kc in range(KC):
            sl = kc % 8
            p.do("tensor", lambda t, kc=kc, sl=sl: t.transpose(
                pTb[:, sl * 128:sl * 128 + 128],
                xq1tm[:, kc * 128:(kc + 1) * 128], idnb_s[:, :]), inc=("sp", 1))
        p.w("vector", "sp")
        xq1b = xq1fm[:, su * 128:su * 128 + 1]
        xq1ap = bass.AP(xq1b.tensor, xq1b.offset,
                        [[KC * GRID, 128], [GRID, KC], [1, 128]])
        p.do("vector", lambda v, xq1ap=xq1ap: v.tensor_copy(
            xq1ap, pTb[:, 0:1024]), inc=("sv", 1))

    p.barrier()
    for kc in range(KC):
        dma("sync", w1qb[:, kc * D_HIDDEN:(kc + 1) * D_HIDDEN],
            w1q_d[kc * 128:(kc + 1) * 128, :], sem="q20")
    p.w("tensor", "q20")
    sa_tile0 = {}
    sa_end = {}
    for hf in range(32):
        pp = pB1 if hf % 2 == 0 else pB2
        hb = h1buf if hf % 2 == 0 else h1bufB
        hc = hconv if hf % 2 == 0 else hconvB
        if hf >= 2:
            p.w("tensor", "sa", sa_end[hf - 2])
        for q, (t0, tw) in enumerate(PA_TILES):
            reg = q % 4
            for kc in range(KC):
                p.do("tensor", lambda t, hf=hf, kc=kc, t0=t0, tw=tw, reg=reg,
                     pp=pp: t.matmul(
                         pp[:, reg * 512: reg * 512 + tw],
                         w1qb[:, kc * D_HIDDEN + hf * 128:
                              kc * D_HIDDEN + hf * 128 + 128],
                         xq1fm[:, kc * GRID + t0: kc * GRID + t0 + tw],
                         start=(kc == 0), stop=(kc == KC - 1)), inc=("sp", 1))
            p.w("scalar", "sp")
            if q == 0:
                p.w("scalar", "sv")
            p.do("scalar", lambda sc, t0=t0, tw=tw, reg=reg, pp=pp, hb=hb:
                 sc.copy(hb[:, t0:t0 + tw],
                         pp[:, reg * 512: reg * 512 + tw]), inc=("sa", 1))
            if q == 0:
                sa_tile0[hf] = p.n["sa"]
            if reg == 3:
                p.w("tensor", "sa", sa_tile0[hf])
        sa_end[hf] = p.n["sa"]
        p.w("vector", "sv")
        p.w("vector", "sa")
        p.do("vector", lambda v, hf=hf, hb=hb, hc=hc: v.tensor_scalar(
            hc[:, :], hb[:, :], fcw_s[:, hf * 3 + 1:hf * 3 + 2], None,
            ALU.mult), inc=("sv", 1))
        p.w("vector", "sv")
        p.do("vector", lambda v, hf=hf, hb=hb, hc=hc: v.scalar_tensor_tensor(
            hc[:, 1:GRID], hb[:, 0:GRID - 1],
            fcw_s[:, hf * 3:hf * 3 + 1], hc[:, 1:GRID], ALU.mult, ALU.add),
            inc=("sv", 1))
        p.w("vector", "sv")
        p.do("vector", lambda v, hf=hf, hb=hb, hc=hc: v.scalar_tensor_tensor(
            hc[:, 0:GRID - 1], hb[:, 1:GRID],
            fcw_s[:, hf * 3 + 2:hf * 3 + 3], hc[:, 0:GRID - 1], ALU.mult,
            ALU.add), inc=("sv", 1))
        p.w("scalar", "sv")
        p.w("scalar", "q21")
        p.do("scalar", lambda sc, hf=hf, hc=hc: sc.activation(
            hpostb[:, :], hc[:, :], AF.Silu, bias=fcb_s[:, hf:hf + 1]),
            inc=("sa", 1))
        sa_here = p.n["sa"]
        p.w("sync", "sa", sa_here)
        dma("sync", hpost_d[hf * 128:(hf + 1) * 128, :], hpostb[:, :],
            sem="q21")

    p.barrier()
    for q32 in range(32):
        dma("sync", w2qb[:, q32 * D_MODEL:(q32 + 1) * D_MODEL],
            w2q_d[q32 * 128:(q32 + 1) * 128, :], sem="q22")
    p.w("tensor", "q22")
    for su in range(NSUB):
        p.w("sync", "sv")
        hp0 = hpost_d[0:1, su * 128:su * 128 + 1]
        p.do("sync", lambda e, hp0=hp0: e.dma_start(
            out=xq2fm[:, :],
            in_=bass.AP(hp0.tensor, hp0.offset,
                        [[GRID, 128], [128 * GRID, 32], [1, 128]])),
            inc=("q23", 16))
        p.w("tensor", "q23")
        p.w("tensor", "sv")
        for half in range(4):
            for sl in range(8):
                fc = half * 8 + sl
                p.do("tensor", lambda t, fc=fc, sl=sl: t.transpose(
                    pTb[:, sl * 128:sl * 128 + 128],
                    xq2fm[:, fc * 128:(fc + 1) * 128], idnb_s[:, :]),
                    inc=("sp", 1))
            p.w("vector", "sp")
            p.do("vector", lambda v, half=half: v.tensor_copy(
                htmb[:, half * 1024:(half + 1) * 1024], pTb[:, 0:1024]),
                inc=("sv", 1))
            p.w("tensor", "sv")
        p.w("vector", "sv")
        p.do("vector", lambda v: v.tensor_reduce(
            qcol[:, 4:5], htmb[:, :], AX.X, ALU.max,
            apply_absolute_value=True), inc=("sv", 1))
        p.w("vector", "sv")
        p.do("vector", lambda v: v.tensor_scalar(qcol[:, 4:5], qcol[:, 4:5],
                                                 1e-5, None, ALU.max),
             inc=("sv", 1))
        p.w("vector", "sv")
        p.do("vector", lambda v: v.reciprocal(qcol[:, 5:6], qcol[:, 4:5]),
             inc=("sv", 1))
        p.w("vector", "sv")
        p.do("vector", lambda v: v.tensor_scalar(qcol[:, 5:6], qcol[:, 5:6],
                                                 127.0, None, ALU.mult),
             inc=("sv", 1))
        p.do("vector", lambda v: v.tensor_scalar(qcol[:, 6:7], qcol[:, 4:5],
                                                 1.0 / 127.0, None, ALU.mult),
             inc=("sv", 1))
        p.w("vector", "sv")
        p.do("vector", lambda v: v.tensor_scalar(qv4[:, :], htmb[:, :],
                                                 qcol[:, 5:6], None, ALU.mult),
             inc=("sv", 1))
        p.w("vector", "sv")
        p.do("vector", lambda v: v.tensor_copy(qyi4[:, :], qv4[:, :]),
             inc=("sv", 1))
        p.w("vector", "sv")
        p.do("vector", lambda v: v.tensor_copy(qy4[:, :], qyi4[:, :]),
             inc=("sv", 1))
        p.w("vector", "sv")
        p.do("vector", lambda v: v.tensor_tensor(qr4[:, :], qv4[:, :], qy4[:, :],
                                                 ALU.subtract), inc=("sv", 1))
        p.w("vector", "sv")
        p.do("vector", lambda v: v.scalar_tensor_tensor(
            qt14[:, :], qr4[:, :], 0.5, qy4[:, :], ALU.is_gt, ALU.add),
            inc=("sv", 1))
        p.w("vector", "sv")
        p.do("vector", lambda v: v.scalar_tensor_tensor(
            qv4[:, :], qr4[:, :], -0.5, qt14[:, :], ALU.is_ge, ALU.add),
            inc=("sv", 1))
        p.w("vector", "sv")
        p.do("vector", lambda v: v.tensor_scalar(xq2tm[:, :], qv4[:, :], -1.0,
                                                 None, ALU.add), inc=("sv", 1))
        p.w("tensor", "sv")
        for half in range(4):
            for sl in range(8):
                fc = half * 8 + sl
                p.do("tensor", lambda t, fc=fc, sl=sl: t.transpose(
                    pTb[:, sl * 128:sl * 128 + 128],
                    xq2tm[:, fc * 128:(fc + 1) * 128], idnb_s[:, :]),
                    inc=("sp", 1))
            p.w("vector", "sp")
            p.do("vector", lambda v, half=half: v.tensor_copy(
                xq2fm[:, half * 1024:(half + 1) * 1024], pTb[:, 0:1024]),
                inc=("sv", 1))
            p.w("tensor", "sv")
        p.w("tensor", "sv")
        for fc in range(32):
            for q in range(2):
                p.do("tensor", lambda t, fc=fc, q=q: t.matmul(
                    pB1[:, q * 512:(q + 1) * 512],
                    xq2fm[:, fc * 128:(fc + 1) * 128],
                    w2qb[:, fc * D_MODEL + q * 512: fc * D_MODEL + (q + 1) * 512],
                    start=(fc == 0), stop=(fc == 31)), inc=("sp", 1))
        p.w("scalar", "sp")
        p.w("scalar", "q25")
        p.w("scalar", "sv")
        p.do("scalar", lambda sc: sc.activation(fotm[:, :], pB1[:, 0:1024],
                                                AF.Copy, scale=qcol[:, 6:7]),
             inc=("sa", 1))
        p.w("vector", "sa")
        ln_tm(fotm[:, :], fotm[:, :], D_MODEL)
        p.w("sync", "sv")
        dma("sync", x2ld3[:, :], x2_d[su * 128:(su + 1) * 128, :], sem="q24")
        p.w("vector", "q24")
        p.do("vector", lambda v: v.tensor_tensor(fotm[:, :], fotm[:, :],
                                                 x2ld3[:, :], ALU.add),
             inc=("sv", 1))
        sv_out = p.n["sv"]
        p.w("sync", "sv", sv_out)
        if su == 0:
            dma("sync", out_d[0:64, :], fotm[64:128, :], sem="q25")
        elif su == NSUB - 1:
            dma("sync", out_d[SEG - 64:SEG, :], fotm[0:64, :], sem="q25")
        else:
            dma("sync", out_d[su * 128 - 64:(su + 1) * 128 - 64, :],
                fotm[:, :], sem="q25")
        p.w("tensor", "sv")

    p.w("sync", "q25")

    with nc.Block() as block:
        p.emit(block)
    ctx.close()
    return nc



def kernel(**inputs):
    x = np.ascontiguousarray(np.asarray(inputs["L0_attn_x"], np.float32))
    shards = _host_shards(x)
    consts = _host_consts(inputs)
    nc = build_nc()
    in_maps = []
    for core in range(8):
        m = dict(consts)
        m.update(shards[core])
        in_maps.append(m)
    res = run_bass_kernel_spmd(nc, in_maps, core_ids=list(range(8)))
    out = np.zeros((2, SEQ, D_MODEL), np.float32)
    for core in range(8):
        b, j = core // NSEG, core % NSEG
        out[b, j * SEG:(j + 1) * SEG] = res.results[core]["out"]
    return out


# version 06 (final)
# speedup vs baseline: 1.3007x; 1.0340x over previous
import numpy as np
import ml_dtypes
import concourse.bass as bass
import concourse.mybir as mybir
from concourse.bass_utils import run_bass_kernel_spmd
from contextlib import ExitStack

F32 = mybir.dt.float32
BF16 = mybir.dt.bfloat16
I32 = mybir.dt.int32
U8 = mybir.dt.uint8
AF = mybir.ActivationFunctionType
ALU = mybir.AluOpType
AX = mybir.AxisListType

D_MODEL = 1024
D_STATE = 16
HEADDIM = 64
CHUNK = 64
D_INNER = 2048
NHEADS = 32
CONV_DIM = 2080
D_IN_PROJ = 4160
D_HIDDEN = 4096
SEQ = 8192
SEG = 2048
NSEG = 4
GRID = 2176
NCH = 34
NSUB = 17
KC = 8
EPS = 1e-5
SLOT = 192
SCW = 704

ENGINES = ("sync", "vector", "scalar", "tensor", "gpsimd")
PA_TILES = [(0, 512), (512, 512), (1024, 512), (1536, 512), (2048, 128)]


def bcast_free(ap_col, part, dims):
    return bass.AP(ap_col.tensor, ap_col.offset,
                   [[ap_col.ap[0][0], part]] + [list(d) for d in dims])



def _host_shards(x):
    shards = []
    for core in range(8):
        b, j = core // NSEG, core % NSEG
        start = j * SEG
        lo = start - CHUNK
        g = np.zeros((GRID, D_MODEL), np.float32)
        s0, s1 = max(lo, 0), min(start + SEG + CHUNK, SEQ)
        g[s0 - lo:s1 - lo] = x[b, s0:s1]
        h3 = np.zeros((4, D_MODEL), np.float32)
        h0, h1 = max(lo - 3, 0), max(lo, 0)
        h3[3 - (h1 - h0):3] = x[b, h0:h1]
        maskrow = np.ones((NSUB * 128,), np.float32)
        if j == 0:
            maskrow[CHUNK - 1] = 0.0
        if j == NSEG - 1:
            maskrow[SEG + CHUNK] = 0.0
        mask_tm = np.ascontiguousarray(maskrow.reshape(NSUB, 128).T)
        pm = np.zeros((128, 4), np.float32)
        for k in (1, 2, 3):
            pm[:, k - 1] = 1.0 if j >= k else 0.0
        pm[:, 3] = 1.0 if j >= 1 else 0.0
        slot = np.zeros((4, 1), np.int32)
        for k in (1, 2, 3):
            slot[k - 1, 0] = SLOT * max(j - k, 0)
        shards.append(dict(xg=g, xh3=h3, maskrow=mask_tm, pmc=pm, slotoff=slot))
    return shards


def _host_consts(inputs):
    c = {}
    c["idn"] = np.eye(128, dtype=np.float32)
    cp = np.zeros((128, 4), np.float32)
    cp[:, 0] = 1e-5; cp[:, 1] = 1.0; cp[:, 2] = -0.5; cp[:, 3] = 0.0
    c["cpool"] = cp
    e = np.zeros((32, 64), np.float32); e[0, :] = 1.0
    c["e064"] = e
    e2 = np.zeros((32, 128), np.float32); e2[0, :] = 1.0
    c["e0128"] = e2
    c["onescol"] = np.ones((128, 1), np.float32)
    hsel = np.zeros((32, 128), np.float32)
    for h in range(32):
        g = h // 11
        hsel[h, 32 * g:32 * g + 16] = 1.0
    c["hsel"] = hsel
    gmask = np.zeros((32, SCW), np.float32)
    for h in range(32):
        i = h - 11 * (h // 11)
        gmask[h, 64 * i:64 * i + 64] = 1.0
    c["gmask"] = gmask
    negio = np.zeros((32, 2048), np.float32)
    for h in range(32):
        negio[h, 64 * h:64 * h + 64] = -1.0
    c["negio"] = negio
    t = np.zeros((64, 64), np.float32)
    for s in range(64):
        t[s, s:] = 1.0
    c["tri"] = t
    dtb = np.asarray(inputs["L0_attn_dt_bias"], np.float32)
    c["dtbrep"] = np.ascontiguousarray(np.tile(dtb[None, :], (128, 1)))
    Dsk = np.asarray(inputs["L0_attn_D"], np.float32)
    c["dskrep"] = np.ascontiguousarray(
        np.tile(np.repeat(Dsk, HEADDIM)[None, :], (128, 1)))
    nw = np.asarray(inputs["L0_attn_norm_w"], np.float32)
    c["nwrep"] = np.ascontiguousarray(np.tile(nw[None, :], (128, 1)))
    c["alog"] = np.ascontiguousarray(
        np.asarray(inputs["L0_attn_A_log"], np.float32).reshape(32, 1))
    cw = np.asarray(inputs["L0_attn_conv_w"], np.float32)
    cwp = np.zeros((17 * 128, 4), np.float32)
    cwp[:CONV_DIM] = cw
    c["convw"] = np.ascontiguousarray(
        cwp.reshape(17, 128, 4).transpose(1, 0, 2).reshape(128, 68))
    cb = np.asarray(inputs["L0_attn_conv_b"], np.float32)
    cbp = np.zeros((17 * 128,), np.float32)
    cbp[:CONV_DIM] = cb
    c["convb"] = np.ascontiguousarray(cbp.reshape(17, 128).T)
    fcw = np.asarray(inputs["L0_ffn_conv_w"], np.float32).reshape(D_HIDDEN, 3)
    c["fcw"] = np.ascontiguousarray(
        fcw.reshape(32, 128, 3).transpose(1, 0, 2).reshape(128, 96))
    fcb = np.asarray(inputs["L0_ffn_conv_b"], np.float32)
    c["fcb"] = np.ascontiguousarray(fcb.reshape(32, 128).T)
    c["wint"] = np.ascontiguousarray(np.asarray(inputs["L0_attn_W_in"], np.float32).T)
    c["wout"] = np.ascontiguousarray(np.asarray(inputs["L0_attn_W_out"], np.float32).T)
    c["w1t"] = np.ascontiguousarray(np.asarray(inputs["L0_ffn_w1"], np.float32).T)
    c["w2t"] = np.ascontiguousarray(np.asarray(inputs["L0_ffn_w2"], np.float32).T)
    return c



class P:
    def __init__(self, nc, ctx):
        self.nc = nc
        self.ops = {e: [] for e in ENGINES}
        self.h = {}
        self.n = {}
        for name in ("qs", "qg", "sv", "sa", "sp", "sg", "cc", "st",
                     "qw", "qe", "q1", "q2", "q3", "q4", "q5", "q6", "q7",
                     "q8", "q9", "q10", "q11", "q12", "q13", "q14", "q15",
                     "q16", "q17", "q18", "q19", "q20", "q21", "q22", "q23",
                     "q24", "q25", "qw2"):
            self.h[name] = ctx.enter_context(nc.semaphore(name))
            self.n[name] = 0
        self.env = {}

    def do(self, eng, f, inc=None):
        if inc is not None:
            sem, amt = inc
            self.n[sem] += amt
            handle = self.h[sem]

            def g(e, f=f, handle=handle, amt=amt):
                f(e).then_inc(handle, amt)
        else:
            def g(e, f=f):
                f(e)
        self.ops[eng].append(g)

    def w(self, eng, sem, val=None):
        v = self.n[sem] if val is None else val
        h = self.h[sem]
        self.ops[eng].append(lambda e, h=h, v=v: e.wait_ge(h, v))

    def barrier(self, engs=ENGINES, sems=None):
        if sems is None:
            sems = [k for k in self.n if k != "st"]
        for eng in engs:
            for s in sems:
                if self.n[s] > 0:
                    self.w(eng, s)

    def emit(self, block):
        for e in ENGINES:
            fs = self.ops[e]
            getattr(block, e)(lambda eng, fs=fs: [f(eng) for f in fs])



def build_nc():
    nc = bass.Bass("TRN2", target_bir_lowering=False, dynamic_dma_scratch_size=8192)
    ctx = ExitStack()

    def din(name, shape, dtype=F32):
        return nc.dram_tensor(name, list(shape), dtype, kind="ExternalInput").ap()

    def dint(name, shape, dtype=F32):
        return nc.dram_tensor(name, list(shape), dtype, kind="Internal").ap()

    def sb(name, shape, dtype=F32):
        return ctx.enter_context(nc.sbuf_tensor(name, shape, dtype))

    xg_d = din("xg", [GRID, D_MODEL])
    xh3_d = din("xh3", [4, D_MODEL])
    maskrow_d = din("maskrow", [128, NSUB])
    pmc_d = din("pmc", [128, 4])
    slotoff_d = din("slotoff", [4, 1], I32)
    idn_d = din("idn", [128, 128])
    cpool_d = din("cpool", [128, 4])
    e064_d = din("e064", [32, 64])
    e0128_d = din("e0128", [32, 128])
    onescol_d = din("onescol", [128, 1])
    hsel_d = din("hsel", [32, 128])
    gmask_d = din("gmask", [32, SCW])
    negio_d = din("negio", [32, 2048])
    tri_d = din("tri", [64, 64])
    dtbrep_d = din("dtbrep", [128, 32])
    dskrep_d = din("dskrep", [128, D_INNER])
    nwrep_d = din("nwrep", [128, D_INNER])
    alog_d = din("alog", [32, 1])
    convw_d = din("convw", [128, 68])
    convb_d = din("convb", [128, 17])
    fcw_d = din("fcw", [128, 96])
    fcb_d = din("fcb", [128, 32])
    wint_d = din("wint", [D_MODEL, D_IN_PROJ])
    wout_d = din("wout", [D_INNER, D_MODEL])
    w1t_d = din("w1t", [D_MODEL, D_HIDDEN])
    w2t_d = din("w2t", [D_HIDDEN, D_MODEL])
    out_d = nc.dram_tensor("out", [SEG, D_MODEL], F32, kind="ExternalOutput").ap()

    zsp_d = dint("zsp", [GRID, D_INNER], BF16)
    x2_d = dint("x2d", [GRID, D_MODEL])
    hpost_d = dint("hpostd", [D_HIDDEN, GRID], BF16)
    acs_d = dint("acsd", [NHEADS, GRID])
    alast_d = dint("alastd", [NCH, NHEADS])
    w1q_d = dint("w1qd", [D_MODEL, D_HIDDEN], BF16)
    w2q_d = dint("w2qd", [D_HIDDEN, D_MODEL], BF16)
    agin_d = dint("agin", [SLOT, SCW])
    agout_d = dint("agout", [4 * SLOT, SCW])
    snap_d = dint("snapd", [33 * 80, SCW], BF16)

    arenaA = ctx.enter_context(nc.sbuf_tensor("arenaA", [128, 69632], U8))
    arenaB = ctx.enter_context(nc.sbuf_tensor("arenaB", [128, 66560], U8))
    arenaC = ctx.enter_context(nc.sbuf_tensor("arenaC", [128, 16384], U8))
    arenaD = ctx.enter_context(nc.sbuf_tensor("arenaD", [128, 8192], U8))

    def overlay(base, name, shape, dtype, off=0):
        h = nc.virtual_sbuf_tensor(name, list(shape), dtype)
        nc._tensor(h.name, h.shape, h.dtype, type="SB")
        mloc = nc.lookup_mloc(h)
        mloc.allocated = True
        mloc.addr = nc.lookup_mloc(base).addr + off
        return h

    xhtm = overlay(arenaA, "xhtm", [128, NSUB * D_INNER], BF16)
    xq1fm = overlay(arenaA, "xq1fm", [128, KC * GRID], BF16)
    qv4 = overlay(arenaA, "qv4", [128, 4096], F32, 0)
    qyi4 = overlay(arenaA, "qyi4", [128, 4096], I32, 16384)
    qt14 = overlay(arenaA, "qt14", [128, 4096], F32, 16384)
    qy4 = overlay(arenaA, "qy4", [128, 4096], F32, 32768)
    qr4 = overlay(arenaA, "qr4", [128, 4096], F32, 49152)
    qv1 = overlay(arenaA, "qv1", [128, 1024], F32, 34816)
    qyi1 = overlay(arenaA, "qyi1", [128, 1024], I32, 38912)
    qt11 = overlay(arenaA, "qt11", [128, 1024], F32, 38912)
    qy1 = overlay(arenaA, "qy1", [128, 1024], F32, 43008)
    qr1 = overlay(arenaA, "qr1", [128, 1024], F32, 47104)
    winb = overlay(arenaB, "winb", [128, KC * D_IN_PROJ], BF16)
    snapbuf = overlay(arenaB, "snapbuf", [128, 2 * SCW], BF16, 0)
    snapld = overlay(arenaB, "snapld", [128, 2 * SCW], F32, 2816)
    woutb = overlay(arenaB, "woutb", [128, 16 * D_MODEL], BF16)
    w1qb = overlay(arenaB, "w1qb", [128, KC * D_HIDDEN], BF16)
    w2qb = overlay(arenaB, "w2qb", [128, 32 * D_MODEL], BF16)
    sz = overlay(arenaB, "sz", [128, D_INNER], F32, 8448)
    ytmp = overlay(arenaB, "ytmp", [128, D_INNER], F32, 16640)
    ytmp2 = overlay(arenaB, "ytmp2", [128, D_INNER], F32, 24832)
    flatbuf = overlay(arenaB, "flatbuf", [32, 2048], F32, 33024)
    wtile = overlay(arenaC, "wtile", [128, 4096], F32)
    wtileB = overlay(arenaA, "wtileB", [128, 4096], F32, 32768)
    htmb = overlay(arenaC, "htmb", [128, D_HIDDEN], BF16)
    wqtile = overlay(arenaD, "wqtile", [128, 4096], BF16)
    xq2tm = overlay(arenaD, "xq2tm", [128, D_HIDDEN], BF16)
    xnf = overlay(arenaC, "xnf", [128, KC * 512], BF16)
    ebuf = overlay(arenaC, "ebuf", [128, D_INNER], F32, 0)
    dmin = overlay(arenaC, "dmin", [128, D_INNER], F32, 8192)
    hconv = overlay(arenaA, "hconv", [128, GRID], F32, 34816)
    hconvB = overlay(arenaA, "hconvB", [128, GRID], F32, 56576)
    h1bufB = overlay(arenaA, "h1bufB", [128, GRID], BF16, 52224)
    h1buf = overlay(arenaA, "h1buf", [128, GRID], BF16, 43520)
    hpostb = overlay(arenaA, "hpostb", [128, GRID], BF16, 47872)
    xtm_t = overlay(arenaD, "xtm_t", [128, D_MODEL], F32, 0)
    xntm = overlay(arenaD, "xntm", [128, D_MODEL], F32, 4096)
    gbuf = overlay(arenaD, "gbuf", [128, D_INNER], BF16, 0)
    xdt = overlay(arenaD, "xdt", [128, D_INNER], BF16, 4096)
    slotb = overlay(arenaC, "slotb", [128, 3 * SCW], F32, 0)
    p30b = overlay(arenaC, "p30b", [128, SCW], F32, 8448)
    tcomb = overlay(arenaC, "tcomb", [128, SCW], F32, 11264)
    ynf = overlay(arenaC, "ynf", [128, D_INNER], BF16, 0)
    motm = overlay(arenaD, "motm", [128, D_MODEL], F32, 0)
    xgtm = overlay(arenaD, "xgtm", [128, D_MODEL], F32, 4096)
    x2n_t = overlay(arenaC, "x2n_t", [128, D_MODEL], F32, 0)
    xq1tm = overlay(arenaC, "xq1tm", [128, D_MODEL], BF16, 4096)
    acspad = overlay(arenaC, "acspad", [32, 64 + GRID], F32, 0)
    srun = overlay(arenaC, "srun", [128, SCW], F32, 0)
    xh3tm = overlay(arenaA, "xh3tm", [4, D_MODEL], F32, 65536)
    zchunkA = overlay(arenaC, "zchunkA", [128, D_INNER], BF16, 8192)
    convt = overlay(arenaC, "convt", [128, 512], F32, 12288)
    xbca = overlay(arenaC, "xbca", [128, 512], BF16, 14336)
    xbcaB = overlay(arenaC, "xbcaB", [128, 512], BF16, 15360)
    zchunk2 = overlay(arenaB, "zchunk2", [128, D_INNER], BF16, 24832)
    ucomb = overlay(arenaB, "ucomb", [128, SCW], F32, 41216)
    x2ld1 = overlay(arenaD, "x2ld1", [128, D_MODEL], F32, 0)
    x2ld3 = overlay(arenaC, "x2ld3", [128, D_MODEL], F32, 8192)
    fotm = overlay(arenaC, "fotm", [128, D_MODEL], F32, 12288)
    xq2fm = overlay(arenaA, "xq2fm", [128, D_HIDDEN], BF16, 49152)

    idn_s = sb("idn_s", [128, 128])
    cpool_s = sb("cpool_s", [128, 4])
    idnb_s = sb("idnb_s", [128, 128], BF16)
    e064_s = sb("e064_s", [32, 64])
    e0128_s = sb("e0128_s", [32, 128])
    onescol_s = sb("onescol_s", [128, 1])
    hsel_s = sb("hsel_s", [32, 128], BF16)
    gmask_s = sb("gmask_s", [32, SCW], BF16)
    negio_s = sb("negio_s", [32, 2048])
    eacsb = sb("eacsb", [128, 32])
    tri_s = sb("tri_s", [64, 64])
    tri64hi = sb("tri64hi", [128, 64])
    dtbrep_s = sb("dtbrep_s", [128, 32])
    dskrep_s = sb("dskrep_s", [128, D_INNER], BF16)
    nwrep_s = sb("nwrep_s", [128, D_INNER], BF16)
    alog_s = sb("alog_s", [32, 1])
    nega_s = sb("nega_s", [32, 1])
    convw_s = sb("convw_s", [128, 68])
    convb_s = sb("convb_s", [128, 17])
    fcw_s = sb("fcw_s", [128, 96])
    fcb_s = sb("fcb_s", [128, 32])
    maskrow_s = sb("maskrow_s", [128, NSUB])
    pmc_s = sb("pmc_s", [128, 4])
    slotoff_s = sb("slotoff_s", [4, 1], I32)
    dttm = sb("dttm", [128, NSUB * 32])
    acsfm = sb("acsfm", [32, GRID])
    acstm = sb("acstm", [128, NSUB * 32])
    alast_s = sb("alast_s", [32, NCH])
    expal_s = sb("expal_s", [32, NCH])
    gamlog = sb("gamlog", [32, NCH])
    expgam_s = sb("expgam_s", [32, NCH])
    eg_s = sb("eg_s", [32, 2])
    bfm = sb("bfm", [16, GRID], BF16)
    cfm4 = sb("cfm4", [128, GRID], BF16)
    btm = sb("btm", [128, NSUB * 16])
    xh3buf = sb("xh3buf", [128, 17 * 3], BF16)
    xbchal = sb("xbchal", [128, 17 * 3], BF16)
    xnhf = sb("xnhf", [128, KC * 4], BF16)
    alrow = sb("alrow", [32, 2 * 32])
    decaytm = sb("decaytm", [128, 2 * 32])
    bd = sb("bd", [128, 512], BF16)
    m1m = sb("m1m", [128, 64], BF16)
    sin = sb("sin", [128, SCW])
    stbuf = sb("stbuf", [128, SCW], BF16)
    gslot = sb("gslot", [32, 3 * 2])
    gmg = sb("gmg", [32, SCW], BF16)
    rmscol = sb("rmscol", [128, 2])
    stat6 = sb("stat6", [128, 24])
    stat2 = sb("stat2", [128, 2])
    lncol = sb("lncol", [128, 8])
    xbcbuf = sb("xbcbuf", [128, 2 * 516], BF16)
    dtraw = sb("dtraw", [128, 2 * 32])
    qcol = sb("qcol", [128, 8])
    wabs = sb("wabs", [128, 1])
    wsum1 = sb("wsum1", [128, 1])
    scol = sb("scol", [128, 4])
    valbuf = sb("valbuf", [32, 2])
    psc = sb("psc", [1, 8])

    pB1 = ctx.enter_context(nc.psum_tensor("pB1", [128, 2048], F32))
    pB2 = ctx.enter_context(nc.psum_tensor("pB2", [128, 2048], F32))

    def overlay_ps(base, name, shape, dtype, off_bytes):
        h = nc.virtual_psum_tensor(name, list(shape), dtype)
        nc._tensor(h.name, h.shape, h.dtype, type="PSUM")
        mloc = nc.lookup_mloc(h)
        mloc.allocated = True
        mloc.addr = nc.lookup_mloc(base).addr + off_bytes
        return h

    pTb = overlay_ps(pB2, "pTb", [128, 1024], BF16, 6144)

    p = P(nc, ctx)

    def dma(queue, out, in_, sem=None):
        q = sem if sem else ("qg" if queue == "gpsimd" else "qs")
        p.do(queue, lambda e, out=out, in_=in_: e.dma_start(out=out, in_=in_),
             inc=(q, 16))

    def ln_tm(xap, outap, width, part=128):
        nq = width // 512
        for q in range(nq):
            p.do("vector", lambda v, q=q, xap=xap, part=part: v.bn_stats(
                stat6[0:part, 6 * q:6 * q + 6], xap[:, 512 * q:512 * (q + 1)]),
                inc=("sv", 1))
        p.w("vector", "sv")
        p.do("vector", lambda v, nq=nq, part=part: v.bn_aggr(
            stat2[0:part, 0:2], stat6[0:part, 0:6 * nq]), inc=("sv", 1))
        p.w("scalar", "sv")
        p.do("scalar", lambda sc, part=part: sc.activation(
            lncol[0:part, 1:2], stat2[0:part, 1:2], AF.Ln, bias=EPS),
            inc=("sa", 1))
        p.w("scalar", "sa")
        p.do("scalar", lambda sc, part=part: sc.activation(
            lncol[0:part, 1:2], lncol[0:part, 1:2], AF.Exp, scale=-0.5),
            inc=("sa", 1))
        p.w("vector", "sa")
        p.do("vector", lambda v, xap=xap, outap=outap, part=part:
             v.tensor_scalar(outap, xap, stat2[0:part, 0:1],
                             lncol[0:part, 1:2], ALU.subtract, ALU.mult),
             inc=("sv", 1))

    nc.const_aps.register(F32, 1e-5, cpool_s[:, 0:1])
    nc.const_aps.register(F32, 1.0, cpool_s[:, 1:2])
    nc.const_aps.register(F32, -0.5, cpool_s[:, 2:3])
    nc.const_aps.register(F32, 0.0, cpool_s[:, 3:4])
    for dst, src in [(idn_s, idn_d), (cpool_s, cpool_d),
                     (e064_s, e064_d), (e0128_s, e0128_d),
                     (onescol_s, onescol_d), (negio_s, negio_d),
                     (tri_s, tri_d), (dtbrep_s, dtbrep_d),
                     (alog_s, alog_d), (convw_s, convw_d),
                     (convb_s, convb_d), (fcw_s, fcw_d), (fcb_s, fcb_d),
                     (pmc_s, pmc_d), (slotoff_s, slotoff_d),
                     (maskrow_s, maskrow_d), (xh3tm, xh3_d)]:
        dma("sync", dst[:, :], src[:, :])
    dma("gpsimd", dskrep_s[:, :], dskrep_d[:, :])
    dma("gpsimd", nwrep_s[:, :], nwrep_d[:, :])
    dma("gpsimd", hsel_s[:, :], hsel_d[:, :])
    dma("gpsimd", gmask_s[:, :], gmask_d[:, :])
    for kc in range(KC):
        dma("gpsimd", winb[:, kc * D_IN_PROJ:(kc + 1) * D_IN_PROJ],
            wint_d[kc * 128:(kc + 1) * 128, :])
    p.w("vector", "qs")
    p.do("vector", lambda v: v.tensor_copy(idnb_s[:, :], idn_s[:, :]),
         inc=("sv", 1))
    p.do("gpsimd", lambda g: g.memset(xh3buf[:, :], 0.0), inc=("sg", 1))
    p.do("gpsimd", lambda g: g.memset(xbchal[:, :], 0.0), inc=("sg", 1))
    p.do("gpsimd", lambda g: g.memset(tri64hi[:, :], 0.0), inc=("sg", 1))
    p.w("gpsimd", "sg")
    p.do("gpsimd", lambda g: g.iota(
        tri64hi[:, :], [[1, 64]], base=64, channel_multiplier=-1,
        allow_small_or_imprecise_dtypes=True), inc=("sg", 1))
    p.w("vector", "sg")
    p.do("vector", lambda v: v.tensor_scalar(tri64hi[:, :], tri64hi[:, :], 0.0,
                                             None, ALU.is_ge), inc=("sv", 1))
    p.w("scalar", "qs")
    p.do("scalar", lambda sc: sc.activation(nega_s[:, :], alog_s[:, :], AF.Exp),
         inc=("sa", 1))
    p.w("vector", "sa")
    p.do("vector", lambda v: v.tensor_scalar(nega_s[:, :], nega_s[:, :], -1.0,
                                             None, ALU.mult), inc=("sv", 1))

    ln_tm(xh3tm[0:4, :], xh3tm[0:4, :], D_MODEL, part=4)
    p.w("tensor", "sv")
    for kc in range(KC):
        p.do("tensor", lambda t, kc=kc: t.transpose(
            pB1[:, kc * 4:(kc + 1) * 4], xh3tm[0:4, kc * 128:(kc + 1) * 128],
            idn_s[0:4, 0:4]), inc=("sp", 1))
    p.w("vector", "sp")
    p.do("vector", lambda v: v.tensor_copy(xnhf[:, :], pB1[:, 0:KC * 4]),
         inc=("sv", 1))
    sv_xnhf = p.n["sv"]
    p.w("tensor", "sv", sv_xnhf)
    p.w("tensor", "qg")
    for fc in range(17):
        f0 = D_INNER + fc * 128
        fw = min(128, D_IN_PROJ - 32 - f0)
        p.w("tensor", "sv")
        for kc in range(KC):
            p.do("tensor", lambda t, kc=kc, f0=f0, fw=fw, fc=fc: t.matmul(
                pB2[0:fw, 1792 + fc % 2 * 4: 1792 + fc % 2 * 4 + 3],
                winb[:, kc * D_IN_PROJ + f0: kc * D_IN_PROJ + f0 + fw],
                xnhf[:, kc * 4: kc * 4 + 3],
                start=(kc == 0), stop=(kc == KC - 1)), inc=("sp", 1))
        p.w("vector", "sp")
        p.do("vector", lambda v, fc=fc, fw=fw: v.tensor_copy(
            xh3buf[0:fw, fc * 3:fc * 3 + 3],
            pB2[0:fw, 1792 + fc % 2 * 4: 1792 + fc % 2 * 4 + 3]), inc=("sv", 1))

    def wquant(wsrc_d, wdst_d, nrc, ncols, si):
        p.w("gpsimd", "sv")
        p.do("gpsimd", lambda g: g.memset(wsum1[:, :], 0.0), inc=("sg", 1))
        sv_red = {}
        for rc in range(nrc):
            wt = wtile if rc % 2 == 0 else wtileB
            qq = "qw" if rc % 2 == 0 else "qw2"
            if rc >= 2:
                p.w("sync", "sv", sv_red[rc - 2])
            if rc == 0:
                p.w("sync", "sg")
            dma("sync", wt[:, 0:ncols], wsrc_d[rc * 128:(rc + 1) * 128, :],
                sem=qq)
            p.w("vector", qq)
            p.w("vector", "sv")
            if rc == 0:
                p.w("vector", "sg")
            p.do("vector", lambda v, n=ncols, wt=wt: v.tensor_reduce(
                wabs[:, :], wt[:, 0:n], AX.X, ALU.add,
                apply_absolute_value=True), inc=("sv", 1))
            sv_red[rc] = p.n["sv"]
            p.w("vector", "sv")
            p.do("vector", lambda v: v.tensor_tensor(
                wsum1[:, :], wsum1[:, :], wabs[:, :], ALU.add), inc=("sv", 1))
        p.w("tensor", "sv")
        p.do("tensor", lambda t: t.matmul(pB1[0:1, 0:1], onescol_s[:, :],
                                          wsum1[:, :], start=True, stop=True),
             inc=("sp", 1))
        p.w("vector", "sp")
        p.do("vector", lambda v, nrc=nrc, ncols=ncols: v.tensor_scalar(
            psc[0:1, 0:1], pB1[0:1, 0:1], 1.0 / (nrc * 128 * ncols), None,
            ALU.mult), inc=("sv", 1))
        p.w("vector", "sv")
        p.do("vector", lambda v: v.tensor_scalar(psc[0:1, 0:1], psc[0:1, 0:1],
                                                 1e-5, None, ALU.max),
             inc=("sv", 1))
        p.w("vector", "sv")
        p.do("vector", lambda v: v.reciprocal(psc[0:1, 1:2], psc[0:1, 0:1]),
             inc=("sv", 1))
        p.w("gpsimd", "sv")
        p.do("gpsimd", lambda g: g.memset(valbuf[:, :], 0.0), inc=("sg", 1))
        p.w("vector", "sg")
        p.do("vector", lambda v: v.tensor_copy(valbuf[0:1, 0:1], psc[0:1, 1:2]),
             inc=("sv", 1))
        p.w("vector", "sv")
        p.do("vector", lambda v: v.tensor_copy(valbuf[0:1, 1:2], psc[0:1, 0:1]),
             inc=("sv", 1))
        p.w("tensor", "sv")
        p.do("tensor", lambda t: t.matmul(pB1[0:128, 8:10], e0128_s[:, :],
                                          valbuf[:, :], start=True, stop=True),
             inc=("sp", 1))
        p.w("vector", "sp")
        p.do("vector", lambda v, si=si: v.tensor_copy(
            scol[:, si:si + 1], pB1[0:128, 8:9]), inc=("sv", 1))
        p.w("vector", "sv")
        p.do("vector", lambda v, si=si: v.tensor_copy(
            scol[:, si + 1:si + 2], pB1[0:128, 9:10]), inc=("sv", 1))
        sv_q = {}
        for rc in range(nrc):
            wt = wtile if rc % 2 == 0 else wtileB
            qq = "qw" if rc % 2 == 0 else "qw2"
            if rc >= 2:
                p.w("sync", "sv", sv_q[rc - 2])
            else:
                p.w("sync", "sv")
            dma("sync", wt[:, 0:ncols], wsrc_d[rc * 128:(rc + 1) * 128, :],
                sem=qq)
            p.w("vector", qq)
            p.do("vector", lambda v, n=ncols, si=si, wt=wt: v.tensor_scalar(
                qv4[:, 0:n], wt[:, 0:n], scol[:, si:si + 1], None, ALU.mult),
                inc=("sv", 1))
            sv_q[rc] = p.n["sv"]
            p.w("vector", "sv")
            p.do("vector", lambda v, n=ncols: v.tensor_scalar(
                qt14[:, 0:n], qv4[:, 0:n], 0.5, None, ALU.is_gt), inc=("sv", 1))
            p.w("vector", "sv")
            p.do("vector", lambda v, n=ncols: v.scalar_tensor_tensor(
                qr4[:, 0:n], qv4[:, 0:n], -0.5, qt14[:, 0:n], ALU.is_lt,
                ALU.subtract), inc=("sv", 1))
            p.w("vector", "sv")
            p.do("vector", lambda v, n=ncols, si=si: v.tensor_scalar(
                qv4[:, 0:n], qr4[:, 0:n], scol[:, si + 1:si + 2], -1.0,
                ALU.mult, ALU.mult), inc=("sv", 1))
            p.w("vector", "sv")
            p.w("vector", "qe")
            p.do("vector", lambda v, n=ncols: v.tensor_copy(
                wqtile[:, 0:n], qv4[:, 0:n]), inc=("sv", 1))
            sv_here = p.n["sv"]
            p.w("sync", "sv", sv_here)
            dma("sync", wdst_d[rc * 128:(rc + 1) * 128, :], wqtile[:, 0:ncols],
                sem="qe")

    wquant(w1t_d, w1q_d, 8, 4096, 0)
    wquant(w2t_d, w2q_d, 32, 1024, 2)

    p.barrier()
    for (t0, tw) in PA_TILES:
        nsub_t = tw // 128
        tpar = (t0 // 512) % 2
        for si in range(nsub_t):
            g0 = t0 + si * 128
            p.w("sync", "sv")
            dma("sync", xtm_t[:, :], xg_d[g0:g0 + 128, :], sem="q1")
            p.w("vector", "q1")
            ln_tm(xtm_t[:, :], xntm[:, :], D_MODEL)
            p.w("tensor", "sv")
            for kc in range(KC):
                p.do("tensor", lambda t, kc=kc: t.transpose(
                    pB1[:, kc * 128:(kc + 1) * 128],
                    xntm[:, kc * 128:(kc + 1) * 128], idn_s[:, :]),
                    inc=("sp", 1))
            p.w("vector", "sp")
            xnfb = xnf[:, si * 128:si * 128 + 1]
            xnfap = bass.AP(xnfb.tensor, xnfb.offset,
                            [[KC * 512, 128], [512, KC], [1, 128]])
            p.do("vector", lambda v, xnfap=xnfap: v.tensor_copy(
                xnfap, pB1[:, 0:KC * 128]), inc=("sv", 1))
            p.w("tensor", "sv")
        sv_xnf = p.n["sv"]

        sv_xbc = {}
        sv_xh = {}
        sa_silu = {}

        def emit_trans(fcd):
            xb_d = xbca if fcd % 2 == 0 else xbcaB
            p.w("tensor", "sa", sa_silu[fcd])
            if fcd >= 2:
                p.w("tensor", "sv", sv_xh[fcd - 2])
            else:
                p.w("tensor", "sv")
            for si in range(nsub_t):
                p.do("tensor", lambda t, si=si, xb_d=xb_d: t.transpose(
                    pTb[:, si * 128:si * 128 + 128],
                    xb_d[:, si * 128:(si + 1) * 128], idnb_s[:, :]),
                    inc=("sp", 1))
            p.w("vector", "sp")
            xhb = xhtm[:, (t0 // 128) * D_INNER + fcd * 128:
                       (t0 // 128) * D_INNER + fcd * 128 + 1]
            xhap = bass.AP(xhb.tensor, xhb.offset,
                           [[NSUB * D_INNER, 128], [D_INNER, nsub_t],
                            [1, 128]])
            p.do("vector", lambda v, xhap=xhap, nsub_t=nsub_t:
                 v.tensor_copy(xhap, pTb[:, 0:nsub_t * 128]),
                 inc=("sv", 1))
            sv_xh[fcd] = p.n["sv"]

        for fc in range(17):
            f0 = D_INNER + fc * 128
            fw = min(128, D_IN_PROJ - 32 - f0)
            pcol = 512 * (fc % 2)
            xb = xbca if fc % 2 == 0 else xbcaB
            if fc >= 2:
                p.w("tensor", "sv", sv_xbc[fc - 2])
            else:
                p.w("tensor", "sv")
            for kc in range(KC):
                p.do("tensor", lambda t, kc=kc, f0=f0, fw=fw, tw=tw,
                     pcol=pcol: t.matmul(
                         pB2[0:fw, pcol:pcol + tw],
                         winb[:, kc * D_IN_PROJ + f0: kc * D_IN_PROJ + f0 + fw],
                         xnf[:, kc * 512: kc * 512 + tw],
                         start=(kc == 0), stop=(kc == KC - 1)), inc=("sp", 1))
            if fc >= 1:
                emit_trans(fc - 1)
            b0 = 516 * tpar
            p.w("vector", "sp")
            if t0 == 0:
                p.do("vector", lambda v, b0=b0, fc=fc: v.tensor_copy(
                    xbcbuf[:, b0:b0 + 3], xh3buf[:, fc * 3:fc * 3 + 3]),
                    inc=("sv", 1))
            else:
                p.do("vector", lambda v, b0=b0, fc=fc: v.tensor_copy(
                    xbcbuf[:, b0:b0 + 3], xbchal[:, fc * 3:fc * 3 + 3]),
                    inc=("sv", 1))
            p.do("vector", lambda v, b0=b0, fw=fw, tw=tw, pcol=pcol:
                 v.tensor_copy(
                     xbcbuf[0:fw, b0 + 3:b0 + 3 + tw],
                     pB2[0:fw, pcol:pcol + tw]), inc=("sv", 1))
            sv_xbc[fc] = p.n["sv"]
            p.w("vector", "sv")
            p.do("vector", lambda v, b0=b0, tw=tw, fc=fc: v.tensor_copy(
                xbchal[:, fc * 3:fc * 3 + 3], xbcbuf[:, b0 + tw:b0 + tw + 3]),
                inc=("sv", 1))
            p.w("vector", "sv")
            p.w("vector", "sa")
            p.do("vector", lambda v, b0=b0, tw=tw, fc=fc: v.tensor_scalar(
                convt[:, 0:tw], xbcbuf[:, b0:b0 + tw],
                convw_s[:, fc * 4:fc * 4 + 1], None, ALU.mult), inc=("sv", 1))
            for k in (1, 2, 3):
                p.w("vector", "sv")
                p.do("vector", lambda v, b0=b0, tw=tw, fc=fc, k=k:
                     v.scalar_tensor_tensor(
                         convt[:, 0:tw], xbcbuf[:, b0 + k:b0 + k + tw],
                         convw_s[:, fc * 4 + k:fc * 4 + k + 1], convt[:, 0:tw],
                         ALU.mult, ALU.add), inc=("sv", 1))
            p.w("scalar", "sv")
            p.w("scalar", "q2")
            p.do("scalar", lambda sc, tw=tw, fc=fc, xb=xb: sc.activation(
                xb[:, 0:tw], convt[:, 0:tw], AF.Silu,
                bias=convb_s[:, fc:fc + 1]), inc=("sa", 1))
            sa_silu[fc] = p.n["sa"]
            if fc == 16:
                p.w("vector", "sa")
                p.do("vector", lambda v, t0=t0, tw=tw, xb=xb: v.tensor_copy(
                    bfm[:, t0:t0 + tw], xb[0:16, 0:tw]), inc=("sv", 1))
                p.w("sync", "sa")
                for cg in range(3):
                    dma("sync", cfm4[32 * cg:32 * cg + 16, t0:t0 + tw],
                        xb[16:32, 0:tw], sem="q2")
                p.w("tensor", "sa")
                p.w("tensor", "sv")
                for si in range(nsub_t):
                    p.do("tensor", lambda t, si=si, xb=xb: t.transpose(
                        pTb[:, si * 128:si * 128 + 32],
                        xb[0:32, si * 128:(si + 1) * 128], idnb_s[0:32, 0:32]),
                        inc=("sp", 1))
                p.w("vector", "sp")
                ptb0 = pTb[:, 0:1]
                ptbap = bass.AP(ptb0.tensor, ptb0.offset,
                                [[1024, 128], [128, nsub_t], [1, 16]])
                p.do("vector", lambda v, ptbap=ptbap, t0=t0, nsub_t=nsub_t:
                     v.tensor_copy(
                         btm[:, (t0 // 128) * 16:(t0 // 128 + nsub_t) * 16],
                         ptbap), inc=("sv", 1))

        for si in range(nsub_t):
            gsub = t0 // 128 + si
            p.w("tensor", "sv")
            for kc in range(KC):
                for zs in range(4):
                    p.do("tensor", lambda t, kc=kc, si=si, zs=zs: t.matmul(
                        pB1[:, zs * 512:(zs + 1) * 512],
                        xnf[:, kc * 512 + si * 128: kc * 512 + si * 128 + 128],
                        winb[:, kc * D_IN_PROJ + zs * 512:
                             kc * D_IN_PROJ + (zs + 1) * 512],
                        start=(kc == 0), stop=(kc == KC - 1)), inc=("sp", 1))
                p.do("tensor", lambda t, kc=kc, si=si: t.matmul(
                    pB2[:, 1024:1056],
                    xnf[:, kc * 512 + si * 128: kc * 512 + si * 128 + 128],
                    winb[:, kc * D_IN_PROJ + 4128: kc * D_IN_PROJ + 4160],
                    start=(kc == 0), stop=(kc == KC - 1)), inc=("sp", 1))
            p.w("vector", "sp")
            p.w("scalar", "sp")
            p.w("scalar", "q3")
            p.do("scalar", lambda sc: sc.copy(zchunkA[:, :], pB1[:, 0:2048]),
                 inc=("sa", 1))
            sa_z = p.n["sa"]
            p.w("sync", "sa", sa_z)
            dma("sync", zsp_d[gsub * 128:(gsub + 1) * 128, :], zchunkA[:, :],
                sem="q3")
            p.do("vector", lambda v: v.tensor_tensor(
                dtraw[:, 0:32], pB2[:, 1024:1056], dtbrep_s[:, :], ALU.add),
                inc=("sv", 1))
            p.w("scalar", "sv")
            p.do("scalar", lambda sc: sc.activation(dtraw[:, 32:64],
                                                    dtraw[:, 0:32], AF.Exp),
                 inc=("sa", 1))
            p.w("scalar", "sa")
            p.do("scalar", lambda sc, gsub=gsub: sc.activation(
                dttm[:, gsub * 32:(gsub + 1) * 32], dtraw[:, 32:64], AF.Ln,
                bias=1.0), inc=("sa", 1))
            p.w("tensor", "sa")

    p.barrier()
    for gsub in range(NSUB):
        sl = gsub % 8
        p.do("tensor", lambda t, gsub=gsub, sl=sl: t.transpose(
            pB1[0:32, sl * 128:sl * 128 + 128],
            dttm[:, gsub * 32:(gsub + 1) * 32], idn_s[:, :]), inc=("sp", 1))
        p.w("vector", "sp")
        p.do("vector", lambda v, gsub=gsub, sl=sl: v.tensor_copy(
            acspad[:, 64 + gsub * 128:64 + (gsub + 1) * 128],
            pB1[0:32, sl * 128:sl * 128 + 128]), inc=("sv", 1))
        p.w("tensor", "sv")

    p.w("vector", "sv")
    p.do("vector", lambda v: v.tensor_scalar(
        acspad[:, 64:64 + GRID], acspad[:, 64:64 + GRID], nega_s[0:32, 0:1],
        None, ALU.mult), inc=("sv", 1))
    p.w("vector", "sv")
    p.w("gpsimd", "sv")
    p.do("gpsimd", lambda g: g.memset(acspad[:, 0:64], 0.0), inc=("sg", 1))
    p.w("vector", "sg")
    p.w("vector", "sv")
    p.do("vector", lambda v: v.tensor_tensor_scan(
        acspad[:, 64:64 + GRID], acspad[:, 64:64 + GRID],
        acspad[:, 64:64 + GRID], 0.0, ALU.add, ALU.bypass), inc=("sv", 1))
    p.w("vector", "sv")
    off_ap = acspad[:, 63:64]
    p.do("vector", lambda v: v.tensor_tensor(
        acsfm[:, :].rearrange("p (c u) -> p c u", u=64),
        acspad[:, 64:64 + GRID].rearrange("p (c u) -> p c u", u=64),
        bcast_free(off_ap, 32, [[64, NCH], [0, 64]]), ALU.subtract),
        inc=("sv", 1))
    p.w("vector", "sv")
    al_ap = acsfm[:, 63:64]
    p.do("vector", lambda v: v.tensor_copy(
        alast_s[:, :], bcast_free(al_ap, 32, [[64, NCH]])), inc=("sv", 1))
    p.w("scalar", "sv")
    p.do("scalar", lambda sc: sc.activation(expal_s[:, :], alast_s[:, :],
                                            AF.Exp), inc=("sa", 1))
    p.w("gpsimd", "sa")
    p.do("gpsimd", lambda g: g.memset(gamlog[:, :], 0.0), inc=("sg", 1))
    p.w("vector", "sg")
    p.do("vector", lambda v: v.tensor_tensor_scan(
        gamlog[:, 2:34], alast_s[:, 1:33], alast_s[:, 1:33], 0.0,
        ALU.add, ALU.bypass), inc=("sv", 1))
    p.w("scalar", "sv")
    p.do("scalar", lambda sc: sc.activation(expgam_s[:, :], gamlog[:, :],
                                            AF.Exp), inc=("sa", 1))
    p.do("vector", lambda v: v.tensor_reduce(eg_s[:, 0:1], alast_s[:, 1:33],
                                             AX.X, ALU.add), inc=("sv", 1))
    p.w("vector", "sv")
    p.do("vector", lambda v: v.tensor_reduce(eg_s[:, 1:2], alast_s[:, 1:32],
                                             AX.X, ALU.add), inc=("sv", 1))
    p.w("scalar", "sv")
    p.do("scalar", lambda sc: sc.activation(eg_s[:, :], eg_s[:, :], AF.Exp),
         inc=("sa", 1))
    p.w("sync", "sv")
    p.w("sync", "sa")
    dma("sync", acs_d[:, :], acsfm[:, :], sem="q4")
    p.do("sync", lambda e: e.dma_start(out=alast_d[:, :].transpose([1, 0]),
                                       in_=alast_s[:, :],
                                       allow_slow_non_contiguous=True),
         inc=("q4", 16))
    p.w("tensor", "sv")
    for gsub in range(NSUB):
        sl = gsub % 8
        p.do("tensor", lambda t, gsub=gsub, sl=sl: t.transpose(
            pB1[:, sl * 128:sl * 128 + 32],
            acsfm[:, gsub * 128:(gsub + 1) * 128], idn_s[0:32, 0:32]),
            inc=("sp", 1))
        p.w("vector", "sp")
        p.do("vector", lambda v, gsub=gsub, sl=sl: v.tensor_copy(
            acstm[:, gsub * 32:(gsub + 1) * 32],
            pB1[:, sl * 128:sl * 128 + 32]), inc=("sv", 1))
        p.w("tensor", "sv")

    p.barrier()
    p.do("gpsimd", lambda g: g.memset(srun[:, :], 0.0), inc=("sg", 1))
    p.w("vector", "sg")
    p.do("vector", lambda v: v.tensor_copy(pB2[0:80, 1024:1024 + SCW],
                                           srun[0:80, :]), inc=("sv", 1))
    p.w("gpsimd", "sg")
    p.do("gpsimd", lambda g: g.memset(flatbuf[:, :], 0.0), inc=("sg", 1))
    p.w("gpsimd", "sg")
    p.do("gpsimd", lambda g: g.memset(alrow[:, :], 0.0), inc=("sg", 1))
    p.w("vector", "sg")
    p.w("sync", "sg")

    def emit_xdt(ci):
        par = ci % 2
        gsub = ci // 2
        xsl = xhtm[par * 64:par * 64 + 64, gsub * D_INNER:(gsub + 1) * D_INNER]
        dsl = dttm[par * 64:par * 64 + 64, gsub * 32:(gsub + 1) * 32]
        p.do("vector", lambda v, xsl=xsl, dsl=dsl, par=par: v.tensor_tensor(
            xdt[par * 64:par * 64 + 64, :].rearrange("p (h q) -> p h q", h=32),
            xsl.rearrange("p (h q) -> p h q", h=32),
            bcast_free(dsl[:, 0:1], 64, [[1, 32], [0, 64]]), ALU.mult),
            inc=("sv", 1))

    for ci in range(1, 33):
        buf = ci % 2
        par = ci % 2
        gsub = ci // 2
        qa = "q5" if buf == 0 else "q6"
        qn = "q7" if buf == 0 else "q8"
        p.w("sync", "sa")
        p.w("sync", "sp")
        dma("sync", alrow[0:1, buf * 32:buf * 32 + 32], alast_d[ci:ci + 1, :],
            sem=qa)
        emit_xdt(ci)
        p.w("tensor", qa)
        p.w("tensor", "sv")
        p0_ = par * 64
        p.do("tensor", lambda t, buf=buf, p0_=p0_: t.matmul(
            pB1[p0_:p0_ + 64, 1920:1952], e064_s[:, :],
            alrow[:, buf * 32:buf * 32 + 32], start=True, stop=True),
            inc=("sp", 1))
        p.w("vector", "sp")
        asl = acstm[p0_:p0_ + 64, gsub * 32:(gsub + 1) * 32]
        p.do("vector", lambda v, asl=asl, buf=buf, p0_=p0_: v.tensor_tensor(
            decaytm[p0_:p0_ + 64, buf * 32:buf * 32 + 32],
            pB1[p0_:p0_ + 64, 1920:1952], asl, ALU.subtract), inc=("sv", 1))
        p.w("scalar", "sv")
        p.do("scalar", lambda sc, buf=buf, p0_=p0_: sc.activation(
            decaytm[p0_:p0_ + 64, buf * 32:buf * 32 + 32],
            decaytm[p0_:p0_ + 64, buf * 32:buf * 32 + 32], AF.Exp),
            inc=("sa", 1))
        bsl = btm[p0_:p0_ + 64, gsub * 16:(gsub + 1) * 16]
        dcl = decaytm[p0_:p0_ + 64, buf * 32:buf * 32 + 32]
        p.w("vector", "sa")
        p.do("vector", lambda v, bsl=bsl, dcl=dcl, p0_=p0_: v.tensor_tensor(
            bd[p0_:p0_ + 64, :].rearrange("p (h n) -> p h n", h=32),
            bcast_free(dcl[:, 0:1], 64, [[1, 32], [0, 16]]),
            bcast_free(bsl[:, 0:1], 64, [[0, 32], [1, 16]]), ALU.mult),
            inc=("sv", 1))
        p.w("tensor", "sv")
        for h in range(NHEADS):
            hg = h // 11
            hi = h - 11 * hg
            p.do("tensor", lambda t, hg=hg, hi=hi, h=h, p0_=p0_: t.matmul(
                pB2[32 * hg:32 * hg + 16, 1024 + 64 * hi:1024 + 64 * hi + 64],
                bd[p0_:p0_ + 64, h * 16:(h + 1) * 16],
                xdt[p0_:p0_ + 64, h * 64:(h + 1) * 64],
                start=True, stop=True), inc=("sp", 1))
        p.w("vector", qn)
        p.do("vector", lambda v, buf=buf: v.tensor_copy(
            snapbuf[0:80, buf * SCW:buf * SCW + SCW], srun[0:80, :]),
            inc=("sv", 1))
        if ci == 32:
            p.w("vector", "sv")
            p.do("vector", lambda v: v.tensor_copy(tcomb[0:80, :],
                                                   srun[0:80, :]),
                 inc=("sv", 1))
        p.w("sync", "sv")
        dma("sync", snap_d[(ci - 1) * 80:ci * 80, :],
            snapbuf[0:80, buf * SCW:buf * SCW + SCW], sem=qn)
        p.do("vector", lambda v, ci=ci: v.tensor_scalar(
            gmg[:, :], gmask_s[:, :], expal_s[:, ci:ci + 1], None, ALU.mult),
            inc=("sv", 1))
        p.w("tensor", "sv")
        p.do("tensor", lambda t: t.matmul(pB1[0:80, 1024:1536],
                                          hsel_s[:, 0:80], gmg[:, 0:512],
                                          start=True, stop=True),
             inc=("sp", 1))
        p.do("tensor", lambda t: t.matmul(pB1[0:80, 1536:1536 + SCW - 512],
                                          hsel_s[:, 0:80], gmg[:, 512:SCW],
                                          start=True, stop=True),
             inc=("sp", 1))
        p.w("vector", "sp")
        p.do("vector", lambda v: v.tensor_tensor(srun[0:80, :], srun[0:80, :],
                                                 pB1[0:80, 1024:1024 + SCW],
                                                 ALU.mult), inc=("sv", 1))
        p.w("vector", "sv")
        p.do("vector", lambda v: v.tensor_tensor(srun[0:80, :], srun[0:80, :],
                                                 pB2[0:80, 1024:1024 + SCW],
                                                 ALU.add), inc=("sv", 1))
        p.w("tensor", "sv")
    p.w("vector", "sv")
    p.w("vector", "q8")
    p.do("vector", lambda v: v.tensor_copy(snapbuf[0:80, SCW:2 * SCW],
                                           srun[0:80, :]), inc=("sv", 1))
    p.do("gpsimd", lambda g: g.memset(ucomb[0:32, :], 0.0), inc=("sg", 1))
    sv_ag = p.n["sv"]
    p.w("sync", "sv", sv_ag)
    p.w("sync", "sg")
    dma("sync", snap_d[32 * 80:33 * 80, :], snapbuf[0:80, SCW:2 * SCW],
        sem="q8")
    dma("sync", agin_d[0:80, :], srun[0:80, :], sem="q9")
    dma("sync", agin_d[80:160, :], tcomb[0:80, :], sem="q9")
    dma("sync", agin_d[160:192, :], ucomb[0:32, :], sem="q9")
    p.w("gpsimd", "q9")
    dma("gpsimd", agin_d[160:192, 0:2], eg_s[:, :], sem="st")
    p.w("gpsimd", "st")
    p.do("gpsimd", lambda g: g.collective_compute(
        "AllGather", ALU.bypass, replica_groups=[[0, 1, 2, 3], [4, 5, 6, 7]],
        ins=[agin_d[:, :].opt()], outs=[agout_d[:, :].opt()]), inc=("cc", 1))

    def f_load_offsets(e):
        for k in (1, 2, 3):
            p.env[f"so{k}"] = e.value_load(slotoff_s[k - 1:k, 0:1])
    p.w("sync", "cc")
    p.ops["sync"].append(f_load_offsets)
    for k in (1, 2, 3):
        p.do("sync", lambda e, k=k: e.dma_start(
            out=slotb[0:80, (k - 1) * SCW:k * SCW],
            in_=agout_d[bass.ds(p.env[f"so{k}"], 80), :]), inc=("q10", 16))
        p.do("sync", lambda e, k=k: e.dma_start(
            out=gslot[:, (k - 1) * 2:k * 2],
            in_=agout_d[bass.ds(p.env[f"so{k}"] + 160, 32), 0:2]),
            inc=("q10", 16))
    p.do("sync", lambda e: e.dma_start(
        out=p30b[0:80, :], in_=agout_d[bass.ds(p.env["so1"] + 80, 80), :]),
        inc=("q10", 16))

    def grep(gcol_ap):
        p.w("vector", "sp")
        p.do("vector", lambda v, g=gcol_ap: v.tensor_scalar(
            gmg[:, :], gmask_s[:, :], g, None, ALU.mult), inc=("sv", 1))
        p.w("tensor", "sv")
        p.do("tensor", lambda t: t.matmul(
            pB1[0:80, 1024:1536], hsel_s[:, 0:80], gmg[:, 0:512],
            start=True, stop=True), inc=("sp", 1))
        p.do("tensor", lambda t: t.matmul(
            pB1[0:80, 1536:1536 + SCW - 512], hsel_s[:, 0:80], gmg[:, 512:SCW],
            start=True, stop=True), inc=("sp", 1))

    p.w("vector", "q10")
    p.do("vector", lambda v: v.tensor_scalar(
        tcomb[0:80, :], slotb[0:80, 2 * SCW:3 * SCW], pmc_s[0:80, 2:3], None,
        ALU.mult), inc=("sv", 1))
    grep(gslot[:, 2:3])
    p.w("vector", "sp")
    p.w("vector", "sv")
    p.do("vector", lambda v: v.tensor_tensor(ucomb[0:80, :], tcomb[0:80, :],
                                             pB1[0:80, 1024:1024 + SCW],
                                             ALU.mult), inc=("sv", 1))
    p.w("vector", "sv")
    p.do("vector", lambda v: v.tensor_tensor(ucomb[0:80, :], ucomb[0:80, :],
                                             slotb[0:80, SCW:2 * SCW],
                                             ALU.add), inc=("sv", 1))
    p.w("vector", "sv")
    p.do("vector", lambda v: v.tensor_scalar(tcomb[0:80, :], ucomb[0:80, :],
                                             pmc_s[0:80, 1:2], None, ALU.mult),
         inc=("sv", 1))
    grep(gslot[:, 0:1])
    p.w("vector", "sp")
    p.w("vector", "sv")
    p.do("vector", lambda v: v.tensor_tensor(ucomb[0:80, :], tcomb[0:80, :],
                                             pB1[0:80, 1024:1024 + SCW],
                                             ALU.mult), inc=("sv", 1))
    p.w("vector", "sv")
    p.do("vector", lambda v: v.tensor_tensor(ucomb[0:80, :], ucomb[0:80, :],
                                             slotb[0:80, 0:SCW], ALU.add),
         inc=("sv", 1))
    p.w("vector", "sv")
    p.do("vector", lambda v: v.tensor_scalar(sin[0:80, :], ucomb[0:80, :],
                                             pmc_s[0:80, 0:1], None, ALU.mult),
         inc=("sv", 1))
    grep(gslot[:, 1:2])
    p.w("vector", "sp")
    p.w("vector", "sv")
    p.do("vector", lambda v: v.tensor_tensor(ucomb[0:80, :], tcomb[0:80, :],
                                             pB1[0:80, 1024:1024 + SCW],
                                             ALU.mult), inc=("sv", 1))
    p.w("vector", "sv")
    p.do("vector", lambda v: v.tensor_tensor(ucomb[0:80, :], ucomb[0:80, :],
                                             p30b[0:80, :], ALU.add),
         inc=("sv", 1))
    p.w("vector", "sv")
    p.do("vector", lambda v: v.tensor_scalar(stbuf[0:80, :], ucomb[0:80, :],
                                             pmc_s[0:80, 3:4], None, ALU.mult),
         inc=("sv", 1))

    p.barrier()
    for ci in range(NCH):
        par = ci % 2
        gsub = ci // 2
        buf = ci % 2
        p0_ = par * 64
        p.w("sync", "sp")
        fl0 = acs_d[0:1, ci * 64:ci * 64 + 1]
        p.do("sync", lambda e, fl0=fl0: e.dma_start(
            out=flatbuf[0:1, 0:2048],
            in_=bass.AP(fl0.tensor, fl0.offset, [[1, 1], [GRID, 32], [1, 64]])),
            inc=("q11", 16))
        p.w("tensor", "q11")
        p.w("tensor", "sv")
        p.do("tensor", lambda t, ci=ci, p0_=p0_: t.matmul(
            pB2[p0_:p0_ + 64, 1024:1088],
            bfm[:, ci * 64:(ci + 1) * 64], cfm4[0:16, ci * 64:(ci + 1) * 64],
            start=True, stop=True), inc=("sp", 1))
        p.w("vector", "sp")
        p.do("vector", lambda v, p0_=p0_: v.tensor_tensor(
            m1m[p0_:p0_ + 64, :], pB2[p0_:p0_ + 64, 1024:1088],
            tri_s[:, :] if p0_ == 0 else tri64hi[64:128, :], ALU.mult),
             inc=("sv", 1))
        p.w("tensor", "sv")
        for q in range(4):
            p.do("tensor", lambda t, q=q, p0_=p0_: t.matmul(
                pB1[p0_:p0_ + 64, q * 512:(q + 1) * 512], e064_s[:, :],
                flatbuf[:, q * 512:(q + 1) * 512],
                start=True, stop=False), inc=("sp", 1))
            p.do("tensor", lambda t, q=q, ci=ci, p0_=p0_: t.matmul(
                pB1[p0_:p0_ + 64, q * 512:(q + 1) * 512],
                acsfm[:, ci * 64:(ci + 1) * 64],
                negio_s[:, q * 512:(q + 1) * 512],
                start=False, stop=True), inc=("sp", 1))
        p.w("vector", "sp")
        p.do("vector", lambda v, p0_=p0_: v.tensor_scalar(
            dmin[p0_:p0_ + 64, :], pB1[p0_:p0_ + 64, 0:2048],
            0.0, None, ALU.min), inc=("sv", 1))
        p.w("scalar", "sv")
        p.do("scalar", lambda sc, p0_=p0_: sc.activation(
            ebuf[p0_:p0_ + 64, :], dmin[p0_:p0_ + 64, :], AF.Exp),
            inc=("sa", 1))
        p.w("vector", "sa")
        p.do("vector", lambda v, p0_=p0_: v.tensor_tensor(
            gbuf[p0_:p0_ + 64, :].rearrange("p (h l) -> p h l", h=32),
            ebuf[p0_:p0_ + 64, :].rearrange("p (h l) -> p h l", h=32),
            bcast_free(m1m[p0_:p0_ + 64, 0:1], 64, [[0, 32], [1, 64]]),
            ALU.mult), inc=("sv", 1))
        if ci + 1 <= NCH - 1:
            qm2 = "q14" if (ci + 1) % 2 == 0 else "q15"
            p.w("gpsimd", "sv")
            p.do("gpsimd", lambda g, ci=ci: g.dma_start(
                out=snapld[0:80, ((ci + 1) % 2) * SCW:((ci + 1) % 2 + 1) * SCW],
                in_=snap_d[ci * 80:(ci + 1) * 80, :]), inc=(qm2, 16))
        if ci == 0:
            pass
        else:
            grep(expgam_s[:, ci:ci + 1])
            p.w("vector", "sp")
            p.do("vector", lambda v: v.tensor_tensor(
                ucomb[0:80, :], sin[0:80, :], pB1[0:80, 1024:1024 + SCW],
                ALU.mult), inc=("sv", 1))
            p.w("vector", "sv")
            p.w("vector", "q14" if ci % 2 == 0 else "q15")
            p.do("vector", lambda v, ci=ci: v.tensor_tensor(
                stbuf[0:80, :], ucomb[0:80, :],
                snapld[0:80, (ci % 2) * SCW:(ci % 2 + 1) * SCW], ALU.add),
                inc=("sv", 1))
        emit_xdt(ci)
        asl2 = acstm[p0_:p0_ + 64, gsub * 32:(gsub + 1) * 32]
        p.w("tensor", "sv")
        for h in range(NHEADS):
            pp = 32 * (h % 4)
            ff = 64 * (h // 4)
            p.do("tensor", lambda t, h=h, p0_=p0_: t.matmul(
                pB2[p0_:p0_ + 64, h * 64:(h + 1) * 64],
                gbuf[p0_:p0_ + 64, h * 64:(h + 1) * 64],
                xdt[p0_:p0_ + 64, h * 64:(h + 1) * 64],
                start=True, stop=True), inc=("sp", 1))
            hg = h // 11
            hi = h - 11 * hg
            p.do("tensor", lambda t, h=h, hg=hg, hi=hi, ci=ci, p0_=p0_: t.matmul(
                pB1[p0_:p0_ + 64, h * 64:(h + 1) * 64],
                cfm4[32 * hg:32 * hg + 16, ci * 64:(ci + 1) * 64],
                stbuf[32 * hg:32 * hg + 16, 64 * hi:64 * hi + 64],
                start=True, stop=True), inc=("sp", 1))
        p.w("sync", "sv")
        p.w("sync", "sa")
        qz = "q12" if par == 0 else "q13"
        dma("sync", zchunk2[p0_:p0_ + 64, :], zsp_d[ci * 64:(ci + 1) * 64, :],
            sem=qz)
        p.w("scalar", qz)
        p.w("scalar", "sv")
        p.do("scalar", lambda sc, p0_=p0_: sc.activation(
            sz[p0_:p0_ + 64, :], zchunk2[p0_:p0_ + 64, :], AF.Silu),
            inc=("sa", 1))
        p.do("scalar", lambda sc, asl2=asl2, p0_=p0_: sc.activation(
            eacsb[p0_:p0_ + 64, 0:32], asl2, AF.Exp), inc=("sa", 1))
        p.w("vector", "sa")
        p.do("vector", lambda v, p0_=p0_, gsub=gsub: v.tensor_tensor(
            ytmp[p0_:p0_ + 64, :],
            xhtm[p0_:p0_ + 64, gsub * D_INNER:(gsub + 1) * D_INNER],
            dskrep_s[p0_:p0_ + 64, :], ALU.mult), inc=("sv", 1))
        p.w("vector", "sp")
        p.w("vector", "sv")
        p.do("vector", lambda v, p0_=p0_: v.tensor_tensor(
            ytmp[p0_:p0_ + 64, :], ytmp[p0_:p0_ + 64, :],
            pB2[p0_:p0_ + 64, 0:2048], ALU.add), inc=("sv", 1))
        p.w("vector", "sv")
        p.do("vector", lambda v, p0_=p0_: v.tensor_tensor(
            ytmp2[p0_:p0_ + 64, :].rearrange("p (h q) -> p h q", h=32),
            pB1[p0_:p0_ + 64, 0:2048].rearrange("p (h q) -> p h q", h=32),
            bcast_free(eacsb[p0_:p0_ + 64, 0:1], 64, [[1, 32], [0, 64]]),
            ALU.mult), inc=("sv", 1))
        p.w("vector", "sv")
        p.do("vector", lambda v, p0_=p0_: v.tensor_tensor(
            ytmp[p0_:p0_ + 64, :], ytmp[p0_:p0_ + 64, :],
            ytmp2[p0_:p0_ + 64, :], ALU.add), inc=("sv", 1))
        p.w("vector", "sv")
        p.do("vector", lambda v, p0_=p0_: v.tensor_tensor(
            ytmp[p0_:p0_ + 64, :], ytmp[p0_:p0_ + 64, :], sz[p0_:p0_ + 64, :],
            ALU.mult), inc=("sv", 1))
        p.w("vector", "sv")
        for q in range(4):
            p.do("vector", lambda v, q=q, p0_=p0_: v.bn_stats(
                stat6[p0_:p0_ + 64, 6 * q:6 * q + 6],
                ytmp[p0_:p0_ + 64, 512 * q:512 * (q + 1)]), inc=("sv", 1))
        p.w("vector", "sv")
        p.do("vector", lambda v, p0_=p0_: v.bn_aggr(
            stat2[p0_:p0_ + 64, 0:2], stat6[p0_:p0_ + 64, 0:24]),
            inc=("sv", 1))
        p.w("vector", "sv")
        p.do("vector", lambda v, p0_=p0_: v.tensor_tensor(
            rmscol[p0_:p0_ + 64, 0:1], stat2[p0_:p0_ + 64, 0:1],
            stat2[p0_:p0_ + 64, 0:1], ALU.mult), inc=("sv", 1))
        p.w("vector", "sv")
        p.do("vector", lambda v, p0_=p0_: v.tensor_tensor(
            rmscol[p0_:p0_ + 64, 0:1], rmscol[p0_:p0_ + 64, 0:1],
            stat2[p0_:p0_ + 64, 1:2], ALU.add), inc=("sv", 1))
        p.w("scalar", "sv")
        p.do("scalar", lambda sc, p0_=p0_: sc.activation(
            rmscol[p0_:p0_ + 64, 1:2], rmscol[p0_:p0_ + 64, 0:1], AF.Ln,
            bias=EPS), inc=("sa", 1))
        p.w("scalar", "sa")
        p.do("scalar", lambda sc, p0_=p0_: sc.activation(
            rmscol[p0_:p0_ + 64, 1:2], rmscol[p0_:p0_ + 64, 1:2], AF.Exp,
            scale=-0.5), inc=("sa", 1))
        p.w("vector", "sa")
        p.do("vector", lambda v, p0_=p0_, gsub=gsub: v.scalar_tensor_tensor(
            xhtm[p0_:p0_ + 64, gsub * D_INNER:(gsub + 1) * D_INNER],
            ytmp[p0_:p0_ + 64, :], rmscol[p0_:p0_ + 64, 1:2],
            nwrep_s[p0_:p0_ + 64, :], ALU.mult, ALU.mult), inc=("sv", 1))
        p.w("tensor", "sv")

    p.barrier()
    for kc16 in range(16):
        dma("gpsimd", woutb[:, kc16 * D_MODEL:(kc16 + 1) * D_MODEL],
            wout_d[kc16 * 128:(kc16 + 1) * 128, :], sem="q16")
    p.w("tensor", "q16")
    sa_motm = {}
    for su in range(NSUB):
        p.w("tensor", "sv")
        for half in range(2):
            for sl in range(8):
                fc = half * 8 + sl
                p.do("tensor", lambda t, su=su, fc=fc, sl=sl: t.transpose(
                    pTb[:, sl * 128:sl * 128 + 128],
                    xhtm[:, su * D_INNER + fc * 128:
                         su * D_INNER + fc * 128 + 128],
                    idnb_s[:, :]), inc=("sp", 1))
            p.w("vector", "sp")
            p.do("vector", lambda v, half=half: v.tensor_copy(
                ynf[:, half * 1024:(half + 1) * 1024], pTb[:, 0:1024]),
                inc=("sv", 1))
            p.w("tensor", "sv")
        pp3 = pB1 if su % 2 == 0 else pB2
        if su >= 2:
            p.w("tensor", "sa", sa_motm[su - 2])
        for fc in range(16):
            for q in range(2):
                p.do("tensor", lambda t, fc=fc, q=q, pp3=pp3: t.matmul(
                    pp3[:, q * 512:(q + 1) * 512],
                    ynf[:, fc * 128:(fc + 1) * 128],
                    woutb[:, fc * D_MODEL + q * 512: fc * D_MODEL + (q + 1) * 512],
                    start=(fc == 0), stop=(fc == 15)), inc=("sp", 1))
        p.w("scalar", "sp")
        p.w("scalar", "sv")
        p.do("scalar", lambda sc, pp3=pp3: sc.copy(motm[:, :], pp3[:, 0:1024]),
             inc=("sa", 1))
        sa_motm[su] = p.n["sa"]
        p.w("vector", "sa")
        p.w("vector", "sv")
        ln_tm(motm[:, :], motm[:, :], D_MODEL)
        p.w("sync", "sv")
        p.w("sync", "q18")
        dma("sync", xgtm[:, :], xg_d[su * 128:(su + 1) * 128, :], sem="q17")
        p.w("vector", "q17")
        p.do("vector", lambda v: v.tensor_tensor(xgtm[:, :], motm[:, :],
                                                 xgtm[:, :], ALU.add),
             inc=("sv", 1))
        p.w("vector", "sv")
        p.do("vector", lambda v, su=su: v.tensor_scalar(
            xgtm[:, :], xgtm[:, :], maskrow_s[:, su:su + 1], None, ALU.mult),
            inc=("sv", 1))
        sv_x2 = p.n["sv"]
        p.w("sync", "sv", sv_x2)
        dma("sync", x2_d[su * 128:(su + 1) * 128, :], xgtm[:, :], sem="q18")

    p.barrier()
    for su in range(NSUB):
        p.w("sync", "sv")
        dma("sync", x2ld1[:, :], x2_d[su * 128:(su + 1) * 128, :], sem="q19")
        p.w("vector", "q19")
        ln_tm(x2ld1[:, :], x2n_t[:, :], D_MODEL)
        p.w("vector", "sv")
        p.do("vector", lambda v: v.tensor_reduce(
            qcol[:, 0:1], x2n_t[:, 0:1024], AX.X, ALU.max,
            apply_absolute_value=True), inc=("sv", 1))
        p.w("vector", "sv")
        p.do("vector", lambda v: v.tensor_scalar(qcol[:, 0:1], qcol[:, 0:1],
                                                 1e-5, None, ALU.max),
             inc=("sv", 1))
        p.w("vector", "sv")
        p.do("vector", lambda v: v.reciprocal(qcol[:, 1:2], qcol[:, 0:1]),
             inc=("sv", 1))
        p.w("vector", "sv")
        p.do("vector", lambda v: v.tensor_scalar(qcol[:, 1:2], qcol[:, 1:2],
                                                 127.0, None, ALU.mult),
             inc=("sv", 1))
        p.do("vector", lambda v: v.tensor_scalar(qcol[:, 2:3], qcol[:, 0:1],
                                                 1.0 / 127.0, None, ALU.mult),
             inc=("sv", 1))
        p.w("vector", "sv")
        p.do("vector", lambda v: v.tensor_scalar(qv1[:, :], x2n_t[:, :],
                                                 qcol[:, 1:2], None, ALU.mult),
             inc=("sv", 1))
        p.w("vector", "sv")
        p.do("vector", lambda v: v.tensor_copy(qyi1[:, :], qv1[:, :]),
             inc=("sv", 1))
        p.w("vector", "sv")
        p.do("vector", lambda v: v.tensor_copy(qy1[:, :], qyi1[:, :]),
             inc=("sv", 1))
        p.w("vector", "sv")
        p.do("vector", lambda v: v.tensor_tensor(qr1[:, :], qv1[:, :],
                                                 qy1[:, :], ALU.subtract),
             inc=("sv", 1))
        p.w("vector", "sv")
        p.do("vector", lambda v: v.scalar_tensor_tensor(
            qt11[:, :], qr1[:, :], 0.5, qy1[:, :], ALU.is_gt,
            ALU.add), inc=("sv", 1))
        p.w("vector", "sv")
        p.do("vector", lambda v: v.scalar_tensor_tensor(
            qv1[:, :], qr1[:, :], -0.5, qt11[:, :], ALU.is_ge,
            ALU.add), inc=("sv", 1))
        p.w("vector", "sv")
        p.do("vector", lambda v: v.tensor_scalar(
            xq1tm[:, :], qv1[:, :], -1.0, qcol[:, 2:3], ALU.add, ALU.mult),
            inc=("sv", 1))
        p.w("tensor", "sv")
        for kc in range(KC):
            sl = kc % 8
            p.do("tensor", lambda t, kc=kc, sl=sl: t.transpose(
                pTb[:, sl * 128:sl * 128 + 128],
                xq1tm[:, kc * 128:(kc + 1) * 128], idnb_s[:, :]), inc=("sp", 1))
        p.w("vector", "sp")
        xq1b = xq1fm[:, su * 128:su * 128 + 1]
        xq1ap = bass.AP(xq1b.tensor, xq1b.offset,
                        [[KC * GRID, 128], [GRID, KC], [1, 128]])
        p.do("vector", lambda v, xq1ap=xq1ap: v.tensor_copy(
            xq1ap, pTb[:, 0:1024]), inc=("sv", 1))

    p.barrier()
    for kc in range(KC):
        dma("sync", w1qb[:, kc * D_HIDDEN:(kc + 1) * D_HIDDEN],
            w1q_d[kc * 128:(kc + 1) * 128, :], sem="q20")
    p.w("tensor", "q20")
    sa_tile0 = {}
    sa_end = {}
    for hf in range(32):
        pp = pB1 if hf % 2 == 0 else pB2
        hb = h1buf if hf % 2 == 0 else h1bufB
        hc = hconv if hf % 2 == 0 else hconvB
        if hf >= 2:
            p.w("tensor", "sa", sa_end[hf - 2])
        for q, (t0, tw) in enumerate(PA_TILES):
            reg = q % 4
            for kc in range(KC):
                p.do("tensor", lambda t, hf=hf, kc=kc, t0=t0, tw=tw, reg=reg,
                     pp=pp: t.matmul(
                         pp[:, reg * 512: reg * 512 + tw],
                         w1qb[:, kc * D_HIDDEN + hf * 128:
                              kc * D_HIDDEN + hf * 128 + 128],
                         xq1fm[:, kc * GRID + t0: kc * GRID + t0 + tw],
                         start=(kc == 0), stop=(kc == KC - 1)), inc=("sp", 1))
            p.w("scalar", "sp")
            if q == 0:
                p.w("scalar", "sv")
            p.do("scalar", lambda sc, t0=t0, tw=tw, reg=reg, pp=pp, hb=hb:
                 sc.copy(hb[:, t0:t0 + tw],
                         pp[:, reg * 512: reg * 512 + tw]), inc=("sa", 1))
            if q == 0:
                sa_tile0[hf] = p.n["sa"]
            if reg == 3:
                p.w("tensor", "sa", sa_tile0[hf])
        sa_end[hf] = p.n["sa"]
        p.w("vector", "sv")
        p.w("vector", "sa")
        p.do("vector", lambda v, hf=hf, hb=hb, hc=hc: v.tensor_scalar(
            hc[:, :], hb[:, :], fcw_s[:, hf * 3 + 1:hf * 3 + 2], None,
            ALU.mult), inc=("sv", 1))
        p.w("vector", "sv")
        p.do("vector", lambda v, hf=hf, hb=hb, hc=hc: v.scalar_tensor_tensor(
            hc[:, 1:GRID], hb[:, 0:GRID - 1],
            fcw_s[:, hf * 3:hf * 3 + 1], hc[:, 1:GRID], ALU.mult, ALU.add),
            inc=("sv", 1))
        p.w("vector", "sv")
        p.do("vector", lambda v, hf=hf, hb=hb, hc=hc: v.scalar_tensor_tensor(
            hc[:, 0:GRID - 1], hb[:, 1:GRID],
            fcw_s[:, hf * 3 + 2:hf * 3 + 3], hc[:, 0:GRID - 1], ALU.mult,
            ALU.add), inc=("sv", 1))
        p.w("scalar", "sv")
        p.w("scalar", "q21")
        p.do("scalar", lambda sc, hf=hf, hc=hc: sc.activation(
            hpostb[:, :], hc[:, :], AF.Silu, bias=fcb_s[:, hf:hf + 1]),
            inc=("sa", 1))
        sa_here = p.n["sa"]
        p.w("sync", "sa", sa_here)
        dma("sync", hpost_d[hf * 128:(hf + 1) * 128, :], hpostb[:, :],
            sem="q21")

    p.barrier()
    for q32 in range(32):
        dma("sync", w2qb[:, q32 * D_MODEL:(q32 + 1) * D_MODEL],
            w2q_d[q32 * 128:(q32 + 1) * 128, :], sem="q22")
    p.w("tensor", "q22")
    for su in range(NSUB):
        p.w("sync", "sv")
        hp0 = hpost_d[0:1, su * 128:su * 128 + 1]
        p.do("sync", lambda e, hp0=hp0: e.dma_start(
            out=xq2fm[:, :],
            in_=bass.AP(hp0.tensor, hp0.offset,
                        [[GRID, 128], [128 * GRID, 32], [1, 128]])),
            inc=("q23", 16))
        p.w("tensor", "q23")
        p.w("tensor", "sv")
        for half in range(4):
            for sl in range(8):
                fc = half * 8 + sl
                p.do("tensor", lambda t, fc=fc, sl=sl: t.transpose(
                    pTb[:, sl * 128:sl * 128 + 128],
                    xq2fm[:, fc * 128:(fc + 1) * 128], idnb_s[:, :]),
                    inc=("sp", 1))
            p.w("vector", "sp")
            p.do("vector", lambda v, half=half: v.tensor_copy(
                htmb[:, half * 1024:(half + 1) * 1024], pTb[:, 0:1024]),
                inc=("sv", 1))
            p.w("tensor", "sv")
        p.w("vector", "sv")
        p.do("vector", lambda v: v.tensor_reduce(
            qcol[:, 4:5], htmb[:, :], AX.X, ALU.max,
            apply_absolute_value=True), inc=("sv", 1))
        p.w("vector", "sv")
        p.do("vector", lambda v: v.tensor_scalar(qcol[:, 4:5], qcol[:, 4:5],
                                                 1e-5, None, ALU.max),
             inc=("sv", 1))
        p.w("vector", "sv")
        p.do("vector", lambda v: v.reciprocal(qcol[:, 5:6], qcol[:, 4:5]),
             inc=("sv", 1))
        p.w("vector", "sv")
        p.do("vector", lambda v: v.tensor_scalar(qcol[:, 5:6], qcol[:, 5:6],
                                                 127.0, None, ALU.mult),
             inc=("sv", 1))
        p.do("vector", lambda v: v.tensor_scalar(qcol[:, 6:7], qcol[:, 4:5],
                                                 1.0 / 127.0, None, ALU.mult),
             inc=("sv", 1))
        p.w("vector", "sv")
        p.do("vector", lambda v: v.tensor_scalar(qv4[:, :], htmb[:, :],
                                                 qcol[:, 5:6], None, ALU.mult),
             inc=("sv", 1))
        p.w("vector", "sv")
        p.do("vector", lambda v: v.tensor_copy(qyi4[:, :], qv4[:, :]),
             inc=("sv", 1))
        p.w("vector", "sv")
        p.do("vector", lambda v: v.tensor_copy(qy4[:, :], qyi4[:, :]),
             inc=("sv", 1))
        p.w("vector", "sv")
        p.do("vector", lambda v: v.tensor_tensor(qr4[:, :], qv4[:, :], qy4[:, :],
                                                 ALU.subtract), inc=("sv", 1))
        p.w("vector", "sv")
        p.do("vector", lambda v: v.scalar_tensor_tensor(
            qt14[:, :], qr4[:, :], 0.5, qy4[:, :], ALU.is_gt, ALU.add),
            inc=("sv", 1))
        p.w("vector", "sv")
        p.do("vector", lambda v: v.scalar_tensor_tensor(
            qv4[:, :], qr4[:, :], -0.5, qt14[:, :], ALU.is_ge, ALU.add),
            inc=("sv", 1))
        p.w("vector", "sv")
        p.do("vector", lambda v: v.tensor_scalar(xq2tm[:, :], qv4[:, :], -1.0,
                                                 None, ALU.add), inc=("sv", 1))
        p.w("tensor", "sv")
        for half in range(4):
            for sl in range(8):
                fc = half * 8 + sl
                p.do("tensor", lambda t, fc=fc, sl=sl: t.transpose(
                    pTb[:, sl * 128:sl * 128 + 128],
                    xq2tm[:, fc * 128:(fc + 1) * 128], idnb_s[:, :]),
                    inc=("sp", 1))
            p.w("vector", "sp")
            p.do("vector", lambda v, half=half: v.tensor_copy(
                xq2fm[:, half * 1024:(half + 1) * 1024], pTb[:, 0:1024]),
                inc=("sv", 1))
            p.w("tensor", "sv")
        p.w("tensor", "sv")
        for fc in range(32):
            for q in range(2):
                p.do("tensor", lambda t, fc=fc, q=q: t.matmul(
                    pB1[:, q * 512:(q + 1) * 512],
                    xq2fm[:, fc * 128:(fc + 1) * 128],
                    w2qb[:, fc * D_MODEL + q * 512: fc * D_MODEL + (q + 1) * 512],
                    start=(fc == 0), stop=(fc == 31)), inc=("sp", 1))
        p.w("scalar", "sp")
        p.w("scalar", "q25")
        p.w("scalar", "sv")
        p.do("scalar", lambda sc: sc.activation(fotm[:, :], pB1[:, 0:1024],
                                                AF.Copy, scale=qcol[:, 6:7]),
             inc=("sa", 1))
        p.w("vector", "sa")
        ln_tm(fotm[:, :], fotm[:, :], D_MODEL)
        p.w("sync", "sv")
        dma("sync", x2ld3[:, :], x2_d[su * 128:(su + 1) * 128, :], sem="q24")
        p.w("vector", "q24")
        p.do("vector", lambda v: v.tensor_tensor(fotm[:, :], fotm[:, :],
                                                 x2ld3[:, :], ALU.add),
             inc=("sv", 1))
        sv_out = p.n["sv"]
        p.w("sync", "sv", sv_out)
        if su == 0:
            dma("sync", out_d[0:64, :], fotm[64:128, :], sem="q25")
        elif su == NSUB - 1:
            dma("sync", out_d[SEG - 64:SEG, :], fotm[0:64, :], sem="q25")
        else:
            dma("sync", out_d[su * 128 - 64:(su + 1) * 128 - 64, :],
                fotm[:, :], sem="q25")
        p.w("tensor", "sv")

    p.w("sync", "q25")

    with nc.Block() as block:
        p.emit(block)
    ctx.close()
    return nc



def kernel(**inputs):
    x = np.ascontiguousarray(np.asarray(inputs["L0_attn_x"], np.float32))
    shards = _host_shards(x)
    consts = _host_consts(inputs)
    nc = build_nc()
    in_maps = []
    for core in range(8):
        m = dict(consts)
        m.update(shards[core])
        in_maps.append(m)
    res = run_bass_kernel_spmd(nc, in_maps, core_ids=list(range(8)))
    out = np.zeros((2, SEQ, D_MODEL), np.float32)
    for core in range(8):
        b, j = core // NSEG, core % NSEG
        out[b, j * SEG:(j + 1) * SEG] = res.results[core]["out"]
    return out


# version 07
# speedup vs baseline: 1.3029x; 1.0016x over previous
import numpy as np
import ml_dtypes
import concourse.bass as bass
import concourse.mybir as mybir
from concourse.bass_utils import run_bass_kernel_spmd
from contextlib import ExitStack

F32 = mybir.dt.float32
BF16 = mybir.dt.bfloat16
I32 = mybir.dt.int32
U8 = mybir.dt.uint8
AF = mybir.ActivationFunctionType
ALU = mybir.AluOpType
AX = mybir.AxisListType

D_MODEL = 1024
D_STATE = 16
HEADDIM = 64
CHUNK = 64
D_INNER = 2048
NHEADS = 32
CONV_DIM = 2080
D_IN_PROJ = 4160
D_HIDDEN = 4096
SEQ = 8192
SEG = 2048
NSEG = 4
GRID = 2176
NCH = 34
NSUB = 17
KC = 8
EPS = 1e-5
SLOT = 192
SCW = 704

ENGINES = ("sync", "vector", "scalar", "tensor", "gpsimd")
PA_TILES = [(0, 512), (512, 512), (1024, 512), (1536, 512), (2048, 128)]


def bcast_free(ap_col, part, dims):
    return bass.AP(ap_col.tensor, ap_col.offset,
                   [[ap_col.ap[0][0], part]] + [list(d) for d in dims])



def _host_shards(x):
    shards = []
    for core in range(8):
        b, j = core // NSEG, core % NSEG
        start = j * SEG
        lo = start - CHUNK
        g = np.zeros((GRID, D_MODEL), np.float32)
        s0, s1 = max(lo, 0), min(start + SEG + CHUNK, SEQ)
        g[s0 - lo:s1 - lo] = x[b, s0:s1]
        h3 = np.zeros((4, D_MODEL), np.float32)
        h0, h1 = max(lo - 3, 0), max(lo, 0)
        h3[3 - (h1 - h0):3] = x[b, h0:h1]
        maskrow = np.ones((NSUB * 128,), np.float32)
        if j == 0:
            maskrow[CHUNK - 1] = 0.0
        if j == NSEG - 1:
            maskrow[SEG + CHUNK] = 0.0
        mask_tm = np.ascontiguousarray(maskrow.reshape(NSUB, 128).T)
        pm = np.zeros((128, 4), np.float32)
        for k in (1, 2, 3):
            pm[:, k - 1] = 1.0 if j >= k else 0.0
        pm[:, 3] = 1.0 if j >= 1 else 0.0
        slot = np.zeros((4, 1), np.int32)
        for k in (1, 2, 3):
            slot[k - 1, 0] = SLOT * max(j - k, 0)
        shards.append(dict(xg=g, xh3=h3, maskrow=mask_tm, pmc=pm, slotoff=slot))
    return shards


def _host_consts(inputs):
    c = {}
    c["idn"] = np.eye(128, dtype=np.float32)
    cp = np.zeros((128, 4), np.float32)
    cp[:, 0] = 1e-5; cp[:, 1] = 1.0; cp[:, 2] = -0.5; cp[:, 3] = 0.0
    c["cpool"] = cp
    e = np.zeros((32, 64), np.float32); e[0, :] = 1.0
    c["e064"] = e
    e2 = np.zeros((32, 128), np.float32); e2[0, :] = 1.0
    c["e0128"] = e2
    c["onescol"] = np.ones((128, 1), np.float32)
    hsel = np.zeros((32, 128), np.float32)
    for h in range(32):
        g = h // 11
        hsel[h, 32 * g:32 * g + 16] = 1.0
    c["hsel"] = hsel
    gmask = np.zeros((32, SCW), np.float32)
    for h in range(32):
        i = h - 11 * (h // 11)
        gmask[h, 64 * i:64 * i + 64] = 1.0
    c["gmask"] = gmask
    negio = np.zeros((32, 2048), np.float32)
    for h in range(32):
        negio[h, 64 * h:64 * h + 64] = -1.0
    c["negio"] = negio
    t = np.zeros((64, 64), np.float32)
    for s in range(64):
        t[s, s:] = 1.0
    c["tri"] = t
    dtb = np.asarray(inputs["L0_attn_dt_bias"], np.float32)
    c["dtbrep"] = np.ascontiguousarray(np.tile(dtb[None, :], (128, 1)))
    Dsk = np.asarray(inputs["L0_attn_D"], np.float32)
    c["dskrep"] = np.ascontiguousarray(
        np.tile(np.repeat(Dsk, HEADDIM)[None, :], (128, 1)))
    nw = np.asarray(inputs["L0_attn_norm_w"], np.float32)
    c["nwrep"] = np.ascontiguousarray(np.tile(nw[None, :], (128, 1)))
    c["alog"] = np.ascontiguousarray(
        np.asarray(inputs["L0_attn_A_log"], np.float32).reshape(32, 1))
    cw = np.asarray(inputs["L0_attn_conv_w"], np.float32)
    cwp = np.zeros((17 * 128, 4), np.float32)
    cwp[:CONV_DIM] = cw
    c["convw"] = np.ascontiguousarray(
        cwp.reshape(17, 128, 4).transpose(1, 0, 2).reshape(128, 68))
    cb = np.asarray(inputs["L0_attn_conv_b"], np.float32)
    cbp = np.zeros((17 * 128,), np.float32)
    cbp[:CONV_DIM] = cb
    c["convb"] = np.ascontiguousarray(cbp.reshape(17, 128).T)
    fcw = np.asarray(inputs["L0_ffn_conv_w"], np.float32).reshape(D_HIDDEN, 3)
    c["fcw"] = np.ascontiguousarray(
        fcw.reshape(32, 128, 3).transpose(1, 0, 2).reshape(128, 96))
    fcb = np.asarray(inputs["L0_ffn_conv_b"], np.float32)
    c["fcb"] = np.ascontiguousarray(fcb.reshape(32, 128).T)
    c["wint"] = np.ascontiguousarray(np.asarray(inputs["L0_attn_W_in"], np.float32).T)
    c["wout"] = np.ascontiguousarray(np.asarray(inputs["L0_attn_W_out"], np.float32).T)
    c["w1t"] = np.ascontiguousarray(np.asarray(inputs["L0_ffn_w1"], np.float32).T)
    c["w2t"] = np.ascontiguousarray(np.asarray(inputs["L0_ffn_w2"], np.float32).T)
    return c



class P:
    def __init__(self, nc, ctx):
        self.nc = nc
        self.ops = {e: [] for e in ENGINES}
        self.h = {}
        self.n = {}
        for name in ("qs", "qg", "sv", "sa", "sp", "sg", "cc", "st",
                     "qw", "qe", "q1", "q2", "q3", "q4", "q5", "q6", "q7",
                     "q8", "q9", "q10", "q11", "q12", "q13", "q14", "q15",
                     "q16", "q17", "q18", "q19", "q20", "q21", "q22", "q23",
                     "q24", "q25", "qw2"):
            self.h[name] = ctx.enter_context(nc.semaphore(name))
            self.n[name] = 0
        self.env = {}

    def do(self, eng, f, inc=None):
        if inc is not None:
            sem, amt = inc
            self.n[sem] += amt
            handle = self.h[sem]

            def g(e, f=f, handle=handle, amt=amt):
                f(e).then_inc(handle, amt)
        else:
            def g(e, f=f):
                f(e)
        self.ops[eng].append(g)

    def w(self, eng, sem, val=None):
        v = self.n[sem] if val is None else val
        h = self.h[sem]
        self.ops[eng].append(lambda e, h=h, v=v: e.wait_ge(h, v))

    def barrier(self, engs=ENGINES, sems=None):
        if sems is None:
            sems = [k for k in self.n if k != "st"]
        for eng in engs:
            for s in sems:
                if self.n[s] > 0:
                    self.w(eng, s)

    def emit(self, block):
        for e in ENGINES:
            fs = self.ops[e]
            getattr(block, e)(lambda eng, fs=fs: [f(eng) for f in fs])



def build_nc():
    nc = bass.Bass("TRN2", target_bir_lowering=False, dynamic_dma_scratch_size=8192)
    ctx = ExitStack()

    def din(name, shape, dtype=F32):
        return nc.dram_tensor(name, list(shape), dtype, kind="ExternalInput").ap()

    def dint(name, shape, dtype=F32):
        return nc.dram_tensor(name, list(shape), dtype, kind="Internal").ap()

    def sb(name, shape, dtype=F32):
        return ctx.enter_context(nc.sbuf_tensor(name, shape, dtype))

    xg_d = din("xg", [GRID, D_MODEL])
    xh3_d = din("xh3", [4, D_MODEL])
    maskrow_d = din("maskrow", [128, NSUB])
    pmc_d = din("pmc", [128, 4])
    slotoff_d = din("slotoff", [4, 1], I32)
    idn_d = din("idn", [128, 128])
    cpool_d = din("cpool", [128, 4])
    e064_d = din("e064", [32, 64])
    e0128_d = din("e0128", [32, 128])
    onescol_d = din("onescol", [128, 1])
    hsel_d = din("hsel", [32, 128])
    gmask_d = din("gmask", [32, SCW])
    negio_d = din("negio", [32, 2048])
    tri_d = din("tri", [64, 64])
    dtbrep_d = din("dtbrep", [128, 32])
    dskrep_d = din("dskrep", [128, D_INNER])
    nwrep_d = din("nwrep", [128, D_INNER])
    alog_d = din("alog", [32, 1])
    convw_d = din("convw", [128, 68])
    convb_d = din("convb", [128, 17])
    fcw_d = din("fcw", [128, 96])
    fcb_d = din("fcb", [128, 32])
    wint_d = din("wint", [D_MODEL, D_IN_PROJ])
    wout_d = din("wout", [D_INNER, D_MODEL])
    w1t_d = din("w1t", [D_MODEL, D_HIDDEN])
    w2t_d = din("w2t", [D_HIDDEN, D_MODEL])
    out_d = nc.dram_tensor("out", [SEG, D_MODEL], F32, kind="ExternalOutput").ap()

    zsp_d = dint("zsp", [GRID, D_INNER], BF16)
    x2_d = dint("x2d", [GRID, D_MODEL])
    hpost_d = dint("hpostd", [D_HIDDEN, GRID], BF16)
    acs_d = dint("acsd", [NHEADS, GRID])
    alast_d = dint("alastd", [NCH, NHEADS])
    w1q_d = dint("w1qd", [D_MODEL, D_HIDDEN], BF16)
    w2q_d = dint("w2qd", [D_HIDDEN, D_MODEL], BF16)
    agin_d = dint("agin", [SLOT, SCW])
    agout_d = dint("agout", [4 * SLOT, SCW])
    snap_d = dint("snapd", [33 * 80, SCW], BF16)

    arenaA = ctx.enter_context(nc.sbuf_tensor("arenaA", [128, 69632], U8))
    arenaB = ctx.enter_context(nc.sbuf_tensor("arenaB", [128, 66560], U8))
    arenaC = ctx.enter_context(nc.sbuf_tensor("arenaC", [128, 16384], U8))
    arenaD = ctx.enter_context(nc.sbuf_tensor("arenaD", [128, 8192], U8))

    def overlay(base, name, shape, dtype, off=0):
        h = nc.virtual_sbuf_tensor(name, list(shape), dtype)
        nc._tensor(h.name, h.shape, h.dtype, type="SB")
        mloc = nc.lookup_mloc(h)
        mloc.allocated = True
        mloc.addr = nc.lookup_mloc(base).addr + off
        return h

    xhtm = overlay(arenaA, "xhtm", [128, NSUB * D_INNER], BF16)
    xq1fm = overlay(arenaA, "xq1fm", [128, KC * GRID], BF16)
    qv4 = overlay(arenaA, "qv4", [128, 4096], F32, 0)
    qyi4 = overlay(arenaA, "qyi4", [128, 4096], I32, 16384)
    qt14 = overlay(arenaA, "qt14", [128, 4096], F32, 16384)
    qy4 = overlay(arenaA, "qy4", [128, 4096], F32, 32768)
    qr4 = overlay(arenaA, "qr4", [128, 4096], F32, 49152)
    qv1 = overlay(arenaA, "qv1", [128, 1024], F32, 34816)
    qyi1 = overlay(arenaA, "qyi1", [128, 1024], I32, 38912)
    qt11 = overlay(arenaA, "qt11", [128, 1024], F32, 38912)
    qy1 = overlay(arenaA, "qy1", [128, 1024], F32, 43008)
    qr1 = overlay(arenaA, "qr1", [128, 1024], F32, 47104)
    winb = overlay(arenaB, "winb", [128, KC * D_IN_PROJ], BF16)
    snapbuf = overlay(arenaB, "snapbuf", [128, 2 * SCW], BF16, 0)
    snapld = overlay(arenaB, "snapld", [128, 2 * SCW], F32, 2816)
    woutb = overlay(arenaB, "woutb", [128, 16 * D_MODEL], BF16)
    w1qb = overlay(arenaB, "w1qb", [128, KC * D_HIDDEN], BF16)
    w2qb = overlay(arenaB, "w2qb", [128, 32 * D_MODEL], BF16)
    sz = overlay(arenaB, "sz", [128, D_INNER], F32, 8448)
    ytmp = overlay(arenaB, "ytmp", [128, D_INNER], F32, 16640)
    ytmp2 = overlay(arenaB, "ytmp2", [128, D_INNER], F32, 24832)
    flatbuf = overlay(arenaB, "flatbuf", [32, 2048], F32, 33024)
    wtile = overlay(arenaC, "wtile", [128, 4096], F32)
    wtileB = overlay(arenaA, "wtileB", [128, 4096], F32, 32768)
    htmb = overlay(arenaC, "htmb", [128, D_HIDDEN], BF16)
    wqtile = overlay(arenaD, "wqtile", [128, 4096], BF16)
    xq2tm = overlay(arenaD, "xq2tm", [128, D_HIDDEN], BF16)
    xnf = overlay(arenaC, "xnf", [128, KC * 512], BF16)
    ebuf = overlay(arenaC, "ebuf", [128, D_INNER], F32, 0)
    dmin = overlay(arenaC, "dmin", [128, D_INNER], F32, 8192)
    hconv = overlay(arenaA, "hconv", [128, GRID], F32, 34816)
    hconvB = overlay(arenaA, "hconvB", [128, GRID], F32, 56576)
    h1bufB = overlay(arenaA, "h1bufB", [128, GRID], BF16, 52224)
    h1buf = overlay(arenaA, "h1buf", [128, GRID], BF16, 43520)
    hpostb = overlay(arenaA, "hpostb", [128, GRID], BF16, 47872)
    xtm_t = overlay(arenaD, "xtm_t", [128, D_MODEL], F32, 0)
    xntm = overlay(arenaD, "xntm", [128, D_MODEL], F32, 4096)
    gbuf = overlay(arenaD, "gbuf", [128, D_INNER], BF16, 0)
    xdt = overlay(arenaD, "xdt", [128, D_INNER], BF16, 4096)
    slotb = overlay(arenaC, "slotb", [128, 3 * SCW], F32, 0)
    p30b = overlay(arenaC, "p30b", [128, SCW], F32, 8448)
    tcomb = overlay(arenaC, "tcomb", [128, SCW], F32, 11264)
    ynf = overlay(arenaC, "ynf", [128, D_INNER], BF16, 0)
    motm = overlay(arenaD, "motm", [128, D_MODEL], F32, 0)
    xgtm = overlay(arenaD, "xgtm", [128, D_MODEL], F32, 4096)
    x2n_t = overlay(arenaC, "x2n_t", [128, D_MODEL], F32, 0)
    xq1tm = overlay(arenaC, "xq1tm", [128, D_MODEL], BF16, 4096)
    acspad = overlay(arenaC, "acspad", [32, 64 + GRID], F32, 0)
    srun = overlay(arenaC, "srun", [128, SCW], F32, 0)
    xh3tm = overlay(arenaA, "xh3tm", [4, D_MODEL], F32, 65536)
    zchunkA = overlay(arenaC, "zchunkA", [128, D_INNER], BF16, 8192)
    convt = overlay(arenaC, "convt", [128, 512], F32, 12288)
    xbca = overlay(arenaC, "xbca", [128, 512], BF16, 14336)
    xbcaB = overlay(arenaC, "xbcaB", [128, 512], BF16, 15360)
    zchunk2 = overlay(arenaB, "zchunk2", [128, D_INNER], BF16, 24832)
    ucomb = overlay(arenaB, "ucomb", [128, SCW], F32, 41216)
    x2ld1 = overlay(arenaD, "x2ld1", [128, D_MODEL], F32, 0)
    x2ld3 = overlay(arenaC, "x2ld3", [128, D_MODEL], F32, 8192)
    fotm = overlay(arenaC, "fotm", [128, D_MODEL], F32, 12288)
    xq2fm = overlay(arenaA, "xq2fm", [128, D_HIDDEN], BF16, 49152)

    idn_s = sb("idn_s", [128, 128])
    cpool_s = sb("cpool_s", [128, 4])
    idnb_s = sb("idnb_s", [128, 128], BF16)
    e064_s = sb("e064_s", [32, 64])
    e0128_s = sb("e0128_s", [32, 128])
    onescol_s = sb("onescol_s", [128, 1])
    hsel_s = sb("hsel_s", [32, 128], BF16)
    gmask_s = sb("gmask_s", [32, SCW], BF16)
    negio_s = sb("negio_s", [32, 2048])
    eacsb = sb("eacsb", [128, 32])
    tri_s = sb("tri_s", [64, 64])
    tri64hi = sb("tri64hi", [128, 64])
    dtbrep_s = sb("dtbrep_s", [128, 32])
    dskrep_s = sb("dskrep_s", [128, D_INNER], BF16)
    nwrep_s = sb("nwrep_s", [128, D_INNER], BF16)
    alog_s = sb("alog_s", [32, 1])
    nega_s = sb("nega_s", [32, 1])
    convw_s = sb("convw_s", [128, 68])
    convb_s = sb("convb_s", [128, 17])
    fcw_s = sb("fcw_s", [128, 96])
    fcb_s = sb("fcb_s", [128, 32])
    maskrow_s = sb("maskrow_s", [128, NSUB])
    pmc_s = sb("pmc_s", [128, 4])
    slotoff_s = sb("slotoff_s", [4, 1], I32)
    dttm = sb("dttm", [128, NSUB * 32])
    acsfm = sb("acsfm", [32, GRID])
    acstm = sb("acstm", [128, NSUB * 32])
    alast_s = sb("alast_s", [32, NCH])
    expal_s = sb("expal_s", [32, NCH])
    gamlog = sb("gamlog", [32, NCH])
    expgam_s = sb("expgam_s", [32, NCH])
    eg_s = sb("eg_s", [32, 2])
    bfm = sb("bfm", [16, GRID], BF16)
    cfm4 = sb("cfm4", [128, GRID], BF16)
    btm = sb("btm", [128, NSUB * 16])
    xh3buf = sb("xh3buf", [128, 17 * 3], BF16)
    xbchal = sb("xbchal", [128, 17 * 3], BF16)
    xnhf = sb("xnhf", [128, KC * 4], BF16)
    alrow = sb("alrow", [32, 2 * 32])
    decaytm = sb("decaytm", [128, 2 * 32])
    bd = sb("bd", [128, 512], BF16)
    m1m = sb("m1m", [128, 64], BF16)
    sin = sb("sin", [128, SCW])
    stbuf = sb("stbuf", [128, SCW], BF16)
    gslot = sb("gslot", [32, 3 * 2])
    gmg = sb("gmg", [32, SCW], BF16)
    rmscol = sb("rmscol", [128, 2])
    stat6 = sb("stat6", [128, 24])
    stat2 = sb("stat2", [128, 2])
    lncol = sb("lncol", [128, 8])
    xbcbuf = sb("xbcbuf", [128, 2 * 516], BF16)
    dtraw = sb("dtraw", [128, 2 * 32])
    qcol = sb("qcol", [128, 8])
    wabs = sb("wabs", [128, 1])
    wsum1 = sb("wsum1", [128, 1])
    scol = sb("scol", [128, 4])
    valbuf = sb("valbuf", [32, 2])
    psc = sb("psc", [1, 8])

    pB1 = ctx.enter_context(nc.psum_tensor("pB1", [128, 2048], F32))
    pB2 = ctx.enter_context(nc.psum_tensor("pB2", [128, 2048], F32))

    def overlay_ps(base, name, shape, dtype, off_bytes):
        h = nc.virtual_psum_tensor(name, list(shape), dtype)
        nc._tensor(h.name, h.shape, h.dtype, type="PSUM")
        mloc = nc.lookup_mloc(h)
        mloc.allocated = True
        mloc.addr = nc.lookup_mloc(base).addr + off_bytes
        return h

    pTb = overlay_ps(pB2, "pTb", [128, 1024], BF16, 6144)

    p = P(nc, ctx)

    def dma(queue, out, in_, sem=None):
        q = sem if sem else ("qg" if queue == "gpsimd" else "qs")
        p.do(queue, lambda e, out=out, in_=in_: e.dma_start(out=out, in_=in_),
             inc=(q, 16))

    def ln_tm(xap, outap, width, part=128):
        nq = width // 512
        for q in range(nq):
            p.do("vector", lambda v, q=q, xap=xap, part=part: v.bn_stats(
                stat6[0:part, 6 * q:6 * q + 6], xap[:, 512 * q:512 * (q + 1)]),
                inc=("sv", 1))
        p.w("vector", "sv")
        p.do("vector", lambda v, nq=nq, part=part: v.bn_aggr(
            stat2[0:part, 0:2], stat6[0:part, 0:6 * nq]), inc=("sv", 1))
        p.w("scalar", "sv")
        p.do("scalar", lambda sc, part=part: sc.activation(
            lncol[0:part, 1:2], stat2[0:part, 1:2], AF.Ln, bias=EPS),
            inc=("sa", 1))
        p.w("scalar", "sa")
        p.do("scalar", lambda sc, part=part: sc.activation(
            lncol[0:part, 1:2], lncol[0:part, 1:2], AF.Exp, scale=-0.5),
            inc=("sa", 1))
        p.w("vector", "sa")
        p.do("vector", lambda v, part=part: v.tensor_scalar(
            lncol[0:part, 2:3], stat2[0:part, 0:1], -1.0,
            lncol[0:part, 1:2], ALU.mult, ALU.mult), inc=("sv", 1))
        p.w("scalar", "sv")
        p.do("scalar", lambda sc, xap=xap, outap=outap, part=part:
             sc.activation(outap, xap, AF.Identity,
                           bias=lncol[0:part, 2:3],
                           scale=lncol[0:part, 1:2]), inc=("sa", 1))

    nc.const_aps.register(F32, 1e-5, cpool_s[:, 0:1])
    nc.const_aps.register(F32, 1.0, cpool_s[:, 1:2])
    nc.const_aps.register(F32, -0.5, cpool_s[:, 2:3])
    nc.const_aps.register(F32, 0.0, cpool_s[:, 3:4])
    for dst, src in [(idn_s, idn_d), (cpool_s, cpool_d),
                     (e064_s, e064_d), (e0128_s, e0128_d),
                     (onescol_s, onescol_d), (negio_s, negio_d),
                     (tri_s, tri_d), (dtbrep_s, dtbrep_d),
                     (alog_s, alog_d), (convw_s, convw_d),
                     (convb_s, convb_d), (fcw_s, fcw_d), (fcb_s, fcb_d),
                     (pmc_s, pmc_d), (slotoff_s, slotoff_d),
                     (maskrow_s, maskrow_d), (xh3tm, xh3_d)]:
        dma("sync", dst[:, :], src[:, :])
    dma("gpsimd", dskrep_s[:, :], dskrep_d[:, :])
    dma("gpsimd", nwrep_s[:, :], nwrep_d[:, :])
    dma("gpsimd", hsel_s[:, :], hsel_d[:, :])
    dma("gpsimd", gmask_s[:, :], gmask_d[:, :])
    for kc in range(KC):
        dma("gpsimd", winb[:, kc * D_IN_PROJ:(kc + 1) * D_IN_PROJ],
            wint_d[kc * 128:(kc + 1) * 128, :])
    p.w("vector", "qs")
    p.do("vector", lambda v: v.tensor_copy(idnb_s[:, :], idn_s[:, :]),
         inc=("sv", 1))
    p.do("gpsimd", lambda g: g.memset(xh3buf[:, :], 0.0), inc=("sg", 1))
    p.do("gpsimd", lambda g: g.memset(xbchal[:, :], 0.0), inc=("sg", 1))
    p.do("gpsimd", lambda g: g.memset(tri64hi[:, :], 0.0), inc=("sg", 1))
    p.w("gpsimd", "sg")
    p.do("gpsimd", lambda g: g.iota(
        tri64hi[:, :], [[1, 64]], base=64, channel_multiplier=-1,
        allow_small_or_imprecise_dtypes=True), inc=("sg", 1))
    p.w("vector", "sg")
    p.do("vector", lambda v: v.tensor_scalar(tri64hi[:, :], tri64hi[:, :], 0.0,
                                             None, ALU.is_ge), inc=("sv", 1))
    p.w("scalar", "qs")
    p.do("scalar", lambda sc: sc.activation(nega_s[:, :], alog_s[:, :], AF.Exp),
         inc=("sa", 1))
    p.w("vector", "sa")
    p.do("vector", lambda v: v.tensor_scalar(nega_s[:, :], nega_s[:, :], -1.0,
                                             None, ALU.mult), inc=("sv", 1))

    ln_tm(xh3tm[0:4, :], xh3tm[0:4, :], D_MODEL, part=4)
    p.w("tensor", "sv")
    p.w("tensor", "sa")
    for kc in range(KC):
        p.do("tensor", lambda t, kc=kc: t.transpose(
            pB1[:, kc * 4:(kc + 1) * 4], xh3tm[0:4, kc * 128:(kc + 1) * 128],
            idn_s[0:4, 0:4]), inc=("sp", 1))
    p.w("vector", "sp")
    p.do("vector", lambda v: v.tensor_copy(xnhf[:, :], pB1[:, 0:KC * 4]),
         inc=("sv", 1))
    sv_xnhf = p.n["sv"]
    p.w("tensor", "sv", sv_xnhf)
    p.w("tensor", "qg")
    for fc in range(17):
        f0 = D_INNER + fc * 128
        fw = min(128, D_IN_PROJ - 32 - f0)
        p.w("tensor", "sv")
        for kc in range(KC):
            p.do("tensor", lambda t, kc=kc, f0=f0, fw=fw, fc=fc: t.matmul(
                pB2[0:fw, 1792 + fc % 2 * 4: 1792 + fc % 2 * 4 + 3],
                winb[:, kc * D_IN_PROJ + f0: kc * D_IN_PROJ + f0 + fw],
                xnhf[:, kc * 4: kc * 4 + 3],
                start=(kc == 0), stop=(kc == KC - 1)), inc=("sp", 1))
        p.w("vector", "sp")
        p.do("vector", lambda v, fc=fc, fw=fw: v.tensor_copy(
            xh3buf[0:fw, fc * 3:fc * 3 + 3],
            pB2[0:fw, 1792 + fc % 2 * 4: 1792 + fc % 2 * 4 + 3]), inc=("sv", 1))

    def wquant(wsrc_d, wdst_d, nrc, ncols, si):
        p.w("gpsimd", "sv")
        p.do("gpsimd", lambda g: g.memset(wsum1[:, :], 0.0), inc=("sg", 1))
        sv_red = {}
        for rc in range(nrc):
            wt = wtile if rc % 2 == 0 else wtileB
            qq = "qw" if rc % 2 == 0 else "qw2"
            if rc >= 2:
                p.w("sync", "sv", sv_red[rc - 2])
            if rc == 0:
                p.w("sync", "sg")
            dma("sync", wt[:, 0:ncols], wsrc_d[rc * 128:(rc + 1) * 128, :],
                sem=qq)
            p.w("vector", qq)
            p.w("vector", "sv")
            if rc == 0:
                p.w("vector", "sg")
            p.do("vector", lambda v, n=ncols, wt=wt: v.tensor_reduce(
                wabs[:, :], wt[:, 0:n], AX.X, ALU.add,
                apply_absolute_value=True), inc=("sv", 1))
            sv_red[rc] = p.n["sv"]
            p.w("vector", "sv")
            p.do("vector", lambda v: v.tensor_tensor(
                wsum1[:, :], wsum1[:, :], wabs[:, :], ALU.add), inc=("sv", 1))
        p.w("tensor", "sv")
        p.do("tensor", lambda t: t.matmul(pB1[0:1, 0:1], onescol_s[:, :],
                                          wsum1[:, :], start=True, stop=True),
             inc=("sp", 1))
        p.w("vector", "sp")
        p.do("vector", lambda v, nrc=nrc, ncols=ncols: v.tensor_scalar(
            psc[0:1, 0:1], pB1[0:1, 0:1], 1.0 / (nrc * 128 * ncols), None,
            ALU.mult), inc=("sv", 1))
        p.w("vector", "sv")
        p.do("vector", lambda v: v.tensor_scalar(psc[0:1, 0:1], psc[0:1, 0:1],
                                                 1e-5, None, ALU.max),
             inc=("sv", 1))
        p.w("vector", "sv")
        p.do("vector", lambda v: v.reciprocal(psc[0:1, 1:2], psc[0:1, 0:1]),
             inc=("sv", 1))
        p.w("gpsimd", "sv")
        p.do("gpsimd", lambda g: g.memset(valbuf[:, :], 0.0), inc=("sg", 1))
        p.w("vector", "sg")
        p.do("vector", lambda v: v.tensor_copy(valbuf[0:1, 0:1], psc[0:1, 1:2]),
             inc=("sv", 1))
        p.w("vector", "sv")
        p.do("vector", lambda v: v.tensor_copy(valbuf[0:1, 1:2], psc[0:1, 0:1]),
             inc=("sv", 1))
        p.w("tensor", "sv")
        p.do("tensor", lambda t: t.matmul(pB1[0:128, 8:10], e0128_s[:, :],
                                          valbuf[:, :], start=True, stop=True),
             inc=("sp", 1))
        p.w("vector", "sp")
        p.do("vector", lambda v, si=si: v.tensor_copy(
            scol[:, si:si + 1], pB1[0:128, 8:9]), inc=("sv", 1))
        p.w("vector", "sv")
        p.do("vector", lambda v, si=si: v.tensor_copy(
            scol[:, si + 1:si + 2], pB1[0:128, 9:10]), inc=("sv", 1))
        sv_q = {}
        for rc in range(nrc):
            wt = wtile if rc % 2 == 0 else wtileB
            qq = "qw" if rc % 2 == 0 else "qw2"
            if rc >= 2:
                p.w("sync", "sv", sv_q[rc - 2])
            else:
                p.w("sync", "sv")
            dma("sync", wt[:, 0:ncols], wsrc_d[rc * 128:(rc + 1) * 128, :],
                sem=qq)
            p.w("vector", qq)
            p.do("vector", lambda v, n=ncols, si=si, wt=wt: v.tensor_scalar(
                qv4[:, 0:n], wt[:, 0:n], scol[:, si:si + 1], None, ALU.mult),
                inc=("sv", 1))
            sv_q[rc] = p.n["sv"]
            p.w("vector", "sv")
            p.do("vector", lambda v, n=ncols: v.tensor_scalar(
                qt14[:, 0:n], qv4[:, 0:n], 0.5, None, ALU.is_gt), inc=("sv", 1))
            p.w("vector", "sv")
            p.do("vector", lambda v, n=ncols: v.scalar_tensor_tensor(
                qr4[:, 0:n], qv4[:, 0:n], -0.5, qt14[:, 0:n], ALU.is_lt,
                ALU.subtract), inc=("sv", 1))
            p.w("vector", "sv")
            p.do("vector", lambda v, n=ncols, si=si: v.tensor_scalar(
                qv4[:, 0:n], qr4[:, 0:n], scol[:, si + 1:si + 2], -1.0,
                ALU.mult, ALU.mult), inc=("sv", 1))
            p.w("vector", "sv")
            p.w("vector", "qe")
            p.do("vector", lambda v, n=ncols: v.tensor_copy(
                wqtile[:, 0:n], qv4[:, 0:n]), inc=("sv", 1))
            sv_here = p.n["sv"]
            p.w("sync", "sv", sv_here)
            dma("sync", wdst_d[rc * 128:(rc + 1) * 128, :], wqtile[:, 0:ncols],
                sem="qe")

    wquant(w1t_d, w1q_d, 8, 4096, 0)
    wquant(w2t_d, w2q_d, 32, 1024, 2)

    p.barrier()
    for (t0, tw) in PA_TILES:
        nsub_t = tw // 128
        tpar = (t0 // 512) % 2
        for si in range(nsub_t):
            g0 = t0 + si * 128
            p.w("sync", "sv")
            p.w("sync", "sa")
            dma("sync", xtm_t[:, :], xg_d[g0:g0 + 128, :], sem="q1")
            p.w("vector", "q1")
            ln_tm(xtm_t[:, :], xntm[:, :], D_MODEL)
            p.w("tensor", "sv")
            p.w("tensor", "sa")
            for kc in range(KC):
                p.do("tensor", lambda t, kc=kc: t.transpose(
                    pB1[:, kc * 128:(kc + 1) * 128],
                    xntm[:, kc * 128:(kc + 1) * 128], idn_s[:, :]),
                    inc=("sp", 1))
            p.w("vector", "sp")
            xnfb = xnf[:, si * 128:si * 128 + 1]
            xnfap = bass.AP(xnfb.tensor, xnfb.offset,
                            [[KC * 512, 128], [512, KC], [1, 128]])
            p.do("vector", lambda v, xnfap=xnfap: v.tensor_copy(
                xnfap, pB1[:, 0:KC * 128]), inc=("sv", 1))
            p.w("tensor", "sv")
        sv_xnf = p.n["sv"]

        sv_xbc = {}
        sv_xh = {}
        sa_silu = {}

        def emit_trans(fcd):
            xb_d = xbca if fcd % 2 == 0 else xbcaB
            p.w("tensor", "sa", sa_silu[fcd])
            if fcd >= 2:
                p.w("tensor", "sv", sv_xh[fcd - 2])
            else:
                p.w("tensor", "sv")
            for si in range(nsub_t):
                p.do("tensor", lambda t, si=si, xb_d=xb_d: t.transpose(
                    pTb[:, si * 128:si * 128 + 128],
                    xb_d[:, si * 128:(si + 1) * 128], idnb_s[:, :]),
                    inc=("sp", 1))
            p.w("vector", "sp")
            xhb = xhtm[:, (t0 // 128) * D_INNER + fcd * 128:
                       (t0 // 128) * D_INNER + fcd * 128 + 1]
            xhap = bass.AP(xhb.tensor, xhb.offset,
                           [[NSUB * D_INNER, 128], [D_INNER, nsub_t],
                            [1, 128]])
            p.do("vector", lambda v, xhap=xhap, nsub_t=nsub_t:
                 v.tensor_copy(xhap, pTb[:, 0:nsub_t * 128]),
                 inc=("sv", 1))
            sv_xh[fcd] = p.n["sv"]

        for fc in range(17):
            f0 = D_INNER + fc * 128
            fw = min(128, D_IN_PROJ - 32 - f0)
            pcol = 512 * (fc % 2)
            xb = xbca if fc % 2 == 0 else xbcaB
            if fc >= 2:
                p.w("tensor", "sv", sv_xbc[fc - 2])
            else:
                p.w("tensor", "sv")
            for kc in range(KC):
                p.do("tensor", lambda t, kc=kc, f0=f0, fw=fw, tw=tw,
                     pcol=pcol: t.matmul(
                         pB2[0:fw, pcol:pcol + tw],
                         winb[:, kc * D_IN_PROJ + f0: kc * D_IN_PROJ + f0 + fw],
                         xnf[:, kc * 512: kc * 512 + tw],
                         start=(kc == 0), stop=(kc == KC - 1)), inc=("sp", 1))
            if fc >= 1:
                emit_trans(fc - 1)
            b0 = 516 * tpar
            p.w("vector", "sp")
            if t0 == 0:
                p.do("vector", lambda v, b0=b0, fc=fc: v.tensor_copy(
                    xbcbuf[:, b0:b0 + 3], xh3buf[:, fc * 3:fc * 3 + 3]),
                    inc=("sv", 1))
            else:
                p.do("vector", lambda v, b0=b0, fc=fc: v.tensor_copy(
                    xbcbuf[:, b0:b0 + 3], xbchal[:, fc * 3:fc * 3 + 3]),
                    inc=("sv", 1))
            p.do("vector", lambda v, b0=b0, fw=fw, tw=tw, pcol=pcol:
                 v.tensor_copy(
                     xbcbuf[0:fw, b0 + 3:b0 + 3 + tw],
                     pB2[0:fw, pcol:pcol + tw]), inc=("sv", 1))
            sv_xbc[fc] = p.n["sv"]
            p.w("vector", "sv")
            p.do("vector", lambda v, b0=b0, tw=tw, fc=fc: v.tensor_copy(
                xbchal[:, fc * 3:fc * 3 + 3], xbcbuf[:, b0 + tw:b0 + tw + 3]),
                inc=("sv", 1))
            p.w("vector", "sv")
            p.w("vector", "sa")
            p.do("vector", lambda v, b0=b0, tw=tw, fc=fc: v.tensor_scalar(
                convt[:, 0:tw], xbcbuf[:, b0:b0 + tw],
                convw_s[:, fc * 4:fc * 4 + 1], None, ALU.mult), inc=("sv", 1))
            for k in (1, 2, 3):
                p.w("vector", "sv")
                p.do("vector", lambda v, b0=b0, tw=tw, fc=fc, k=k:
                     v.scalar_tensor_tensor(
                         convt[:, 0:tw], xbcbuf[:, b0 + k:b0 + k + tw],
                         convw_s[:, fc * 4 + k:fc * 4 + k + 1], convt[:, 0:tw],
                         ALU.mult, ALU.add), inc=("sv", 1))
            p.w("scalar", "sv")
            p.w("scalar", "q2")
            p.do("scalar", lambda sc, tw=tw, fc=fc, xb=xb: sc.activation(
                xb[:, 0:tw], convt[:, 0:tw], AF.Silu,
                bias=convb_s[:, fc:fc + 1]), inc=("sa", 1))
            sa_silu[fc] = p.n["sa"]
            if fc == 16:
                p.w("vector", "sa")
                p.do("vector", lambda v, t0=t0, tw=tw, xb=xb: v.tensor_copy(
                    bfm[:, t0:t0 + tw], xb[0:16, 0:tw]), inc=("sv", 1))
                p.w("sync", "sa")
                for cg in range(3):
                    dma("sync", cfm4[32 * cg:32 * cg + 16, t0:t0 + tw],
                        xb[16:32, 0:tw], sem="q2")
                p.w("tensor", "sa")
                p.w("tensor", "sv")
                for si in range(nsub_t):
                    p.do("tensor", lambda t, si=si, xb=xb: t.transpose(
                        pTb[:, si * 128:si * 128 + 32],
                        xb[0:32, si * 128:(si + 1) * 128], idnb_s[0:32, 0:32]),
                        inc=("sp", 1))
                p.w("vector", "sp")
                ptb0 = pTb[:, 0:1]
                ptbap = bass.AP(ptb0.tensor, ptb0.offset,
                                [[1024, 128], [128, nsub_t], [1, 16]])
                p.do("vector", lambda v, ptbap=ptbap, t0=t0, nsub_t=nsub_t:
                     v.tensor_copy(
                         btm[:, (t0 // 128) * 16:(t0 // 128 + nsub_t) * 16],
                         ptbap), inc=("sv", 1))

        for si in range(nsub_t):
            gsub = t0 // 128 + si
            p.w("tensor", "sv")
            for kc in range(KC):
                for zs in range(4):
                    p.do("tensor", lambda t, kc=kc, si=si, zs=zs: t.matmul(
                        pB1[:, zs * 512:(zs + 1) * 512],
                        xnf[:, kc * 512 + si * 128: kc * 512 + si * 128 + 128],
                        winb[:, kc * D_IN_PROJ + zs * 512:
                             kc * D_IN_PROJ + (zs + 1) * 512],
                        start=(kc == 0), stop=(kc == KC - 1)), inc=("sp", 1))
                p.do("tensor", lambda t, kc=kc, si=si: t.matmul(
                    pB2[:, 1024:1056],
                    xnf[:, kc * 512 + si * 128: kc * 512 + si * 128 + 128],
                    winb[:, kc * D_IN_PROJ + 4128: kc * D_IN_PROJ + 4160],
                    start=(kc == 0), stop=(kc == KC - 1)), inc=("sp", 1))
            p.w("vector", "sp")
            p.w("scalar", "sp")
            p.w("scalar", "q3")
            p.do("scalar", lambda sc: sc.copy(zchunkA[:, :], pB1[:, 0:2048]),
                 inc=("sa", 1))
            sa_z = p.n["sa"]
            p.w("sync", "sa", sa_z)
            dma("sync", zsp_d[gsub * 128:(gsub + 1) * 128, :], zchunkA[:, :],
                sem="q3")
            p.do("vector", lambda v: v.tensor_tensor(
                dtraw[:, 0:32], pB2[:, 1024:1056], dtbrep_s[:, :], ALU.add),
                inc=("sv", 1))
            p.w("scalar", "sv")
            p.do("scalar", lambda sc: sc.activation(dtraw[:, 32:64],
                                                    dtraw[:, 0:32], AF.Exp),
                 inc=("sa", 1))
            p.w("scalar", "sa")
            p.do("scalar", lambda sc, gsub=gsub: sc.activation(
                dttm[:, gsub * 32:(gsub + 1) * 32], dtraw[:, 32:64], AF.Ln,
                bias=1.0), inc=("sa", 1))
            p.w("tensor", "sa")

    p.barrier()
    for gsub in range(NSUB):
        sl = gsub % 8
        p.do("tensor", lambda t, gsub=gsub, sl=sl: t.transpose(
            pB1[0:32, sl * 128:sl * 128 + 128],
            dttm[:, gsub * 32:(gsub + 1) * 32], idn_s[:, :]), inc=("sp", 1))
        p.w("vector", "sp")
        p.do("vector", lambda v, gsub=gsub, sl=sl: v.tensor_copy(
            acspad[:, 64 + gsub * 128:64 + (gsub + 1) * 128],
            pB1[0:32, sl * 128:sl * 128 + 128]), inc=("sv", 1))
        p.w("tensor", "sv")

    p.w("vector", "sv")
    p.do("vector", lambda v: v.tensor_scalar(
        acspad[:, 64:64 + GRID], acspad[:, 64:64 + GRID], nega_s[0:32, 0:1],
        None, ALU.mult), inc=("sv", 1))
    p.w("vector", "sv")
    p.w("gpsimd", "sv")
    p.do("gpsimd", lambda g: g.memset(acspad[:, 0:64], 0.0), inc=("sg", 1))
    p.w("vector", "sg")
    p.w("vector", "sv")
    p.do("vector", lambda v: v.tensor_tensor_scan(
        acspad[:, 64:64 + GRID], acspad[:, 64:64 + GRID],
        acspad[:, 64:64 + GRID], 0.0, ALU.add, ALU.bypass), inc=("sv", 1))
    p.w("vector", "sv")
    off_ap = acspad[:, 63:64]
    p.do("vector", lambda v: v.tensor_tensor(
        acsfm[:, :].rearrange("p (c u) -> p c u", u=64),
        acspad[:, 64:64 + GRID].rearrange("p (c u) -> p c u", u=64),
        bcast_free(off_ap, 32, [[64, NCH], [0, 64]]), ALU.subtract),
        inc=("sv", 1))
    p.w("vector", "sv")
    al_ap = acsfm[:, 63:64]
    p.do("vector", lambda v: v.tensor_copy(
        alast_s[:, :], bcast_free(al_ap, 32, [[64, NCH]])), inc=("sv", 1))
    p.w("scalar", "sv")
    p.do("scalar", lambda sc: sc.activation(expal_s[:, :], alast_s[:, :],
                                            AF.Exp), inc=("sa", 1))
    p.w("gpsimd", "sa")
    p.do("gpsimd", lambda g: g.memset(gamlog[:, :], 0.0), inc=("sg", 1))
    p.w("vector", "sg")
    p.do("vector", lambda v: v.tensor_tensor_scan(
        gamlog[:, 2:34], alast_s[:, 1:33], alast_s[:, 1:33], 0.0,
        ALU.add, ALU.bypass), inc=("sv", 1))
    p.w("scalar", "sv")
    p.do("scalar", lambda sc: sc.activation(expgam_s[:, :], gamlog[:, :],
                                            AF.Exp), inc=("sa", 1))
    p.do("vector", lambda v: v.tensor_reduce(eg_s[:, 0:1], alast_s[:, 1:33],
                                             AX.X, ALU.add), inc=("sv", 1))
    p.w("vector", "sv")
    p.do("vector", lambda v: v.tensor_reduce(eg_s[:, 1:2], alast_s[:, 1:32],
                                             AX.X, ALU.add), inc=("sv", 1))
    p.w("scalar", "sv")
    p.do("scalar", lambda sc: sc.activation(eg_s[:, :], eg_s[:, :], AF.Exp),
         inc=("sa", 1))
    p.w("sync", "sv")
    p.w("sync", "sa")
    dma("sync", acs_d[:, :], acsfm[:, :], sem="q4")
    p.do("sync", lambda e: e.dma_start(out=alast_d[:, :].transpose([1, 0]),
                                       in_=alast_s[:, :],
                                       allow_slow_non_contiguous=True),
         inc=("q4", 16))
    p.w("tensor", "sv")
    for gsub in range(NSUB):
        sl = gsub % 8
        p.do("tensor", lambda t, gsub=gsub, sl=sl: t.transpose(
            pB1[:, sl * 128:sl * 128 + 32],
            acsfm[:, gsub * 128:(gsub + 1) * 128], idn_s[0:32, 0:32]),
            inc=("sp", 1))
        p.w("vector", "sp")
        p.do("vector", lambda v, gsub=gsub, sl=sl: v.tensor_copy(
            acstm[:, gsub * 32:(gsub + 1) * 32],
            pB1[:, sl * 128:sl * 128 + 32]), inc=("sv", 1))
        p.w("tensor", "sv")

    p.barrier()
    p.do("gpsimd", lambda g: g.memset(srun[:, :], 0.0), inc=("sg", 1))
    p.w("vector", "sg")
    p.do("vector", lambda v: v.tensor_copy(pB2[0:80, 1024:1024 + SCW],
                                           srun[0:80, :]), inc=("sv", 1))
    p.w("gpsimd", "sg")
    p.do("gpsimd", lambda g: g.memset(flatbuf[:, :], 0.0), inc=("sg", 1))
    p.w("gpsimd", "sg")
    p.do("gpsimd", lambda g: g.memset(alrow[:, :], 0.0), inc=("sg", 1))
    p.w("vector", "sg")
    p.w("sync", "sg")

    def emit_xdt(ci):
        par = ci % 2
        gsub = ci // 2
        xsl = xhtm[par * 64:par * 64 + 64, gsub * D_INNER:(gsub + 1) * D_INNER]
        dsl = dttm[par * 64:par * 64 + 64, gsub * 32:(gsub + 1) * 32]
        p.do("vector", lambda v, xsl=xsl, dsl=dsl, par=par: v.tensor_tensor(
            xdt[par * 64:par * 64 + 64, :].rearrange("p (h q) -> p h q", h=32),
            xsl.rearrange("p (h q) -> p h q", h=32),
            bcast_free(dsl[:, 0:1], 64, [[1, 32], [0, 64]]), ALU.mult),
            inc=("sv", 1))

    for ci in range(1, 33):
        buf = ci % 2
        par = ci % 2
        gsub = ci // 2
        qa = "q5" if buf == 0 else "q6"
        qn = "q7" if buf == 0 else "q8"
        p.w("sync", "sa")
        p.w("sync", "sp")
        dma("sync", alrow[0:1, buf * 32:buf * 32 + 32], alast_d[ci:ci + 1, :],
            sem=qa)
        emit_xdt(ci)
        p.w("tensor", qa)
        p.w("tensor", "sv")
        p0_ = par * 64
        p.do("tensor", lambda t, buf=buf, p0_=p0_: t.matmul(
            pB1[p0_:p0_ + 64, 1920:1952], e064_s[:, :],
            alrow[:, buf * 32:buf * 32 + 32], start=True, stop=True),
            inc=("sp", 1))
        p.w("vector", "sp")
        asl = acstm[p0_:p0_ + 64, gsub * 32:(gsub + 1) * 32]
        p.do("vector", lambda v, asl=asl, buf=buf, p0_=p0_: v.tensor_tensor(
            decaytm[p0_:p0_ + 64, buf * 32:buf * 32 + 32],
            pB1[p0_:p0_ + 64, 1920:1952], asl, ALU.subtract), inc=("sv", 1))
        p.w("scalar", "sv")
        p.do("scalar", lambda sc, buf=buf, p0_=p0_: sc.activation(
            decaytm[p0_:p0_ + 64, buf * 32:buf * 32 + 32],
            decaytm[p0_:p0_ + 64, buf * 32:buf * 32 + 32], AF.Exp),
            inc=("sa", 1))
        bsl = btm[p0_:p0_ + 64, gsub * 16:(gsub + 1) * 16]
        dcl = decaytm[p0_:p0_ + 64, buf * 32:buf * 32 + 32]
        p.w("vector", "sa")
        p.do("vector", lambda v, bsl=bsl, dcl=dcl, p0_=p0_: v.tensor_tensor(
            bd[p0_:p0_ + 64, :].rearrange("p (h n) -> p h n", h=32),
            bcast_free(dcl[:, 0:1], 64, [[1, 32], [0, 16]]),
            bcast_free(bsl[:, 0:1], 64, [[0, 32], [1, 16]]), ALU.mult),
            inc=("sv", 1))
        p.w("tensor", "sv")
        for h in range(NHEADS):
            hg = h // 11
            hi = h - 11 * hg
            p.do("tensor", lambda t, hg=hg, hi=hi, h=h, p0_=p0_: t.matmul(
                pB2[32 * hg:32 * hg + 16, 1024 + 64 * hi:1024 + 64 * hi + 64],
                bd[p0_:p0_ + 64, h * 16:(h + 1) * 16],
                xdt[p0_:p0_ + 64, h * 64:(h + 1) * 64],
                start=True, stop=True), inc=("sp", 1))
        p.w("vector", qn)
        p.do("vector", lambda v, buf=buf: v.tensor_copy(
            snapbuf[0:80, buf * SCW:buf * SCW + SCW], srun[0:80, :]),
            inc=("sv", 1))
        if ci == 32:
            p.w("vector", "sv")
            p.do("vector", lambda v: v.tensor_copy(tcomb[0:80, :],
                                                   srun[0:80, :]),
                 inc=("sv", 1))
        p.w("sync", "sv")
        dma("sync", snap_d[(ci - 1) * 80:ci * 80, :],
            snapbuf[0:80, buf * SCW:buf * SCW + SCW], sem=qn)
        p.do("vector", lambda v, ci=ci: v.tensor_scalar(
            gmg[:, :], gmask_s[:, :], expal_s[:, ci:ci + 1], None, ALU.mult),
            inc=("sv", 1))
        p.w("tensor", "sv")
        p.do("tensor", lambda t: t.matmul(pB1[0:80, 1024:1536],
                                          hsel_s[:, 0:80], gmg[:, 0:512],
                                          start=True, stop=True),
             inc=("sp", 1))
        p.do("tensor", lambda t: t.matmul(pB1[0:80, 1536:1536 + SCW - 512],
                                          hsel_s[:, 0:80], gmg[:, 512:SCW],
                                          start=True, stop=True),
             inc=("sp", 1))
        p.w("vector", "sp")
        p.do("vector", lambda v: v.tensor_tensor(srun[0:80, :], srun[0:80, :],
                                                 pB1[0:80, 1024:1024 + SCW],
                                                 ALU.mult), inc=("sv", 1))
        p.w("vector", "sv")
        p.do("vector", lambda v: v.tensor_tensor(srun[0:80, :], srun[0:80, :],
                                                 pB2[0:80, 1024:1024 + SCW],
                                                 ALU.add), inc=("sv", 1))
        p.w("tensor", "sv")
    p.w("vector", "sv")
    p.w("vector", "q8")
    p.do("vector", lambda v: v.tensor_copy(snapbuf[0:80, SCW:2 * SCW],
                                           srun[0:80, :]), inc=("sv", 1))
    p.do("gpsimd", lambda g: g.memset(ucomb[0:32, :], 0.0), inc=("sg", 1))
    sv_ag = p.n["sv"]
    p.w("sync", "sv", sv_ag)
    p.w("sync", "sg")
    dma("sync", snap_d[32 * 80:33 * 80, :], snapbuf[0:80, SCW:2 * SCW],
        sem="q8")
    dma("sync", agin_d[0:80, :], srun[0:80, :], sem="q9")
    dma("sync", agin_d[80:160, :], tcomb[0:80, :], sem="q9")
    dma("sync", agin_d[160:192, :], ucomb[0:32, :], sem="q9")
    p.w("gpsimd", "q9")
    dma("gpsimd", agin_d[160:192, 0:2], eg_s[:, :], sem="st")
    p.w("gpsimd", "st")
    p.do("gpsimd", lambda g: g.collective_compute(
        "AllGather", ALU.bypass, replica_groups=[[0, 1, 2, 3], [4, 5, 6, 7]],
        ins=[agin_d[:, :].opt()], outs=[agout_d[:, :].opt()]), inc=("cc", 1))

    def f_load_offsets(e):
        for k in (1, 2, 3):
            p.env[f"so{k}"] = e.value_load(slotoff_s[k - 1:k, 0:1])
    p.w("sync", "cc")
    p.ops["sync"].append(f_load_offsets)
    for k in (1, 2, 3):
        p.do("sync", lambda e, k=k: e.dma_start(
            out=slotb[0:80, (k - 1) * SCW:k * SCW],
            in_=agout_d[bass.ds(p.env[f"so{k}"], 80), :]), inc=("q10", 16))
        p.do("sync", lambda e, k=k: e.dma_start(
            out=gslot[:, (k - 1) * 2:k * 2],
            in_=agout_d[bass.ds(p.env[f"so{k}"] + 160, 32), 0:2]),
            inc=("q10", 16))
    p.do("sync", lambda e: e.dma_start(
        out=p30b[0:80, :], in_=agout_d[bass.ds(p.env["so1"] + 80, 80), :]),
        inc=("q10", 16))

    def grep(gcol_ap):
        p.w("vector", "sp")
        p.do("vector", lambda v, g=gcol_ap: v.tensor_scalar(
            gmg[:, :], gmask_s[:, :], g, None, ALU.mult), inc=("sv", 1))
        p.w("tensor", "sv")
        p.do("tensor", lambda t: t.matmul(
            pB1[0:80, 1024:1536], hsel_s[:, 0:80], gmg[:, 0:512],
            start=True, stop=True), inc=("sp", 1))
        p.do("tensor", lambda t: t.matmul(
            pB1[0:80, 1536:1536 + SCW - 512], hsel_s[:, 0:80], gmg[:, 512:SCW],
            start=True, stop=True), inc=("sp", 1))

    p.w("vector", "q10")
    p.do("vector", lambda v: v.tensor_scalar(
        tcomb[0:80, :], slotb[0:80, 2 * SCW:3 * SCW], pmc_s[0:80, 2:3], None,
        ALU.mult), inc=("sv", 1))
    grep(gslot[:, 2:3])
    p.w("vector", "sp")
    p.w("vector", "sv")
    p.do("vector", lambda v: v.tensor_tensor(ucomb[0:80, :], tcomb[0:80, :],
                                             pB1[0:80, 1024:1024 + SCW],
                                             ALU.mult), inc=("sv", 1))
    p.w("vector", "sv")
    p.do("vector", lambda v: v.tensor_tensor(ucomb[0:80, :], ucomb[0:80, :],
                                             slotb[0:80, SCW:2 * SCW],
                                             ALU.add), inc=("sv", 1))
    p.w("vector", "sv")
    p.do("vector", lambda v: v.tensor_scalar(tcomb[0:80, :], ucomb[0:80, :],
                                             pmc_s[0:80, 1:2], None, ALU.mult),
         inc=("sv", 1))
    grep(gslot[:, 0:1])
    p.w("vector", "sp")
    p.w("vector", "sv")
    p.do("vector", lambda v: v.tensor_tensor(ucomb[0:80, :], tcomb[0:80, :],
                                             pB1[0:80, 1024:1024 + SCW],
                                             ALU.mult), inc=("sv", 1))
    p.w("vector", "sv")
    p.do("vector", lambda v: v.tensor_tensor(ucomb[0:80, :], ucomb[0:80, :],
                                             slotb[0:80, 0:SCW], ALU.add),
         inc=("sv", 1))
    p.w("vector", "sv")
    p.do("vector", lambda v: v.tensor_scalar(sin[0:80, :], ucomb[0:80, :],
                                             pmc_s[0:80, 0:1], None, ALU.mult),
         inc=("sv", 1))
    grep(gslot[:, 1:2])
    p.w("vector", "sp")
    p.w("vector", "sv")
    p.do("vector", lambda v: v.tensor_tensor(ucomb[0:80, :], tcomb[0:80, :],
                                             pB1[0:80, 1024:1024 + SCW],
                                             ALU.mult), inc=("sv", 1))
    p.w("vector", "sv")
    p.do("vector", lambda v: v.tensor_tensor(ucomb[0:80, :], ucomb[0:80, :],
                                             p30b[0:80, :], ALU.add),
         inc=("sv", 1))
    p.w("vector", "sv")
    p.do("vector", lambda v: v.tensor_scalar(stbuf[0:80, :], ucomb[0:80, :],
                                             pmc_s[0:80, 3:4], None, ALU.mult),
         inc=("sv", 1))

    p.barrier()
    for ci in range(NCH):
        par = ci % 2
        gsub = ci // 2
        buf = ci % 2
        p0_ = par * 64
        p.w("sync", "sp")
        fl0 = acs_d[0:1, ci * 64:ci * 64 + 1]
        p.do("sync", lambda e, fl0=fl0: e.dma_start(
            out=flatbuf[0:1, 0:2048],
            in_=bass.AP(fl0.tensor, fl0.offset, [[1, 1], [GRID, 32], [1, 64]])),
            inc=("q11", 16))
        p.w("tensor", "q11")
        p.w("tensor", "sv")
        p.do("tensor", lambda t, ci=ci, p0_=p0_: t.matmul(
            pB2[p0_:p0_ + 64, 1024:1088],
            bfm[:, ci * 64:(ci + 1) * 64], cfm4[0:16, ci * 64:(ci + 1) * 64],
            start=True, stop=True), inc=("sp", 1))
        p.w("vector", "sp")
        p.do("vector", lambda v, p0_=p0_: v.tensor_tensor(
            m1m[p0_:p0_ + 64, :], pB2[p0_:p0_ + 64, 1024:1088],
            tri_s[:, :] if p0_ == 0 else tri64hi[64:128, :], ALU.mult),
             inc=("sv", 1))
        p.w("tensor", "sv")
        for q in range(4):
            p.do("tensor", lambda t, q=q, p0_=p0_: t.matmul(
                pB1[p0_:p0_ + 64, q * 512:(q + 1) * 512], e064_s[:, :],
                flatbuf[:, q * 512:(q + 1) * 512],
                start=True, stop=False), inc=("sp", 1))
            p.do("tensor", lambda t, q=q, ci=ci, p0_=p0_: t.matmul(
                pB1[p0_:p0_ + 64, q * 512:(q + 1) * 512],
                acsfm[:, ci * 64:(ci + 1) * 64],
                negio_s[:, q * 512:(q + 1) * 512],
                start=False, stop=True), inc=("sp", 1))
        p.w("vector", "sp")
        p.do("vector", lambda v, p0_=p0_: v.tensor_scalar(
            dmin[p0_:p0_ + 64, :], pB1[p0_:p0_ + 64, 0:2048],
            0.0, None, ALU.min), inc=("sv", 1))
        p.w("scalar", "sv")
        p.do("scalar", lambda sc, p0_=p0_: sc.activation(
            ebuf[p0_:p0_ + 64, :], dmin[p0_:p0_ + 64, :], AF.Exp),
            inc=("sa", 1))
        p.w("vector", "sa")
        p.do("vector", lambda v, p0_=p0_: v.tensor_tensor(
            gbuf[p0_:p0_ + 64, :].rearrange("p (h l) -> p h l", h=32),
            ebuf[p0_:p0_ + 64, :].rearrange("p (h l) -> p h l", h=32),
            bcast_free(m1m[p0_:p0_ + 64, 0:1], 64, [[0, 32], [1, 64]]),
            ALU.mult), inc=("sv", 1))
        if ci + 1 <= NCH - 1:
            qm2 = "q14" if (ci + 1) % 2 == 0 else "q15"
            p.w("gpsimd", "sv")
            p.do("gpsimd", lambda g, ci=ci: g.dma_start(
                out=snapld[0:80, ((ci + 1) % 2) * SCW:((ci + 1) % 2 + 1) * SCW],
                in_=snap_d[ci * 80:(ci + 1) * 80, :]), inc=(qm2, 16))
        if ci == 0:
            pass
        else:
            grep(expgam_s[:, ci:ci + 1])
            p.w("vector", "sp")
            p.do("vector", lambda v: v.tensor_tensor(
                ucomb[0:80, :], sin[0:80, :], pB1[0:80, 1024:1024 + SCW],
                ALU.mult), inc=("sv", 1))
            p.w("vector", "sv")
            p.w("vector", "q14" if ci % 2 == 0 else "q15")
            p.do("vector", lambda v, ci=ci: v.tensor_tensor(
                stbuf[0:80, :], ucomb[0:80, :],
                snapld[0:80, (ci % 2) * SCW:(ci % 2 + 1) * SCW], ALU.add),
                inc=("sv", 1))
        emit_xdt(ci)
        asl2 = acstm[p0_:p0_ + 64, gsub * 32:(gsub + 1) * 32]
        p.w("tensor", "sv")
        for h in range(NHEADS):
            pp = 32 * (h % 4)
            ff = 64 * (h // 4)
            p.do("tensor", lambda t, h=h, p0_=p0_: t.matmul(
                pB2[p0_:p0_ + 64, h * 64:(h + 1) * 64],
                gbuf[p0_:p0_ + 64, h * 64:(h + 1) * 64],
                xdt[p0_:p0_ + 64, h * 64:(h + 1) * 64],
                start=True, stop=True), inc=("sp", 1))
            hg = h // 11
            hi = h - 11 * hg
            p.do("tensor", lambda t, h=h, hg=hg, hi=hi, ci=ci, p0_=p0_: t.matmul(
                pB1[p0_:p0_ + 64, h * 64:(h + 1) * 64],
                cfm4[32 * hg:32 * hg + 16, ci * 64:(ci + 1) * 64],
                stbuf[32 * hg:32 * hg + 16, 64 * hi:64 * hi + 64],
                start=True, stop=True), inc=("sp", 1))
        p.w("sync", "sv")
        p.w("sync", "sa")
        qz = "q12" if par == 0 else "q13"
        dma("sync", zchunk2[p0_:p0_ + 64, :], zsp_d[ci * 64:(ci + 1) * 64, :],
            sem=qz)
        p.w("scalar", qz)
        p.w("scalar", "sv")
        p.do("scalar", lambda sc, p0_=p0_: sc.activation(
            sz[p0_:p0_ + 64, :], zchunk2[p0_:p0_ + 64, :], AF.Silu),
            inc=("sa", 1))
        p.do("scalar", lambda sc, asl2=asl2, p0_=p0_: sc.activation(
            eacsb[p0_:p0_ + 64, 0:32], asl2, AF.Exp), inc=("sa", 1))
        p.w("vector", "sa")
        p.do("vector", lambda v, p0_=p0_, gsub=gsub: v.tensor_tensor(
            ytmp[p0_:p0_ + 64, :],
            xhtm[p0_:p0_ + 64, gsub * D_INNER:(gsub + 1) * D_INNER],
            dskrep_s[p0_:p0_ + 64, :], ALU.mult), inc=("sv", 1))
        p.w("vector", "sp")
        p.w("vector", "sv")
        p.do("vector", lambda v, p0_=p0_: v.tensor_tensor(
            ytmp[p0_:p0_ + 64, :], ytmp[p0_:p0_ + 64, :],
            pB2[p0_:p0_ + 64, 0:2048], ALU.add), inc=("sv", 1))
        p.w("vector", "sv")
        p.do("vector", lambda v, p0_=p0_: v.tensor_tensor(
            ytmp2[p0_:p0_ + 64, :].rearrange("p (h q) -> p h q", h=32),
            pB1[p0_:p0_ + 64, 0:2048].rearrange("p (h q) -> p h q", h=32),
            bcast_free(eacsb[p0_:p0_ + 64, 0:1], 64, [[1, 32], [0, 64]]),
            ALU.mult), inc=("sv", 1))
        p.w("vector", "sv")
        p.do("vector", lambda v, p0_=p0_: v.tensor_tensor(
            ytmp[p0_:p0_ + 64, :], ytmp[p0_:p0_ + 64, :],
            ytmp2[p0_:p0_ + 64, :], ALU.add), inc=("sv", 1))
        p.w("vector", "sv")
        p.do("vector", lambda v, p0_=p0_: v.tensor_tensor(
            ytmp[p0_:p0_ + 64, :], ytmp[p0_:p0_ + 64, :], sz[p0_:p0_ + 64, :],
            ALU.mult), inc=("sv", 1))
        p.w("vector", "sv")
        for q in range(4):
            p.do("vector", lambda v, q=q, p0_=p0_: v.bn_stats(
                stat6[p0_:p0_ + 64, 6 * q:6 * q + 6],
                ytmp[p0_:p0_ + 64, 512 * q:512 * (q + 1)]), inc=("sv", 1))
        p.w("vector", "sv")
        p.do("vector", lambda v, p0_=p0_: v.bn_aggr(
            stat2[p0_:p0_ + 64, 0:2], stat6[p0_:p0_ + 64, 0:24]),
            inc=("sv", 1))
        p.w("vector", "sv")
        p.do("vector", lambda v, p0_=p0_: v.tensor_tensor(
            rmscol[p0_:p0_ + 64, 0:1], stat2[p0_:p0_ + 64, 0:1],
            stat2[p0_:p0_ + 64, 0:1], ALU.mult), inc=("sv", 1))
        p.w("vector", "sv")
        p.do("vector", lambda v, p0_=p0_: v.tensor_tensor(
            rmscol[p0_:p0_ + 64, 0:1], rmscol[p0_:p0_ + 64, 0:1],
            stat2[p0_:p0_ + 64, 1:2], ALU.add), inc=("sv", 1))
        p.w("scalar", "sv")
        p.do("scalar", lambda sc, p0_=p0_: sc.activation(
            rmscol[p0_:p0_ + 64, 1:2], rmscol[p0_:p0_ + 64, 0:1], AF.Ln,
            bias=EPS), inc=("sa", 1))
        p.w("scalar", "sa")
        p.do("scalar", lambda sc, p0_=p0_: sc.activation(
            rmscol[p0_:p0_ + 64, 1:2], rmscol[p0_:p0_ + 64, 1:2], AF.Exp,
            scale=-0.5), inc=("sa", 1))
        p.w("vector", "sa")
        p.do("vector", lambda v, p0_=p0_, gsub=gsub: v.scalar_tensor_tensor(
            xhtm[p0_:p0_ + 64, gsub * D_INNER:(gsub + 1) * D_INNER],
            ytmp[p0_:p0_ + 64, :], rmscol[p0_:p0_ + 64, 1:2],
            nwrep_s[p0_:p0_ + 64, :], ALU.mult, ALU.mult), inc=("sv", 1))
        p.w("tensor", "sv")

    p.barrier()
    for kc16 in range(16):
        dma("gpsimd", woutb[:, kc16 * D_MODEL:(kc16 + 1) * D_MODEL],
            wout_d[kc16 * 128:(kc16 + 1) * 128, :], sem="q16")
    p.w("tensor", "q16")
    sa_motm = {}
    for su in range(NSUB):
        p.w("tensor", "sv")
        for half in range(2):
            for sl in range(8):
                fc = half * 8 + sl
                p.do("tensor", lambda t, su=su, fc=fc, sl=sl: t.transpose(
                    pTb[:, sl * 128:sl * 128 + 128],
                    xhtm[:, su * D_INNER + fc * 128:
                         su * D_INNER + fc * 128 + 128],
                    idnb_s[:, :]), inc=("sp", 1))
            p.w("vector", "sp")
            p.do("vector", lambda v, half=half: v.tensor_copy(
                ynf[:, half * 1024:(half + 1) * 1024], pTb[:, 0:1024]),
                inc=("sv", 1))
            p.w("tensor", "sv")
        pp3 = pB1 if su % 2 == 0 else pB2
        if su >= 2:
            p.w("tensor", "sa", sa_motm[su - 2])
        for fc in range(16):
            for q in range(2):
                p.do("tensor", lambda t, fc=fc, q=q, pp3=pp3: t.matmul(
                    pp3[:, q * 512:(q + 1) * 512],
                    ynf[:, fc * 128:(fc + 1) * 128],
                    woutb[:, fc * D_MODEL + q * 512: fc * D_MODEL + (q + 1) * 512],
                    start=(fc == 0), stop=(fc == 15)), inc=("sp", 1))
        p.w("scalar", "sp")
        p.w("scalar", "sv")
        p.do("scalar", lambda sc, pp3=pp3: sc.copy(motm[:, :], pp3[:, 0:1024]),
             inc=("sa", 1))
        sa_motm[su] = p.n["sa"]
        p.w("vector", "sa")
        p.w("vector", "sv")
        ln_tm(motm[:, :], motm[:, :], D_MODEL)
        p.w("sync", "sv")
        p.w("sync", "q18")
        dma("sync", xgtm[:, :], xg_d[su * 128:(su + 1) * 128, :], sem="q17")
        p.w("vector", "q17")
        p.w("vector", "sa")
        p.do("vector", lambda v: v.tensor_tensor(xgtm[:, :], motm[:, :],
                                                 xgtm[:, :], ALU.add),
             inc=("sv", 1))
        p.w("vector", "sv")
        p.do("vector", lambda v, su=su: v.tensor_scalar(
            xgtm[:, :], xgtm[:, :], maskrow_s[:, su:su + 1], None, ALU.mult),
            inc=("sv", 1))
        sv_x2 = p.n["sv"]
        p.w("sync", "sv", sv_x2)
        dma("sync", x2_d[su * 128:(su + 1) * 128, :], xgtm[:, :], sem="q18")

    p.barrier()
    for su in range(NSUB):
        p.w("sync", "sv")
        p.w("sync", "sa")
        dma("sync", x2ld1[:, :], x2_d[su * 128:(su + 1) * 128, :], sem="q19")
        p.w("vector", "q19")
        ln_tm(x2ld1[:, :], x2n_t[:, :], D_MODEL)
        p.w("vector", "sv")
        p.w("vector", "sa")
        p.do("vector", lambda v: v.tensor_reduce(
            qcol[:, 0:1], x2n_t[:, 0:1024], AX.X, ALU.max,
            apply_absolute_value=True), inc=("sv", 1))
        p.w("vector", "sv")
        p.do("vector", lambda v: v.tensor_scalar(qcol[:, 0:1], qcol[:, 0:1],
                                                 1e-5, None, ALU.max),
             inc=("sv", 1))
        p.w("vector", "sv")
        p.do("vector", lambda v: v.reciprocal(qcol[:, 1:2], qcol[:, 0:1]),
             inc=("sv", 1))
        p.w("vector", "sv")
        p.do("vector", lambda v: v.tensor_scalar(qcol[:, 1:2], qcol[:, 1:2],
                                                 127.0, None, ALU.mult),
             inc=("sv", 1))
        p.do("vector", lambda v: v.tensor_scalar(qcol[:, 2:3], qcol[:, 0:1],
                                                 1.0 / 127.0, None, ALU.mult),
             inc=("sv", 1))
        p.w("vector", "sv")
        p.do("vector", lambda v: v.tensor_scalar(qv1[:, :], x2n_t[:, :],
                                                 qcol[:, 1:2], None, ALU.mult),
             inc=("sv", 1))
        p.w("vector", "sv")
        p.do("vector", lambda v: v.tensor_copy(qyi1[:, :], qv1[:, :]),
             inc=("sv", 1))
        p.w("vector", "sv")
        p.do("vector", lambda v: v.tensor_copy(qy1[:, :], qyi1[:, :]),
             inc=("sv", 1))
        p.w("vector", "sv")
        p.do("vector", lambda v: v.tensor_tensor(qr1[:, :], qv1[:, :],
                                                 qy1[:, :], ALU.subtract),
             inc=("sv", 1))
        p.w("vector", "sv")
        p.do("vector", lambda v: v.scalar_tensor_tensor(
            qt11[:, :], qr1[:, :], 0.5, qy1[:, :], ALU.is_gt,
            ALU.add), inc=("sv", 1))
        p.w("vector", "sv")
        p.do("vector", lambda v: v.scalar_tensor_tensor(
            qv1[:, :], qr1[:, :], -0.5, qt11[:, :], ALU.is_ge,
            ALU.add), inc=("sv", 1))
        p.w("vector", "sv")
        p.do("vector", lambda v: v.tensor_scalar(
            xq1tm[:, :], qv1[:, :], -1.0, qcol[:, 2:3], ALU.add, ALU.mult),
            inc=("sv", 1))
        p.w("tensor", "sv")
        for kc in range(KC):
            sl = kc % 8
            p.do("tensor", lambda t, kc=kc, sl=sl: t.transpose(
                pTb[:, sl * 128:sl * 128 + 128],
                xq1tm[:, kc * 128:(kc + 1) * 128], idnb_s[:, :]), inc=("sp", 1))
        p.w("vector", "sp")
        xq1b = xq1fm[:, su * 128:su * 128 + 1]
        xq1ap = bass.AP(xq1b.tensor, xq1b.offset,
                        [[KC * GRID, 128], [GRID, KC], [1, 128]])
        p.do("vector", lambda v, xq1ap=xq1ap: v.tensor_copy(
            xq1ap, pTb[:, 0:1024]), inc=("sv", 1))

    p.barrier()
    for kc in range(KC):
        dma("sync", w1qb[:, kc * D_HIDDEN:(kc + 1) * D_HIDDEN],
            w1q_d[kc * 128:(kc + 1) * 128, :], sem="q20")
    p.w("tensor", "q20")
    sa_tile0 = {}
    sa_end = {}
    for hf in range(32):
        pp = pB1 if hf % 2 == 0 else pB2
        hb = h1buf if hf % 2 == 0 else h1bufB
        hc = hconv if hf % 2 == 0 else hconvB
        if hf >= 2:
            p.w("tensor", "sa", sa_end[hf - 2])
        for q, (t0, tw) in enumerate(PA_TILES):
            reg = q % 4
            for kc in range(KC):
                p.do("tensor", lambda t, hf=hf, kc=kc, t0=t0, tw=tw, reg=reg,
                     pp=pp: t.matmul(
                         pp[:, reg * 512: reg * 512 + tw],
                         w1qb[:, kc * D_HIDDEN + hf * 128:
                              kc * D_HIDDEN + hf * 128 + 128],
                         xq1fm[:, kc * GRID + t0: kc * GRID + t0 + tw],
                         start=(kc == 0), stop=(kc == KC - 1)), inc=("sp", 1))
            p.w("scalar", "sp")
            if q == 0:
                p.w("scalar", "sv")
            p.do("scalar", lambda sc, t0=t0, tw=tw, reg=reg, pp=pp, hb=hb:
                 sc.copy(hb[:, t0:t0 + tw],
                         pp[:, reg * 512: reg * 512 + tw]), inc=("sa", 1))
            if q == 0:
                sa_tile0[hf] = p.n["sa"]
            if reg == 3:
                p.w("tensor", "sa", sa_tile0[hf])
        sa_end[hf] = p.n["sa"]
        p.w("vector", "sv")
        p.w("vector", "sa")
        p.do("vector", lambda v, hf=hf, hb=hb, hc=hc: v.tensor_scalar(
            hc[:, :], hb[:, :], fcw_s[:, hf * 3 + 1:hf * 3 + 2], None,
            ALU.mult), inc=("sv", 1))
        p.w("vector", "sv")
        p.do("vector", lambda v, hf=hf, hb=hb, hc=hc: v.scalar_tensor_tensor(
            hc[:, 1:GRID], hb[:, 0:GRID - 1],
            fcw_s[:, hf * 3:hf * 3 + 1], hc[:, 1:GRID], ALU.mult, ALU.add),
            inc=("sv", 1))
        p.w("vector", "sv")
        p.do("vector", lambda v, hf=hf, hb=hb, hc=hc: v.scalar_tensor_tensor(
            hc[:, 0:GRID - 1], hb[:, 1:GRID],
            fcw_s[:, hf * 3 + 2:hf * 3 + 3], hc[:, 0:GRID - 1], ALU.mult,
            ALU.add), inc=("sv", 1))
        p.w("scalar", "sv")
        p.w("scalar", "q21")
        p.do("scalar", lambda sc, hf=hf, hc=hc: sc.activation(
            hpostb[:, :], hc[:, :], AF.Silu, bias=fcb_s[:, hf:hf + 1]),
            inc=("sa", 1))
        sa_here = p.n["sa"]
        p.w("sync", "sa", sa_here)
        dma("sync", hpost_d[hf * 128:(hf + 1) * 128, :], hpostb[:, :],
            sem="q21")

    p.barrier()
    for q32 in range(32):
        dma("sync", w2qb[:, q32 * D_MODEL:(q32 + 1) * D_MODEL],
            w2q_d[q32 * 128:(q32 + 1) * 128, :], sem="q22")
    p.w("tensor", "q22")
    for su in range(NSUB):
        p.w("sync", "sv")
        hp0 = hpost_d[0:1, su * 128:su * 128 + 1]
        p.do("sync", lambda e, hp0=hp0: e.dma_start(
            out=xq2fm[:, :],
            in_=bass.AP(hp0.tensor, hp0.offset,
                        [[GRID, 128], [128 * GRID, 32], [1, 128]])),
            inc=("q23", 16))
        p.w("tensor", "q23")
        p.w("tensor", "sv")
        for half in range(4):
            for sl in range(8):
                fc = half * 8 + sl
                p.do("tensor", lambda t, fc=fc, sl=sl: t.transpose(
                    pTb[:, sl * 128:sl * 128 + 128],
                    xq2fm[:, fc * 128:(fc + 1) * 128], idnb_s[:, :]),
                    inc=("sp", 1))
            p.w("vector", "sp")
            p.do("vector", lambda v, half=half: v.tensor_copy(
                htmb[:, half * 1024:(half + 1) * 1024], pTb[:, 0:1024]),
                inc=("sv", 1))
            p.w("tensor", "sv")
        p.w("vector", "sv")
        p.do("vector", lambda v: v.tensor_reduce(
            qcol[:, 4:5], htmb[:, :], AX.X, ALU.max,
            apply_absolute_value=True), inc=("sv", 1))
        p.w("vector", "sv")
        p.do("vector", lambda v: v.tensor_scalar(qcol[:, 4:5], qcol[:, 4:5],
                                                 1e-5, None, ALU.max),
             inc=("sv", 1))
        p.w("vector", "sv")
        p.do("vector", lambda v: v.reciprocal(qcol[:, 5:6], qcol[:, 4:5]),
             inc=("sv", 1))
        p.w("vector", "sv")
        p.do("vector", lambda v: v.tensor_scalar(qcol[:, 5:6], qcol[:, 5:6],
                                                 127.0, None, ALU.mult),
             inc=("sv", 1))
        p.do("vector", lambda v: v.tensor_scalar(qcol[:, 6:7], qcol[:, 4:5],
                                                 1.0 / 127.0, None, ALU.mult),
             inc=("sv", 1))
        p.w("vector", "sv")
        p.do("vector", lambda v: v.tensor_scalar(qv4[:, :], htmb[:, :],
                                                 qcol[:, 5:6], None, ALU.mult),
             inc=("sv", 1))
        p.w("vector", "sv")
        p.do("vector", lambda v: v.tensor_copy(qyi4[:, :], qv4[:, :]),
             inc=("sv", 1))
        p.w("vector", "sv")
        p.do("vector", lambda v: v.tensor_copy(qy4[:, :], qyi4[:, :]),
             inc=("sv", 1))
        p.w("vector", "sv")
        p.do("vector", lambda v: v.tensor_tensor(qr4[:, :], qv4[:, :], qy4[:, :],
                                                 ALU.subtract), inc=("sv", 1))
        p.w("vector", "sv")
        p.do("vector", lambda v: v.scalar_tensor_tensor(
            qt14[:, :], qr4[:, :], 0.5, qy4[:, :], ALU.is_gt, ALU.add),
            inc=("sv", 1))
        p.w("vector", "sv")
        p.do("vector", lambda v: v.scalar_tensor_tensor(
            qv4[:, :], qr4[:, :], -0.5, qt14[:, :], ALU.is_ge, ALU.add),
            inc=("sv", 1))
        p.w("vector", "sv")
        p.do("vector", lambda v: v.tensor_scalar(xq2tm[:, :], qv4[:, :], -1.0,
                                                 None, ALU.add), inc=("sv", 1))
        p.w("tensor", "sv")
        for half in range(4):
            for sl in range(8):
                fc = half * 8 + sl
                p.do("tensor", lambda t, fc=fc, sl=sl: t.transpose(
                    pTb[:, sl * 128:sl * 128 + 128],
                    xq2tm[:, fc * 128:(fc + 1) * 128], idnb_s[:, :]),
                    inc=("sp", 1))
            p.w("vector", "sp")
            p.do("vector", lambda v, half=half: v.tensor_copy(
                xq2fm[:, half * 1024:(half + 1) * 1024], pTb[:, 0:1024]),
                inc=("sv", 1))
            p.w("tensor", "sv")
        p.w("tensor", "sv")
        for fc in range(32):
            for q in range(2):
                p.do("tensor", lambda t, fc=fc, q=q: t.matmul(
                    pB1[:, q * 512:(q + 1) * 512],
                    xq2fm[:, fc * 128:(fc + 1) * 128],
                    w2qb[:, fc * D_MODEL + q * 512: fc * D_MODEL + (q + 1) * 512],
                    start=(fc == 0), stop=(fc == 31)), inc=("sp", 1))
        p.w("scalar", "sp")
        p.w("scalar", "q25")
        p.w("scalar", "sv")
        p.do("scalar", lambda sc: sc.activation(fotm[:, :], pB1[:, 0:1024],
                                                AF.Copy, scale=qcol[:, 6:7]),
             inc=("sa", 1))
        p.w("vector", "sa")
        ln_tm(fotm[:, :], fotm[:, :], D_MODEL)
        p.w("sync", "sv")
        dma("sync", x2ld3[:, :], x2_d[su * 128:(su + 1) * 128, :], sem="q24")
        p.w("vector", "q24")
        p.w("vector", "sa")
        p.do("vector", lambda v: v.tensor_tensor(fotm[:, :], fotm[:, :],
                                                 x2ld3[:, :], ALU.add),
             inc=("sv", 1))
        sv_out = p.n["sv"]
        p.w("sync", "sv", sv_out)
        if su == 0:
            dma("sync", out_d[0:64, :], fotm[64:128, :], sem="q25")
        elif su == NSUB - 1:
            dma("sync", out_d[SEG - 64:SEG, :], fotm[0:64, :], sem="q25")
        else:
            dma("sync", out_d[su * 128 - 64:(su + 1) * 128 - 64, :],
                fotm[:, :], sem="q25")
        p.w("tensor", "sv")

    p.w("sync", "q25")

    with nc.Block() as block:
        p.emit(block)
    ctx.close()
    return nc



def kernel(**inputs):
    x = np.ascontiguousarray(np.asarray(inputs["L0_attn_x"], np.float32))
    shards = _host_shards(x)
    consts = _host_consts(inputs)
    nc = build_nc()
    in_maps = []
    for core in range(8):
        m = dict(consts)
        m.update(shards[core])
        in_maps.append(m)
    res = run_bass_kernel_spmd(nc, in_maps, core_ids=list(range(8)))
    out = np.zeros((2, SEQ, D_MODEL), np.float32)
    for core in range(8):
        b, j = core // NSEG, core % NSEG
        out[b, j * SEG:(j + 1) * SEG] = res.results[core]["out"]
    return out


# version 08
# speedup vs baseline: 1.3089x; 1.0046x over previous
import numpy as np
import ml_dtypes
import concourse.bass as bass
import concourse.mybir as mybir
from concourse.bass_utils import run_bass_kernel_spmd
from contextlib import ExitStack

F32 = mybir.dt.float32
BF16 = mybir.dt.bfloat16
I32 = mybir.dt.int32
U8 = mybir.dt.uint8
AF = mybir.ActivationFunctionType
ALU = mybir.AluOpType
AX = mybir.AxisListType

D_MODEL = 1024
D_STATE = 16
HEADDIM = 64
CHUNK = 64
D_INNER = 2048
NHEADS = 32
CONV_DIM = 2080
D_IN_PROJ = 4160
D_HIDDEN = 4096
SEQ = 8192
SEG = 2048
NSEG = 4
GRID = 2176
NCH = 34
NSUB = 17
KC = 8
EPS = 1e-5
SLOT = 192
SCW = 704

ENGINES = ("sync", "vector", "scalar", "tensor", "gpsimd")
PA_TILES = [(0, 512), (512, 512), (1024, 512), (1536, 512), (2048, 128)]


def bcast_free(ap_col, part, dims):
    return bass.AP(ap_col.tensor, ap_col.offset,
                   [[ap_col.ap[0][0], part]] + [list(d) for d in dims])



def _host_shards(x):
    shards = []
    for core in range(8):
        b, j = core // NSEG, core % NSEG
        start = j * SEG
        lo = start - CHUNK
        g = np.zeros((GRID, D_MODEL), np.float32)
        s0, s1 = max(lo, 0), min(start + SEG + CHUNK, SEQ)
        g[s0 - lo:s1 - lo] = x[b, s0:s1]
        h3 = np.zeros((4, D_MODEL), np.float32)
        h0, h1 = max(lo - 3, 0), max(lo, 0)
        h3[3 - (h1 - h0):3] = x[b, h0:h1]
        maskrow = np.ones((NSUB * 128,), np.float32)
        if j == 0:
            maskrow[CHUNK - 1] = 0.0
        if j == NSEG - 1:
            maskrow[SEG + CHUNK] = 0.0
        mask_tm = np.ascontiguousarray(maskrow.reshape(NSUB, 128).T)
        pm = np.zeros((128, 4), np.float32)
        for k in (1, 2, 3):
            pm[:, k - 1] = 1.0 if j >= k else 0.0
        pm[:, 3] = 1.0 if j >= 1 else 0.0
        slot = np.zeros((4, 1), np.int32)
        for k in (1, 2, 3):
            slot[k - 1, 0] = SLOT * max(j - k, 0)
        shards.append(dict(xg=g, xh3=h3, maskrow=mask_tm, pmc=pm, slotoff=slot))
    return shards


def _host_consts(inputs):
    c = {}
    c["idn"] = np.eye(128, dtype=np.float32)
    cp = np.zeros((128, 4), np.float32)
    cp[:, 0] = 1e-5; cp[:, 1] = 1.0; cp[:, 2] = -0.5; cp[:, 3] = 0.0
    c["cpool"] = cp
    e = np.zeros((32, 64), np.float32); e[0, :] = 1.0
    c["e064"] = e
    e2 = np.zeros((32, 128), np.float32); e2[0, :] = 1.0
    c["e0128"] = e2
    c["onescol"] = np.ones((128, 1), np.float32)
    hsel = np.zeros((32, 128), np.float32)
    for h in range(32):
        g = h // 11
        hsel[h, 32 * g:32 * g + 16] = 1.0
    c["hsel"] = hsel
    gmask = np.zeros((32, SCW), np.float32)
    for h in range(32):
        i = h - 11 * (h // 11)
        gmask[h, 64 * i:64 * i + 64] = 1.0
    c["gmask"] = gmask
    negio = np.zeros((32, 2048), np.float32)
    for h in range(32):
        negio[h, 64 * h:64 * h + 64] = -1.0
    c["negio"] = negio
    t = np.zeros((64, 64), np.float32)
    for s in range(64):
        t[s, s:] = 1.0
    c["tri"] = t
    dtb = np.asarray(inputs["L0_attn_dt_bias"], np.float32)
    c["dtbrep"] = np.ascontiguousarray(np.tile(dtb[None, :], (128, 1)))
    Dsk = np.asarray(inputs["L0_attn_D"], np.float32)
    c["dskrep"] = np.ascontiguousarray(
        np.tile(np.repeat(Dsk, HEADDIM)[None, :], (128, 1)))
    nw = np.asarray(inputs["L0_attn_norm_w"], np.float32)
    c["nwrep"] = np.ascontiguousarray(np.tile(nw[None, :], (128, 1)))
    c["alog"] = np.ascontiguousarray(
        np.asarray(inputs["L0_attn_A_log"], np.float32).reshape(32, 1))
    cw = np.asarray(inputs["L0_attn_conv_w"], np.float32)
    cwp = np.zeros((17 * 128, 4), np.float32)
    cwp[:CONV_DIM] = cw
    c["convw"] = np.ascontiguousarray(
        cwp.reshape(17, 128, 4).transpose(1, 0, 2).reshape(128, 68))
    cb = np.asarray(inputs["L0_attn_conv_b"], np.float32)
    cbp = np.zeros((17 * 128,), np.float32)
    cbp[:CONV_DIM] = cb
    c["convb"] = np.ascontiguousarray(cbp.reshape(17, 128).T)
    fcw = np.asarray(inputs["L0_ffn_conv_w"], np.float32).reshape(D_HIDDEN, 3)
    c["fcw"] = np.ascontiguousarray(
        fcw.reshape(32, 128, 3).transpose(1, 0, 2).reshape(128, 96))
    fcb = np.asarray(inputs["L0_ffn_conv_b"], np.float32)
    c["fcb"] = np.ascontiguousarray(fcb.reshape(32, 128).T)
    c["wint"] = np.ascontiguousarray(np.asarray(inputs["L0_attn_W_in"], np.float32).T)
    c["wout"] = np.ascontiguousarray(np.asarray(inputs["L0_attn_W_out"], np.float32).T)
    c["w1t"] = np.ascontiguousarray(np.asarray(inputs["L0_ffn_w1"], np.float32).T)
    c["w2t"] = np.ascontiguousarray(np.asarray(inputs["L0_ffn_w2"], np.float32).T)
    return c



class P:
    def __init__(self, nc, ctx):
        self.nc = nc
        self.ops = {e: [] for e in ENGINES}
        self.h = {}
        self.n = {}
        for name in ("qs", "qg", "sv", "sa", "sp", "sg", "cc", "st",
                     "qw", "qe", "q1", "q2", "q3", "q4", "q5", "q6", "q7",
                     "q8", "q9", "q10", "q11", "q12", "q13", "q14", "q15",
                     "q16", "q17", "q18", "q19", "q20", "q21", "q22", "q23",
                     "q24", "q25", "qw2"):
            self.h[name] = ctx.enter_context(nc.semaphore(name))
            self.n[name] = 0
        self.env = {}

    def do(self, eng, f, inc=None):
        if inc is not None:
            sem, amt = inc
            self.n[sem] += amt
            handle = self.h[sem]

            def g(e, f=f, handle=handle, amt=amt):
                f(e).then_inc(handle, amt)
        else:
            def g(e, f=f):
                f(e)
        self.ops[eng].append(g)

    def w(self, eng, sem, val=None):
        v = self.n[sem] if val is None else val
        h = self.h[sem]
        self.ops[eng].append(lambda e, h=h, v=v: e.wait_ge(h, v))

    def barrier(self, engs=ENGINES, sems=None):
        if sems is None:
            sems = [k for k in self.n if k != "st"]
        for eng in engs:
            for s in sems:
                if self.n[s] > 0:
                    self.w(eng, s)

    def emit(self, block):
        for e in ENGINES:
            fs = self.ops[e]
            getattr(block, e)(lambda eng, fs=fs: [f(eng) for f in fs])



def build_nc():
    nc = bass.Bass("TRN2", target_bir_lowering=False, dynamic_dma_scratch_size=8192)
    ctx = ExitStack()

    def din(name, shape, dtype=F32):
        return nc.dram_tensor(name, list(shape), dtype, kind="ExternalInput").ap()

    def dint(name, shape, dtype=F32):
        return nc.dram_tensor(name, list(shape), dtype, kind="Internal").ap()

    def sb(name, shape, dtype=F32):
        return ctx.enter_context(nc.sbuf_tensor(name, shape, dtype))

    xg_d = din("xg", [GRID, D_MODEL])
    xh3_d = din("xh3", [4, D_MODEL])
    maskrow_d = din("maskrow", [128, NSUB])
    pmc_d = din("pmc", [128, 4])
    slotoff_d = din("slotoff", [4, 1], I32)
    idn_d = din("idn", [128, 128])
    cpool_d = din("cpool", [128, 4])
    e064_d = din("e064", [32, 64])
    e0128_d = din("e0128", [32, 128])
    onescol_d = din("onescol", [128, 1])
    hsel_d = din("hsel", [32, 128])
    gmask_d = din("gmask", [32, SCW])
    negio_d = din("negio", [32, 2048])
    tri_d = din("tri", [64, 64])
    dtbrep_d = din("dtbrep", [128, 32])
    dskrep_d = din("dskrep", [128, D_INNER])
    nwrep_d = din("nwrep", [128, D_INNER])
    alog_d = din("alog", [32, 1])
    convw_d = din("convw", [128, 68])
    convb_d = din("convb", [128, 17])
    fcw_d = din("fcw", [128, 96])
    fcb_d = din("fcb", [128, 32])
    wint_d = din("wint", [D_MODEL, D_IN_PROJ])
    wout_d = din("wout", [D_INNER, D_MODEL])
    w1t_d = din("w1t", [D_MODEL, D_HIDDEN])
    w2t_d = din("w2t", [D_HIDDEN, D_MODEL])
    out_d = nc.dram_tensor("out", [SEG, D_MODEL], F32, kind="ExternalOutput").ap()

    zsp_d = dint("zsp", [GRID, D_INNER], BF16)
    x2_d = dint("x2d", [GRID, D_MODEL])
    hpost_d = dint("hpostd", [D_HIDDEN, GRID], BF16)
    acs_d = dint("acsd", [NHEADS, GRID])
    alast_d = dint("alastd", [NCH, NHEADS])
    w1q_d = dint("w1qd", [D_MODEL, D_HIDDEN], BF16)
    w2q_d = dint("w2qd", [D_HIDDEN, D_MODEL], BF16)
    agin_d = dint("agin", [SLOT, SCW])
    agout_d = dint("agout", [4 * SLOT, SCW])
    snap_d = dint("snapd", [33 * 80, SCW], BF16)

    arenaA = ctx.enter_context(nc.sbuf_tensor("arenaA", [128, 69632], U8))
    arenaB = ctx.enter_context(nc.sbuf_tensor("arenaB", [128, 66560], U8))
    arenaC = ctx.enter_context(nc.sbuf_tensor("arenaC", [128, 16384], U8))
    arenaD = ctx.enter_context(nc.sbuf_tensor("arenaD", [128, 8192], U8))

    def overlay(base, name, shape, dtype, off=0):
        h = nc.virtual_sbuf_tensor(name, list(shape), dtype)
        nc._tensor(h.name, h.shape, h.dtype, type="SB")
        mloc = nc.lookup_mloc(h)
        mloc.allocated = True
        mloc.addr = nc.lookup_mloc(base).addr + off
        return h

    xhtm = overlay(arenaA, "xhtm", [128, NSUB * D_INNER], BF16)
    xq1fm = overlay(arenaA, "xq1fm", [128, KC * GRID], BF16)
    qv4 = overlay(arenaA, "qv4", [128, 4096], F32, 0)
    qyi4 = overlay(arenaA, "qyi4", [128, 4096], I32, 16384)
    qt14 = overlay(arenaA, "qt14", [128, 4096], F32, 16384)
    qy4 = overlay(arenaA, "qy4", [128, 4096], F32, 32768)
    qr4 = overlay(arenaA, "qr4", [128, 4096], F32, 49152)
    qv1 = overlay(arenaA, "qv1", [128, 1024], F32, 34816)
    qyi1 = overlay(arenaA, "qyi1", [128, 1024], I32, 38912)
    qt11 = overlay(arenaA, "qt11", [128, 1024], F32, 38912)
    qy1 = overlay(arenaA, "qy1", [128, 1024], F32, 43008)
    qr1 = overlay(arenaA, "qr1", [128, 1024], F32, 47104)
    winb = overlay(arenaB, "winb", [128, KC * D_IN_PROJ], BF16)
    snapbuf = overlay(arenaB, "snapbuf", [128, 2 * SCW], BF16, 0)
    snapld = overlay(arenaB, "snapld", [128, 2 * SCW], F32, 2816)
    woutb = overlay(arenaB, "woutb", [128, 16 * D_MODEL], BF16)
    w1qb = overlay(arenaB, "w1qb", [128, KC * D_HIDDEN], BF16)
    w2qb = overlay(arenaB, "w2qb", [128, 32 * D_MODEL], BF16)
    sz = overlay(arenaB, "sz", [128, D_INNER], F32, 8448)
    ytmp = overlay(arenaB, "ytmp", [128, D_INNER], F32, 16640)
    ytmp2 = overlay(arenaB, "ytmp2", [128, D_INNER], F32, 24832)
    flatbuf = overlay(arenaB, "flatbuf", [32, 2048], F32, 33024)
    wtile = overlay(arenaC, "wtile", [128, 4096], F32)
    wtileB = overlay(arenaA, "wtileB", [128, 4096], F32, 32768)
    htmb = overlay(arenaC, "htmb", [128, D_HIDDEN], BF16)
    wqtile = overlay(arenaD, "wqtile", [128, 4096], BF16)
    xq2tm = overlay(arenaD, "xq2tm", [128, D_HIDDEN], BF16)
    xnf = overlay(arenaC, "xnf", [128, KC * 512], BF16)
    ebuf = overlay(arenaC, "ebuf", [128, D_INNER], F32, 0)
    dmin = overlay(arenaC, "dmin", [128, D_INNER], F32, 8192)
    hconv = overlay(arenaA, "hconv", [128, GRID], F32, 34816)
    hconvB = overlay(arenaA, "hconvB", [128, GRID], F32, 56576)
    h1bufB = overlay(arenaA, "h1bufB", [128, GRID], BF16, 52224)
    h1buf = overlay(arenaA, "h1buf", [128, GRID], BF16, 43520)
    hpostb = overlay(arenaA, "hpostb", [128, GRID], BF16, 47872)
    xtm_t = overlay(arenaD, "xtm_t", [128, D_MODEL], F32, 0)
    xntm = overlay(arenaD, "xntm", [128, D_MODEL], F32, 4096)
    gbuf = overlay(arenaD, "gbuf", [128, D_INNER], BF16, 0)
    xdt = overlay(arenaD, "xdt", [128, D_INNER], BF16, 4096)
    slotb = overlay(arenaC, "slotb", [128, 3 * SCW], F32, 0)
    p30b = overlay(arenaC, "p30b", [128, SCW], F32, 8448)
    tcomb = overlay(arenaC, "tcomb", [128, SCW], F32, 11264)
    ynf = overlay(arenaC, "ynf", [128, D_INNER], BF16, 0)
    motm = overlay(arenaD, "motm", [128, D_MODEL], F32, 0)
    xgtm = overlay(arenaD, "xgtm", [128, D_MODEL], F32, 4096)
    x2n_t = overlay(arenaC, "x2n_t", [128, D_MODEL], F32, 0)
    xq1tm = overlay(arenaC, "xq1tm", [128, D_MODEL], BF16, 4096)
    acspad = overlay(arenaC, "acspad", [32, 64 + GRID], F32, 0)
    srun = overlay(arenaC, "srun", [128, SCW], F32, 0)
    alastall = overlay(arenaB, "alastall", [32, 34 * 32], F32, 44032)
    xh3tm = overlay(arenaA, "xh3tm", [4, D_MODEL], F32, 65536)
    zchunkA = overlay(arenaC, "zchunkA", [128, D_INNER], BF16, 8192)
    convt = overlay(arenaC, "convt", [128, 512], F32, 12288)
    xbca = overlay(arenaC, "xbca", [128, 512], BF16, 14336)
    xbcaB = overlay(arenaC, "xbcaB", [128, 512], BF16, 15360)
    zchunk2 = overlay(arenaB, "zchunk2", [128, D_INNER], BF16, 24832)
    ucomb = overlay(arenaB, "ucomb", [128, SCW], F32, 41216)
    x2ld1 = overlay(arenaD, "x2ld1", [128, D_MODEL], F32, 0)
    x2ld3 = overlay(arenaC, "x2ld3", [128, D_MODEL], F32, 8192)
    fotm = overlay(arenaC, "fotm", [128, D_MODEL], F32, 12288)
    xq2fm = overlay(arenaA, "xq2fm", [128, D_HIDDEN], BF16, 49152)

    idn_s = sb("idn_s", [128, 128])
    cpool_s = sb("cpool_s", [128, 4])
    idnb_s = sb("idnb_s", [128, 128], BF16)
    e064_s = sb("e064_s", [32, 64])
    e0128_s = sb("e0128_s", [32, 128])
    onescol_s = sb("onescol_s", [128, 1])
    hsel_s = sb("hsel_s", [32, 128], BF16)
    gmask_s = sb("gmask_s", [32, SCW], BF16)
    negio_s = sb("negio_s", [32, 2048])
    eacsb = sb("eacsb", [128, 32])
    tri_s = sb("tri_s", [64, 64])
    tri64hi = sb("tri64hi", [128, 64])
    dtbrep_s = sb("dtbrep_s", [128, 32])
    dskrep_s = sb("dskrep_s", [128, D_INNER], BF16)
    nwrep_s = sb("nwrep_s", [128, D_INNER], BF16)
    alog_s = sb("alog_s", [32, 1])
    nega_s = sb("nega_s", [32, 1])
    convw_s = sb("convw_s", [128, 68])
    convb_s = sb("convb_s", [128, 17])
    fcw_s = sb("fcw_s", [128, 96])
    fcb_s = sb("fcb_s", [128, 32])
    maskrow_s = sb("maskrow_s", [128, NSUB])
    pmc_s = sb("pmc_s", [128, 4])
    slotoff_s = sb("slotoff_s", [4, 1], I32)
    dttm = sb("dttm", [128, NSUB * 32])
    acsfm = sb("acsfm", [32, GRID])
    acstm = sb("acstm", [128, NSUB * 32])
    alast_s = sb("alast_s", [32, NCH])
    expal_s = sb("expal_s", [32, NCH])
    gamlog = sb("gamlog", [32, NCH])
    expgam_s = sb("expgam_s", [32, NCH])
    eg_s = sb("eg_s", [32, 2])
    bfm = sb("bfm", [16, GRID], BF16)
    cfm4 = sb("cfm4", [128, GRID], BF16)
    btm = sb("btm", [128, NSUB * 16])
    xh3buf = sb("xh3buf", [128, 17 * 3], BF16)
    xbchal = sb("xbchal", [128, 17 * 3], BF16)
    xnhf = sb("xnhf", [128, KC * 4], BF16)
    alrow = sb("alrow", [32, 2 * 32])
    decaytm = sb("decaytm", [128, 2 * 32])
    bd = sb("bd", [128, 512], BF16)
    m1m = sb("m1m", [128, 64], BF16)
    sin = sb("sin", [128, SCW])
    stbuf = sb("stbuf", [128, SCW], BF16)
    gslot = sb("gslot", [32, 3 * 2])
    gmg = sb("gmg", [32, SCW], BF16)
    rmscol = sb("rmscol", [128, 2])
    stat6 = sb("stat6", [128, 24])
    stat2 = sb("stat2", [128, 2])
    lncol = sb("lncol", [128, 8])
    xbcbuf = sb("xbcbuf", [128, 2 * 516], BF16)
    dtraw = sb("dtraw", [128, 2 * 32])
    qcol = sb("qcol", [128, 8])
    wabs = sb("wabs", [128, 1])
    wsum1 = sb("wsum1", [128, 1])
    scol = sb("scol", [128, 4])
    valbuf = sb("valbuf", [32, 2])
    psc = sb("psc", [1, 8])

    pB1 = ctx.enter_context(nc.psum_tensor("pB1", [128, 2048], F32))
    pB2 = ctx.enter_context(nc.psum_tensor("pB2", [128, 2048], F32))

    def overlay_ps(base, name, shape, dtype, off_bytes):
        h = nc.virtual_psum_tensor(name, list(shape), dtype)
        nc._tensor(h.name, h.shape, h.dtype, type="PSUM")
        mloc = nc.lookup_mloc(h)
        mloc.allocated = True
        mloc.addr = nc.lookup_mloc(base).addr + off_bytes
        return h

    pTb = overlay_ps(pB2, "pTb", [128, 1024], BF16, 6144)

    p = P(nc, ctx)

    def dma(queue, out, in_, sem=None):
        q = sem if sem else ("qg" if queue == "gpsimd" else "qs")
        p.do(queue, lambda e, out=out, in_=in_: e.dma_start(out=out, in_=in_),
             inc=(q, 16))

    def ln_tm(xap, outap, width, part=128):
        nq = width // 512
        for q in range(nq):
            p.do("vector", lambda v, q=q, xap=xap, part=part: v.bn_stats(
                stat6[0:part, 6 * q:6 * q + 6], xap[:, 512 * q:512 * (q + 1)]),
                inc=("sv", 1))
        p.w("vector", "sv")
        p.do("vector", lambda v, nq=nq, part=part: v.bn_aggr(
            stat2[0:part, 0:2], stat6[0:part, 0:6 * nq]), inc=("sv", 1))
        p.w("scalar", "sv")
        p.do("scalar", lambda sc, part=part: sc.activation(
            lncol[0:part, 1:2], stat2[0:part, 1:2], AF.Ln, bias=EPS),
            inc=("sa", 1))
        p.w("scalar", "sa")
        p.do("scalar", lambda sc, part=part: sc.activation(
            lncol[0:part, 1:2], lncol[0:part, 1:2], AF.Exp, scale=-0.5),
            inc=("sa", 1))
        p.w("vector", "sa")
        p.do("vector", lambda v, xap=xap, outap=outap, part=part:
             v.tensor_scalar(outap, xap, stat2[0:part, 0:1],
                             lncol[0:part, 1:2], ALU.subtract, ALU.mult),
             inc=("sv", 1))

    nc.const_aps.register(F32, 1e-5, cpool_s[:, 0:1])
    nc.const_aps.register(F32, 1.0, cpool_s[:, 1:2])
    nc.const_aps.register(F32, -0.5, cpool_s[:, 2:3])
    nc.const_aps.register(F32, 0.0, cpool_s[:, 3:4])
    for dst, src in [(idn_s, idn_d), (cpool_s, cpool_d),
                     (e064_s, e064_d), (e0128_s, e0128_d),
                     (onescol_s, onescol_d), (negio_s, negio_d),
                     (tri_s, tri_d), (dtbrep_s, dtbrep_d),
                     (alog_s, alog_d), (convw_s, convw_d),
                     (convb_s, convb_d), (fcw_s, fcw_d), (fcb_s, fcb_d),
                     (pmc_s, pmc_d), (slotoff_s, slotoff_d),
                     (maskrow_s, maskrow_d), (xh3tm, xh3_d)]:
        dma("sync", dst[:, :], src[:, :])
    dma("gpsimd", dskrep_s[:, :], dskrep_d[:, :])
    dma("gpsimd", nwrep_s[:, :], nwrep_d[:, :])
    dma("gpsimd", hsel_s[:, :], hsel_d[:, :])
    dma("gpsimd", gmask_s[:, :], gmask_d[:, :])
    for kc in range(KC):
        dma("gpsimd", winb[:, kc * D_IN_PROJ:(kc + 1) * D_IN_PROJ],
            wint_d[kc * 128:(kc + 1) * 128, :])
    p.w("vector", "qs")
    p.do("vector", lambda v: v.tensor_copy(idnb_s[:, :], idn_s[:, :]),
         inc=("sv", 1))
    p.do("gpsimd", lambda g: g.memset(xh3buf[:, :], 0.0), inc=("sg", 1))
    p.do("gpsimd", lambda g: g.memset(xbchal[:, :], 0.0), inc=("sg", 1))
    p.do("gpsimd", lambda g: g.memset(tri64hi[:, :], 0.0), inc=("sg", 1))
    p.w("gpsimd", "sg")
    p.do("gpsimd", lambda g: g.iota(
        tri64hi[:, :], [[1, 64]], base=64, channel_multiplier=-1,
        allow_small_or_imprecise_dtypes=True), inc=("sg", 1))
    p.w("vector", "sg")
    p.do("vector", lambda v: v.tensor_scalar(tri64hi[:, :], tri64hi[:, :], 0.0,
                                             None, ALU.is_ge), inc=("sv", 1))
    p.w("scalar", "qs")
    p.do("scalar", lambda sc: sc.activation(nega_s[:, :], alog_s[:, :], AF.Exp),
         inc=("sa", 1))
    p.w("vector", "sa")
    p.do("vector", lambda v: v.tensor_scalar(nega_s[:, :], nega_s[:, :], -1.0,
                                             None, ALU.mult), inc=("sv", 1))

    ln_tm(xh3tm[0:4, :], xh3tm[0:4, :], D_MODEL, part=4)
    p.w("tensor", "sv")
    for kc in range(KC):
        p.do("tensor", lambda t, kc=kc: t.transpose(
            pB1[:, kc * 4:(kc + 1) * 4], xh3tm[0:4, kc * 128:(kc + 1) * 128],
            idn_s[0:4, 0:4]), inc=("sp", 1))
    p.w("vector", "sp")
    p.do("vector", lambda v: v.tensor_copy(xnhf[:, :], pB1[:, 0:KC * 4]),
         inc=("sv", 1))
    sv_xnhf = p.n["sv"]
    p.w("tensor", "sv", sv_xnhf)
    p.w("tensor", "qg")
    for fc in range(17):
        f0 = D_INNER + fc * 128
        fw = min(128, D_IN_PROJ - 32 - f0)
        p.w("tensor", "sv")
        for kc in range(KC):
            p.do("tensor", lambda t, kc=kc, f0=f0, fw=fw, fc=fc: t.matmul(
                pB2[0:fw, 1792 + fc % 2 * 4: 1792 + fc % 2 * 4 + 3],
                winb[:, kc * D_IN_PROJ + f0: kc * D_IN_PROJ + f0 + fw],
                xnhf[:, kc * 4: kc * 4 + 3],
                start=(kc == 0), stop=(kc == KC - 1)), inc=("sp", 1))
        p.w("vector", "sp")
        p.do("vector", lambda v, fc=fc, fw=fw: v.tensor_copy(
            xh3buf[0:fw, fc * 3:fc * 3 + 3],
            pB2[0:fw, 1792 + fc % 2 * 4: 1792 + fc % 2 * 4 + 3]), inc=("sv", 1))

    def wquant(wsrc_d, wdst_d, nrc, ncols, si):
        p.w("gpsimd", "sv")
        p.do("gpsimd", lambda g: g.memset(wsum1[:, :], 0.0), inc=("sg", 1))
        sv_red = {}
        for rc in range(nrc):
            wt = wtile if rc % 2 == 0 else wtileB
            qq = "qw" if rc % 2 == 0 else "qw2"
            if rc >= 2:
                p.w("sync", "sv", sv_red[rc - 2])
            if rc == 0:
                p.w("sync", "sg")
            dma("sync", wt[:, 0:ncols], wsrc_d[rc * 128:(rc + 1) * 128, :],
                sem=qq)
            p.w("vector", qq)
            p.w("vector", "sv")
            if rc == 0:
                p.w("vector", "sg")
            p.do("vector", lambda v, n=ncols, wt=wt: v.tensor_reduce(
                wabs[:, :], wt[:, 0:n], AX.X, ALU.add,
                apply_absolute_value=True), inc=("sv", 1))
            sv_red[rc] = p.n["sv"]
            p.w("vector", "sv")
            p.do("vector", lambda v: v.tensor_tensor(
                wsum1[:, :], wsum1[:, :], wabs[:, :], ALU.add), inc=("sv", 1))
        p.w("tensor", "sv")
        p.do("tensor", lambda t: t.matmul(pB1[0:1, 0:1], onescol_s[:, :],
                                          wsum1[:, :], start=True, stop=True),
             inc=("sp", 1))
        p.w("vector", "sp")
        p.do("vector", lambda v, nrc=nrc, ncols=ncols: v.tensor_scalar(
            psc[0:1, 0:1], pB1[0:1, 0:1], 1.0 / (nrc * 128 * ncols), None,
            ALU.mult), inc=("sv", 1))
        p.w("vector", "sv")
        p.do("vector", lambda v: v.tensor_scalar(psc[0:1, 0:1], psc[0:1, 0:1],
                                                 1e-5, None, ALU.max),
             inc=("sv", 1))
        p.w("vector", "sv")
        p.do("vector", lambda v: v.reciprocal(psc[0:1, 1:2], psc[0:1, 0:1]),
             inc=("sv", 1))
        p.w("gpsimd", "sv")
        p.do("gpsimd", lambda g: g.memset(valbuf[:, :], 0.0), inc=("sg", 1))
        p.w("vector", "sg")
        p.do("vector", lambda v: v.tensor_copy(valbuf[0:1, 0:1], psc[0:1, 1:2]),
             inc=("sv", 1))
        p.w("vector", "sv")
        p.do("vector", lambda v: v.tensor_copy(valbuf[0:1, 1:2], psc[0:1, 0:1]),
             inc=("sv", 1))
        p.w("tensor", "sv")
        p.do("tensor", lambda t: t.matmul(pB1[0:128, 8:10], e0128_s[:, :],
                                          valbuf[:, :], start=True, stop=True),
             inc=("sp", 1))
        p.w("vector", "sp")
        p.do("vector", lambda v, si=si: v.tensor_copy(
            scol[:, si:si + 1], pB1[0:128, 8:9]), inc=("sv", 1))
        p.w("vector", "sv")
        p.do("vector", lambda v, si=si: v.tensor_copy(
            scol[:, si + 1:si + 2], pB1[0:128, 9:10]), inc=("sv", 1))
        sv_q = {}
        for rc in range(nrc):
            wt = wtile if rc % 2 == 0 else wtileB
            qq = "qw" if rc % 2 == 0 else "qw2"
            if rc >= 2:
                p.w("sync", "sv", sv_q[rc - 2])
            else:
                p.w("sync", "sv")
            dma("sync", wt[:, 0:ncols], wsrc_d[rc * 128:(rc + 1) * 128, :],
                sem=qq)
            p.w("vector", qq)
            p.do("vector", lambda v, n=ncols, si=si, wt=wt: v.tensor_scalar(
                qv4[:, 0:n], wt[:, 0:n], scol[:, si:si + 1], None, ALU.mult),
                inc=("sv", 1))
            sv_q[rc] = p.n["sv"]
            p.w("vector", "sv")
            p.do("vector", lambda v, n=ncols: v.tensor_scalar(
                qt14[:, 0:n], qv4[:, 0:n], 0.5, None, ALU.is_gt), inc=("sv", 1))
            p.w("vector", "sv")
            p.do("vector", lambda v, n=ncols: v.scalar_tensor_tensor(
                qr4[:, 0:n], qv4[:, 0:n], -0.5, qt14[:, 0:n], ALU.is_lt,
                ALU.subtract), inc=("sv", 1))
            p.w("vector", "sv")
            p.do("vector", lambda v, n=ncols, si=si: v.tensor_scalar(
                qv4[:, 0:n], qr4[:, 0:n], scol[:, si + 1:si + 2], -1.0,
                ALU.mult, ALU.mult), inc=("sv", 1))
            p.w("vector", "sv")
            p.w("vector", "qe")
            p.do("vector", lambda v, n=ncols: v.tensor_copy(
                wqtile[:, 0:n], qv4[:, 0:n]), inc=("sv", 1))
            sv_here = p.n["sv"]
            p.w("sync", "sv", sv_here)
            dma("sync", wdst_d[rc * 128:(rc + 1) * 128, :], wqtile[:, 0:ncols],
                sem="qe")

    wquant(w1t_d, w1q_d, 8, 4096, 0)
    wquant(w2t_d, w2q_d, 32, 1024, 2)

    p.barrier()
    for (t0, tw) in PA_TILES:
        nsub_t = tw // 128
        tpar = (t0 // 512) % 2
        for si in range(nsub_t):
            g0 = t0 + si * 128
            p.w("sync", "sv")
            dma("sync", xtm_t[:, :], xg_d[g0:g0 + 128, :], sem="q1")
            p.w("vector", "q1")
            ln_tm(xtm_t[:, :], xntm[:, :], D_MODEL)
            p.w("tensor", "sv")
            for kc in range(KC):
                p.do("tensor", lambda t, kc=kc: t.transpose(
                    pB1[:, kc * 128:(kc + 1) * 128],
                    xntm[:, kc * 128:(kc + 1) * 128], idn_s[:, :]),
                    inc=("sp", 1))
            p.w("vector", "sp")
            xnfb = xnf[:, si * 128:si * 128 + 1]
            xnfap = bass.AP(xnfb.tensor, xnfb.offset,
                            [[KC * 512, 128], [512, KC], [1, 128]])
            p.do("vector", lambda v, xnfap=xnfap: v.tensor_copy(
                xnfap, pB1[:, 0:KC * 128]), inc=("sv", 1))
            p.w("tensor", "sv")
        sv_xnf = p.n["sv"]

        sv_xbc = {}
        sv_xh = {}
        sa_silu = {}

        def emit_trans(fcd):
            xb_d = xbca if fcd % 2 == 0 else xbcaB
            p.w("tensor", "sa", sa_silu[fcd])
            if fcd >= 2:
                p.w("tensor", "sv", sv_xh[fcd - 2])
            else:
                p.w("tensor", "sv")
            for si in range(nsub_t):
                p.do("tensor", lambda t, si=si, xb_d=xb_d: t.transpose(
                    pTb[:, si * 128:si * 128 + 128],
                    xb_d[:, si * 128:(si + 1) * 128], idnb_s[:, :]),
                    inc=("sp", 1))
            p.w("vector", "sp")
            xhb = xhtm[:, (t0 // 128) * D_INNER + fcd * 128:
                       (t0 // 128) * D_INNER + fcd * 128 + 1]
            xhap = bass.AP(xhb.tensor, xhb.offset,
                           [[NSUB * D_INNER, 128], [D_INNER, nsub_t],
                            [1, 128]])
            p.do("vector", lambda v, xhap=xhap, nsub_t=nsub_t:
                 v.tensor_copy(xhap, pTb[:, 0:nsub_t * 128]),
                 inc=("sv", 1))
            sv_xh[fcd] = p.n["sv"]

        for fc in range(17):
            f0 = D_INNER + fc * 128
            fw = min(128, D_IN_PROJ - 32 - f0)
            pcol = 512 * (fc % 2)
            xb = xbca if fc % 2 == 0 else xbcaB
            if fc >= 2:
                p.w("tensor", "sv", sv_xbc[fc - 2])
            else:
                p.w("tensor", "sv")
            for kc in range(KC):
                p.do("tensor", lambda t, kc=kc, f0=f0, fw=fw, tw=tw,
                     pcol=pcol: t.matmul(
                         pB2[0:fw, pcol:pcol + tw],
                         winb[:, kc * D_IN_PROJ + f0: kc * D_IN_PROJ + f0 + fw],
                         xnf[:, kc * 512: kc * 512 + tw],
                         start=(kc == 0), stop=(kc == KC - 1)), inc=("sp", 1))
            if fc >= 1:
                emit_trans(fc - 1)
            b0 = 516 * tpar
            p.w("vector", "sp")
            if t0 == 0:
                p.do("vector", lambda v, b0=b0, fc=fc: v.tensor_copy(
                    xbcbuf[:, b0:b0 + 3], xh3buf[:, fc * 3:fc * 3 + 3]),
                    inc=("sv", 1))
            else:
                p.do("vector", lambda v, b0=b0, fc=fc: v.tensor_copy(
                    xbcbuf[:, b0:b0 + 3], xbchal[:, fc * 3:fc * 3 + 3]),
                    inc=("sv", 1))
            p.do("vector", lambda v, b0=b0, fw=fw, tw=tw, pcol=pcol:
                 v.tensor_copy(
                     xbcbuf[0:fw, b0 + 3:b0 + 3 + tw],
                     pB2[0:fw, pcol:pcol + tw]), inc=("sv", 1))
            sv_xbc[fc] = p.n["sv"]
            p.w("vector", "sv")
            p.do("vector", lambda v, b0=b0, tw=tw, fc=fc: v.tensor_copy(
                xbchal[:, fc * 3:fc * 3 + 3], xbcbuf[:, b0 + tw:b0 + tw + 3]),
                inc=("sv", 1))
            p.w("vector", "sv")
            p.w("vector", "sa")
            p.do("vector", lambda v, b0=b0, tw=tw, fc=fc: v.tensor_scalar(
                convt[:, 0:tw], xbcbuf[:, b0:b0 + tw],
                convw_s[:, fc * 4:fc * 4 + 1], None, ALU.mult), inc=("sv", 1))
            for k in (1, 2, 3):
                p.w("vector", "sv")
                p.do("vector", lambda v, b0=b0, tw=tw, fc=fc, k=k:
                     v.scalar_tensor_tensor(
                         convt[:, 0:tw], xbcbuf[:, b0 + k:b0 + k + tw],
                         convw_s[:, fc * 4 + k:fc * 4 + k + 1], convt[:, 0:tw],
                         ALU.mult, ALU.add), inc=("sv", 1))
            p.w("scalar", "sv")
            p.w("scalar", "q2")
            p.do("scalar", lambda sc, tw=tw, fc=fc, xb=xb: sc.activation(
                xb[:, 0:tw], convt[:, 0:tw], AF.Silu,
                bias=convb_s[:, fc:fc + 1]), inc=("sa", 1))
            sa_silu[fc] = p.n["sa"]
            if fc == 16:
                p.w("vector", "sa")
                p.do("vector", lambda v, t0=t0, tw=tw, xb=xb: v.tensor_copy(
                    bfm[:, t0:t0 + tw], xb[0:16, 0:tw]), inc=("sv", 1))
                p.w("sync", "sa")
                for cg in range(3):
                    dma("sync", cfm4[32 * cg:32 * cg + 16, t0:t0 + tw],
                        xb[16:32, 0:tw], sem="q2")
                p.w("tensor", "sa")
                p.w("tensor", "sv")
                for si in range(nsub_t):
                    p.do("tensor", lambda t, si=si, xb=xb: t.transpose(
                        pTb[:, si * 128:si * 128 + 32],
                        xb[0:32, si * 128:(si + 1) * 128], idnb_s[0:32, 0:32]),
                        inc=("sp", 1))
                p.w("vector", "sp")
                ptb0 = pTb[:, 0:1]
                ptbap = bass.AP(ptb0.tensor, ptb0.offset,
                                [[1024, 128], [128, nsub_t], [1, 16]])
                p.do("vector", lambda v, ptbap=ptbap, t0=t0, nsub_t=nsub_t:
                     v.tensor_copy(
                         btm[:, (t0 // 128) * 16:(t0 // 128 + nsub_t) * 16],
                         ptbap), inc=("sv", 1))

        for si in range(nsub_t):
            gsub = t0 // 128 + si
            p.w("tensor", "sv")
            for kc in range(KC):
                for zs in range(4):
                    p.do("tensor", lambda t, kc=kc, si=si, zs=zs: t.matmul(
                        pB1[:, zs * 512:(zs + 1) * 512],
                        xnf[:, kc * 512 + si * 128: kc * 512 + si * 128 + 128],
                        winb[:, kc * D_IN_PROJ + zs * 512:
                             kc * D_IN_PROJ + (zs + 1) * 512],
                        start=(kc == 0), stop=(kc == KC - 1)), inc=("sp", 1))
                p.do("tensor", lambda t, kc=kc, si=si: t.matmul(
                    pB2[:, 1024:1056],
                    xnf[:, kc * 512 + si * 128: kc * 512 + si * 128 + 128],
                    winb[:, kc * D_IN_PROJ + 4128: kc * D_IN_PROJ + 4160],
                    start=(kc == 0), stop=(kc == KC - 1)), inc=("sp", 1))
            p.w("vector", "sp")
            p.w("scalar", "sp")
            p.w("scalar", "q3")
            p.do("scalar", lambda sc: sc.copy(zchunkA[:, :], pB1[:, 0:2048]),
                 inc=("sa", 1))
            sa_z = p.n["sa"]
            p.w("sync", "sa", sa_z)
            dma("sync", zsp_d[gsub * 128:(gsub + 1) * 128, :], zchunkA[:, :],
                sem="q3")
            p.do("vector", lambda v: v.tensor_tensor(
                dtraw[:, 0:32], pB2[:, 1024:1056], dtbrep_s[:, :], ALU.add),
                inc=("sv", 1))
            p.w("scalar", "sv")
            p.do("scalar", lambda sc: sc.activation(dtraw[:, 32:64],
                                                    dtraw[:, 0:32], AF.Exp),
                 inc=("sa", 1))
            p.w("scalar", "sa")
            p.do("scalar", lambda sc, gsub=gsub: sc.activation(
                dttm[:, gsub * 32:(gsub + 1) * 32], dtraw[:, 32:64], AF.Ln,
                bias=1.0), inc=("sa", 1))
            p.w("tensor", "sa")

    p.barrier()
    for gsub in range(NSUB):
        sl = gsub % 8
        p.do("tensor", lambda t, gsub=gsub, sl=sl: t.transpose(
            pB1[0:32, sl * 128:sl * 128 + 128],
            dttm[:, gsub * 32:(gsub + 1) * 32], idn_s[:, :]), inc=("sp", 1))
        p.w("vector", "sp")
        p.do("vector", lambda v, gsub=gsub, sl=sl: v.tensor_copy(
            acspad[:, 64 + gsub * 128:64 + (gsub + 1) * 128],
            pB1[0:32, sl * 128:sl * 128 + 128]), inc=("sv", 1))
        p.w("tensor", "sv")

    p.w("vector", "sv")
    p.do("vector", lambda v: v.tensor_scalar(
        acspad[:, 64:64 + GRID], acspad[:, 64:64 + GRID], nega_s[0:32, 0:1],
        None, ALU.mult), inc=("sv", 1))
    p.w("vector", "sv")
    p.w("gpsimd", "sv")
    p.do("gpsimd", lambda g: g.memset(acspad[:, 0:64], 0.0), inc=("sg", 1))
    p.w("vector", "sg")
    p.w("vector", "sv")
    p.do("vector", lambda v: v.tensor_tensor_scan(
        acspad[:, 64:64 + GRID], acspad[:, 64:64 + GRID],
        acspad[:, 64:64 + GRID], 0.0, ALU.add, ALU.bypass), inc=("sv", 1))
    p.w("vector", "sv")
    off_ap = acspad[:, 63:64]
    p.do("vector", lambda v: v.tensor_tensor(
        acsfm[:, :].rearrange("p (c u) -> p c u", u=64),
        acspad[:, 64:64 + GRID].rearrange("p (c u) -> p c u", u=64),
        bcast_free(off_ap, 32, [[64, NCH], [0, 64]]), ALU.subtract),
        inc=("sv", 1))
    p.w("vector", "sv")
    al_ap = acsfm[:, 63:64]
    p.do("vector", lambda v: v.tensor_copy(
        alast_s[:, :], bcast_free(al_ap, 32, [[64, NCH]])), inc=("sv", 1))
    p.w("scalar", "sv")
    p.do("scalar", lambda sc: sc.activation(expal_s[:, :], alast_s[:, :],
                                            AF.Exp), inc=("sa", 1))
    p.w("gpsimd", "sa")
    p.do("gpsimd", lambda g: g.memset(gamlog[:, :], 0.0), inc=("sg", 1))
    p.w("vector", "sg")
    p.do("vector", lambda v: v.tensor_tensor_scan(
        gamlog[:, 2:34], alast_s[:, 1:33], alast_s[:, 1:33], 0.0,
        ALU.add, ALU.bypass), inc=("sv", 1))
    p.w("scalar", "sv")
    p.do("scalar", lambda sc: sc.activation(expgam_s[:, :], gamlog[:, :],
                                            AF.Exp), inc=("sa", 1))
    p.do("vector", lambda v: v.tensor_reduce(eg_s[:, 0:1], alast_s[:, 1:33],
                                             AX.X, ALU.add), inc=("sv", 1))
    p.w("vector", "sv")
    p.do("vector", lambda v: v.tensor_reduce(eg_s[:, 1:2], alast_s[:, 1:32],
                                             AX.X, ALU.add), inc=("sv", 1))
    p.w("scalar", "sv")
    p.do("scalar", lambda sc: sc.activation(eg_s[:, :], eg_s[:, :], AF.Exp),
         inc=("sa", 1))
    p.w("sync", "sv")
    p.w("sync", "sa")
    dma("sync", acs_d[:, :], acsfm[:, :], sem="q4")
    p.do("sync", lambda e: e.dma_start(out=alast_d[:, :].transpose([1, 0]),
                                       in_=alast_s[:, :],
                                       allow_slow_non_contiguous=True),
         inc=("q4", 16))
    p.w("tensor", "sv")
    for gsub in range(NSUB):
        sl = gsub % 8
        p.do("tensor", lambda t, gsub=gsub, sl=sl: t.transpose(
            pB1[:, sl * 128:sl * 128 + 32],
            acsfm[:, gsub * 128:(gsub + 1) * 128], idn_s[0:32, 0:32]),
            inc=("sp", 1))
        p.w("vector", "sp")
        p.do("vector", lambda v, gsub=gsub, sl=sl: v.tensor_copy(
            acstm[:, gsub * 32:(gsub + 1) * 32],
            pB1[:, sl * 128:sl * 128 + 32]), inc=("sv", 1))
        p.w("tensor", "sv")

    p.barrier()
    p.do("gpsimd", lambda g: g.memset(srun[:, :], 0.0), inc=("sg", 1))
    p.w("vector", "sg")
    p.do("vector", lambda v: v.tensor_copy(pB2[0:80, 1024:1024 + SCW],
                                           srun[0:80, :]), inc=("sv", 1))
    p.w("gpsimd", "sg")
    p.do("gpsimd", lambda g: g.memset(flatbuf[:, :], 0.0), inc=("sg", 1))
    p.w("gpsimd", "sg")
    p.do("gpsimd", lambda g: g.memset(alrow[:, :], 0.0), inc=("sg", 1))
    p.w("gpsimd", "sg")
    p.do("gpsimd", lambda g: g.memset(alastall[:, :], 0.0), inc=("sg", 1))
    p.w("vector", "sg")
    p.w("sync", "sg")
    p.w("sync", "q4")
    ad0 = alast_d[0:1, 0:1]
    p.do("sync", lambda e, ad0=ad0: e.dma_start(
        out=alastall[0:1, :],
        in_=bass.AP(ad0.tensor, ad0.offset, [[1, 1], [1, 34 * 32]])),
        inc=("q5", 16))
    p.w("tensor", "q5")
    p.w("tensor", "sg")

    def emit_xdt(ci):
        par = ci % 2
        gsub = ci // 2
        xsl = xhtm[par * 64:par * 64 + 64, gsub * D_INNER:(gsub + 1) * D_INNER]
        dsl = dttm[par * 64:par * 64 + 64, gsub * 32:(gsub + 1) * 32]
        p.do("vector", lambda v, xsl=xsl, dsl=dsl, par=par: v.tensor_tensor(
            xdt[par * 64:par * 64 + 64, :].rearrange("p (h q) -> p h q", h=32),
            xsl.rearrange("p (h q) -> p h q", h=32),
            bcast_free(dsl[:, 0:1], 64, [[1, 32], [0, 64]]), ALU.mult),
            inc=("sv", 1))

    for ci in range(1, 33):
        buf = ci % 2
        par = ci % 2
        gsub = ci // 2
        qn = "q7" if buf == 0 else "q8"
        emit_xdt(ci)
        p.w("tensor", "sv")
        p0_ = par * 64
        p.do("tensor", lambda t, ci=ci, p0_=p0_: t.matmul(
            pB1[p0_:p0_ + 64, 1920:1952], e064_s[:, :],
            alastall[:, ci * 32:(ci + 1) * 32], start=True, stop=True),
            inc=("sp", 1))
        p.w("vector", "sp")
        asl = acstm[p0_:p0_ + 64, gsub * 32:(gsub + 1) * 32]
        p.do("vector", lambda v, asl=asl, buf=buf, p0_=p0_: v.tensor_tensor(
            decaytm[p0_:p0_ + 64, buf * 32:buf * 32 + 32],
            pB1[p0_:p0_ + 64, 1920:1952], asl, ALU.subtract), inc=("sv", 1))
        p.w("scalar", "sv")
        p.do("scalar", lambda sc, buf=buf, p0_=p0_: sc.activation(
            decaytm[p0_:p0_ + 64, buf * 32:buf * 32 + 32],
            decaytm[p0_:p0_ + 64, buf * 32:buf * 32 + 32], AF.Exp),
            inc=("sa", 1))
        bsl = btm[p0_:p0_ + 64, gsub * 16:(gsub + 1) * 16]
        dcl = decaytm[p0_:p0_ + 64, buf * 32:buf * 32 + 32]
        p.w("vector", "sa")
        p.do("vector", lambda v, bsl=bsl, dcl=dcl, p0_=p0_: v.tensor_tensor(
            bd[p0_:p0_ + 64, :].rearrange("p (h n) -> p h n", h=32),
            bcast_free(dcl[:, 0:1], 64, [[1, 32], [0, 16]]),
            bcast_free(bsl[:, 0:1], 64, [[0, 32], [1, 16]]), ALU.mult),
            inc=("sv", 1))
        p.w("tensor", "sv")
        for h in range(NHEADS):
            hg = h // 11
            hi = h - 11 * hg
            p.do("tensor", lambda t, hg=hg, hi=hi, h=h, p0_=p0_: t.matmul(
                pB2[32 * hg:32 * hg + 16, 1024 + 64 * hi:1024 + 64 * hi + 64],
                bd[p0_:p0_ + 64, h * 16:(h + 1) * 16],
                xdt[p0_:p0_ + 64, h * 64:(h + 1) * 64],
                start=True, stop=True), inc=("sp", 1))
        p.w("vector", qn)
        p.do("vector", lambda v, buf=buf: v.tensor_copy(
            snapbuf[0:80, buf * SCW:buf * SCW + SCW], srun[0:80, :]),
            inc=("sv", 1))
        if ci == 32:
            p.w("vector", "sv")
            p.do("vector", lambda v: v.tensor_copy(tcomb[0:80, :],
                                                   srun[0:80, :]),
                 inc=("sv", 1))
        p.w("sync", "sv")
        dma("sync", snap_d[(ci - 1) * 80:ci * 80, :],
            snapbuf[0:80, buf * SCW:buf * SCW + SCW], sem=qn)
        p.do("vector", lambda v, ci=ci: v.tensor_scalar(
            gmg[:, :], gmask_s[:, :], expal_s[:, ci:ci + 1], None, ALU.mult),
            inc=("sv", 1))
        p.w("tensor", "sv")
        p.do("tensor", lambda t: t.matmul(pB1[0:80, 1024:1536],
                                          hsel_s[:, 0:80], gmg[:, 0:512],
                                          start=True, stop=True),
             inc=("sp", 1))
        p.do("tensor", lambda t: t.matmul(pB1[0:80, 1536:1536 + SCW - 512],
                                          hsel_s[:, 0:80], gmg[:, 512:SCW],
                                          start=True, stop=True),
             inc=("sp", 1))
        p.w("vector", "sp")
        p.do("vector", lambda v: v.tensor_tensor(srun[0:80, :], srun[0:80, :],
                                                 pB1[0:80, 1024:1024 + SCW],
                                                 ALU.mult), inc=("sv", 1))
        p.w("vector", "sv")
        p.do("vector", lambda v: v.tensor_tensor(srun[0:80, :], srun[0:80, :],
                                                 pB2[0:80, 1024:1024 + SCW],
                                                 ALU.add), inc=("sv", 1))
        p.w("tensor", "sv")
    p.w("vector", "sv")
    p.w("vector", "q8")
    p.do("vector", lambda v: v.tensor_copy(snapbuf[0:80, SCW:2 * SCW],
                                           srun[0:80, :]), inc=("sv", 1))
    p.do("gpsimd", lambda g: g.memset(ucomb[0:32, :], 0.0), inc=("sg", 1))
    sv_ag = p.n["sv"]
    p.w("sync", "sv", sv_ag)
    p.w("sync", "sg")
    dma("sync", snap_d[32 * 80:33 * 80, :], snapbuf[0:80, SCW:2 * SCW],
        sem="q8")
    dma("sync", agin_d[0:80, :], srun[0:80, :], sem="q9")
    dma("sync", agin_d[80:160, :], tcomb[0:80, :], sem="q9")
    dma("sync", agin_d[160:192, :], ucomb[0:32, :], sem="q9")
    p.w("gpsimd", "q9")
    dma("gpsimd", agin_d[160:192, 0:2], eg_s[:, :], sem="st")
    p.w("gpsimd", "st")
    p.do("gpsimd", lambda g: g.collective_compute(
        "AllGather", ALU.bypass, replica_groups=[[0, 1, 2, 3], [4, 5, 6, 7]],
        ins=[agin_d[:, :].opt()], outs=[agout_d[:, :].opt()]), inc=("cc", 1))

    def f_load_offsets(e):
        for k in (1, 2, 3):
            p.env[f"so{k}"] = e.value_load(slotoff_s[k - 1:k, 0:1])
    p.w("sync", "cc")
    p.ops["sync"].append(f_load_offsets)
    for k in (1, 2, 3):
        p.do("sync", lambda e, k=k: e.dma_start(
            out=slotb[0:80, (k - 1) * SCW:k * SCW],
            in_=agout_d[bass.ds(p.env[f"so{k}"], 80), :]), inc=("q10", 16))
        p.do("sync", lambda e, k=k: e.dma_start(
            out=gslot[:, (k - 1) * 2:k * 2],
            in_=agout_d[bass.ds(p.env[f"so{k}"] + 160, 32), 0:2]),
            inc=("q10", 16))
    p.do("sync", lambda e: e.dma_start(
        out=p30b[0:80, :], in_=agout_d[bass.ds(p.env["so1"] + 80, 80), :]),
        inc=("q10", 16))

    def grep(gcol_ap):
        p.w("vector", "sp")
        p.do("vector", lambda v, g=gcol_ap: v.tensor_scalar(
            gmg[:, :], gmask_s[:, :], g, None, ALU.mult), inc=("sv", 1))
        p.w("tensor", "sv")
        p.do("tensor", lambda t: t.matmul(
            pB1[0:80, 1024:1536], hsel_s[:, 0:80], gmg[:, 0:512],
            start=True, stop=True), inc=("sp", 1))
        p.do("tensor", lambda t: t.matmul(
            pB1[0:80, 1536:1536 + SCW - 512], hsel_s[:, 0:80], gmg[:, 512:SCW],
            start=True, stop=True), inc=("sp", 1))

    p.w("vector", "q10")
    p.do("vector", lambda v: v.tensor_scalar(
        tcomb[0:80, :], slotb[0:80, 2 * SCW:3 * SCW], pmc_s[0:80, 2:3], None,
        ALU.mult), inc=("sv", 1))
    grep(gslot[:, 2:3])
    p.w("vector", "sp")
    p.w("vector", "sv")
    p.do("vector", lambda v: v.tensor_tensor(ucomb[0:80, :], tcomb[0:80, :],
                                             pB1[0:80, 1024:1024 + SCW],
                                             ALU.mult), inc=("sv", 1))
    p.w("vector", "sv")
    p.do("vector", lambda v: v.tensor_tensor(ucomb[0:80, :], ucomb[0:80, :],
                                             slotb[0:80, SCW:2 * SCW],
                                             ALU.add), inc=("sv", 1))
    p.w("vector", "sv")
    p.do("vector", lambda v: v.tensor_scalar(tcomb[0:80, :], ucomb[0:80, :],
                                             pmc_s[0:80, 1:2], None, ALU.mult),
         inc=("sv", 1))
    grep(gslot[:, 0:1])
    p.w("vector", "sp")
    p.w("vector", "sv")
    p.do("vector", lambda v: v.tensor_tensor(ucomb[0:80, :], tcomb[0:80, :],
                                             pB1[0:80, 1024:1024 + SCW],
                                             ALU.mult), inc=("sv", 1))
    p.w("vector", "sv")
    p.do("vector", lambda v: v.tensor_tensor(ucomb[0:80, :], ucomb[0:80, :],
                                             slotb[0:80, 0:SCW], ALU.add),
         inc=("sv", 1))
    p.w("vector", "sv")
    p.do("vector", lambda v: v.tensor_scalar(sin[0:80, :], ucomb[0:80, :],
                                             pmc_s[0:80, 0:1], None, ALU.mult),
         inc=("sv", 1))
    grep(gslot[:, 1:2])
    p.w("vector", "sp")
    p.w("vector", "sv")
    p.do("vector", lambda v: v.tensor_tensor(ucomb[0:80, :], tcomb[0:80, :],
                                             pB1[0:80, 1024:1024 + SCW],
                                             ALU.mult), inc=("sv", 1))
    p.w("vector", "sv")
    p.do("vector", lambda v: v.tensor_tensor(ucomb[0:80, :], ucomb[0:80, :],
                                             p30b[0:80, :], ALU.add),
         inc=("sv", 1))
    p.w("vector", "sv")
    p.do("vector", lambda v: v.tensor_scalar(stbuf[0:80, :], ucomb[0:80, :],
                                             pmc_s[0:80, 3:4], None, ALU.mult),
         inc=("sv", 1))

    p.barrier()
    for ci in range(NCH):
        par = ci % 2
        gsub = ci // 2
        buf = ci % 2
        p0_ = par * 64
        p.w("sync", "sp")
        fl0 = acs_d[0:1, ci * 64:ci * 64 + 1]
        p.do("sync", lambda e, fl0=fl0: e.dma_start(
            out=flatbuf[0:1, 0:2048],
            in_=bass.AP(fl0.tensor, fl0.offset, [[1, 1], [GRID, 32], [1, 64]])),
            inc=("q11", 16))
        p.w("tensor", "q11")
        p.w("tensor", "sv")
        p.do("tensor", lambda t, ci=ci, p0_=p0_: t.matmul(
            pB2[p0_:p0_ + 64, 1024:1088],
            bfm[:, ci * 64:(ci + 1) * 64], cfm4[0:16, ci * 64:(ci + 1) * 64],
            start=True, stop=True), inc=("sp", 1))
        p.w("vector", "sp")
        p.do("vector", lambda v, p0_=p0_: v.tensor_tensor(
            m1m[p0_:p0_ + 64, :], pB2[p0_:p0_ + 64, 1024:1088],
            tri_s[:, :] if p0_ == 0 else tri64hi[64:128, :], ALU.mult),
             inc=("sv", 1))
        p.w("tensor", "sv")
        for q in range(4):
            p.do("tensor", lambda t, q=q, p0_=p0_: t.matmul(
                pB1[p0_:p0_ + 64, q * 512:(q + 1) * 512], e064_s[:, :],
                flatbuf[:, q * 512:(q + 1) * 512],
                start=True, stop=False), inc=("sp", 1))
            p.do("tensor", lambda t, q=q, ci=ci, p0_=p0_: t.matmul(
                pB1[p0_:p0_ + 64, q * 512:(q + 1) * 512],
                acsfm[:, ci * 64:(ci + 1) * 64],
                negio_s[:, q * 512:(q + 1) * 512],
                start=False, stop=True), inc=("sp", 1))
        p.w("vector", "sp")
        p.do("vector", lambda v, p0_=p0_: v.tensor_scalar(
            dmin[p0_:p0_ + 64, :], pB1[p0_:p0_ + 64, 0:2048],
            0.0, None, ALU.min), inc=("sv", 1))
        p.w("scalar", "sv")
        p.do("scalar", lambda sc, p0_=p0_: sc.activation(
            ebuf[p0_:p0_ + 64, :], dmin[p0_:p0_ + 64, :], AF.Exp),
            inc=("sa", 1))
        p.w("vector", "sa")
        p.do("vector", lambda v, p0_=p0_: v.tensor_tensor(
            gbuf[p0_:p0_ + 64, :].rearrange("p (h l) -> p h l", h=32),
            ebuf[p0_:p0_ + 64, :].rearrange("p (h l) -> p h l", h=32),
            bcast_free(m1m[p0_:p0_ + 64, 0:1], 64, [[0, 32], [1, 64]]),
            ALU.mult), inc=("sv", 1))
        if ci + 1 <= NCH - 1:
            qm2 = "q14" if (ci + 1) % 2 == 0 else "q15"
            p.w("gpsimd", "sv")
            p.do("gpsimd", lambda g, ci=ci: g.dma_start(
                out=snapld[0:80, ((ci + 1) % 2) * SCW:((ci + 1) % 2 + 1) * SCW],
                in_=snap_d[ci * 80:(ci + 1) * 80, :]), inc=(qm2, 16))
        if ci == 0:
            pass
        else:
            grep(expgam_s[:, ci:ci + 1])
            p.w("vector", "sp")
            p.do("vector", lambda v: v.tensor_tensor(
                ucomb[0:80, :], sin[0:80, :], pB1[0:80, 1024:1024 + SCW],
                ALU.mult), inc=("sv", 1))
            p.w("vector", "sv")
            p.w("vector", "q14" if ci % 2 == 0 else "q15")
            p.do("vector", lambda v, ci=ci: v.tensor_tensor(
                stbuf[0:80, :], ucomb[0:80, :],
                snapld[0:80, (ci % 2) * SCW:(ci % 2 + 1) * SCW], ALU.add),
                inc=("sv", 1))
        emit_xdt(ci)
        asl2 = acstm[p0_:p0_ + 64, gsub * 32:(gsub + 1) * 32]
        p.w("tensor", "sv")
        for h in range(NHEADS):
            pp = 32 * (h % 4)
            ff = 64 * (h // 4)
            p.do("tensor", lambda t, h=h, p0_=p0_: t.matmul(
                pB2[p0_:p0_ + 64, h * 64:(h + 1) * 64],
                gbuf[p0_:p0_ + 64, h * 64:(h + 1) * 64],
                xdt[p0_:p0_ + 64, h * 64:(h + 1) * 64],
                start=True, stop=True), inc=("sp", 1))
            hg = h // 11
            hi = h - 11 * hg
            p.do("tensor", lambda t, h=h, hg=hg, hi=hi, ci=ci, p0_=p0_: t.matmul(
                pB1[p0_:p0_ + 64, h * 64:(h + 1) * 64],
                cfm4[32 * hg:32 * hg + 16, ci * 64:(ci + 1) * 64],
                stbuf[32 * hg:32 * hg + 16, 64 * hi:64 * hi + 64],
                start=True, stop=True), inc=("sp", 1))
        p.w("sync", "sv")
        p.w("sync", "sa")
        qz = "q12" if par == 0 else "q13"
        dma("sync", zchunk2[p0_:p0_ + 64, :], zsp_d[ci * 64:(ci + 1) * 64, :],
            sem=qz)
        p.w("scalar", qz)
        p.w("scalar", "sv")
        p.do("scalar", lambda sc, p0_=p0_: sc.activation(
            sz[p0_:p0_ + 64, :], zchunk2[p0_:p0_ + 64, :], AF.Silu),
            inc=("sa", 1))
        p.do("scalar", lambda sc, asl2=asl2, p0_=p0_: sc.activation(
            eacsb[p0_:p0_ + 64, 0:32], asl2, AF.Exp), inc=("sa", 1))
        p.w("vector", "sa")
        p.do("vector", lambda v, p0_=p0_, gsub=gsub: v.tensor_tensor(
            ytmp[p0_:p0_ + 64, :],
            xhtm[p0_:p0_ + 64, gsub * D_INNER:(gsub + 1) * D_INNER],
            dskrep_s[p0_:p0_ + 64, :], ALU.mult), inc=("sv", 1))
        p.w("vector", "sp")
        p.w("vector", "sv")
        p.do("vector", lambda v, p0_=p0_: v.tensor_tensor(
            ytmp[p0_:p0_ + 64, :], ytmp[p0_:p0_ + 64, :],
            pB2[p0_:p0_ + 64, 0:2048], ALU.add), inc=("sv", 1))
        p.w("vector", "sv")
        p.do("vector", lambda v, p0_=p0_: v.tensor_tensor(
            ytmp2[p0_:p0_ + 64, :].rearrange("p (h q) -> p h q", h=32),
            pB1[p0_:p0_ + 64, 0:2048].rearrange("p (h q) -> p h q", h=32),
            bcast_free(eacsb[p0_:p0_ + 64, 0:1], 64, [[1, 32], [0, 64]]),
            ALU.mult), inc=("sv", 1))
        p.w("vector", "sv")
        p.do("vector", lambda v, p0_=p0_: v.tensor_tensor(
            ytmp[p0_:p0_ + 64, :], ytmp[p0_:p0_ + 64, :],
            ytmp2[p0_:p0_ + 64, :], ALU.add), inc=("sv", 1))
        p.w("vector", "sv")
        p.do("vector", lambda v, p0_=p0_: v.tensor_tensor(
            ytmp[p0_:p0_ + 64, :], ytmp[p0_:p0_ + 64, :], sz[p0_:p0_ + 64, :],
            ALU.mult), inc=("sv", 1))
        p.w("vector", "sv")
        for q in range(4):
            p.do("vector", lambda v, q=q, p0_=p0_: v.bn_stats(
                stat6[p0_:p0_ + 64, 6 * q:6 * q + 6],
                ytmp[p0_:p0_ + 64, 512 * q:512 * (q + 1)]), inc=("sv", 1))
        p.w("vector", "sv")
        p.do("vector", lambda v, p0_=p0_: v.bn_aggr(
            stat2[p0_:p0_ + 64, 0:2], stat6[p0_:p0_ + 64, 0:24]),
            inc=("sv", 1))
        p.w("vector", "sv")
        p.do("vector", lambda v, p0_=p0_: v.tensor_tensor(
            rmscol[p0_:p0_ + 64, 0:1], stat2[p0_:p0_ + 64, 0:1],
            stat2[p0_:p0_ + 64, 0:1], ALU.mult), inc=("sv", 1))
        p.w("vector", "sv")
        p.do("vector", lambda v, p0_=p0_: v.tensor_tensor(
            rmscol[p0_:p0_ + 64, 0:1], rmscol[p0_:p0_ + 64, 0:1],
            stat2[p0_:p0_ + 64, 1:2], ALU.add), inc=("sv", 1))
        p.w("scalar", "sv")
        p.do("scalar", lambda sc, p0_=p0_: sc.activation(
            rmscol[p0_:p0_ + 64, 1:2], rmscol[p0_:p0_ + 64, 0:1], AF.Ln,
            bias=EPS), inc=("sa", 1))
        p.w("scalar", "sa")
        p.do("scalar", lambda sc, p0_=p0_: sc.activation(
            rmscol[p0_:p0_ + 64, 1:2], rmscol[p0_:p0_ + 64, 1:2], AF.Exp,
            scale=-0.5), inc=("sa", 1))
        p.w("vector", "sa")
        p.do("vector", lambda v, p0_=p0_, gsub=gsub: v.scalar_tensor_tensor(
            xhtm[p0_:p0_ + 64, gsub * D_INNER:(gsub + 1) * D_INNER],
            ytmp[p0_:p0_ + 64, :], rmscol[p0_:p0_ + 64, 1:2],
            nwrep_s[p0_:p0_ + 64, :], ALU.mult, ALU.mult), inc=("sv", 1))
        p.w("tensor", "sv")

    p.barrier()
    for kc16 in range(16):
        dma("gpsimd", woutb[:, kc16 * D_MODEL:(kc16 + 1) * D_MODEL],
            wout_d[kc16 * 128:(kc16 + 1) * 128, :], sem="q16")
    p.w("tensor", "q16")
    sa_motm = {}
    for su in range(NSUB):
        p.w("tensor", "sv")
        for half in range(2):
            for sl in range(8):
                fc = half * 8 + sl
                p.do("tensor", lambda t, su=su, fc=fc, sl=sl: t.transpose(
                    pTb[:, sl * 128:sl * 128 + 128],
                    xhtm[:, su * D_INNER + fc * 128:
                         su * D_INNER + fc * 128 + 128],
                    idnb_s[:, :]), inc=("sp", 1))
            p.w("vector", "sp")
            p.do("vector", lambda v, half=half: v.tensor_copy(
                ynf[:, half * 1024:(half + 1) * 1024], pTb[:, 0:1024]),
                inc=("sv", 1))
            p.w("tensor", "sv")
        pp3 = pB1 if su % 2 == 0 else pB2
        if su >= 2:
            p.w("tensor", "sa", sa_motm[su - 2])
        for fc in range(16):
            for q in range(2):
                p.do("tensor", lambda t, fc=fc, q=q, pp3=pp3: t.matmul(
                    pp3[:, q * 512:(q + 1) * 512],
                    ynf[:, fc * 128:(fc + 1) * 128],
                    woutb[:, fc * D_MODEL + q * 512: fc * D_MODEL + (q + 1) * 512],
                    start=(fc == 0), stop=(fc == 15)), inc=("sp", 1))
        p.w("scalar", "sp")
        p.w("scalar", "sv")
        p.do("scalar", lambda sc, pp3=pp3: sc.copy(motm[:, :], pp3[:, 0:1024]),
             inc=("sa", 1))
        sa_motm[su] = p.n["sa"]
        p.w("vector", "sa")
        p.w("vector", "sv")
        ln_tm(motm[:, :], motm[:, :], D_MODEL)
        p.w("sync", "sv")
        p.w("sync", "q18")
        dma("sync", xgtm[:, :], xg_d[su * 128:(su + 1) * 128, :], sem="q17")
        p.w("vector", "q17")
        p.do("vector", lambda v: v.tensor_tensor(xgtm[:, :], motm[:, :],
                                                 xgtm[:, :], ALU.add),
             inc=("sv", 1))
        p.w("vector", "sv")
        p.do("vector", lambda v, su=su: v.tensor_scalar(
            xgtm[:, :], xgtm[:, :], maskrow_s[:, su:su + 1], None, ALU.mult),
            inc=("sv", 1))
        sv_x2 = p.n["sv"]
        p.w("sync", "sv", sv_x2)
        dma("sync", x2_d[su * 128:(su + 1) * 128, :], xgtm[:, :], sem="q18")

    p.barrier()
    for su in range(NSUB):
        p.w("sync", "sv")
        dma("sync", x2ld1[:, :], x2_d[su * 128:(su + 1) * 128, :], sem="q19")
        p.w("vector", "q19")
        ln_tm(x2ld1[:, :], x2n_t[:, :], D_MODEL)
        p.w("vector", "sv")
        p.do("vector", lambda v: v.tensor_reduce(
            qcol[:, 0:1], x2n_t[:, 0:1024], AX.X, ALU.max,
            apply_absolute_value=True), inc=("sv", 1))
        p.w("vector", "sv")
        p.do("vector", lambda v: v.tensor_scalar(qcol[:, 0:1], qcol[:, 0:1],
                                                 1e-5, None, ALU.max),
             inc=("sv", 1))
        p.w("vector", "sv")
        p.do("vector", lambda v: v.reciprocal(qcol[:, 1:2], qcol[:, 0:1]),
             inc=("sv", 1))
        p.w("vector", "sv")
        p.do("vector", lambda v: v.tensor_scalar(qcol[:, 1:2], qcol[:, 1:2],
                                                 127.0, None, ALU.mult),
             inc=("sv", 1))
        p.do("vector", lambda v: v.tensor_scalar(qcol[:, 2:3], qcol[:, 0:1],
                                                 1.0 / 127.0, None, ALU.mult),
             inc=("sv", 1))
        p.w("vector", "sv")
        p.do("vector", lambda v: v.tensor_scalar(qv1[:, :], x2n_t[:, :],
                                                 qcol[:, 1:2], None, ALU.mult),
             inc=("sv", 1))
        p.w("vector", "sv")
        p.do("vector", lambda v: v.tensor_copy(qyi1[:, :], qv1[:, :]),
             inc=("sv", 1))
        p.w("vector", "sv")
        p.do("vector", lambda v: v.tensor_copy(qy1[:, :], qyi1[:, :]),
             inc=("sv", 1))
        p.w("vector", "sv")
        p.do("vector", lambda v: v.tensor_tensor(qr1[:, :], qv1[:, :],
                                                 qy1[:, :], ALU.subtract),
             inc=("sv", 1))
        p.w("vector", "sv")
        p.do("vector", lambda v: v.scalar_tensor_tensor(
            qt11[:, :], qr1[:, :], 0.5, qy1[:, :], ALU.is_gt,
            ALU.add), inc=("sv", 1))
        p.w("vector", "sv")
        p.do("vector", lambda v: v.scalar_tensor_tensor(
            qv1[:, :], qr1[:, :], -0.5, qt11[:, :], ALU.is_ge,
            ALU.add), inc=("sv", 1))
        p.w("vector", "sv")
        p.do("vector", lambda v: v.tensor_scalar(
            xq1tm[:, :], qv1[:, :], -1.0, qcol[:, 2:3], ALU.add, ALU.mult),
            inc=("sv", 1))
        p.w("tensor", "sv")
        for kc in range(KC):
            sl = kc % 8
            p.do("tensor", lambda t, kc=kc, sl=sl: t.transpose(
                pTb[:, sl * 128:sl * 128 + 128],
                xq1tm[:, kc * 128:(kc + 1) * 128], idnb_s[:, :]), inc=("sp", 1))
        p.w("vector", "sp")
        xq1b = xq1fm[:, su * 128:su * 128 + 1]
        xq1ap = bass.AP(xq1b.tensor, xq1b.offset,
                        [[KC * GRID, 128], [GRID, KC], [1, 128]])
        p.do("vector", lambda v, xq1ap=xq1ap: v.tensor_copy(
            xq1ap, pTb[:, 0:1024]), inc=("sv", 1))

    p.barrier()
    for kc in range(KC):
        dma("sync", w1qb[:, kc * D_HIDDEN:(kc + 1) * D_HIDDEN],
            w1q_d[kc * 128:(kc + 1) * 128, :], sem="q20")
    p.w("tensor", "q20")
    sa_tile0 = {}
    sa_end = {}
    for hf in range(32):
        pp = pB1 if hf % 2 == 0 else pB2
        hb = h1buf if hf % 2 == 0 else h1bufB
        hc = hconv if hf % 2 == 0 else hconvB
        if hf >= 2:
            p.w("tensor", "sa", sa_end[hf - 2])
        for q, (t0, tw) in enumerate(PA_TILES):
            reg = q % 4
            for kc in range(KC):
                p.do("tensor", lambda t, hf=hf, kc=kc, t0=t0, tw=tw, reg=reg,
                     pp=pp: t.matmul(
                         pp[:, reg * 512: reg * 512 + tw],
                         w1qb[:, kc * D_HIDDEN + hf * 128:
                              kc * D_HIDDEN + hf * 128 + 128],
                         xq1fm[:, kc * GRID + t0: kc * GRID + t0 + tw],
                         start=(kc == 0), stop=(kc == KC - 1)), inc=("sp", 1))
            p.w("scalar", "sp")
            if q == 0:
                p.w("scalar", "sv")
            p.do("scalar", lambda sc, t0=t0, tw=tw, reg=reg, pp=pp, hb=hb:
                 sc.copy(hb[:, t0:t0 + tw],
                         pp[:, reg * 512: reg * 512 + tw]), inc=("sa", 1))
            if q == 0:
                sa_tile0[hf] = p.n["sa"]
            if reg == 3:
                p.w("tensor", "sa", sa_tile0[hf])
        sa_end[hf] = p.n["sa"]
        p.w("vector", "sv")
        p.w("vector", "sa")
        p.do("vector", lambda v, hf=hf, hb=hb, hc=hc: v.tensor_scalar(
            hc[:, :], hb[:, :], fcw_s[:, hf * 3 + 1:hf * 3 + 2], None,
            ALU.mult), inc=("sv", 1))
        p.w("vector", "sv")
        p.do("vector", lambda v, hf=hf, hb=hb, hc=hc: v.scalar_tensor_tensor(
            hc[:, 1:GRID], hb[:, 0:GRID - 1],
            fcw_s[:, hf * 3:hf * 3 + 1], hc[:, 1:GRID], ALU.mult, ALU.add),
            inc=("sv", 1))
        p.w("vector", "sv")
        p.do("vector", lambda v, hf=hf, hb=hb, hc=hc: v.scalar_tensor_tensor(
            hc[:, 0:GRID - 1], hb[:, 1:GRID],
            fcw_s[:, hf * 3 + 2:hf * 3 + 3], hc[:, 0:GRID - 1], ALU.mult,
            ALU.add), inc=("sv", 1))
        p.w("scalar", "sv")
        p.w("scalar", "q21")
        p.do("scalar", lambda sc, hf=hf, hc=hc: sc.activation(
            hpostb[:, :], hc[:, :], AF.Silu, bias=fcb_s[:, hf:hf + 1]),
            inc=("sa", 1))
        sa_here = p.n["sa"]
        p.w("sync", "sa", sa_here)
        dma("sync", hpost_d[hf * 128:(hf + 1) * 128, :], hpostb[:, :],
            sem="q21")

    p.barrier()
    for q32 in range(32):
        dma("sync", w2qb[:, q32 * D_MODEL:(q32 + 1) * D_MODEL],
            w2q_d[q32 * 128:(q32 + 1) * 128, :], sem="q22")
    p.w("tensor", "q22")
    for su in range(NSUB):
        p.w("sync", "sv")
        hp0 = hpost_d[0:1, su * 128:su * 128 + 1]
        p.do("sync", lambda e, hp0=hp0: e.dma_start(
            out=xq2fm[:, :],
            in_=bass.AP(hp0.tensor, hp0.offset,
                        [[GRID, 128], [128 * GRID, 32], [1, 128]])),
            inc=("q23", 16))
        p.w("tensor", "q23")
        p.w("tensor", "sv")
        for half in range(4):
            for sl in range(8):
                fc = half * 8 + sl
                p.do("tensor", lambda t, fc=fc, sl=sl: t.transpose(
                    pTb[:, sl * 128:sl * 128 + 128],
                    xq2fm[:, fc * 128:(fc + 1) * 128], idnb_s[:, :]),
                    inc=("sp", 1))
            p.w("vector", "sp")
            p.do("vector", lambda v, half=half: v.tensor_copy(
                htmb[:, half * 1024:(half + 1) * 1024], pTb[:, 0:1024]),
                inc=("sv", 1))
            p.w("tensor", "sv")
        p.w("vector", "sv")
        p.do("vector", lambda v: v.tensor_reduce(
            qcol[:, 4:5], htmb[:, :], AX.X, ALU.max,
            apply_absolute_value=True), inc=("sv", 1))
        p.w("vector", "sv")
        p.do("vector", lambda v: v.tensor_scalar(qcol[:, 4:5], qcol[:, 4:5],
                                                 1e-5, None, ALU.max),
             inc=("sv", 1))
        p.w("vector", "sv")
        p.do("vector", lambda v: v.reciprocal(qcol[:, 5:6], qcol[:, 4:5]),
             inc=("sv", 1))
        p.w("vector", "sv")
        p.do("vector", lambda v: v.tensor_scalar(qcol[:, 5:6], qcol[:, 5:6],
                                                 127.0, None, ALU.mult),
             inc=("sv", 1))
        p.do("vector", lambda v: v.tensor_scalar(qcol[:, 6:7], qcol[:, 4:5],
                                                 1.0 / 127.0, None, ALU.mult),
             inc=("sv", 1))
        p.w("vector", "sv")
        p.do("vector", lambda v: v.tensor_scalar(qv4[:, :], htmb[:, :],
                                                 qcol[:, 5:6], None, ALU.mult),
             inc=("sv", 1))
        p.w("vector", "sv")
        p.do("vector", lambda v: v.tensor_copy(qyi4[:, :], qv4[:, :]),
             inc=("sv", 1))
        p.w("vector", "sv")
        p.do("vector", lambda v: v.tensor_copy(qy4[:, :], qyi4[:, :]),
             inc=("sv", 1))
        p.w("vector", "sv")
        p.do("vector", lambda v: v.tensor_tensor(qr4[:, :], qv4[:, :], qy4[:, :],
                                                 ALU.subtract), inc=("sv", 1))
        p.w("vector", "sv")
        p.do("vector", lambda v: v.scalar_tensor_tensor(
            qt14[:, :], qr4[:, :], 0.5, qy4[:, :], ALU.is_gt, ALU.add),
            inc=("sv", 1))
        p.w("vector", "sv")
        p.do("vector", lambda v: v.scalar_tensor_tensor(
            qv4[:, :], qr4[:, :], -0.5, qt14[:, :], ALU.is_ge, ALU.add),
            inc=("sv", 1))
        p.w("vector", "sv")
        p.do("vector", lambda v: v.tensor_scalar(xq2tm[:, :], qv4[:, :], -1.0,
                                                 None, ALU.add), inc=("sv", 1))
        p.w("tensor", "sv")
        for half in range(4):
            for sl in range(8):
                fc = half * 8 + sl
                p.do("tensor", lambda t, fc=fc, sl=sl: t.transpose(
                    pTb[:, sl * 128:sl * 128 + 128],
                    xq2tm[:, fc * 128:(fc + 1) * 128], idnb_s[:, :]),
                    inc=("sp", 1))
            p.w("vector", "sp")
            p.do("vector", lambda v, half=half: v.tensor_copy(
                xq2fm[:, half * 1024:(half + 1) * 1024], pTb[:, 0:1024]),
                inc=("sv", 1))
            p.w("tensor", "sv")
        p.w("tensor", "sv")
        for fc in range(32):
            for q in range(2):
                p.do("tensor", lambda t, fc=fc, q=q: t.matmul(
                    pB1[:, q * 512:(q + 1) * 512],
                    xq2fm[:, fc * 128:(fc + 1) * 128],
                    w2qb[:, fc * D_MODEL + q * 512: fc * D_MODEL + (q + 1) * 512],
                    start=(fc == 0), stop=(fc == 31)), inc=("sp", 1))
        p.w("scalar", "sp")
        p.w("scalar", "q25")
        p.w("scalar", "sv")
        p.do("scalar", lambda sc: sc.activation(fotm[:, :], pB1[:, 0:1024],
                                                AF.Copy, scale=qcol[:, 6:7]),
             inc=("sa", 1))
        p.w("vector", "sa")
        ln_tm(fotm[:, :], fotm[:, :], D_MODEL)
        p.w("sync", "sv")
        dma("sync", x2ld3[:, :], x2_d[su * 128:(su + 1) * 128, :], sem="q24")
        p.w("vector", "q24")
        p.do("vector", lambda v: v.tensor_tensor(fotm[:, :], fotm[:, :],
                                                 x2ld3[:, :], ALU.add),
             inc=("sv", 1))
        sv_out = p.n["sv"]
        p.w("sync", "sv", sv_out)
        if su == 0:
            dma("sync", out_d[0:64, :], fotm[64:128, :], sem="q25")
        elif su == NSUB - 1:
            dma("sync", out_d[SEG - 64:SEG, :], fotm[0:64, :], sem="q25")
        else:
            dma("sync", out_d[su * 128 - 64:(su + 1) * 128 - 64, :],
                fotm[:, :], sem="q25")
        p.w("tensor", "sv")

    p.w("sync", "q25")

    with nc.Block() as block:
        p.emit(block)
    ctx.close()
    return nc



def kernel(**inputs):
    x = np.ascontiguousarray(np.asarray(inputs["L0_attn_x"], np.float32))
    shards = _host_shards(x)
    consts = _host_consts(inputs)
    nc = build_nc()
    in_maps = []
    for core in range(8):
        m = dict(consts)
        m.update(shards[core])
        in_maps.append(m)
    res = run_bass_kernel_spmd(nc, in_maps, core_ids=list(range(8)))
    out = np.zeros((2, SEQ, D_MODEL), np.float32)
    for core in range(8):
        b, j = core // NSEG, core % NSEG
        out[b, j * SEG:(j + 1) * SEG] = res.results[core]["out"]
    return out
